# version 10
# baseline (speedup 1.0000x reference)
import os
import numpy as np
import ml_dtypes
from contextlib import ExitStack

import concourse.bass as bass
import concourse.mybir as mybir
from concourse.bass_utils import run_bass_kernel_spmd

F32 = mybir.dt.float32
BF16 = mybir.dt.bfloat16
U16 = mybir.dt.uint16
AF = mybir.ActivationFunctionType
ALU = mybir.AluOpType
AXL = mybir.AxisListType

NCORES = 8
B = 8
TL = 256
NEG = -1.0e30
SWAP16 = list(range(16, 32)) + list(range(16))

_PERM = np.concatenate([np.arange(0, 64), np.arange(128, 192),
                        np.arange(64, 128), np.arange(192, 256)])
_IPERM = np.argsort(_PERM)


def _dperm():
    p = np.arange(64)
    q16 = p // 16
    j = (q16 // 2) * 16 + p % 16
    return 2 * j + q16 % 2


_DPERM = _dperm()


def _kperm():
    out = np.zeros(1024, np.int64)
    for hp in range(8):
        for p in range(128):
            out[hp * 128 + p] = (2 * hp + p // 64) * 64 + _DPERM[p % 64]
    return out


_KPERM = _kperm()


def _bf(a):
    return np.ascontiguousarray(a.astype(ml_dtypes.bfloat16).view(np.uint16))


def _host_constants(core_id):
    c = {}
    c["ident"] = np.eye(128, dtype=np.float32)
    c["ident_bf"] = _bf(np.eye(128, dtype=np.float32))
    c["ones_bf"] = _bf(np.ones((128, 128), np.float32))
    ne = np.zeros((8, 8, 64), np.float32)
    for k in range(8):
        ne[k, k, :] = -1.0
    c["negeye"] = _bf(ne)
    ind2 = np.zeros((128, 2), np.float32)
    ind2[:64, 0] = 1.0
    ind2[64:, 1] = 1.0
    c["ind2"] = _bf(ind2)
    c["ind2T"] = _bf(ind2.T.copy())
    l_of_p = np.arange(128) % 64
    cc = np.arange(64)
    c["mask_lc"] = np.where(cc[None, :] <= l_of_p[:, None], 0.0, NEG).astype(np.float32)
    c["mask_T"] = np.where(cc[:, None] <= l_of_p[None, :], 0.0, NEG).astype(np.float32)
    p = np.arange(128) % 64
    q16 = p // 16
    j = (q16 // 2) * 16 + p % 16
    is_b = q16 % 2
    inv = 1.0 / 10000.0 ** (np.arange(0, 64, 2, dtype=np.float32) / 64.0)
    c2 = np.arange(128)
    pos = core_id * 256 + (c2 // 64) * 128 + c2 % 64
    ang = pos[None, :].astype(np.float32) * inv[j][:, None]
    c["cosF"] = np.cos(ang).astype(np.float32)
    sinF = np.sin(ang).astype(np.float32)
    c["sinF"] = np.where(is_b[:, None] == 0, -sinF, sinF).astype(np.float32)
    return c


def _shared_layouts(inputs):
    g = lambda n: np.asarray(inputs[n], dtype=np.float32)
    s = {}
    s["WkTp"] = np.ascontiguousarray(g("L0_attn_W_k").T[:, _KPERM])
    s["WvT"] = np.ascontiguousarray(g("L0_attn_W_v").T)
    s["WgT"] = np.ascontiguousarray(g("L0_attn_W_g").T)
    s["WpT"] = np.ascontiguousarray(g("L0_attn_W_p").T)
    for n in ("wf", "uf", "wi", "ui", "wc", "uc", "wo", "uo"):
        s[n + "T"] = np.ascontiguousarray(g("L0_ffn_" + n).T)
    s["wmod"] = np.ascontiguousarray(g("L0_wmod"))
    s["cT"] = np.ascontiguousarray(g("L0_c").T)
    lq = g("L0_attn_latent_queries")[0]
    qbd = np.zeros((8, 128, 128), np.float32)
    for hp in range(8):
        for ph in range(2):
            blk = lq[:, 2 * hp + ph, :][:, _DPERM]
            qbd[hp, 64 * ph:64 * ph + 64, 64 * ph:64 * ph + 64] = blk.T
    s["qbd"] = np.ascontiguousarray(qbd)
    for n in ("bf", "bi", "bc", "bo"):
        s[n + "r"] = np.ascontiguousarray(g("L0_ffn_" + n).reshape(8, 128).T)
    ic = g("L0_ffn_in_c").reshape(8, 128).T
    s["in_c0"] = np.ascontiguousarray(np.broadcast_to(ic[:, :, None], (128, 8, 256)))
    return s


def _core_x(inputs, core_id):
    x = np.asarray(inputs["L0_attn_x"], dtype=np.float32)
    xs = x[:, core_id * 256:(core_id + 1) * 256, :][:, _PERM, :]
    return np.ascontiguousarray(xs.transpose(2, 0, 1))


def _assemble_output(shards):
    y = np.empty((8, 2048, 1024), np.float32)
    for i, sh in enumerate(shards):
        ys = np.asarray(sh).reshape(1024, 8, 256).transpose(1, 2, 0)
        y[:, i * 256:(i + 1) * 256, :] = ys[:, _IPERM, :]
    return y


class Sem:
    def __init__(self, h):
        self.h = h
        self.n = 0

    def bump(self, k=1):
        self.n += k
        return self.n


def _view(arena, off, nslots, dtype, shape):
    ap = arena[:, off:off + nslots]
    if dtype == BF16:
        ap = ap.bitcast(BF16)
    total = 1
    for s in shape:
        total *= s
    assert total == ap.shape[1], (shape, ap.shape)
    if len(shape) > 1:
        pat = " ".join(f"d{i}" for i in range(len(shape)))
        ap = ap.rearrange(f"p ({pat}) -> p {pat}",
                          **{f"d{i}": shape[i] for i in range(len(shape))})
    return ap


FFN_W = ("uf", "ui", "uc", "wf", "wi", "wc", "wo", "uo")


def build_program(debug=(), trunc=None):
    nc = bass.Bass("TRN2", target_bir_lowering=False, detect_race_conditions=False)
    DI = lambda name, shape, dt=F32: nc.dram_tensor(name, shape, dt, kind="ExternalInput").ap()

    x_d = DI("x_fm", [1024, B, TL])
    W_d = {n: DI(n, [1024, 1024]) for n in
           ("WkTp", "WvT", "WgT", "WpT", "wfT", "ufT", "wiT", "uiT", "wcT", "ucT", "woT", "uoT")}
    wmod_d = DI("wmod", [1024, 6144])
    cT_d = DI("cT", [1024, 8])
    qbd_d = DI("qbd", [8, 128, 128])
    ident_d = DI("ident", [128, 128])
    identbf_d = DI("ident_bf", [128, 128], U16)
    onesbf_d = DI("ones_bf", [128, 128], U16)
    negeye_d = DI("negeye", [8, 8, 64], U16)
    ind2_d = DI("ind2", [128, 2], U16)
    ind2T_d = DI("ind2T", [2, 128], U16)
    masklc_d = DI("mask_lc", [128, 64])
    maskT_d = DI("mask_T", [64, 128])
    cosF_d = DI("cosF", [128, 128])
    sinF_d = DI("sinF", [128, 128])
    bias_d = {n: DI(n + "r", [128, 8]) for n in ("bf", "bi", "bc", "bo")}
    inc0_d = DI("in_c0", [128, 8, 256])

    y_d = nc.dram_tensor("y_fm", [1024, B, TL], F32, kind="ExternalOutput").ap()
    x2_d = nc.dram_tensor("x2_scr", [1024, B, TL], F32).ap()
    xwo_d = nc.dram_tensor("xwo_scr", [128, 8, 8, 256], BF16).ap()
    arin_t = nc.dram_tensor("ar_in", [128, 4096], BF16)
    arout_t = nc.dram_tensor("ar_out", [128, 4096], BF16)

    DBG = os.environ.get("K_DBG", "") == "1"
    dbg_d = {}
    if DBG:
        dbg_d["gates"] = nc.dram_tensor("dbg_gates", [128, 16384], U16, kind="ExternalOutput").ap()
        dbg_d["attnB"] = nc.dram_tensor("dbg_attnB", [128, 4096], U16, kind="ExternalOutput").ap()
        dbg_d["attnP"] = nc.dram_tensor("dbg_attnP", [128, 4096], U16, kind="ExternalOutput").ap()
        dbg_d["h2"] = nc.dram_tensor("dbg_h2", [128, 16384], U16, kind="ExternalOutput").ap()
        dbg_d["x2"] = nc.dram_tensor("dbg_x2", [1024, B, TL], F32, kind="ExternalOutput").ap()
        dbg_d["cfin"] = nc.dram_tensor("dbg_cfin", [128, 2048], F32, kind="ExternalOutput").ap()
        dbg_d["uod"] = nc.dram_tensor("dbg_uod", [128, 2048], F32, kind="ExternalOutput").ap()

    st = ExitStack()
    sbuf = lambda name, shape, dt=F32: st.enter_context(nc.sbuf_tensor(name, shape, dt))[:]
    psum = lambda name, shape: st.enter_context(nc.psum_tensor(name, shape, F32))[:]

    arena = st.enter_context(nc.sbuf_tensor("arena", [128, 46080], F32))[:]

    ident = sbuf("identt", [128, 128])
    identbf = sbuf("identbft", [128, 128], BF16)
    onesbf = sbuf("onesbft", [128, 128], BF16)
    qbd = sbuf("qbdt", [128, 8, 128], BF16)
    negeye = sbuf("negeyet", [8, 8, 64], BF16)
    ind2 = sbuf("ind2t", [128, 2], BF16)
    ind2T = sbuf("ind2Tt", [2, 128], BF16)
    masklc = sbuf("masklct", [128, 64])
    maskT = sbuf("maskTt", [64, 128])
    cosF = sbuf("cosFt", [128, 128])
    sinF = sbuf("sinFt", [128, 128])
    modF = sbuf("modFt", [128, 6, 8, 8])
    cTt = sbuf("cTt", [128, 8, 8])
    silubf = sbuf("silubft", [128, 8, 8], BF16)
    Rsc = sbuf("Rsct", [128, 32])
    rdt = sbuf("rdtt", [128, 2, 8])
    RT8 = sbuf("RT8t", [8, 128], BF16)
    fbias = sbuf("fbiast", [128, 4, 8])
    rden = sbuf("rdent", [2, 512], BF16)
    consts = sbuf("constst", [128, 4])
    stch = sbuf("stcht", [128, 3, 512])
    app = sbuf("appt", [128, 2, 512])
    esc = sbuf("esct", [128, 1024])
    bfx = sbuf("bfxt", [128, 2, 512], BF16)
    bfsq = sbuf("bfsqt", [128, 2, 512], BF16)

    nc.const_aps.register(F32, 0.125, consts[:, 0:1])
    nc.const_aps.register(F32, 8.0, consts[:, 1:2])
    nc.const_aps.register(F32, 1e-5, consts[:, 2:3])
    nc.const_aps.register(F32, 1.0 / 1024.0, consts[:, 3:4])

    pA = psum("pA", [128, 512])
    pB = psum("pB", [128, 512])
    pE = psum("pE", [128, 512])
    pF = psum("pF", [128, 512])
    pT = psum("pT", [128, 1024])
    pS1 = psum("pS1", [128, 512])
    pS2 = psum("pS2", [128, 512])

    V = lambda off, n, dt, shape: _view(arena, off, n, dt, shape)
    H = V(0, 8192, BF16, (8, 2048))
    X = V(8192, 16384, F32, (8, 2048))
    Wff = {"uf": V(8192, 4096, BF16, (8, 8, 128)),
           "ui": V(12288, 4096, BF16, (8, 8, 128)),
           "uc": V(16384, 4096, BF16, (8, 8, 128)),
           "wf": V(20480, 4096, BF16, (8, 8, 128)),
           "wi": V(24576, 4096, BF16, (8, 8, 128)),
           "wc": V(28672, 4096, BF16, (8, 8, 128)),
           "wo": V(37888, 4096, BF16, (8, 8, 128)),
           "uo": V(12288, 4096, BF16, (8, 8, 128))}
    uodv = V(8192, 2048, F32, (8, 256))
    modtm = V(24576, 6144, F32, (6144,))
    kS = V(24576, 4096, BF16, (8, 8, 128))
    vS = V(28672, 4096, BF16, (8, 8, 128))
    GW = V(24576, 8192, BF16, (8, 2048))
    woF32 = V(24576, 8192, F32, (8, 1024))
    uoF32 = V(24576, 8192, F32, (8, 1024))
    xwosb = [V(37376, 128, BF16, (256,)), V(37504, 128, BF16, (256,))]
    xtw = [V(41984, 1024, BF16, (8, 256)), V(43008, 1024, BF16, (8, 256))]
    qbdf = V(32768, 1024, F32, (8, 128))
    ropeA = V(32768, 512, F32, (4, 128))
    ropeB = V(33280, 512, F32, (4, 128))
    vtmp = V(33792, 256, BF16, (4, 128))
    attnB = V(32768, 2048, BF16, (8, 8, 64))
    xrd = [V(32768, 2048, F32, (8, 256)), V(34816, 2048, F32, (8, 256))]
    cS = V(32768, 2048, F32, (8, 256))
    cbf = [V(34816, 1024, BF16, (8, 256)), V(35840, 1024, BF16, (8, 256))]
    probsb = [V(36864, 512, BF16, (8, 128)), V(37376, 512, BF16, (8, 128))]
    wbounce = [V(36864, 1024, BF16, (2048,)), V(36864, 1024, BF16, (2048,))]

    gscG = V(36864, 512, F32, (2, 256))
    ybufB = V(36864, 1024, F32, (4, 256))
    Wslot = V(37888, 4096, BF16, (8, 8, 128))
    wmodbf = V(37888, 2048, BF16, (2, 8, 256))
    stagh = [V(41984, 2048, F32, (2048,)), V(44032, 2048, F32, (2048,))]
    x2stg = V(41984, 4096, F32, (8, 2, 256))
    x2stgB = V(32768, 4096, F32, (8, 2, 256))
    xt = [V(32768, 2048, F32, (8, 256)), V(34816, 2048, F32, (8, 256))]
    gscF = esc[:, 0:512].rearrange("p (a t) -> p a t", t=256)
    gscI = esc[:, 512:1024].rearrange("p (a t) -> p a t", t=256)
    x2b = [esc[:, 0:512].rearrange("p (a t) -> p a t", t=256),
           esc[:, 512:1024].rearrange("p (a t) -> p a t", t=256)]
    ybufA = esc.rearrange("p (a t) -> p a t", t=256)

    ops = {e: [] for e in ("tensor", "vector", "scalar", "sync", "gpsimd")}
    em = lambda e, fn: ops[e].append(fn)

    sems = {}
    for name in ("sQ", "sQ2", "sG", "sT", "sV", "sS", "sC", "sO"):
        sems[name] = Sem(st.enter_context(nc.semaphore(name)))
    sQ, sQ2, sG, sT, sV, sS, sC, sO = (sems[k] for k in
                                       ("sQ", "sQ2", "sG", "sT", "sV", "sS", "sC", "sO"))

    def dma(eng, sem, out, in_):
        v = sem.bump(16)
        em(eng, lambda E, o=out, i=in_, s=sem.h: E.dma_start(out=o, in_=i).then_inc(s, 16))
        return v

    def wt(eng, sem, v):
        if v:
            em(eng, lambda E, s=sem.h, vv=v: E.wait_ge(s, vv))

    def vedge():
        em("vector", lambda E: E.engine_nop())
        v = inc("vector", sV)
        wt("vector", sV, v)

    def inc(eng, sem, k=1):
        v = sem.bump(k)
        last = ops[eng][-1]

        def wrapped(E, f=last, s=sem.h, kk=k):
            ins = f(E)
            assert ins is not None, "inc() must follow an instruction-emitting op"
            return ins.then_inc(s, kk)
        ops[eng][-1] = wrapped
        return v

    xld = dma("gpsimd", sG, X.rearrange("p f (b t) -> p f b t", t=256),
              x_d.rearrange("(f p) b t -> p f b t", p=128))
    for out, d in (
            (ident, ident_d), (identbf, identbf_d.bitcast(BF16)), (onesbf, onesbf_d.bitcast(BF16)),
            (qbdf, qbd_d.rearrange("h a b -> a h b")),
            (negeye.rearrange("k h c -> k (h c)"),
             negeye_d.rearrange("k h c -> k (h c)").bitcast(BF16)),
            (ind2, ind2_d.bitcast(BF16)), (ind2T, ind2T_d.bitcast(BF16)),
            (masklc, masklc_d), (maskT, maskT_d), (cosF, cosF_d), (sinF, sinF_d),
            (cTt, cT_d.rearrange("(i p) b -> p i b", p=128)),
            (fbias[:, 0, :], bias_d["bf"]), (fbias[:, 1, :], bias_d["bi"]),
            (fbias[:, 2, :], bias_d["bc"]), (fbias[:, 3, :], bias_d["bo"])):
        dma("gpsimd", sG, out, d)
    cver = sG.n

    def _initc(E):
        E.memset(consts[:, 0:1], 0.125)
        E.memset(consts[:, 1:2], 8.0)
        E.memset(consts[:, 2:3], 1e-5)
        E.memset(consts[:, 3:4], 1.0 / 1024.0)
    em("vector", _initc)
    wt("vector", sG, cver)
    em("vector", lambda E: E.tensor_copy(out=qbd.rearrange("p h c -> p (h c)"),
                                         in_=qbdf.rearrange("p h c -> p (h c)")))
    qbd_ready = inc("vector", sV)

    wt("scalar", sG, cver)
    em("scalar", lambda E: E.activation(out=silubf.rearrange("p i b -> p (i b)"),
                                        in_=cTt.rearrange("p i b -> p (i b)"), func=AF.Silu))
    silu_ready = inc("scalar", sS)
    wt("tensor", sS, silu_ready)

    wmod_r = wmod_d.rearrange("(i p) j -> p i j", p=128)
    wmod_mm_done = {}
    for jt in range(24):
        half = jt % 2
        qeng, qsem = (("sync", sQ) if half == 0 else ("scalar", sQ2))
        if jt >= 2:
            wt(qeng, sV, wmod_cast_done[jt - 2])
        lv = dma(qeng, qsem, stagh[half].rearrange("p (i j) -> p i j", j=256),
                 wmod_r[:, :, jt * 256:(jt + 1) * 256])
        wt("vector", qsem, lv)
        if jt >= 2:
            wt("vector", sT, wmod_mm_done[jt - 2])
        dst = wmodbf[:, half, :, :].rearrange("p i j -> p (i j)")
        em("vector", lambda E, d=dst, h=half: E.tensor_copy(out=d, in_=stagh[h]))
        if jt == 0:
            wmod_cast_done = {}
        wmod_cast_done[jt] = inc("vector", sV)
        wt("tensor", sV, wmod_cast_done[jt])
        bank = pS1 if half == 0 else pS2
        for ic in range(8):
            em("tensor", lambda E, ic=ic, b=bank, h=half:
               E.matmul(b[0:8, 0:256], silubf[:, ic, :], wmodbf[:, h, ic, :],
                        start=(ic == 0), stop=(ic == 7)))
        wmod_mm_done[jt] = inc("tensor", sT)
        wt("vector", sT, wmod_mm_done[jt])
        em("vector", lambda E, jt=jt, b=bank:
           E.tensor_copy(out=modtm[0:8, jt * 256:(jt + 1) * 256], in_=b[0:8, 0:256]))
        inc("vector", sV)

    def emit_mod_tr(t48s):
        for t48 in t48s:
            vidx, fc = t48 // 8, t48 % 8
            em("tensor", lambda E, t48=t48:
               E.transpose(pA[:, 0:8], modtm[0:8, t48 * 128:(t48 + 1) * 128], ident[0:8, 0:8]))
            tv = inc("tensor", sT)
            wt("vector", sT, tv)
            em("vector", lambda E, vidx=vidx, fc=fc:
               E.tensor_copy(out=modF[:, vidx, fc, :], in_=pA[:, 0:8]))
            vv = inc("vector", sV)
            wt("tensor", sV, vv)
        return sV.n

    modA_ready = emit_mod_tr(range(16))

    vedge()
    em("vector", lambda E: E.tensor_scalar(out=modF[:, 1, :, :], in0=modF[:, 1, :, :],
                                           scalar1=1.0, scalar2=None, op0=ALU.add))
    modA_ready = inc("vector", sV)

    modB_ready = emit_mod_tr(range(16, 48))

    vedge()
    em("vector", lambda E: E.tensor_scalar(out=modF[:, 4, :, :], in0=modF[:, 4, :, :],
                                           scalar1=1.0, scalar2=None, op0=ALU.add))
    for vidx in (2, 5):
        g = modF[:, vidx, :, :]
        hi = app[:, 0, 0:64].rearrange("p (a b) -> p a b", b=8)
        lo = app[:, 0, 64:128].rearrange("p (a b) -> p a b", b=8)
        sg = app[:, 0, 128:192].rearrange("p (a b) -> p a b", b=8)
        vedge()
        em("vector", lambda E, g=g, hi=hi: E.tensor_scalar(out=hi, in0=g, scalar1=0.25,
                                                           scalar2=None, op0=ALU.max))
        em("vector", lambda E, g=g, lo=lo: E.tensor_scalar(out=lo, in0=g, scalar1=-0.25,
                                                           scalar2=None, op0=ALU.min))
        em("vector", lambda E, g=g, sg=sg: E.tensor_scalar(out=sg, in0=g, scalar1=0.0,
                                                           scalar2=None, op0=ALU.is_ge))
        vedge()
        em("vector", lambda E, hi=hi, lo=lo: E.tensor_tensor(out=hi, in0=hi, in1=lo,
                                                             op=ALU.subtract))
        em("vector", lambda E, hi=hi, sg=sg: E.tensor_tensor(out=hi, in0=sg, in1=hi,
                                                             op=ALU.mult))
        em("vector", lambda E, g=g, hi=hi, lo=lo: E.tensor_tensor(out=g, in0=hi, in1=lo,
                                                                  op=ALU.add))
    modB_ready = inc("vector", sV)

    def stage_cast(dram, dest, dest_guard=None, nq=4):
        if stage_cast.lastcast is None:
            stage_cast.lastcast = {0: wmod_cast_done[22], 1: wmod_cast_done[23]}
        src = dram.rearrange("(i p) o -> p i o", p=128)
        castv = {}
        for q in range(nq):
            half = q % 2
            wt("sync", sV, stage_cast.lastcast.get(half, 0))
            lv = dma("sync", sQ, stagh[half].rearrange("p (i o) -> p i o", o=256),
                     src[:, :, q * 256:(q + 1) * 256])
            wt("vector", sQ, lv)
            if dest_guard is not None and q == 0:
                wt("vector", dest_guard[0], dest_guard[1])
            sh = stagh[half].rearrange("p (i o) -> p i o", o=256)
            dst = dest[:, :, 2 * q:2 * q + 2, :].rearrange("p i a b -> p i (a b)")
            em("vector", lambda E, d=dst, s=sh: E.tensor_copy(out=d, in_=s))
            castv[q] = inc("vector", sV)
            stage_cast.lastcast[half] = castv[q]
        return castv[nq - 1]
    stage_cast.lastcast = None

    def direct_weight(name, dest, dest_guard=None):
        return stage_cast(W_d[name + "T"], dest, dest_guard=dest_guard)

    def emit_ln(src_fn, cvec, svec, mod_wait=0, prep_fn=None, post_fn=None):
        mmv_par = [0, 0]
        act_par = [0, 0]
        psum_rd = [0]
        pairs = [(app[:, 0, :], app[:, 1, :]),
                 (esc[:, 0:512], esc[:, 512:1024])]
        h2d = [0]

        def stats(tt):
            if prep_fn is not None:
                prep_fn(tt)
            for fc in range(8):
                par = fc % 2
                s = src_fn(fc, tt)
                wt("vector", sT, mmv_par[par])
                em("vector", lambda E, s=s, par=par: E.tensor_copy(out=bfx[:, par, :], in_=s))
                cv = inc("vector", sV)
                wt("scalar", sV, cv)
                wt("scalar", sT, mmv_par[par])
                em("scalar", lambda E, par=par:
                   E.activation(out=bfsq[:, par, :], in_=bfx[:, par, :], func=AF.Square))
                sq2 = inc("scalar", sS)
                wt("tensor", sV, cv)
                wt("tensor", sS, sq2)
                if fc == 0:
                    wt("tensor", sV, psum_rd[0])
                em("tensor", lambda E, par=par, fc=fc:
                   E.matmul(pS1, onesbf, bfx[:, par, :], start=(fc == 0), stop=(fc == 7)))
                em("tensor", lambda E, par=par, fc=fc:
                   E.matmul(pS2, onesbf, bfsq[:, par, :], start=(fc == 0), stop=(fc == 7)))
                mmv_par[par] = inc("tensor", sT)

        def chain_apply(tt, mmv_tail):
            wt("vector", sT, mmv_tail)
            wt("vector", sS, act_par[0])
            wt("vector", sS, act_par[1])
            em("vector", lambda E: E.tensor_scalar(out=stch[:, 0, :], in0=pS1,
                                                   scalar1=1.0 / 1024.0, scalar2=None, op0=ALU.mult))
            em("vector", lambda E: E.tensor_scalar(out=app[:, 1, :], in0=pS2,
                                                   scalar1=1.0 / 1024.0, scalar2=None, op0=ALU.mult))
            psum_rd[0] = inc("vector", sV)
            if tt < 3:
                stats(tt + 1)
            vedge()
            em("vector", lambda E: E.tensor_tensor(out=app[:, 0, :], in0=stch[:, 0, :],
                                                   in1=stch[:, 0, :], op=ALU.mult))
            em("vector", lambda E: E.tensor_tensor(out=app[:, 1, :], in0=app[:, 1, :],
                                                   in1=app[:, 0, :], op=ALU.subtract))
            chv = inc("vector", sV)
            wt("scalar", sV, chv)
            em("scalar", lambda E: E.activation(out=app[:, 0, :], in_=app[:, 1, :],
                                                func=AF.Sqrt, bias=1e-5, scale=1.0))
            sqv = inc("scalar", sS)
            wt("vector", sS, sqv)
            em("vector", lambda E: E.reciprocal(out=stch[:, 1, :], in_=app[:, 0, :]))
            vedge()
            em("vector", lambda E: E.tensor_tensor(out=stch[:, 2, :], in0=stch[:, 0, :],
                                                   in1=stch[:, 1, :], op=ALU.mult))
            inc("vector", sV)
            for fc in range(8):
                par = fc % 2
                s = src_fn(fc, tt)
                t1, t2 = pairs[par]
                if mod_wait and tt == 0 and fc == 0:
                    wt("scalar", sV, mod_wait)
                wt("vector", sS, act_par[par])
                em("vector", lambda E, s=s, t1=t1: E.tensor_tensor(out=t1, in0=s,
                                                                   in1=stch[:, 1, :], op=ALU.mult))
                em("vector", lambda E, t1=t1, t2=t2: E.tensor_tensor(out=t2, in0=t1,
                                                                     in1=stch[:, 2, :],
                                                                     op=ALU.subtract))
                av = inc("vector", sV)
                wt("scalar", sV, av)
                for hf in range(2):
                    b = 2 * tt + hf
                    em("scalar", lambda E, fc=fc, tt=tt, hf=hf, b=b, t2=t2:
                       E.activation(out=H[:, fc, tt * 512 + hf * 256: tt * 512 + hf * 256 + 256],
                                    in_=t2[:, hf * 256:hf * 256 + 256], func=AF.Identity,
                                    bias=modF[:, svec, fc, b:b + 1],
                                    scale=modF[:, cvec, fc, b:b + 1]))
                act_par[par] = inc("scalar", sS)
                h2d[0] = act_par[par]
            if post_fn is not None:
                post_fn(tt)

        stats(0)
        tails = [0, 0, 0, 0]
        tails[0] = mmv_par[1]
        for tt in range(4):
            chain_apply(tt, tails[tt])
            tails[tt + 1 if tt < 3 else 0] = mmv_par[1]
        return h2d[0]

    wt("vector", sG, xld)
    h1_done = emit_ln(lambda fc, tt: X[:, fc, tt * 512:(tt + 1) * 512], 1, 0,
                      mod_wait=modA_ready)

    wk_cast = stage_cast(W_d["WkTp"], Wslot, dest_guard=(sT, sT.n))
    wt("tensor", sV, wk_cast)
    wt("tensor", sS, h1_done)

    Hr = [H[:, ic, :].rearrange("p (b t) -> p b t", t=256) for ic in range(8)]
    rope_free = {}
    for i3 in range(16):
        hp, bg = i3 // 2, i3 % 2
        bank = pA if i3 % 2 == 0 else pB
        wt("tensor", sV, rope_free.get(i3 - 2, 0))
        for ic in range(8):
            em("tensor", lambda E, ic=ic, hp=hp, bg=bg, bank=bank:
               E.matmul(bank.rearrange("p (b c) -> p b c", c=128),
                        Wslot[:, ic, hp, :], Hr[ic][:, bg * 4:(bg + 1) * 4, 0:128],
                        start=(ic == 0), stop=(ic == 7)))
        mv = inc("tensor", sT)
        wt("vector", sT, mv)
        bank3 = bank.rearrange("p (b c) -> p b c", c=128)
        em("vector", lambda E, bank=bank: E.stream_shuffle(
            out=ropeB.rearrange("p b c -> p (b c)"), in_=bank, mask=SWAP16))
        em("vector", lambda E, bank3=bank3: E.tensor_tensor(
            out=ropeA, in0=bank3, in1=cosF.unsqueeze(1).to_broadcast([128, 4, 128]), op=ALU.mult))
        em("vector", lambda E: E.tensor_tensor(
            out=ropeB, in0=ropeB, in1=sinF.unsqueeze(1).to_broadcast([128, 4, 128]), op=ALU.mult))
        em("vector", lambda E, hp=hp, bg=bg: E.tensor_tensor(
            out=kS[:, hp, bg * 4:(bg + 1) * 4, :], in0=ropeA, in1=ropeB, op=ALU.add))
        rope_free[i3] = inc("vector", sV)

    wv_cast = stage_cast(W_d["WvT"], Wslot, dest_guard=(sT, sT.n))
    wt("tensor", sV, wv_cast)
    vt_free = {}
    for i3 in range(16):
        hp, bg = i3 // 2, i3 % 2
        bank = pA if i3 % 2 == 0 else pB
        tbank = pE if i3 % 2 == 0 else pF
        wt("tensor", sV, vt_free.get(i3 - 2, 0))
        for ic in range(8):
            em("tensor", lambda E, ic=ic, hp=hp, bg=bg, bank=bank:
               E.matmul(bank.rearrange("p (b c) -> p b c", c=128),
                        Wslot[:, ic, hp, :], Hr[ic][:, bg * 4:(bg + 1) * 4, 0:128],
                        start=(ic == 0), stop=(ic == 7)))
        mv = inc("tensor", sT)
        wt("vector", sT, mv)
        em("vector", lambda E, bank=bank:
           E.tensor_copy(out=vtmp.rearrange("p b c -> p (b c)"), in_=bank))
        cv = inc("vector", sV)
        wt("tensor", sV, cv)
        for br in range(4):
            em("tensor", lambda E, br=br, tbank=tbank:
               E.transpose(tbank[:, br * 64:(br + 1) * 64].bitcast(BF16), vtmp[:, br, :], identbf))
        tv = inc("tensor", sT)
        wt("vector", sT, tv)
        em("vector", lambda E, hp=hp, bg=bg, tbank=tbank:
           E.tensor_copy(out=vS[:, bg * 4:(bg + 1) * 4, hp, :],
                         in_=tbank[:, 0:256].bitcast(BF16).rearrange("p (b c) -> p b c", c=128)))
        vt_free[i3] = inc("vector", sV)
    vproj_done_sT = sT.n

    wff_cast = {}
    for n in ("uf", "ui", "uc", "wf"):
        wff_cast[n] = stage_cast(W_d[n + "T"], Wff[n],
                                 dest_guard=(sS, h1_done) if n == "uf" else None)

    def _finish_trunc():
        em("gpsimd", lambda E, v=sV.n: E.wait_ge(sV.h, v))
        em("gpsimd", lambda E, v=sT.n: E.wait_ge(sT.h, v))
        em("gpsimd", lambda E, v=sS.n: E.wait_ge(sS.h, v))
        with nc.Block() as block:
            @block.tensor
            def _(E):
                for fn in ops["tensor"]:
                    fn(E)
            @block.vector
            def _(E):
                for fn in ops["vector"]:
                    fn(E)
            @block.scalar
            def _(E):
                for fn in ops["scalar"]:
                    fn(E)
            @block.sync
            def _(E):
                for fn in ops["sync"]:
                    fn(E)
            @block.gpsimd
            def _(E):
                for fn in ops["gpsimd"]:
                    fn(E)
        st.close()
        return nc

    if trunc == "P3":
        return _finish_trunc()

    wt("tensor", sV, qbd_ready)
    probs_free = [0, 0]
    pEF_free = [0, 0]
    pT_free = 0
    pS1_free = 0
    probs_written = 0
    P4B = int(os.environ.get("K_P4B", "8"))
    P4S = int(os.environ.get("K_P4S", "99"))
    for b in range(min(8, P4B)):
        pbuf = probsb[b % 2]
        for ch in range(2):
            wt("tensor", sV, pS1_free)
            for hp in range(8):
                em("tensor", lambda E, hp=hp, b=b, ch=ch:
                   E.matmul(pS1[:, hp * 64:(hp + 1) * 64], qbd[:, hp, :],
                            kS[:, hp, b, ch * 64:(ch + 1) * 64],
                            start=(hp == 0), stop=(hp == 7)))
            lcv = inc("tensor", sT)
            if P4S < 2:
                continue
            wt("tensor", sV, pT_free)
            for hp in range(8):
                em("tensor", lambda E, hp=hp, b=b, ch=ch:
                   E.matmul(pT[0:64, hp * 128:(hp + 1) * 128],
                            kS[:, hp, b, ch * 64:(ch + 1) * 64], qbd[:, hp, :],
                            start=(hp in (0, 4)), stop=False))
            tTv = inc("tensor", sT)
            if P4S < 3:
                continue
            wt("vector", sT, lcv)
            wt("vector", sS, probs_written)
            pS13 = pS1.rearrange("p (h c) -> p h c", c=64)
            esc3 = esc[:, 0:512].rearrange("p (h c) -> p h c", c=64)
            em("vector", lambda E, pS13=pS13: E.reduce_max(out=Rsc[:, 0:8], in_=pS13, axis=AXL.X))
            em("vector", lambda E, pS13=pS13, esc3=esc3: E.tensor_tensor(
                out=esc3, in0=pS13,
                in1=Rsc[:, 0:8].unsqueeze(2).to_broadcast([128, 8, 64]), op=ALU.subtract))
            em("vector", lambda E, esc3=esc3: E.tensor_tensor(
                out=esc3, in0=esc3,
                in1=masklc.unsqueeze(1).to_broadcast([128, 8, 64]), op=ALU.add))
            pS1_free = inc("vector", sV)
            wt("scalar", sV, pS1_free)
            em("scalar", lambda E: E.activation(out=esc[:, 512:1024], in_=esc[:, 0:512],
                                                func=AF.Exp, bias=0.0, scale=0.125))
            ev = inc("scalar", sS)
            wt("vector", sS, ev)
            em("vector", lambda E: E.reduce_sum(
                out=Rsc[:, 8:16], in_=esc[:, 512:1024].rearrange("p (h c) -> p h c", c=64),
                axis=AXL.X))
            vedge()
            em("vector", lambda E, ch=ch: E.reciprocal(out=rdt[:, ch, :], in_=Rsc[:, 8:16]))
            rvv = inc("vector", sV)
            if P4S < 4:
                continue
            wt("tensor", sV, rvv)
            em("tensor", lambda E: E.transpose(pS2[0:8, 0:128], Rsc[:, 0:8], ident))
            trv = inc("tensor", sT)
            wt("vector", sT, trv)
            em("vector", lambda E: E.tensor_copy(out=RT8, in_=pS2[0:8, 0:128]))
            rt8v = inc("vector", sV)
            if P4S < 5:
                continue
            wt("tensor", sV, rt8v)
            for hp in range(8):
                em("tensor", lambda E, hp=hp:
                   E.matmul(pT[0:64, hp * 128:(hp + 1) * 128], negeye[:, hp, :], RT8,
                            start=False, stop=(hp in (3, 7))))
            rsub = inc("tensor", sT)
            wt("vector", sT, rsub)
            pT3 = pT[0:64, :].rearrange("p (h c) -> p h c", c=128)
            esc4 = esc[0:64, :].rearrange("p (h c) -> p h c", c=128)
            em("vector", lambda E, pT3=pT3, esc4=esc4: E.tensor_tensor(
                out=esc4, in0=pT3, in1=maskT.unsqueeze(1).to_broadcast([64, 8, 128]), op=ALU.add))
            pT_free = inc("vector", sV)
            mkv = pT_free
            wt("scalar", sV, mkv)
            wt("scalar", sT, probs_free[b % 2])
            em("scalar", lambda E, pbuf=pbuf, ch=ch:
               E.activation(out=pbuf[64 * ch:64 * ch + 64, :, :].rearrange("p h c -> p (h c)"),
                            in_=esc[0:64, :], func=AF.Exp, bias=0.0, scale=0.125))
            probs_written = inc("scalar", sS)
        if P4S < 6:
            continue
        wt("tensor", sS, probs_written)
        wt("tensor", sV, pEF_free[0])
        for ch in range(2):
            cbank = pE if ch == 0 else pF
            for hp in range(8):
                for ph in range(2):
                    em("tensor", lambda E, ch=ch, hp=hp, ph=ph, b=b, cbank=cbank:
                       E.matmul(cbank[64 * ph:64 * ph + 64, hp * 64:(hp + 1) * 64],
                                probsb[b % 2][64 * ch:64 * ch + 64, hp, ph * 64:ph * 64 + 64],
                                vS[64 * ch:64 * ch + 64, b, hp, ph * 64:ph * 64 + 64],
                                start=True, stop=True, skip_group_check=True))
        probs_free[b % 2] = inc("tensor", sT)
        if P4S < 7:
            continue
        wt("vector", sT, probs_free[b % 2])
        em("vector", lambda E: E.tensor_tensor(
            out=app[:, 0, :].rearrange("p (h d) -> p h d", d=64),
            in0=pE.rearrange("p (h d) -> p h d", d=64),
            in1=rdt[:, 0, :].unsqueeze(2).to_broadcast([128, 8, 64]), op=ALU.mult))
        em("vector", lambda E: E.tensor_tensor(
            out=app[:, 1, :].rearrange("p (h d) -> p h d", d=64),
            in0=pF.rearrange("p (h d) -> p h d", d=64),
            in1=rdt[:, 1, :].unsqueeze(2).to_broadcast([128, 8, 64]), op=ALU.mult))
        em("vector", lambda E, b=b:
           E.tensor_tensor(out=attnB[:, b, :, :].rearrange("p h d -> p (h d)"),
                           in0=app[:, 0, :], in1=app[:, 1, :], op=ALU.add))
        pEF_free[0] = inc("vector", sV)
        pEF_free[1] = pEF_free[0]

    if trunc == "P4":
        return _finish_trunc()

    wt("gpsimd", sV, max(pEF_free))
    if DBG:
        dma("gpsimd", sG, dbg_d["attnP"].bitcast(BF16), attnB.rearrange("p b h d -> p (b h d)"))
    ar_w = dma("gpsimd", sG, arin_t[:, :], attnB.rearrange("p b h d -> p (b h d)"))
    wt("gpsimd", sG, ar_w)
    em("gpsimd", lambda E: E.collective_compute(
        "AllReduce", ALU.add, replica_groups=[list(range(NCORES))],
        ins=[arin_t.ap().opt()], outs=[arout_t.ap().opt()]).then_inc(sC.h, 1))
    ccv = sC.bump(1)
    wt("gpsimd", sC, ccv)
    ar_r = dma("gpsimd", sG, attnB.rearrange("p b h d -> p (b h d)"), arout_t[:, :])
    if DBG:
        wt("gpsimd", sG, ar_r)
        ar_r = dma("gpsimd", sG, dbg_d["attnB"].bitcast(BF16),
                   attnB.rearrange("p b h d -> p (b h d)"))

    if trunc == "CC":
        wt("gpsimd", sG, ar_r)
        return _finish_trunc()

    wg_cast = stage_cast(W_d["WgT"], Wslot, dest_guard=(sT, vproj_done_sT))
    wt("tensor", sV, wg_cast)
    g_free = [0, 0]
    eg_free = 0
    egb = esc[:, 0:256].bitcast(BF16)
    for pc in range(8):
        for tt in range(4):
            it_ = pc * 4 + tt
            bank = pA if it_ % 2 == 0 else pB
            wt("tensor", sS, g_free[it_ % 2])
            for ic in range(8):
                em("tensor", lambda E, ic=ic, pc=pc, tt=tt, bank=bank:
                   E.matmul(bank, Wslot[:, ic, pc, :], H[:, ic, tt * 512:(tt + 1) * 512],
                            start=(ic == 0), stop=(ic == 7)))
            gv = inc("tensor", sT)
            wt("scalar", sT, gv)
            wt("scalar", sV, eg_free)
            em("scalar", lambda E, bank=bank:
               E.activation(out=egb, in_=bank, func=AF.Exp, bias=0.0, scale=0.125))
            g_free[it_ % 2] = inc("scalar", sS)
            wt("tensor", sS, g_free[it_ % 2])
            em("tensor", lambda E: E.matmul(pS2[0:2, 0:512], ind2, egb, start=True, stop=True))
            dnv = inc("tensor", sT)
            wt("vector", sT, dnv)
            em("vector", lambda E: E.reciprocal(out=app[0:2, 0, :], in_=pS2[0:2, 0:512]))
            vedge()
            em("vector", lambda E: E.tensor_copy(out=rden, in_=app[0:2, 0, :]))
            rdv = inc("vector", sV)
            wt("tensor", sV, rdv)
            em("tensor", lambda E: E.matmul(pS1, ind2T, rden, start=True, stop=True))
            exv = inc("tensor", sT)
            wt("vector", sT, exv)
            em("vector", lambda E, pc=pc, tt=tt:
               E.tensor_tensor(out=GW[:, pc, tt * 512:(tt + 1) * 512], in0=egb, in1=pS1,
                               op=ALU.mult))
            eg_free = inc("vector", sV)

    if trunc == "P5":
        return _finish_trunc()

    wp_cast = stage_cast(W_d["WpT"], Wslot, dest_guard=(sT, sT.n))



    if DBG:
        wt("gpsimd", sV, eg_free)
        gdump = dma("gpsimd", sG, dbg_d["gates"].bitcast(BF16),
                    GW.rearrange("p a b -> p (a b)"))
        wt("tensor", sG, gdump)
    wt("tensor", sG, ar_r)
    wt("tensor", sV, eg_free)
    w_free = [0, 0]
    for hp in range(8):
        for b in range(8):
            it_ = hp * 8 + b
            bank = pA if it_ % 2 == 0 else pB
            wt("tensor", sV, w_free[it_ % 2])
            for ph in range(2):
                em("tensor", lambda E, hp=hp, b=b, ph=ph, bank=bank:
                   E.matmul(bank[64 * ph:64 * ph + 64, 0:256],
                            attnB[64 * ph:64 * ph + 64, b, hp, :],
                            GW[64 * ph:64 * ph + 64, hp, b * 256:(b + 1) * 256],
                            start=True, stop=True, skip_group_check=True))
            wv = inc("tensor", sT)
            wt("vector", sT, wv)
            em("vector", lambda E, hp=hp, b=b, bank=bank:
               E.tensor_copy(out=GW[:, hp, b * 256:(b + 1) * 256], in_=bank[:, 0:256]))
            w_free[it_ % 2] = inc("vector", sV)

    if trunc == "P6":
        return _finish_trunc()

    wt("tensor", sV, wp_cast)
    wt("tensor", sV, max(w_free))
    wt("vector", sV, modB_ready)
    x_r = x_d.rearrange("(f p) b t -> p f b t", p=128)
    x2_r = x2_d.rearrange("(f p) b t -> p f b t", p=128)
    p7_free = [0, 0]
    x2b_free = [0, 0]
    xrd_v = [0, 0]
    xrd_used = [0, 0]
    wt("gpsimd", sT, sT.n)
    for oc in range(8):
        wt("gpsimd", sV, xrd_used[oc % 2])
        xrd_v[oc % 2] = dma("gpsimd", sG, xrd[oc % 2].rearrange("p b t -> p (b t)"),
                            x_r[:, oc, :, :].rearrange("p b t -> p (b t)"))
        for tt in range(4):
            it_ = oc * 4 + tt
            bank = pA if it_ % 2 == 0 else pB
            wt("tensor", sV, p7_free[it_ % 2])
            for ic in range(8):
                em("tensor", lambda E, ic=ic, oc=oc, tt=tt, bank=bank:
                   E.matmul(bank, Wslot[:, ic, oc, :], GW[:, ic, tt * 512:(tt + 1) * 512],
                            start=(ic == 0), stop=(ic == 7)))
            pv = inc("tensor", sT)
            wt("vector", sT, pv)
            wt("vector", sG, xrd_v[oc % 2])
            wt("vector", sG, x2b_free[tt % 2])
            for hf in range(2):
                b = tt * 2 + hf
                em("vector", lambda E, oc=oc, tt=tt, hf=hf, b=b, bank=bank:
                   E.scalar_tensor_tensor(
                       out=x2b[tt % 2][:, hf, :],
                       in0=bank[:, hf * 256:hf * 256 + 256],
                       scalar=modF[:, 2, oc, b:b + 1],
                       in1=xrd[oc % 2][:, b, :],
                       op0=ALU.mult, op1=ALU.add))
            p7_free[it_ % 2] = inc("vector", sV)
            xrd_used[oc % 2] = p7_free[it_ % 2]
            wt("gpsimd", sV, p7_free[it_ % 2])
            x2b_free[tt % 2] = dma(
                "gpsimd", sG,
                x2_r[:, oc, 2 * tt:2 * tt + 2, :].rearrange("p b t -> p (b t)"),
                x2b[tt % 2].rearrange("p a t -> p (a t)"))
    x2_written = sG.n
    p7_done_sT = sT.n
    wt("sync", sT, p7_done_sT)
    wolv = dma("sync", sQ, woF32, W_d["woT"].rearrange("(i p) o -> p i o", p=128))
    wt("vector", sQ, wolv)
    em("vector", lambda E: E.tensor_copy(out=Wff["wo"].rearrange("p i o c -> p (i o c)"),
                                         in_=woF32.rearrange("p i o -> p (i o)")))
    wov = inc("vector", sV)
    for n in ("wi", "wc"):
        wff_cast[n] = direct_weight(n, Wff[n], dest_guard=(sV, wov) if n == "wi" else None)
    if trunc == "P7":
        return _finish_trunc()

    x2s_free = [0, 0]
    if trunc == "P7b":
        return _finish_trunc()

    x2bufs = [x2stg, x2stgB]
    p8_lv = [0, 0]

    def p8_load(tt):
        wt("sync", sG, x2_written)
        if tt % 2 == 0:
            wt("sync", sV, max(stage_cast.lastcast.values()))
        wt("sync", sV, x2s_free[tt % 2])
        return dma("sync", sQ, x2bufs[tt % 2], x2_r[:, :, 2 * tt:2 * tt + 2, :])

    def p8_prep(tt):
        p8_lv[tt % 2] = p8_load(tt)
        wt("vector", sQ, p8_lv[tt % 2])

    def p8_post(tt):
        x2s_free[tt % 2] = sV.n

    h2_done = emit_ln(lambda fc, tt: x2bufs[tt % 2][:, fc, :, :].rearrange("p a t -> p (a t)"),
                      4, 3, prep_fn=p8_prep, post_fn=p8_post)

    wt("sync", sS, h2_done)
    c0v = dma("sync", sQ, cS.rearrange("p b t -> p (b t)"), inc0_d.rearrange("p b t -> p (b t)"))
    wt("vector", sQ, c0v)
    em("vector", lambda E: E.tensor_copy(out=cbf[0].rearrange("p b t -> p (b t)"),
                                         in_=cS.rearrange("p b t -> p (b t)")))
    c0_ready = inc("vector", sV)

    for n in ("uf", "ui", "uc", "wf", "wi", "wc"):
        wt("tensor", sV, wff_cast[n])
    wt("tensor", sV, c0_ready)
    wt("tensor", sS, h2_done)

    GATES = (("f", AF.Sigmoid, 0), ("i", AF.Sigmoid, 1), ("c", AF.Tanh, 2))
    wt("tensor", sV, wov)
    wt("scalar", sV, sV.n)
    bank_free = {}
    xwo_free = [0, 0]
    xwo_dma = [0, 0]
    step_done = 0
    for t in range(8):
        cbf_r = cbf[t % 2]
        cbf_w = cbf[(t + 1) % 2]
        wt("tensor", sV, step_done)
        for oc in range(8):
            par = oc % 2
            banks = {"f": (pA if par == 0 else pB),
                     "i": (pE if par == 0 else pF),
                     "c": (pT[:, 0:256] if par == 0 else pT[:, 512:768])}
            gv = {}
            for gname, gfunc, gidx in GATES:
                wt("tensor", sV, bank_free.get((gname, par), 0))
                wname, uname = "w" + gname, "u" + gname
                for ic in range(8):
                    em("tensor", lambda E, ic=ic, oc=oc, t=t, gn=gname, bank=banks[gname]:
                       E.matmul(bank[:, 0:256] if bank.shape[1] > 256 else bank,
                                Wff["w" + gn][:, ic, oc, :], H[:, ic, t * 256:(t + 1) * 256],
                                start=(ic == 0), stop=False))
                for ic in range(8):
                    em("tensor", lambda E, ic=ic, oc=oc, gn=gname, bank=banks[gname], cb=cbf_r:
                       E.matmul(bank[:, 0:256] if bank.shape[1] > 256 else bank,
                                Wff["u" + gn][:, ic, oc, :], cb[:, ic, :],
                                start=False, stop=(ic == 7)))
                gv[gname] = inc("tensor", sT)
            dst = {"f": gscF, "i": gscI, "c": gscG}
            for gname, gfunc, gidx in GATES:
                wt("scalar", sT, gv[gname])
                em("scalar", lambda E, gn=gname, gf=gfunc, gi=gidx, oc=oc, par=par,
                   bank=banks[gname]:
                   E.activation(out=dst[gn][:, par, :],
                                in_=bank[:, 0:256] if bank.shape[1] > 256 else bank,
                                func=gf, bias=fbias[:, gi, oc:oc + 1], scale=1.0))
            actv = inc("scalar", sS)
            wt("vector", sS, actv)
            em("vector", lambda E, oc=oc, par=par: E.tensor_tensor(
                out=cS[:, oc, :], in0=gscF[:, par, :], in1=cS[:, oc, :], op=ALU.mult))
            em("vector", lambda E, par=par: E.tensor_tensor(
                out=gscI[:, par, :], in0=gscI[:, par, :], in1=gscG[:, par, :], op=ALU.mult))
            em("vector", lambda E, oc=oc, par=par: E.tensor_tensor(
                out=cS[:, oc, :], in0=cS[:, oc, :], in1=gscI[:, par, :], op=ALU.add))
            em("vector", lambda E, oc=oc, cw=cbf_w: E.tensor_copy(
                out=cw[:, oc, :], in_=cS[:, oc, :]))
            bv = inc("vector", sV)
            for gname, _, _ in GATES:
                bank_free[(gname, par)] = bv
        for oc in range(8):
            par = oc % 2
            xb = pS1 if par == 0 else pS2
            wt("tensor", sS, xwo_free[par])
            for ic in range(8):
                em("tensor", lambda E, ic=ic, oc=oc, t=t, xb=xb:
                   E.matmul(xb[:, 0:256], Wff["wo"][:, ic, oc, :],
                            H[:, ic, t * 256:(t + 1) * 256],
                            start=(ic == 0), stop=(ic == 7)))
            xv = inc("tensor", sT)
            wt("scalar", sT, xv)
            wt("scalar", sG, xwo_dma[par])
            em("scalar", lambda E, par=par, xb=xb:
               E.copy(out=xwosb[par].rearrange("p a -> p a"), in_=xb[:, 0:256]))
            xwo_free[par] = inc("scalar", sS)
            wt("gpsimd", sS, xwo_free[par])
            xwo_dma[par] = dma("gpsimd", sG, xwo_d[:, t, oc, :].bitcast(BF16)
                               if False else xwo_d[:, t, oc, :],
                               xwosb[par].rearrange("p a -> p a"))
        step_done = sV.n

    wt("sync", sT, sT.n)
    uolv = dma("sync", sQ, uoF32, W_d["uoT"].rearrange("(i p) o -> p i o", p=128))
    wt("vector", sQ, uolv)
    em("vector", lambda E: E.tensor_copy(out=Wff["uo"].rearrange("p i o c -> p (i o c)"),
                                         in_=uoF32.rearrange("p i o -> p (i o)")))
    uov = inc("vector", sV)

    wt("tensor", sV, uov)
    wt("tensor", sS, sS.n)
    cbf_fin = cbf[0]
    for oc in range(8):
        bank = pE if oc % 2 == 0 else pF
        for ic in range(8):
            em("tensor", lambda E, ic=ic, oc=oc, bank=bank:
               E.matmul(bank[:, 0:256], Wff["uo"][:, ic, oc, :], cbf_fin[:, ic, :],
                        start=(ic == 0), stop=(ic == 7)))
        uv = inc("tensor", sT)
        wt("scalar", sT, uv)
        em("scalar", lambda E, oc=oc, bank=bank:
           E.activation(out=uodv[:, oc, :], in_=bank[:, 0:256], func=AF.Identity,
                        bias=fbias[:, 3, oc:oc + 1], scale=1.0))
    uod_done = inc("scalar", sS)

    wt("tensor", sS, uod_done)
    xt_v = [0, 0]
    xt_used = [0, 0]
    tail_free = [0, 0]
    ybuf_free = [0, 0]
    yv_last = 0
    wt("sync", sS, uod_done)
    xtw_v = [0, 0]
    xtw_used = [0, 0]
    for oc in range(8):
        wt("sync", sV, xt_used[oc % 2])
        xt_v[oc % 2] = dma("sync", sQ, xt[oc % 2].rearrange("p b t -> p (b t)"),
                           x2_r[:, oc, :, :].rearrange("p b t -> p (b t)"))
        wt("sync", sV, xtw_used[oc % 2])
        xtw_v[oc % 2] = dma("sync", sQ, xtw[oc % 2], xwo_d[:, :, oc, :])
        for t in range(8):
            par = (oc * 8 + t) % 2
            wt("vector", sV, tail_free[par])
            if t == 0:
                wt("vector", sQ, xtw_v[oc % 2])
            em("vector", lambda E, oc=oc, t=t, par=par: E.tensor_tensor(
                out=app[:, par, 0:256], in0=xtw[oc % 2][:, t, :], in1=uodv[:, oc, :],
                op=ALU.add))
            s1v = inc("vector", sV)
            xtw_used[oc % 2] = s1v
            wt("scalar", sV, s1v)
            em("scalar", lambda E, par=par: E.activation(
                out=app[:, par, 256:512], in_=app[:, par, 0:256], func=AF.Tanh))
            s2v = inc("scalar", sS)
            ybuf = ybufA if t < 4 else ybufB
            yb = 0 if t < 4 else 1
            wt("vector", sS, s2v)
            wt("vector", sQ, xt_v[oc % 2])
            if t in (0, 4):
                wt("vector", sO, ybuf_free[yb])
            em("vector", lambda E, oc=oc, t=t, par=par, ybuf=ybuf: E.scalar_tensor_tensor(
                out=ybuf[:, t % 4, :], in0=app[:, par, 256:512],
                scalar=modF[:, 5, oc, t:t + 1], in1=xt[oc % 2][:, t, :],
                op0=ALU.mult, op1=ALU.add))
            tail_free[par] = inc("vector", sV)
            xt_used[oc % 2] = tail_free[par]
            if t in (3, 7):
                wt("gpsimd", sV, tail_free[par])
                y_r = y_d.rearrange("(f p) b t -> p f b t", p=128)
                tb = t // 4
                ybuf_free[yb] = dma("gpsimd", sO,
                                    y_r[:, oc, 4 * tb:4 * tb + 4, :].rearrange("p b t -> p (b t)"),
                                    ybuf.rearrange("p a t -> p (a t)"))
                yv_last = ybuf_free[yb]

    if DBG:
        wt("gpsimd", sV, sV.n)
        wt("gpsimd", sS, sS.n)
        wt("gpsimd", sT, sT.n)
        dma("gpsimd", sO, dbg_d["h2"].bitcast(BF16), H.rearrange("p a b -> p (a b)"))
        dma("gpsimd", sO, dbg_d["x2"], x2_d)
        dma("gpsimd", sO, dbg_d["cfin"], cS.rearrange("p a b -> p (a b)"))
        dma("gpsimd", sO, dbg_d["uod"], uodv.rearrange("p a b -> p (a b)"))
    em("gpsimd", lambda E, v=sO.n: E.wait_ge(sO.h, v))

    with nc.Block() as block:
        @block.tensor
        def _(E):
            for fn in ops["tensor"]:
                fn(E)

        @block.vector
        def _(E):
            for fn in ops["vector"]:
                fn(E)

        @block.scalar
        def _(E):
            for fn in ops["scalar"]:
                fn(E)

        @block.sync
        def _(E):
            for fn in ops["sync"]:
                fn(E)

        @block.gpsimd
        def _(E):
            for fn in ops["gpsimd"]:
                fn(E)

    st.close()
    return nc


DBG_SRC = {}

_prog_cache = {}


def kernel(**inputs):
    shared = _shared_layouts(inputs)
    nc = build_program()
    in_maps = []
    for core in range(NCORES):
        m = dict(shared)
        m.update(_host_constants(core))
        m["x_fm"] = _core_x(inputs, core)
        dm = {"x_fm": m["x_fm"], "wmod": m["wmod"], "cT": m["cT"], "qbd": m["qbd"],
              "ident": m["ident"], "ident_bf": m["ident_bf"], "ones_bf": m["ones_bf"],
              "negeye": m["negeye"], "ind2": m["ind2"], "ind2T": m["ind2T"],
              "mask_lc": m["mask_lc"], "mask_T": m["mask_T"],
              "cosF": m["cosF"], "sinF": m["sinF"], "in_c0": m["in_c0"]}
        for n in ("WkTp", "WvT", "WgT", "WpT", "wfT", "ufT", "wiT", "uiT", "wcT", "ucT",
                  "woT", "uoT"):
            dm[n] = m[n]
        for n in ("bf", "bi", "bc", "bo"):
            dm[n + "r"] = m[n + "r"]
        in_maps.append(dm)
    res = run_bass_kernel_spmd(nc, in_maps, core_ids=list(range(NCORES)))
    return _assemble_output([res.results[i]["y_fm"] for i in range(NCORES)])


# version 15
# speedup vs baseline: 1.0202x; 1.0202x over previous
import os
import numpy as np
import ml_dtypes
from contextlib import ExitStack

import concourse.bass as bass
import concourse.mybir as mybir
from concourse.bass_utils import run_bass_kernel_spmd

F32 = mybir.dt.float32
BF16 = mybir.dt.bfloat16
U16 = mybir.dt.uint16
AF = mybir.ActivationFunctionType
ALU = mybir.AluOpType
AXL = mybir.AxisListType

NCORES = 8
B = 8
TL = 256
NEG = -1.0e30
SWAP16 = list(range(16, 32)) + list(range(16))

_PERM = np.concatenate([np.arange(0, 64), np.arange(128, 192),
                        np.arange(64, 128), np.arange(192, 256)])
_IPERM = np.argsort(_PERM)


def _dperm():
    p = np.arange(64)
    q16 = p // 16
    j = (q16 // 2) * 16 + p % 16
    return 2 * j + q16 % 2


_DPERM = _dperm()


def _kperm():
    out = np.zeros(1024, np.int64)
    for hp in range(8):
        for p in range(128):
            out[hp * 128 + p] = (2 * hp + p // 64) * 64 + _DPERM[p % 64]
    return out


_KPERM = _kperm()


def _bf(a):
    return np.ascontiguousarray(a.astype(ml_dtypes.bfloat16).view(np.uint16))


def _host_constants(core_id):
    c = {}
    c["ident"] = np.eye(128, dtype=np.float32)
    c["ident_bf"] = _bf(np.eye(128, dtype=np.float32))
    c["ones_bf"] = _bf(np.ones((128, 128), np.float32))
    ne = np.zeros((8, 8, 64), np.float32)
    for k in range(8):
        ne[k, k, :] = -1.0
    c["negeye"] = _bf(ne)
    ind2 = np.zeros((128, 2), np.float32)
    ind2[:64, 0] = 1.0
    ind2[64:, 1] = 1.0
    c["ind2"] = _bf(ind2)
    c["ind2T"] = _bf(ind2.T.copy())
    l_of_p = np.arange(128) % 64
    cc = np.arange(64)
    c["mask_lc"] = np.where(cc[None, :] <= l_of_p[:, None], 0.0, NEG).astype(np.float32)
    c["mask_T"] = np.where(cc[:, None] <= l_of_p[None, :], 0.0, NEG).astype(np.float32)
    p = np.arange(128) % 64
    q16 = p // 16
    j = (q16 // 2) * 16 + p % 16
    is_b = q16 % 2
    inv = 1.0 / 10000.0 ** (np.arange(0, 64, 2, dtype=np.float32) / 64.0)
    c2 = np.arange(128)
    pos = core_id * 256 + (c2 // 64) * 128 + c2 % 64
    ang = pos[None, :].astype(np.float32) * inv[j][:, None]
    c["cosF"] = np.cos(ang).astype(np.float32)
    sinF = np.sin(ang).astype(np.float32)
    c["sinF"] = np.where(is_b[:, None] == 0, -sinF, sinF).astype(np.float32)
    return c


def _shared_layouts(inputs):
    g = lambda n: np.asarray(inputs[n], dtype=np.float32)
    s = {}
    s["WkTp"] = np.ascontiguousarray(g("L0_attn_W_k").T[:, _KPERM])
    s["WvT"] = np.ascontiguousarray(g("L0_attn_W_v").T)
    s["WgT"] = np.ascontiguousarray(g("L0_attn_W_g").T)
    s["WpT"] = np.ascontiguousarray(g("L0_attn_W_p").T)
    for n in ("wf", "uf", "wi", "ui", "wc", "uc", "wo", "uo"):
        s[n + "T"] = np.ascontiguousarray(g("L0_ffn_" + n).T)
    s["wmod"] = np.ascontiguousarray(g("L0_wmod"))
    s["cT"] = np.ascontiguousarray(g("L0_c").T)
    lq = g("L0_attn_latent_queries")[0]
    qbd = np.zeros((8, 128, 128), np.float32)
    for hp in range(8):
        for ph in range(2):
            blk = lq[:, 2 * hp + ph, :][:, _DPERM]
            qbd[hp, 64 * ph:64 * ph + 64, 64 * ph:64 * ph + 64] = blk.T
    s["qbd"] = np.ascontiguousarray(qbd)
    for n in ("bf", "bi", "bc", "bo"):
        s[n + "r"] = np.ascontiguousarray(g("L0_ffn_" + n).reshape(8, 128).T)
    ic = g("L0_ffn_in_c").reshape(8, 128).T
    s["in_c0"] = np.ascontiguousarray(np.broadcast_to(ic[:, :, None], (128, 8, 256)))
    return s


def _core_x(inputs, core_id):
    x = np.asarray(inputs["L0_attn_x"], dtype=np.float32)
    xs = x[:, core_id * 256:(core_id + 1) * 256, :][:, _PERM, :]
    return np.ascontiguousarray(xs.transpose(2, 0, 1))


def _assemble_output(shards):
    y = np.empty((8, 2048, 1024), np.float32)
    for i, sh in enumerate(shards):
        ys = np.asarray(sh).reshape(1024, 8, 256).transpose(1, 2, 0)
        y[:, i * 256:(i + 1) * 256, :] = ys[:, _IPERM, :]
    return y


class Sem:
    def __init__(self, h):
        self.h = h
        self.n = 0

    def bump(self, k=1):
        self.n += k
        return self.n


def _view(arena, off, nslots, dtype, shape):
    ap = arena[:, off:off + nslots]
    if dtype == BF16:
        ap = ap.bitcast(BF16)
    total = 1
    for s in shape:
        total *= s
    assert total == ap.shape[1], (shape, ap.shape)
    if len(shape) > 1:
        pat = " ".join(f"d{i}" for i in range(len(shape)))
        ap = ap.rearrange(f"p ({pat}) -> p {pat}",
                          **{f"d{i}": shape[i] for i in range(len(shape))})
    return ap


FFN_W = ("uf", "ui", "uc", "wf", "wi", "wc", "wo", "uo")


def build_program(debug=(), trunc=None):
    nc = bass.Bass("TRN2", target_bir_lowering=False, detect_race_conditions=False)
    DI = lambda name, shape, dt=F32: nc.dram_tensor(name, shape, dt, kind="ExternalInput").ap()

    x_d = DI("x_fm", [1024, B, TL])
    W_d = {n: DI(n, [1024, 1024]) for n in
           ("WkTp", "WvT", "WgT", "WpT", "wfT", "ufT", "wiT", "uiT", "wcT", "ucT", "woT", "uoT")}
    wmod_d = DI("wmod", [1024, 6144])
    cT_d = DI("cT", [1024, 8])
    qbd_d = DI("qbd", [8, 128, 128])
    ident_d = DI("ident", [128, 128])
    identbf_d = DI("ident_bf", [128, 128], U16)
    onesbf_d = DI("ones_bf", [128, 128], U16)
    negeye_d = DI("negeye", [8, 8, 64], U16)
    ind2_d = DI("ind2", [128, 2], U16)
    ind2T_d = DI("ind2T", [2, 128], U16)
    masklc_d = DI("mask_lc", [128, 64])
    maskT_d = DI("mask_T", [64, 128])
    cosF_d = DI("cosF", [128, 128])
    sinF_d = DI("sinF", [128, 128])
    bias_d = {n: DI(n + "r", [128, 8]) for n in ("bf", "bi", "bc", "bo")}
    inc0_d = DI("in_c0", [128, 8, 256])

    y_d = nc.dram_tensor("y_fm", [1024, B, TL], F32, kind="ExternalOutput").ap()
    x2_d = nc.dram_tensor("x2_scr", [1024, B, TL], F32).ap()
    arin1 = nc.dram_tensor("ar_in1", [128, 2048], BF16)
    arout1 = nc.dram_tensor("ar_out1", [128, 2048], BF16)
    arin2 = nc.dram_tensor("ar_in2", [128, 2048], BF16)
    arout2 = nc.dram_tensor("ar_out2", [128, 2048], BF16)
    arwi = nc.dram_tensor("ar_wi", [128, 64], BF16)
    arwo = nc.dram_tensor("ar_wo", [128, 64], BF16)

    DBG = os.environ.get("K_DBG", "") == "1"
    dbg_d = {}
    if DBG:
        dbg_d["gates"] = nc.dram_tensor("dbg_gates", [128, 16384], U16, kind="ExternalOutput").ap()
        dbg_d["attnB"] = nc.dram_tensor("dbg_attnB", [128, 4096], U16, kind="ExternalOutput").ap()
        dbg_d["attnP"] = nc.dram_tensor("dbg_attnP", [128, 4096], U16, kind="ExternalOutput").ap()
        dbg_d["h2"] = nc.dram_tensor("dbg_h2", [128, 16384], U16, kind="ExternalOutput").ap()
        dbg_d["x2"] = nc.dram_tensor("dbg_x2", [1024, B, TL], F32, kind="ExternalOutput").ap()
        dbg_d["cfin"] = nc.dram_tensor("dbg_cfin", [128, 2048], F32, kind="ExternalOutput").ap()
        dbg_d["uod"] = nc.dram_tensor("dbg_uod", [128, 2048], F32, kind="ExternalOutput").ap()

    st = ExitStack()
    sbuf = lambda name, shape, dt=F32: st.enter_context(nc.sbuf_tensor(name, shape, dt))[:]
    psum = lambda name, shape: st.enter_context(nc.psum_tensor(name, shape, F32))[:]

    arena = st.enter_context(nc.sbuf_tensor("arena", [128, 46080], F32))[:]

    ident = sbuf("identt", [128, 128])
    identbf = sbuf("identbft", [128, 128], BF16)
    onesbf = sbuf("onesbft", [128, 128], BF16)
    qbd = sbuf("qbdt", [128, 8, 128], BF16)
    negeye = sbuf("negeyet", [8, 8, 64], BF16)
    ind2 = sbuf("ind2t", [128, 2], BF16)
    ind2T = sbuf("ind2Tt", [2, 128], BF16)
    masklc = sbuf("masklct", [128, 64])
    maskT = sbuf("maskTt", [64, 128])
    cosF = sbuf("cosFt", [128, 128])
    sinF = sbuf("sinFt", [128, 128])
    modF = sbuf("modFt", [128, 6, 8, 8])
    cTt = sbuf("cTt", [128, 8, 8])
    silubf = sbuf("silubft", [128, 8, 8], BF16)
    Rsc = sbuf("Rsct", [128, 32])
    rdt = sbuf("rdtt", [128, 2, 8])
    RT8 = sbuf("RT8t", [8, 128], BF16)
    fbias = sbuf("fbiast", [128, 4, 8])
    rden = sbuf("rdent", [2, 512], BF16)
    consts = sbuf("constst", [128, 4])
    stch = sbuf("stcht", [128, 3, 512])
    app = sbuf("appt", [128, 2, 512])
    esc = sbuf("esct", [128, 1024])
    bfx = sbuf("bfxt", [128, 2, 512], BF16)
    bfsq = sbuf("bfsqt", [128, 2, 512], BF16)

    nc.const_aps.register(F32, 0.125, consts[:, 0:1])
    nc.const_aps.register(F32, 8.0, consts[:, 1:2])
    nc.const_aps.register(F32, 1e-5, consts[:, 2:3])
    nc.const_aps.register(F32, 1.0 / 1024.0, consts[:, 3:4])

    pA = psum("pA", [128, 512])
    pB = psum("pB", [128, 512])
    pE = psum("pE", [128, 512])
    pF = psum("pF", [128, 512])
    pT = psum("pT", [128, 1024])
    pS1 = psum("pS1", [128, 512])
    pS2 = psum("pS2", [128, 512])

    V = lambda off, n, dt, shape: _view(arena, off, n, dt, shape)
    H = V(0, 8192, BF16, (8, 2048))
    X = V(8192, 16384, F32, (8, 2048))
    Wff = {"uf": V(8192, 4096, BF16, (8, 8, 128)),
           "ui": V(12288, 4096, BF16, (8, 8, 128)),
           "uc": V(16384, 4096, BF16, (8, 8, 128)),
           "wf": V(20480, 4096, BF16, (8, 8, 128)),
           "wi": V(24576, 4096, BF16, (8, 8, 128)),
           "wc": V(28672, 4096, BF16, (8, 8, 128)),
           "wo": V(37888, 4096, BF16, (8, 8, 128)),
           "uo": V(12288, 4096, BF16, (8, 8, 128))}
    uodv = V(8192, 2048, F32, (8, 256))
    modtm = V(24576, 6144, F32, (6144,))
    kS = V(24576, 4096, BF16, (8, 8, 128))
    vS = V(28672, 4096, BF16, (8, 8, 128))
    GW = V(24576, 8192, BF16, (8, 2048))
    qbdf = V(32768, 1024, F32, (8, 128))
    ropeA = V(32768, 512, F32, (4, 128))
    ropeB = V(33280, 512, F32, (4, 128))
    vtmp = V(33792, 256, BF16, (4, 128))
    attnB = V(32768, 2048, BF16, (8, 8, 64))
    xrd = [V(32768, 2048, F32, (8, 256)), V(34816, 2048, F32, (8, 256))]
    cS = V(32768, 2048, F32, (8, 256))
    cbf = [V(34816, 1024, BF16, (8, 256)), V(35840, 1024, BF16, (8, 256))]
    probsb = [V(36864, 512, BF16, (8, 128)), V(37376, 512, BF16, (8, 128))]
    wbounce = [V(36864, 1024, BF16, (2048,)), V(36864, 1024, BF16, (2048,))]

    ptmp7 = V(36864, 1024, F32, (4, 256))
    gscG = V(36864, 512, F32, (2, 256))
    ybufB = V(36864, 1024, F32, (4, 256))
    Wslot = V(37888, 4096, BF16, (8, 8, 128))
    wmodbf = V(37888, 2048, BF16, (2, 8, 256))
    stagh = [V(41984, 2048, F32, (2048,)), V(44032, 2048, F32, (2048,))]
    x2stg = V(41984, 4096, F32, (8, 2, 256))
    x2stgB = V(32768, 4096, F32, (8, 2, 256))
    xt = [V(41984, 2048, F32, (8, 256)), V(44032, 2048, F32, (8, 256))]
    gscF = esc[:, 0:512].rearrange("p (a t) -> p a t", t=256)
    gscI = esc[:, 512:1024].rearrange("p (a t) -> p a t", t=256)
    x2b = [esc[:, 0:512].rearrange("p (a t) -> p a t", t=256),
           esc[:, 512:1024].rearrange("p (a t) -> p a t", t=256)]
    ybufA = esc.rearrange("p (a t) -> p a t", t=256)

    ops = {e: [] for e in ("tensor", "vector", "scalar", "sync", "gpsimd")}
    em = lambda e, fn: ops[e].append(fn)

    sems = {}
    for name in ("sQ", "sQ2", "sG", "sT", "sV", "sS", "sC", "sO", "sP"):
        sems[name] = Sem(st.enter_context(nc.semaphore(name)))
    sQ, sQ2, sG, sT, sV, sS, sC, sO, sP = (sems[k] for k in
                                           ("sQ", "sQ2", "sG", "sT", "sV", "sS", "sC",
                                            "sO", "sP"))

    def dma(eng, sem, out, in_):
        v = sem.bump(16)
        em(eng, lambda E, o=out, i=in_, s=sem.h: E.dma_start(out=o, in_=i).then_inc(s, 16))
        return v

    def wt(eng, sem, v):
        if v:
            em(eng, lambda E, s=sem.h, vv=v: E.wait_ge(s, vv))

    def vedge():
        em("vector", lambda E: E.engine_nop())
        v = inc("vector", sV)
        wt("vector", sV, v)

    def wt_pair(eng, pair):
        wt(eng, sV, pair[0])
        wt(eng, sS, pair[1])

    def wtv(eng, sv):
        if sv[1]:
            wt(eng, sv[0], sv[1])

    def inc(eng, sem, k=1):
        v = sem.bump(k)
        last = ops[eng][-1]

        def wrapped(E, f=last, s=sem.h, kk=k):
            ins = f(E)
            assert ins is not None, "inc() must follow an instruction-emitting op"
            return ins.then_inc(s, kk)
        ops[eng][-1] = wrapped
        return v

    xld = dma("sync", sQ, X.rearrange("p f (b t) -> p f b t", t=256),
              x_d.rearrange("(f p) b t -> p f b t", p=128))
    for out, d in (
            (ident, ident_d), (identbf, identbf_d.bitcast(BF16)), (onesbf, onesbf_d.bitcast(BF16)),
            (qbdf, qbd_d.rearrange("h a b -> a h b")),
            (negeye.rearrange("k h c -> k (h c)"),
             negeye_d.rearrange("k h c -> k (h c)").bitcast(BF16)),
            (ind2, ind2_d.bitcast(BF16)), (ind2T, ind2T_d.bitcast(BF16)),
            (masklc, masklc_d), (maskT, maskT_d), (cosF, cosF_d), (sinF, sinF_d),
            (cTt, cT_d.rearrange("(i p) b -> p i b", p=128)),
            (fbias[:, 0, :], bias_d["bf"]), (fbias[:, 1, :], bias_d["bi"]),
            (fbias[:, 2, :], bias_d["bc"]), (fbias[:, 3, :], bias_d["bo"])):
        dma("gpsimd", sG, out, d)
    cver = sG.n

    wt("gpsimd", sG, cver)
    wdma = dma("gpsimd", sG, arwi[:, :], identbf[:, 0:64])
    wt("gpsimd", sG, wdma)
    em("gpsimd", lambda E: E.collective_compute(
        "AllReduce", ALU.add, replica_groups=[list(range(NCORES))],
        ins=[arwi.ap().opt()], outs=[arwo.ap().opt()]).then_inc(sC.h, 1))
    sC.bump(1)

    def _initc(E):
        E.memset(consts[:, 0:1], 0.125)
        E.memset(consts[:, 1:2], 8.0)
        E.memset(consts[:, 2:3], 1e-5)
        E.memset(consts[:, 3:4], 1.0 / 1024.0)
    em("vector", _initc)
    wt("vector", sG, cver)
    em("vector", lambda E: E.tensor_copy(out=qbd.rearrange("p h c -> p (h c)"),
                                         in_=qbdf.rearrange("p h c -> p (h c)")))
    qbd_ready = inc("vector", sV)

    wt("scalar", sG, cver)
    em("scalar", lambda E: E.activation(out=silubf.rearrange("p i b -> p (i b)"),
                                        in_=cTt.rearrange("p i b -> p (i b)"), func=AF.Silu))
    silu_ready = inc("scalar", sS)
    wt("tensor", sS, silu_ready)

    wmod_r = wmod_d.rearrange("(i p) j -> p i j", p=128)
    wmod_mm_done = {}
    wmod_cast_done = {}
    CENG = {0: ("vector", sV), 1: ("scalar", sS)}
    for jt in range(24):
        half = jt % 2
        ceng, csem = CENG[half]
        qeng, qsem = (("sync", sQ) if half == 0 else ("scalar", sQ2))
        if jt >= 2:
            wt(qeng, csem, wmod_cast_done[jt - 2])
        lv = dma(qeng, qsem, stagh[half].rearrange("p (i j) -> p i j", j=256),
                 wmod_r[:, :, jt * 256:(jt + 1) * 256])
        wt(ceng, qsem, lv)
        if jt >= 2:
            wt(ceng, sT, wmod_mm_done[jt - 2])
        dst = wmodbf[:, half, :, :].rearrange("p i j -> p (i j)")
        if half == 0:
            em(ceng, lambda E, d=dst, h=half: E.tensor_copy(out=d, in_=stagh[h]))
        else:
            em(ceng, lambda E, d=dst, h=half: E.activation(out=d, in_=stagh[h],
                                                           func=AF.Identity))
        wmod_cast_done[jt] = inc(ceng, csem)
        wt("tensor", csem, wmod_cast_done[jt])
        bank = pS1 if half == 0 else pS2
        for ic in range(8):
            em("tensor", lambda E, ic=ic, b=bank, h=half:
               E.matmul(b[0:8, 0:256], silubf[:, ic, :], wmodbf[:, h, ic, :],
                        start=(ic == 0), stop=(ic == 7)))
        wmod_mm_done[jt] = inc("tensor", sT)
        wt("vector", sT, wmod_mm_done[jt])
        em("vector", lambda E, jt=jt, b=bank:
           E.tensor_copy(out=modtm[0:8, jt * 256:(jt + 1) * 256], in_=b[0:8, 0:256]))
        inc("vector", sV)

    def emit_mod_tr(t48s):
        for t48 in t48s:
            vidx, fc = t48 // 8, t48 % 8
            em("tensor", lambda E, t48=t48:
               E.transpose(pA[:, 0:8], modtm[0:8, t48 * 128:(t48 + 1) * 128], ident[0:8, 0:8]))
            tv = inc("tensor", sT)
            wt("vector", sT, tv)
            em("vector", lambda E, vidx=vidx, fc=fc:
               E.tensor_copy(out=modF[:, vidx, fc, :], in_=pA[:, 0:8]))
            vv = inc("vector", sV)
            wt("tensor", sV, vv)
        return sV.n

    modA_ready = emit_mod_tr(range(16))

    vedge()
    em("vector", lambda E: E.tensor_scalar(out=modF[:, 1, :, :], in0=modF[:, 1, :, :],
                                           scalar1=1.0, scalar2=None, op0=ALU.add))
    modA_ready = inc("vector", sV)

    modB_ready = emit_mod_tr(range(16, 48))

    vedge()
    em("vector", lambda E: E.tensor_scalar(out=modF[:, 4, :, :], in0=modF[:, 4, :, :],
                                           scalar1=1.0, scalar2=None, op0=ALU.add))
    for vidx in (2, 5):
        g = modF[:, vidx, :, :]
        hi = app[:, 0, 0:64].rearrange("p (a b) -> p a b", b=8)
        lo = app[:, 0, 64:128].rearrange("p (a b) -> p a b", b=8)
        sg = app[:, 0, 128:192].rearrange("p (a b) -> p a b", b=8)
        vedge()
        em("vector", lambda E, g=g, hi=hi: E.tensor_scalar(out=hi, in0=g, scalar1=0.25,
                                                           scalar2=None, op0=ALU.max))
        em("vector", lambda E, g=g, lo=lo: E.tensor_scalar(out=lo, in0=g, scalar1=-0.25,
                                                           scalar2=None, op0=ALU.min))
        em("vector", lambda E, g=g, sg=sg: E.tensor_scalar(out=sg, in0=g, scalar1=0.0,
                                                           scalar2=None, op0=ALU.is_ge))
        vedge()
        em("vector", lambda E, hi=hi, lo=lo: E.tensor_tensor(out=hi, in0=hi, in1=lo,
                                                             op=ALU.subtract))
        em("vector", lambda E, hi=hi, sg=sg: E.tensor_tensor(out=hi, in0=sg, in1=hi,
                                                             op=ALU.mult))
        em("vector", lambda E, g=g, hi=hi, lo=lo: E.tensor_tensor(out=g, in0=hi, in1=lo,
                                                                  op=ALU.add))
    modB_ready = inc("vector", sV)

    def stage_cast_q(dram, dest, q, dest_guard=None):
        if stage_cast.lastcast is None:
            stage_cast.lastcast = {0: wmod_cast_done[22], 1: wmod_cast_done[23]}
        src = dram.rearrange("(i p) o -> p i o", p=128)
        half = q % 2
        ceng, csem = CENG[half]
        wt("sync", csem, stage_cast.lastcast.get(half, 0))
        lv = dma("sync", sQ, stagh[half].rearrange("p (i o) -> p i o", o=256),
                 src[:, :, q * 256:(q + 1) * 256])
        wt(ceng, sQ, lv)
        if dest_guard is not None:
            wt(ceng, dest_guard[0], dest_guard[1])
        sh = stagh[half].rearrange("p (i o) -> p i o", o=256)
        dst = dest[:, :, 2 * q:2 * q + 2, :].rearrange("p i a b -> p i (a b)")
        if half == 0:
            em(ceng, lambda E, d=dst, s=sh: E.tensor_copy(out=d, in_=s))
        else:
            em(ceng, lambda E, d=dst, s=sh: E.activation(out=d, in_=s,
                                                         func=AF.Identity))
        v = inc(ceng, csem)
        stage_cast.lastcast[half] = v
        return v

    def stage_cast(dram, dest, dest_guard=None, nq=4):
        for q in range(nq):
            stage_cast_q(dram, dest, q, dest_guard=dest_guard if q in (0, 1) else None)
        return (stage_cast.lastcast[0], stage_cast.lastcast[1])
    stage_cast.lastcast = None

    def direct_weight(name, dest, dest_guard=None):
        return stage_cast(W_d[name + "T"], dest, dest_guard=dest_guard)

    def emit_ln(src_fn, cvec, svec, mod_wait=0, prep_fn=None, post_fn=None,
                src_qwait_fn=None):
        mmv_par = [0, 0]
        act_par = [0, 0]
        psum_rd = [0]
        pool_done = [0]
        pairs = [(app[:, 0, :], app[:, 1, :]),
                 (esc[:, 0:512], esc[:, 512:1024])]
        h2d = [0]

        def stats(tt):
            if prep_fn is not None:
                prep_fn(tt)
            for fc in range(8):
                par = fc % 2
                s = src_fn(fc, tt)
                wt("vector", sT, mmv_par[par])
                em("vector", lambda E, s=s, par=par: E.tensor_copy(out=bfx[:, par, :], in_=s))
                cv = inc("vector", sV)
                wt("scalar", sV, cv)
                wt("scalar", sT, mmv_par[par])
                em("scalar", lambda E, par=par:
                   E.activation(out=bfsq[:, par, :], in_=bfx[:, par, :], func=AF.Square))
                sq2 = inc("scalar", sS)
                wt("tensor", sV, cv)
                wt("tensor", sS, sq2)
                if fc == 0:
                    wt("tensor", sV, psum_rd[0])
                em("tensor", lambda E, par=par, fc=fc:
                   E.matmul(pS1, onesbf, bfx[:, par, :], start=(fc == 0), stop=(fc == 7)))
                em("tensor", lambda E, par=par, fc=fc:
                   E.matmul(pS2, onesbf, bfsq[:, par, :], start=(fc == 0), stop=(fc == 7)))
                mmv_par[par] = inc("tensor", sT)

        def chain_apply(tt, mmv_tail):
            wt("vector", sT, mmv_tail)
            wt("vector", sS, act_par[0])
            wt("vector", sS, act_par[1])
            wt("vector", sP, pool_done[0])
            em("vector", lambda E: E.tensor_scalar(out=stch[:, 0, :], in0=pS1,
                                                   scalar1=1.0 / 1024.0, scalar2=None, op0=ALU.mult))
            em("vector", lambda E: E.tensor_scalar(out=app[:, 1, :], in0=pS2,
                                                   scalar1=1.0 / 1024.0, scalar2=None, op0=ALU.mult))
            psum_rd[0] = inc("vector", sV)
            if tt < 3:
                stats(tt + 1)
            vedge()
            em("vector", lambda E: E.tensor_tensor(out=app[:, 0, :], in0=stch[:, 0, :],
                                                   in1=stch[:, 0, :], op=ALU.mult))
            em("vector", lambda E: E.tensor_tensor(out=app[:, 1, :], in0=app[:, 1, :],
                                                   in1=app[:, 0, :], op=ALU.subtract))
            chv = inc("vector", sV)
            wt("scalar", sV, chv)
            em("scalar", lambda E: E.activation(out=app[:, 0, :], in_=app[:, 1, :],
                                                func=AF.Sqrt, bias=1e-5, scale=1.0))
            sqv = inc("scalar", sS)
            wt("vector", sS, sqv)
            em("vector", lambda E: E.reciprocal(out=stch[:, 1, :], in_=app[:, 0, :]))
            vedge()
            em("vector", lambda E: E.tensor_tensor(out=stch[:, 2, :], in0=stch[:, 0, :],
                                                   in1=stch[:, 1, :], op=ALU.mult))
            chain_v = inc("vector", sV)
            for fc in range(8):
                par = fc % 2
                s = src_fn(fc, tt)
                t1, t2 = pairs[par]
                if mod_wait and tt == 0 and fc == 0:
                    wt("scalar", sV, mod_wait)
                if par == 0:
                    aeng, asem = "vector", sV
                else:
                    aeng, asem = "gpsimd", sP
                    if fc == 1:
                        wt("gpsimd", sV, chain_v)
                        if src_qwait_fn is not None:
                            wt("gpsimd", sQ, src_qwait_fn(tt))
                wt(aeng, sS, act_par[par])
                em(aeng, lambda E, s=s, t1=t1: E.tensor_tensor(out=t1, in0=s,
                                                               in1=stch[:, 1, :], op=ALU.mult))
                em(aeng, lambda E, t1=t1, t2=t2: E.tensor_tensor(out=t2, in0=t1,
                                                                 in1=stch[:, 2, :],
                                                                 op=ALU.subtract))
                av = inc(aeng, asem)
                if par == 1:
                    pool_done[0] = av
                wt("scalar", asem, av)
                for hf in range(2):
                    b = 2 * tt + hf
                    em("scalar", lambda E, fc=fc, tt=tt, hf=hf, b=b, t2=t2:
                       E.activation(out=H[:, fc, tt * 512 + hf * 256: tt * 512 + hf * 256 + 256],
                                    in_=t2[:, hf * 256:hf * 256 + 256], func=AF.Identity,
                                    bias=modF[:, svec, fc, b:b + 1],
                                    scale=modF[:, cvec, fc, b:b + 1]))
                act_par[par] = inc("scalar", sS)
                h2d[0] = act_par[par]
            if post_fn is not None:
                post_fn(tt)

        stats(0)
        tails = [0, 0, 0, 0]
        tails[0] = mmv_par[1]
        for tt in range(4):
            chain_apply(tt, tails[tt])
            tails[tt + 1 if tt < 3 else 0] = mmv_par[1]
        return h2d[0]

    wt("vector", sQ, xld)
    h1_done = emit_ln(lambda fc, tt: X[:, fc, tt * 512:(tt + 1) * 512], 1, 0,
                      mod_wait=modA_ready, src_qwait_fn=lambda tt: xld)

    wk_cast = stage_cast(W_d["WkTp"], Wslot, dest_guard=(sT, sT.n))
    wt_pair("tensor", wk_cast)
    wt("tensor", sS, h1_done)

    Hr = [H[:, ic, :].rearrange("p (b t) -> p b t", t=256) for ic in range(8)]
    rope_free = {}
    for i3 in range(16):
        hp, bg = i3 // 2, i3 % 2
        bank = pA if i3 % 2 == 0 else pB
        wt("tensor", sV, rope_free.get(i3 - 2, 0))
        for ic in range(8):
            em("tensor", lambda E, ic=ic, hp=hp, bg=bg, bank=bank:
               E.matmul(bank.rearrange("p (b c) -> p b c", c=128),
                        Wslot[:, ic, hp, :], Hr[ic][:, bg * 4:(bg + 1) * 4, 0:128],
                        start=(ic == 0), stop=(ic == 7)))
        mv = inc("tensor", sT)
        wt("vector", sT, mv)
        bank3 = bank.rearrange("p (b c) -> p b c", c=128)
        em("vector", lambda E, bank=bank: E.stream_shuffle(
            out=ropeB.rearrange("p b c -> p (b c)"), in_=bank, mask=SWAP16))
        em("vector", lambda E, bank3=bank3: E.tensor_tensor(
            out=ropeA, in0=bank3, in1=cosF.unsqueeze(1).to_broadcast([128, 4, 128]), op=ALU.mult))
        em("vector", lambda E: E.tensor_tensor(
            out=ropeB, in0=ropeB, in1=sinF.unsqueeze(1).to_broadcast([128, 4, 128]), op=ALU.mult))
        em("vector", lambda E, hp=hp, bg=bg: E.tensor_tensor(
            out=kS[:, hp, bg * 4:(bg + 1) * 4, :], in0=ropeA, in1=ropeB, op=ALU.add))
        rope_free[i3] = inc("vector", sV)

    wv_cast = stage_cast(W_d["WvT"], Wslot, dest_guard=(sT, sT.n))
    wt_pair("tensor", wv_cast)
    vt_free = {}
    for i3 in range(16):
        hp, bg = i3 // 2, i3 % 2
        bank = pA if i3 % 2 == 0 else pB
        tbank = pE if i3 % 2 == 0 else pF
        wtv("tensor", vt_free.get(i3 - 2, (sV, 0)))
        for ic in range(8):
            em("tensor", lambda E, ic=ic, hp=hp, bg=bg, bank=bank:
               E.matmul(bank.rearrange("p (b c) -> p b c", c=128),
                        Wslot[:, ic, hp, :], Hr[ic][:, bg * 4:(bg + 1) * 4, 0:128],
                        start=(ic == 0), stop=(ic == 7)))
        mv = inc("tensor", sT)
        wt("vector", sT, mv)
        em("vector", lambda E, bank=bank:
           E.tensor_copy(out=vtmp.rearrange("p b c -> p (b c)"), in_=bank))
        cv = inc("vector", sV)
        wt("tensor", sV, cv)
        for br in range(4):
            em("tensor", lambda E, br=br, tbank=tbank:
               E.transpose(tbank[:, br * 64:(br + 1) * 64].bitcast(BF16), vtmp[:, br, :], identbf))
        tv = inc("tensor", sT)
        wt("vector", sT, tv)
        em("vector", lambda E, hp=hp, bg=bg, tbank=tbank:
           E.tensor_copy(out=vS[:, bg * 4:(bg + 1) * 4, hp, :],
                         in_=tbank[:, 0:256].bitcast(BF16).rearrange("p (b c) -> p b c", c=128)))
        vt_free[i3] = (sV, inc("vector", sV))
    vproj_done_sT = sT.n

    wff_cast = {}
    wff_pending = [(n, q) for n in ("uf", "ui", "uc", "wf") for q in range(4)]

    def wff_feed(k=2):
        for _ in range(k):
            if not wff_pending:
                return
            n, q = wff_pending.pop(0)
            g = (sS, h1_done) if (n == "uf" and q in (0, 1)) else None
            stage_cast_q(W_d[n + "T"], Wff[n], q, dest_guard=g)
            if q == 3:
                wff_cast[n] = (stage_cast.lastcast[0], stage_cast.lastcast[1])

    def _finish_trunc():
        em("gpsimd", lambda E, v=sV.n: E.wait_ge(sV.h, v))
        em("gpsimd", lambda E, v=sT.n: E.wait_ge(sT.h, v))
        em("gpsimd", lambda E, v=sS.n: E.wait_ge(sS.h, v))
        with nc.Block() as block:
            @block.tensor
            def _(E):
                for fn in ops["tensor"]:
                    fn(E)
            @block.vector
            def _(E):
                for fn in ops["vector"]:
                    fn(E)
            @block.scalar
            def _(E):
                for fn in ops["scalar"]:
                    fn(E)
            @block.sync
            def _(E):
                for fn in ops["sync"]:
                    fn(E)
            @block.gpsimd
            def _(E):
                for fn in ops["gpsimd"]:
                    fn(E)
        st.close()
        return nc

    if trunc == "P3":
        return _finish_trunc()

    wt("tensor", sV, qbd_ready)
    probs_free = [0, 0]
    pEF_free_t = (sV, 0)
    ev_vec = [0]
    pT_free = 0
    pS1_free = 0
    probs_written = 0
    P4B = int(os.environ.get("K_P4B", "8"))
    P4S = int(os.environ.get("K_P4S", "99"))
    for b in range(min(8, P4B)):
        if b >= 1:
            wff_feed(2)
        pbuf = probsb[b % 2]
        for ch in range(2):
            wt("tensor", sV, pS1_free)
            for hp in range(8):
                em("tensor", lambda E, hp=hp, b=b, ch=ch:
                   E.matmul(pS1[:, hp * 64:(hp + 1) * 64], qbd[:, hp, :],
                            kS[:, hp, b, ch * 64:(ch + 1) * 64],
                            start=(hp == 0), stop=(hp == 7)))
            lcv = inc("tensor", sT)
            if P4S < 2:
                continue
            wt("tensor", sV, pT_free)
            for hp in range(8):
                em("tensor", lambda E, hp=hp, b=b, ch=ch:
                   E.matmul(pT[0:64, hp * 128:(hp + 1) * 128],
                            kS[:, hp, b, ch * 64:(ch + 1) * 64], qbd[:, hp, :],
                            start=(hp in (0, 4)), stop=False))
            tTv = inc("tensor", sT)
            if P4S < 3:
                continue
            wt("vector", sT, lcv)
            wt("vector", sS, probs_written)
            pS13 = pS1.rearrange("p (h c) -> p h c", c=64)
            esc3 = esc[:, 0:512].rearrange("p (h c) -> p h c", c=64)
            em("vector", lambda E, pS13=pS13: E.reduce_max(out=Rsc[:, 0:8], in_=pS13, axis=AXL.X))
            em("vector", lambda E, pS13=pS13, esc3=esc3: E.tensor_tensor(
                out=esc3, in0=pS13,
                in1=Rsc[:, 0:8].unsqueeze(2).to_broadcast([128, 8, 64]), op=ALU.subtract))
            em("vector", lambda E, esc3=esc3: E.tensor_tensor(
                out=esc3, in0=esc3,
                in1=masklc.unsqueeze(1).to_broadcast([128, 8, 64]), op=ALU.add))
            pS1_free = inc("vector", sV)
            wt("scalar", sV, pS1_free)
            em("scalar", lambda E: E.activation(out=esc[:, 512:1024], in_=esc[:, 0:512],
                                                func=AF.Exp, bias=0.0, scale=0.125))
            ev = inc("scalar", sS)
            wt("vector", sS, ev)
            em("vector", lambda E: E.reduce_sum(
                out=Rsc[:, 8:16], in_=esc[:, 512:1024].rearrange("p (h c) -> p h c", c=64),
                axis=AXL.X))
            vedge()
            em("vector", lambda E, ch=ch: E.reciprocal(out=rdt[:, ch, :], in_=Rsc[:, 8:16]))
            rvv = inc("vector", sV)
            if P4S < 4:
                continue
            wt("tensor", sV, rvv)
            em("tensor", lambda E: E.transpose(pS2[0:8, 0:128], Rsc[:, 0:8], ident))
            trv = inc("tensor", sT)
            wt("vector", sT, trv)
            em("vector", lambda E: E.tensor_copy(out=RT8, in_=pS2[0:8, 0:128]))
            rt8v = inc("vector", sV)
            if P4S < 5:
                continue
            wt("tensor", sV, rt8v)
            for hp in range(8):
                em("tensor", lambda E, hp=hp:
                   E.matmul(pT[0:64, hp * 128:(hp + 1) * 128], negeye[:, hp, :], RT8,
                            start=False, stop=(hp in (3, 7))))
            rsub = inc("tensor", sT)
            wt("vector", sT, rsub)
            pT3 = pT[0:64, :].rearrange("p (h c) -> p h c", c=128)
            esc4 = esc[0:64, :].rearrange("p (h c) -> p h c", c=128)
            em("vector", lambda E, pT3=pT3, esc4=esc4: E.tensor_tensor(
                out=esc4, in0=pT3, in1=maskT.unsqueeze(1).to_broadcast([64, 8, 128]), op=ALU.add))
            pT_free = inc("vector", sV)
            mkv = pT_free
            wt("scalar", sV, mkv)
            wt("scalar", sT, probs_free[b % 2])
            em("scalar", lambda E, pbuf=pbuf, ch=ch:
               E.activation(out=pbuf[64 * ch:64 * ch + 64, :, :].rearrange("p h c -> p (h c)"),
                            in_=esc[0:64, :], func=AF.Exp, bias=0.0, scale=0.125))
            probs_written = inc("scalar", sS)
        if P4S < 6:
            continue
        wt("tensor", sS, probs_written)
        wtv("tensor", pEF_free_t)
        for ch in range(2):
            cbank = pE if ch == 0 else pF
            for hp in range(8):
                for ph in range(2):
                    em("tensor", lambda E, ch=ch, hp=hp, ph=ph, b=b, cbank=cbank:
                       E.matmul(cbank[64 * ph:64 * ph + 64, hp * 64:(hp + 1) * 64],
                                probsb[b % 2][64 * ch:64 * ch + 64, hp, ph * 64:ph * 64 + 64],
                                vS[64 * ch:64 * ch + 64, b, hp, ph * 64:ph * 64 + 64],
                                start=True, stop=True, skip_group_check=True))
        probs_free[b % 2] = inc("tensor", sT)
        if P4S < 7:
            continue
        wt("vector", sT, probs_free[b % 2])
        em("vector", lambda E: E.tensor_tensor(
            out=app[:, 0, :].rearrange("p (h d) -> p h d", d=64),
            in0=pE.rearrange("p (h d) -> p h d", d=64),
            in1=rdt[:, 0, :].unsqueeze(2).to_broadcast([128, 8, 64]), op=ALU.mult))
        em("vector", lambda E: E.tensor_tensor(
            out=app[:, 1, :].rearrange("p (h d) -> p h d", d=64),
            in0=pF.rearrange("p (h d) -> p h d", d=64),
            in1=rdt[:, 1, :].unsqueeze(2).to_broadcast([128, 8, 64]), op=ALU.mult))
        em("vector", lambda E, b=b:
           E.tensor_tensor(out=attnB[:, b, :, :].rearrange("p h d -> p (h d)"),
                           in0=app[:, 0, :], in1=app[:, 1, :], op=ALU.add))
        evv = inc("vector", sV)
        pEF_free_t = (sV, evv)
        ev_vec[0] = evv
        if b == 3:
            wt("gpsimd", sV, ev_vec[0])
            arw1 = dma("gpsimd", sG, arin1[:, :],
                       attnB.rearrange("p b h d -> p (b h d)")[:, 0:2048])
            wt("gpsimd", sG, arw1)
            em("gpsimd", lambda E: E.collective_compute(
                "AllReduce", ALU.add, replica_groups=[list(range(NCORES))],
                ins=[arin1.ap().opt()], outs=[arout1.ap().opt()]).then_inc(sC.h, 1))
            cc1 = sC.bump(1)

    wff_feed(len(wff_pending))

    if trunc == "P4":
        return _finish_trunc()

    wt("gpsimd", sV, ev_vec[0])
    arw2 = dma("gpsimd", sG, arin2[:, :],
               attnB.rearrange("p b h d -> p (b h d)")[:, 2048:4096])
    wt("gpsimd", sG, arw2)
    em("gpsimd", lambda E: E.collective_compute(
        "AllReduce", ALU.add, replica_groups=[list(range(NCORES))],
        ins=[arin2.ap().opt()], outs=[arout2.ap().opt()]).then_inc(sC.h, 1))
    cc2 = sC.bump(1)
    wt("gpsimd", sC, cc1)
    ar_r1 = dma("gpsimd", sG, attnB.rearrange("p b h d -> p (b h d)")[:, 0:2048],
                arout1[:, :])
    wt("gpsimd", sC, cc2)
    ar_r = dma("gpsimd", sG, attnB.rearrange("p b h d -> p (b h d)")[:, 2048:4096],
               arout2[:, :])

    if trunc == "CC":
        wt("gpsimd", sG, ar_r)
        return _finish_trunc()

    wg_cast = stage_cast(W_d["WgT"], Wslot, dest_guard=(sT, vproj_done_sT))
    wt_pair("tensor", wg_cast)
    g_free = [0, 0]
    eg_free_t = (sV, 0)
    egb = esc[:, 0:256].bitcast(BF16)
    for pc in range(8):
        for tt in range(4):
            it_ = pc * 4 + tt
            bank = pA if it_ % 2 == 0 else pB
            wt("tensor", sS, g_free[it_ % 2])
            for ic in range(8):
                em("tensor", lambda E, ic=ic, pc=pc, tt=tt, bank=bank:
                   E.matmul(bank, Wslot[:, ic, pc, :], H[:, ic, tt * 512:(tt + 1) * 512],
                            start=(ic == 0), stop=(ic == 7)))
            gv = inc("tensor", sT)
            wt("scalar", sT, gv)
            wtv("scalar", eg_free_t)
            em("scalar", lambda E, bank=bank:
               E.activation(out=egb, in_=bank, func=AF.Exp, bias=0.0, scale=0.125))
            g_free[it_ % 2] = inc("scalar", sS)
            wt("tensor", sS, g_free[it_ % 2])
            em("tensor", lambda E: E.matmul(pS2[0:2, 0:512], ind2, egb, start=True, stop=True))
            dnv = inc("tensor", sT)
            wt("vector", sT, dnv)
            em("vector", lambda E: E.reciprocal(out=app[0:2, 0, :], in_=pS2[0:2, 0:512]))
            vedge()
            em("vector", lambda E: E.tensor_copy(out=rden, in_=app[0:2, 0, :]))
            rdv = inc("vector", sV)
            wt("tensor", sV, rdv)
            wtv("tensor", eg_free_t)
            em("tensor", lambda E: E.matmul(pS1, ind2T, rden, start=True, stop=True))
            exv = inc("tensor", sT)
            wt("vector", sT, exv)
            em("vector", lambda E, pc=pc, tt=tt:
               E.tensor_tensor(out=GW[:, pc, tt * 512:(tt + 1) * 512], in0=egb, in1=pS1,
                               op=ALU.mult))
            eg_free_t = (sV, inc("vector", sV))

    if trunc == "P5":
        return _finish_trunc()

    wp_cast = stage_cast(W_d["WpT"], Wslot, dest_guard=(sT, sT.n))



    wt("tensor", sG, ar_r1)
    wtv("tensor", eg_free_t)
    w_free = [(sV, 0), (sS, 0)]
    for b in range(8):
        if b == 4:
            wt("tensor", sG, ar_r)
        for hp in range(8):
            it_ = hp * 8 + b
            bank = pA if it_ % 2 == 0 else pB
            wtv("tensor", w_free[it_ % 2])
            for ph in range(2):
                em("tensor", lambda E, hp=hp, b=b, ph=ph, bank=bank:
                   E.matmul(bank[64 * ph:64 * ph + 64, 0:256],
                            attnB[64 * ph:64 * ph + 64, b, hp, :],
                            GW[64 * ph:64 * ph + 64, hp, b * 256:(b + 1) * 256],
                            start=True, stop=True, skip_group_check=True))
            wv = inc("tensor", sT)
            if it_ % 2 == 0:
                wt("vector", sT, wv)
                em("vector", lambda E, hp=hp, b=b, bank=bank:
                   E.tensor_copy(out=GW[:, hp, b * 256:(b + 1) * 256], in_=bank[:, 0:256]))
                w_free[0] = (sV, inc("vector", sV))
            else:
                wt("scalar", sT, wv)
                em("scalar", lambda E, hp=hp, b=b, bank=bank:
                   E.activation(out=GW[:, hp, b * 256:(b + 1) * 256], in_=bank[:, 0:256],
                                func=AF.Identity))
                w_free[1] = (sS, inc("scalar", sS))

    if trunc == "P6":
        return _finish_trunc()

    wt_pair("tensor", wp_cast)
    wtv("tensor", w_free[0])
    wtv("tensor", w_free[1])
    wt("vector", sV, modB_ready)
    x_r = x_d.rearrange("(f p) b t -> p f b t", p=128)
    x2_r = x2_d.rearrange("(f p) b t -> p f b t", p=128)
    p7_free = [0, 0]
    x2b_free = [0, 0]
    xrd_v = [0, 0]
    xrd_used = [0, 0]
    wt("gpsimd", sT, sT.n)
    for oc in range(8):
        wt("gpsimd", sV, xrd_used[oc % 2])
        xrd_v[oc % 2] = dma("gpsimd", sG, xrd[oc % 2].rearrange("p b t -> p (b t)"),
                            x_r[:, oc, :, :].rearrange("p b t -> p (b t)"))
        for tt in range(4):
            it_ = oc * 4 + tt
            bank = pA if it_ % 2 == 0 else pB
            wt("tensor", sV, p7_free[it_ % 2])
            for ic in range(8):
                em("tensor", lambda E, ic=ic, oc=oc, tt=tt, bank=bank:
                   E.matmul(bank, Wslot[:, ic, oc, :], GW[:, ic, tt * 512:(tt + 1) * 512],
                            start=(ic == 0), stop=(ic == 7)))
            pv = inc("tensor", sT)
            wt("vector", sT, pv)
            wt("vector", sG, xrd_v[oc % 2])
            wt("vector", sG, x2b_free[tt % 2])
            for hf in range(2):
                b = tt * 2 + hf
                em("vector", lambda E, oc=oc, tt=tt, hf=hf, b=b, bank=bank:
                   E.scalar_tensor_tensor(
                       out=x2b[tt % 2][:, hf, :],
                       in0=bank[:, hf * 256:hf * 256 + 256],
                       scalar=modF[:, 2, oc, b:b + 1],
                       in1=xrd[oc % 2][:, b, :],
                       op0=ALU.mult, op1=ALU.add))
            p7_free[it_ % 2] = inc("vector", sV)
            xrd_used[oc % 2] = p7_free[it_ % 2]
            wt("gpsimd", sV, p7_free[it_ % 2])
            x2b_free[tt % 2] = dma(
                "gpsimd", sG,
                x2_r[:, oc, 2 * tt:2 * tt + 2, :].rearrange("p b t -> p (b t)"),
                x2b[tt % 2].rearrange("p a t -> p (a t)"))
    x2_written = sG.n
    p7_done_sT = sT.n
    for n in ("wi", "wc"):
        wff_cast[n] = direct_weight(n, Wff[n], dest_guard=(sT, p7_done_sT) if n == "wi" else None)
    if trunc == "P7":
        return _finish_trunc()

    x2s_free = [(0, 0), (0, 0)]
    if trunc == "P7b":
        return _finish_trunc()

    x2bufs = [x2stg, x2stgB]
    p8_lv = [0, 0]

    def p8_load(tt):
        wt("sync", sG, x2_written)
        if tt % 2 == 0:
            wt("sync", sV, stage_cast.lastcast[0])
            wt("sync", sS, stage_cast.lastcast[1])
        wt("sync", sV, x2s_free[tt % 2][0])
        wt("sync", sP, x2s_free[tt % 2][1])
        return dma("sync", sQ, x2bufs[tt % 2], x2_r[:, :, 2 * tt:2 * tt + 2, :])

    def p8_prep(tt):
        p8_lv[tt % 2] = p8_load(tt)
        wt("vector", sQ, p8_lv[tt % 2])

    def p8_post(tt):
        x2s_free[tt % 2] = (sV.n, sP.n)

    h2_done = emit_ln(lambda fc, tt: x2bufs[tt % 2][:, fc, :, :].rearrange("p a t -> p (a t)"),
                      4, 3, prep_fn=p8_prep, post_fn=p8_post,
                      src_qwait_fn=lambda tt: p8_lv[tt % 2])

    wt("sync", sS, h2_done)
    c0v = dma("sync", sQ, cS.rearrange("p b t -> p (b t)"), inc0_d.rearrange("p b t -> p (b t)"))
    wt("vector", sQ, c0v)
    em("vector", lambda E: E.tensor_copy(out=cbf[0].rearrange("p b t -> p (b t)"),
                                         in_=cS.rearrange("p b t -> p (b t)")))
    c0_ready = inc("vector", sV)

    wt("tensor", sV, c0_ready)
    wt("tensor", sS, h2_done)

    GATES = (("f", AF.Sigmoid, 0), ("i", AF.Sigmoid, 1), ("c", AF.Tanh, 2))
    gate_waited = set()
    bank_free = {}
    step_done = 0
    for t in range(8):
        cbf_r = cbf[t % 2]
        cbf_w = cbf[(t + 1) % 2]
        wt("tensor", sV, step_done)
        for oc in range(8):
            par = oc % 2
            banks = {"f": (pA if par == 0 else pB),
                     "i": (pE if par == 0 else pF),
                     "c": (pT[:, 0:256] if par == 0 else pT[:, 512:768])}
            gv = {}
            for gname, gfunc, gidx in GATES:
                if gname not in gate_waited:
                    gate_waited.add(gname)
                    wt_pair("tensor", wff_cast["w" + gname])
                    wt_pair("tensor", wff_cast["u" + gname])
                wt("tensor", sV, bank_free.get((gname, par), 0))
                wname, uname = "w" + gname, "u" + gname
                for ic in range(8):
                    em("tensor", lambda E, ic=ic, oc=oc, t=t, gn=gname, bank=banks[gname]:
                       E.matmul(bank[:, 0:256] if bank.shape[1] > 256 else bank,
                                Wff["w" + gn][:, ic, oc, :], H[:, ic, t * 256:(t + 1) * 256],
                                start=(ic == 0), stop=False))
                for ic in range(8):
                    em("tensor", lambda E, ic=ic, oc=oc, gn=gname, bank=banks[gname], cb=cbf_r:
                       E.matmul(bank[:, 0:256] if bank.shape[1] > 256 else bank,
                                Wff["u" + gn][:, ic, oc, :], cb[:, ic, :],
                                start=False, stop=(ic == 7)))
                gv[gname] = inc("tensor", sT)
            dst = {"f": gscF, "i": gscI, "c": gscG}
            for gname, gfunc, gidx in GATES:
                wt("scalar", sT, gv[gname])
                em("scalar", lambda E, gn=gname, gf=gfunc, gi=gidx, oc=oc, par=par,
                   bank=banks[gname]:
                   E.activation(out=dst[gn][:, par, :],
                                in_=bank[:, 0:256] if bank.shape[1] > 256 else bank,
                                func=gf, bias=fbias[:, gi, oc:oc + 1], scale=1.0))
            actv = inc("scalar", sS)
            wt("vector", sS, actv)
            em("vector", lambda E, oc=oc, par=par: E.tensor_tensor(
                out=cS[:, oc, :], in0=gscF[:, par, :], in1=cS[:, oc, :], op=ALU.mult))
            em("vector", lambda E, par=par: E.tensor_tensor(
                out=gscI[:, par, :], in0=gscI[:, par, :], in1=gscG[:, par, :], op=ALU.mult))
            em("vector", lambda E, oc=oc, par=par: E.tensor_tensor(
                out=cS[:, oc, :], in0=cS[:, oc, :], in1=gscI[:, par, :], op=ALU.add))
            em("vector", lambda E, oc=oc, cw=cbf_w: E.tensor_copy(
                out=cw[:, oc, :], in_=cS[:, oc, :]))
            bv = inc("vector", sV)
            for gname, _, _ in GATES:
                bank_free[(gname, par)] = bv
        step_done = sV.n

    wt("sync", sS, h2_done)
    wov = stage_cast(W_d["woT"], Wff["wo"], dest_guard=(sT, p7_done_sT))
    uov = stage_cast(W_d["uoT"], Wff["uo"], dest_guard=(sT, sT.n))

    wt_pair("tensor", uov)
    wt("tensor", sS, sS.n)
    cbf_fin = cbf[0]
    for oc in range(8):
        bank = pE if oc % 2 == 0 else pF
        for ic in range(8):
            em("tensor", lambda E, ic=ic, oc=oc, bank=bank:
               E.matmul(bank[:, 0:256], Wff["uo"][:, ic, oc, :], cbf_fin[:, ic, :],
                        start=(ic == 0), stop=(ic == 7)))
        uv = inc("tensor", sT)
        wt("scalar", sT, uv)
        em("scalar", lambda E, oc=oc, bank=bank:
           E.activation(out=uodv[:, oc, :], in_=bank[:, 0:256], func=AF.Identity,
                        bias=fbias[:, 3, oc:oc + 1], scale=1.0))
    uod_done = inc("scalar", sS)

    wt_pair("tensor", wov)
    wt("tensor", sS, uod_done)
    xt_v = [0, 0]
    xt_used = [0, 0]
    tail_free = [0, 0]
    ybuf_free = [0, 0]
    yv_last = 0
    for oc in range(8):
        wt("sync", sV, max(xt_used[oc % 2], uov[0]))
        wt("sync", sS, uov[1])
        xt_v[oc % 2] = dma("sync", sQ, xt[oc % 2].rearrange("p b t -> p (b t)"),
                           x2_r[:, oc, :, :].rearrange("p b t -> p (b t)"))
        for t in range(8):
            par = (oc * 8 + t) % 2
            bank = pA if par == 0 else pB
            wt("tensor", sV, tail_free[par])
            for ic in range(8):
                em("tensor", lambda E, ic=ic, oc=oc, t=t, bank=bank:
                   E.matmul(bank[:, 0:256], Wff["wo"][:, ic, oc, :],
                            H[:, ic, t * 256:(t + 1) * 256],
                            start=(ic == 0), stop=(ic == 7)))
            xv = inc("tensor", sT)
            wt("vector", sT, xv)
            em("vector", lambda E, oc=oc, bank=bank, par=par: E.tensor_tensor(
                out=app[:, par, 0:256], in0=bank[:, 0:256], in1=uodv[:, oc, :], op=ALU.add))
            s1v = inc("vector", sV)
            wt("scalar", sV, s1v)
            em("scalar", lambda E, par=par: E.activation(
                out=app[:, par, 256:512], in_=app[:, par, 0:256], func=AF.Tanh))
            s2v = inc("scalar", sS)
            ybuf = ybufA if t < 4 else ybufB
            yb = 0 if t < 4 else 1
            wt("vector", sS, s2v)
            wt("vector", sQ, xt_v[oc % 2])
            if t in (0, 4):
                wt("vector", sO, ybuf_free[yb])
            em("vector", lambda E, oc=oc, t=t, par=par, ybuf=ybuf: E.scalar_tensor_tensor(
                out=ybuf[:, t % 4, :], in0=app[:, par, 256:512],
                scalar=modF[:, 5, oc, t:t + 1], in1=xt[oc % 2][:, t, :],
                op0=ALU.mult, op1=ALU.add))
            tail_free[par] = inc("vector", sV)
            xt_used[oc % 2] = tail_free[par]
            if t in (3, 7):
                wt("gpsimd", sV, tail_free[par])
                y_r = y_d.rearrange("(f p) b t -> p f b t", p=128)
                tb = t // 4
                ybuf_free[yb] = dma("gpsimd", sO,
                                    y_r[:, oc, 4 * tb:4 * tb + 4, :].rearrange("p b t -> p (b t)"),
                                    ybuf.rearrange("p a t -> p (a t)"))
                yv_last = ybuf_free[yb]

    if DBG:
        wt("gpsimd", sV, sV.n)
        wt("gpsimd", sS, sS.n)
        wt("gpsimd", sT, sT.n)
        dma("gpsimd", sO, dbg_d["h2"].bitcast(BF16), H.rearrange("p a b -> p (a b)"))
        dma("gpsimd", sO, dbg_d["x2"], x2_d)
        dma("gpsimd", sO, dbg_d["cfin"], cS.rearrange("p a b -> p (a b)"))
        dma("gpsimd", sO, dbg_d["uod"], uodv.rearrange("p a b -> p (a b)"))
    em("gpsimd", lambda E, v=sO.n: E.wait_ge(sO.h, v))

    with nc.Block() as block:
        @block.tensor
        def _(E):
            for fn in ops["tensor"]:
                fn(E)

        @block.vector
        def _(E):
            for fn in ops["vector"]:
                fn(E)

        @block.scalar
        def _(E):
            for fn in ops["scalar"]:
                fn(E)

        @block.sync
        def _(E):
            for fn in ops["sync"]:
                fn(E)

        @block.gpsimd
        def _(E):
            for fn in ops["gpsimd"]:
                fn(E)

    st.close()
    return nc


DBG_SRC = {}

_prog_cache = {}


def kernel(**inputs):
    shared = _shared_layouts(inputs)
    nc = build_program()
    in_maps = []
    for core in range(NCORES):
        m = dict(shared)
        m.update(_host_constants(core))
        m["x_fm"] = _core_x(inputs, core)
        dm = {"x_fm": m["x_fm"], "wmod": m["wmod"], "cT": m["cT"], "qbd": m["qbd"],
              "ident": m["ident"], "ident_bf": m["ident_bf"], "ones_bf": m["ones_bf"],
              "negeye": m["negeye"], "ind2": m["ind2"], "ind2T": m["ind2T"],
              "mask_lc": m["mask_lc"], "mask_T": m["mask_T"],
              "cosF": m["cosF"], "sinF": m["sinF"], "in_c0": m["in_c0"]}
        for n in ("WkTp", "WvT", "WgT", "WpT", "wfT", "ufT", "wiT", "uiT", "wcT", "ucT",
                  "woT", "uoT"):
            dm[n] = m[n]
        for n in ("bf", "bi", "bc", "bo"):
            dm[n + "r"] = m[n + "r"]
        in_maps.append(dm)
    res = run_bass_kernel_spmd(nc, in_maps, core_ids=list(range(NCORES)))
    return _assemble_output([res.results[i]["y_fm"] for i in range(NCORES)])


# version 17
# speedup vs baseline: 1.0379x; 1.0174x over previous
import os
import numpy as np
import ml_dtypes
from contextlib import ExitStack

import concourse.bass as bass
import concourse.mybir as mybir
from concourse.bass_utils import run_bass_kernel_spmd

F32 = mybir.dt.float32
BF16 = mybir.dt.bfloat16
U16 = mybir.dt.uint16
AF = mybir.ActivationFunctionType
ALU = mybir.AluOpType
AXL = mybir.AxisListType

NCORES = 8
B = 8
TL = 256
NEG = -1.0e30
SWAP16 = list(range(16, 32)) + list(range(16))

_PERM = np.concatenate([np.arange(0, 64), np.arange(128, 192),
                        np.arange(64, 128), np.arange(192, 256)])
_IPERM = np.argsort(_PERM)


def _dperm():
    p = np.arange(64)
    q16 = p // 16
    j = (q16 // 2) * 16 + p % 16
    return 2 * j + q16 % 2


_DPERM = _dperm()


def _kperm():
    out = np.zeros(1024, np.int64)
    for hp in range(8):
        for p in range(128):
            out[hp * 128 + p] = (2 * hp + p // 64) * 64 + _DPERM[p % 64]
    return out


_KPERM = _kperm()


def _bf(a):
    return np.ascontiguousarray(a.astype(ml_dtypes.bfloat16).view(np.uint16))


def _host_constants(core_id):
    c = {}
    c["ident"] = np.eye(128, dtype=np.float32)
    c["ident_bf"] = _bf(np.eye(128, dtype=np.float32))
    c["ones_bf"] = _bf(np.ones((128, 128), np.float32))
    ne = np.zeros((8, 8, 64), np.float32)
    for k in range(8):
        ne[k, k, :] = -1.0
    c["negeye"] = _bf(ne)
    ind2 = np.zeros((128, 2), np.float32)
    ind2[:64, 0] = 1.0
    ind2[64:, 1] = 1.0
    c["ind2"] = _bf(ind2)
    c["ind2T"] = _bf(ind2.T.copy())
    l_of_p = np.arange(128) % 64
    cc = np.arange(64)
    c["mask_lc"] = np.where(cc[None, :] <= l_of_p[:, None], 0.0, NEG).astype(np.float32)
    c["mask_T"] = np.where(cc[:, None] <= l_of_p[None, :], 0.0, NEG).astype(np.float32)
    p = np.arange(128) % 64
    q16 = p // 16
    j = (q16 // 2) * 16 + p % 16
    is_b = q16 % 2
    inv = 1.0 / 10000.0 ** (np.arange(0, 64, 2, dtype=np.float32) / 64.0)
    c2 = np.arange(128)
    pos = core_id * 256 + (c2 // 64) * 128 + c2 % 64
    ang = pos[None, :].astype(np.float32) * inv[j][:, None]
    c["cosF"] = np.cos(ang).astype(np.float32)
    sinF = np.sin(ang).astype(np.float32)
    c["sinF"] = np.where(is_b[:, None] == 0, -sinF, sinF).astype(np.float32)
    return c


def _shared_layouts(inputs):
    g = lambda n: np.asarray(inputs[n], dtype=np.float32)
    s = {}
    s["WkTp"] = np.ascontiguousarray(g("L0_attn_W_k").T[:, _KPERM])
    s["WvT"] = np.ascontiguousarray(g("L0_attn_W_v").T)
    s["WgT"] = np.ascontiguousarray(g("L0_attn_W_g").T)
    s["WpT"] = np.ascontiguousarray(g("L0_attn_W_p").T)
    for n in ("wf", "uf", "wi", "ui", "wc", "uc", "wo", "uo"):
        s[n + "T"] = np.ascontiguousarray(g("L0_ffn_" + n).T)
    s["wmod"] = np.ascontiguousarray(g("L0_wmod"))
    s["cT"] = np.ascontiguousarray(g("L0_c").T)
    lq = g("L0_attn_latent_queries")[0]
    qbd = np.zeros((8, 128, 128), np.float32)
    for hp in range(8):
        for ph in range(2):
            blk = lq[:, 2 * hp + ph, :][:, _DPERM]
            qbd[hp, 64 * ph:64 * ph + 64, 64 * ph:64 * ph + 64] = blk.T
    s["qbd"] = np.ascontiguousarray(qbd)
    for n in ("bf", "bi", "bc", "bo"):
        s[n + "r"] = np.ascontiguousarray(g("L0_ffn_" + n).reshape(8, 128).T)
    ic = g("L0_ffn_in_c").reshape(8, 128).T
    s["in_c0"] = np.ascontiguousarray(np.broadcast_to(ic[:, :, None], (128, 8, 256)))
    return s


def _core_x(inputs, core_id):
    x = np.asarray(inputs["L0_attn_x"], dtype=np.float32)
    xs = x[:, core_id * 256:(core_id + 1) * 256, :][:, _PERM, :]
    return np.ascontiguousarray(xs.transpose(2, 0, 1))


def _assemble_output(shards):
    y = np.empty((8, 2048, 1024), np.float32)
    for i, sh in enumerate(shards):
        ys = np.asarray(sh).reshape(1024, 8, 256).transpose(1, 2, 0)
        y[:, i * 256:(i + 1) * 256, :] = ys[:, _IPERM, :]
    return y


class Sem:
    def __init__(self, h):
        self.h = h
        self.n = 0

    def bump(self, k=1):
        self.n += k
        return self.n


def _view(arena, off, nslots, dtype, shape):
    ap = arena[:, off:off + nslots]
    if dtype == BF16:
        ap = ap.bitcast(BF16)
    total = 1
    for s in shape:
        total *= s
    assert total == ap.shape[1], (shape, ap.shape)
    if len(shape) > 1:
        pat = " ".join(f"d{i}" for i in range(len(shape)))
        ap = ap.rearrange(f"p ({pat}) -> p {pat}",
                          **{f"d{i}": shape[i] for i in range(len(shape))})
    return ap


FFN_W = ("uf", "ui", "uc", "wf", "wi", "wc", "wo", "uo")


def build_program(debug=(), trunc=None):
    nc = bass.Bass("TRN2", target_bir_lowering=False, detect_race_conditions=False)
    DI = lambda name, shape, dt=F32: nc.dram_tensor(name, shape, dt, kind="ExternalInput").ap()

    x_d = DI("x_fm", [1024, B, TL])
    W_d = {n: DI(n, [1024, 1024]) for n in
           ("WkTp", "WvT", "WgT", "WpT", "wfT", "ufT", "wiT", "uiT", "wcT", "ucT", "woT", "uoT")}
    wmod_d = DI("wmod", [1024, 6144])
    cT_d = DI("cT", [1024, 8])
    qbd_d = DI("qbd", [8, 128, 128])
    ident_d = DI("ident", [128, 128])
    identbf_d = DI("ident_bf", [128, 128], U16)
    onesbf_d = DI("ones_bf", [128, 128], U16)
    negeye_d = DI("negeye", [8, 8, 64], U16)
    ind2_d = DI("ind2", [128, 2], U16)
    ind2T_d = DI("ind2T", [2, 128], U16)
    masklc_d = DI("mask_lc", [128, 64])
    maskT_d = DI("mask_T", [64, 128])
    cosF_d = DI("cosF", [128, 128])
    sinF_d = DI("sinF", [128, 128])
    bias_d = {n: DI(n + "r", [128, 8]) for n in ("bf", "bi", "bc", "bo")}
    inc0_d = DI("in_c0", [128, 8, 256])

    y_d = nc.dram_tensor("y_fm", [1024, B, TL], F32, kind="ExternalOutput").ap()
    x2_d = nc.dram_tensor("x2_scr", [1024, B, TL], F32).ap()
    arin1 = nc.dram_tensor("ar_in1", [128, 2048], BF16)
    arout1 = nc.dram_tensor("ar_out1", [128, 2048], BF16)
    arin2 = nc.dram_tensor("ar_in2", [128, 2048], BF16)
    arout2 = nc.dram_tensor("ar_out2", [128, 2048], BF16)
    arwi = nc.dram_tensor("ar_wi", [128, 64], BF16)
    arwo = nc.dram_tensor("ar_wo", [128, 64], BF16)

    DBG = os.environ.get("K_DBG", "") == "1"
    dbg_d = {}
    if DBG:
        dbg_d["gates"] = nc.dram_tensor("dbg_gates", [128, 16384], U16, kind="ExternalOutput").ap()
        dbg_d["attnB"] = nc.dram_tensor("dbg_attnB", [128, 4096], U16, kind="ExternalOutput").ap()
        dbg_d["attnP"] = nc.dram_tensor("dbg_attnP", [128, 4096], U16, kind="ExternalOutput").ap()
        dbg_d["h2"] = nc.dram_tensor("dbg_h2", [128, 16384], U16, kind="ExternalOutput").ap()
        dbg_d["x2"] = nc.dram_tensor("dbg_x2", [1024, B, TL], F32, kind="ExternalOutput").ap()
        dbg_d["cfin"] = nc.dram_tensor("dbg_cfin", [128, 2048], F32, kind="ExternalOutput").ap()
        dbg_d["uod"] = nc.dram_tensor("dbg_uod", [128, 2048], F32, kind="ExternalOutput").ap()

    st = ExitStack()
    sbuf = lambda name, shape, dt=F32: st.enter_context(nc.sbuf_tensor(name, shape, dt))[:]
    psum = lambda name, shape: st.enter_context(nc.psum_tensor(name, shape, F32))[:]

    arena = st.enter_context(nc.sbuf_tensor("arena", [128, 46080], F32))[:]

    ident = sbuf("identt", [128, 128])
    identbf = sbuf("identbft", [128, 128], BF16)
    onesbf = sbuf("onesbft", [128, 128], BF16)
    qbd = sbuf("qbdt", [128, 8, 128], BF16)
    negeye = sbuf("negeyet", [8, 8, 64], BF16)
    ind2 = sbuf("ind2t", [128, 2], BF16)
    ind2T = sbuf("ind2Tt", [2, 128], BF16)
    masklc = sbuf("masklct", [128, 64])
    maskT = sbuf("maskTt", [64, 128])
    cosF = sbuf("cosFt", [128, 128])
    sinF = sbuf("sinFt", [128, 128])
    modF = sbuf("modFt", [128, 6, 8, 8])
    cTt = sbuf("cTt", [128, 8, 8])
    silubf = sbuf("silubft", [128, 8, 8], BF16)
    Rsc = sbuf("Rsct", [128, 32])
    rdt = sbuf("rdtt", [128, 2, 8])
    RT8 = sbuf("RT8t", [8, 128], BF16)
    fbias = sbuf("fbiast", [128, 4, 8])
    rden = sbuf("rdent", [2, 512], BF16)
    consts = sbuf("constst", [128, 4])
    stch = sbuf("stcht", [128, 3, 512])
    app = sbuf("appt", [128, 2, 512])
    esc = sbuf("esct", [128, 1024])
    bfx = sbuf("bfxt", [128, 2, 512], BF16)
    bfsq = sbuf("bfsqt", [128, 2, 512], BF16)

    nc.const_aps.register(F32, 0.125, consts[:, 0:1])
    nc.const_aps.register(F32, 8.0, consts[:, 1:2])
    nc.const_aps.register(F32, 1e-5, consts[:, 2:3])
    nc.const_aps.register(F32, 1.0 / 1024.0, consts[:, 3:4])

    pA = psum("pA", [128, 512])
    pB = psum("pB", [128, 512])
    pE = psum("pE", [128, 512])
    pF = psum("pF", [128, 512])
    pT = psum("pT", [128, 1024])
    pS1 = psum("pS1", [128, 512])
    pS2 = psum("pS2", [128, 512])

    V = lambda off, n, dt, shape: _view(arena, off, n, dt, shape)
    H = V(0, 8192, BF16, (8, 2048))
    X = V(8192, 16384, F32, (8, 2048))
    Wff = {"uf": V(8192, 4096, BF16, (8, 8, 128)),
           "ui": V(12288, 4096, BF16, (8, 8, 128)),
           "uc": V(16384, 4096, BF16, (8, 8, 128)),
           "wf": V(20480, 4096, BF16, (8, 8, 128)),
           "wi": V(24576, 4096, BF16, (8, 8, 128)),
           "wc": V(28672, 4096, BF16, (8, 8, 128)),
           "wo": V(37888, 4096, BF16, (8, 8, 128)),
           "uo": V(12288, 4096, BF16, (8, 8, 128))}
    uodv = V(8192, 2048, F32, (8, 256))
    modtm = V(24576, 6144, F32, (6144,))
    kS = V(24576, 4096, BF16, (8, 8, 128))
    vS = V(28672, 4096, BF16, (8, 8, 128))
    GW = V(24576, 8192, BF16, (8, 2048))
    qbdf = V(32768, 1024, F32, (8, 128))
    ropeA = V(32768, 512, F32, (4, 128))
    ropeB = V(33280, 512, F32, (4, 128))
    vtmp = V(33792, 256, BF16, (4, 128))
    attnB = V(32768, 2048, BF16, (8, 8, 64))
    xrd = [V(32768, 2048, F32, (8, 256)), V(34816, 2048, F32, (8, 256))]
    cS = V(32768, 2048, F32, (8, 256))
    cbf = [V(34816, 1024, BF16, (8, 256)), V(35840, 1024, BF16, (8, 256))]
    probsb = [V(36864, 512, BF16, (8, 128)), V(37376, 512, BF16, (8, 128))]
    wbounce = [V(36864, 1024, BF16, (2048,)), V(36864, 1024, BF16, (2048,))]

    ptmp7 = V(36864, 1024, F32, (4, 256))
    gscG = V(36864, 512, F32, (2, 256))
    ybufB = V(36864, 1024, F32, (4, 256))
    Wslot = V(37888, 4096, BF16, (8, 8, 128))
    wmodbf = V(37888, 2048, BF16, (2, 8, 256))
    stagh = [V(41984, 2048, F32, (2048,)), V(44032, 2048, F32, (2048,))]
    x2stg = V(41984, 4096, F32, (8, 2, 256))
    x2stgB = V(32768, 4096, F32, (8, 2, 256))
    xt = [V(41984, 2048, F32, (8, 256)), V(44032, 2048, F32, (8, 256))]
    gscF = esc[:, 0:512].rearrange("p (a t) -> p a t", t=256)
    gscI = esc[:, 512:1024].rearrange("p (a t) -> p a t", t=256)
    x2b = [esc[:, 0:512].rearrange("p (a t) -> p a t", t=256),
           esc[:, 512:1024].rearrange("p (a t) -> p a t", t=256)]
    ybufA = esc.rearrange("p (a t) -> p a t", t=256)

    ops = {e: [] for e in ("tensor", "vector", "scalar", "sync", "gpsimd")}
    em = lambda e, fn: ops[e].append(fn)

    sems = {}
    for name in ("sQ", "sQ2", "sG", "sT", "sV", "sS", "sC", "sO", "sP"):
        sems[name] = Sem(st.enter_context(nc.semaphore(name)))
    sQ, sQ2, sG, sT, sV, sS, sC, sO, sP = (sems[k] for k in
                                           ("sQ", "sQ2", "sG", "sT", "sV", "sS", "sC",
                                            "sO", "sP"))

    def dma(eng, sem, out, in_):
        v = sem.bump(16)
        em(eng, lambda E, o=out, i=in_, s=sem.h: E.dma_start(out=o, in_=i).then_inc(s, 16))
        return v

    def wt(eng, sem, v):
        if v:
            em(eng, lambda E, s=sem.h, vv=v: E.wait_ge(s, vv))

    def vedge():
        em("vector", lambda E: E.engine_nop())
        v = inc("vector", sV)
        wt("vector", sV, v)

    def wt_pair(eng, pair):
        wt(eng, sV, pair[0])
        wt(eng, sS, pair[1])

    def wtv(eng, sv):
        if sv[1]:
            wt(eng, sv[0], sv[1])

    def inc(eng, sem, k=1):
        v = sem.bump(k)
        last = ops[eng][-1]

        def wrapped(E, f=last, s=sem.h, kk=k):
            ins = f(E)
            assert ins is not None, "inc() must follow an instruction-emitting op"
            return ins.then_inc(s, kk)
        ops[eng][-1] = wrapped
        return v

    xld = dma("sync", sQ, X.rearrange("p f (b t) -> p f b t", t=256),
              x_d.rearrange("(f p) b t -> p f b t", p=128))
    for out, d in (
            (ident, ident_d), (identbf, identbf_d.bitcast(BF16)), (onesbf, onesbf_d.bitcast(BF16)),
            (qbdf, qbd_d.rearrange("h a b -> a h b")),
            (negeye.rearrange("k h c -> k (h c)"),
             negeye_d.rearrange("k h c -> k (h c)").bitcast(BF16)),
            (ind2, ind2_d.bitcast(BF16)), (ind2T, ind2T_d.bitcast(BF16)),
            (masklc, masklc_d), (maskT, maskT_d), (cosF, cosF_d), (sinF, sinF_d),
            (cTt, cT_d.rearrange("(i p) b -> p i b", p=128)),
            (fbias[:, 0, :], bias_d["bf"]), (fbias[:, 1, :], bias_d["bi"]),
            (fbias[:, 2, :], bias_d["bc"]), (fbias[:, 3, :], bias_d["bo"])):
        dma("gpsimd", sG, out, d)
    cver = sG.n

    wt("gpsimd", sG, cver)
    wdma = dma("gpsimd", sG, arwi[:, :], identbf[:, 0:64])
    wt("gpsimd", sG, wdma)
    em("gpsimd", lambda E: E.collective_compute(
        "AllReduce", ALU.add, replica_groups=[list(range(NCORES))],
        ins=[arwi.ap().opt()], outs=[arwo.ap().opt()]).then_inc(sC.h, 1))
    sC.bump(1)

    def _initc(E):
        E.memset(consts[:, 0:1], 0.125)
        E.memset(consts[:, 1:2], 8.0)
        E.memset(consts[:, 2:3], 1e-5)
        E.memset(consts[:, 3:4], 1.0 / 1024.0)
    em("vector", _initc)
    wt("vector", sG, cver)
    em("vector", lambda E: E.tensor_copy(out=qbd.rearrange("p h c -> p (h c)"),
                                         in_=qbdf.rearrange("p h c -> p (h c)")))
    qbd_ready = inc("vector", sV)

    wt("scalar", sG, cver)
    em("scalar", lambda E: E.activation(out=silubf.rearrange("p i b -> p (i b)"),
                                        in_=cTt.rearrange("p i b -> p (i b)"), func=AF.Silu))
    silu_ready = inc("scalar", sS)
    wt("tensor", sS, silu_ready)

    wmod_r = wmod_d.rearrange("(i p) j -> p i j", p=128)
    wmod_mm_done = {}
    wmod_cast_done = {}
    CENG = {0: ("vector", sV), 1: ("scalar", sS)}
    for jt in range(24):
        half = jt % 2
        ceng, csem = CENG[half]
        qeng, qsem = (("sync", sQ) if half == 0 else ("scalar", sQ2))
        if jt >= 2:
            wt(qeng, csem, wmod_cast_done[jt - 2])
        lv = dma(qeng, qsem, stagh[half].rearrange("p (i j) -> p i j", j=256),
                 wmod_r[:, :, jt * 256:(jt + 1) * 256])
        wt(ceng, qsem, lv)
        if jt >= 2:
            wt(ceng, sT, wmod_mm_done[jt - 2])
        dst = wmodbf[:, half, :, :].rearrange("p i j -> p (i j)")
        if half == 0:
            em(ceng, lambda E, d=dst, h=half: E.tensor_copy(out=d, in_=stagh[h]))
        else:
            em(ceng, lambda E, d=dst, h=half: E.activation(out=d, in_=stagh[h],
                                                           func=AF.Identity))
        wmod_cast_done[jt] = inc(ceng, csem)
        wt("tensor", csem, wmod_cast_done[jt])
        bank = pS1 if half == 0 else pS2
        for ic in range(8):
            em("tensor", lambda E, ic=ic, b=bank, h=half:
               E.matmul(b[0:8, 0:256], silubf[:, ic, :], wmodbf[:, h, ic, :],
                        start=(ic == 0), stop=(ic == 7)))
        wmod_mm_done[jt] = inc("tensor", sT)
        wt("vector", sT, wmod_mm_done[jt])
        em("vector", lambda E, jt=jt, b=bank:
           E.tensor_copy(out=modtm[0:8, jt * 256:(jt + 1) * 256], in_=b[0:8, 0:256]))
        inc("vector", sV)

    def emit_mod_tr(t48s):
        for t48 in t48s:
            vidx, fc = t48 // 8, t48 % 8
            em("tensor", lambda E, t48=t48:
               E.transpose(pA[:, 0:8], modtm[0:8, t48 * 128:(t48 + 1) * 128], ident[0:8, 0:8]))
            tv = inc("tensor", sT)
            wt("vector", sT, tv)
            em("vector", lambda E, vidx=vidx, fc=fc:
               E.tensor_copy(out=modF[:, vidx, fc, :], in_=pA[:, 0:8]))
            vv = inc("vector", sV)
            wt("tensor", sV, vv)
        return sV.n

    modA_ready = emit_mod_tr(range(16))

    vedge()
    em("vector", lambda E: E.tensor_scalar(out=modF[:, 1, :, :], in0=modF[:, 1, :, :],
                                           scalar1=1.0, scalar2=None, op0=ALU.add))
    modA_ready = inc("vector", sV)

    modB_ready = emit_mod_tr(range(16, 48))

    vedge()
    em("vector", lambda E: E.tensor_scalar(out=modF[:, 4, :, :], in0=modF[:, 4, :, :],
                                           scalar1=1.0, scalar2=None, op0=ALU.add))
    for vidx in (2, 5):
        g = modF[:, vidx, :, :]
        hi = app[:, 0, 0:64].rearrange("p (a b) -> p a b", b=8)
        lo = app[:, 0, 64:128].rearrange("p (a b) -> p a b", b=8)
        sg = app[:, 0, 128:192].rearrange("p (a b) -> p a b", b=8)
        vedge()
        em("vector", lambda E, g=g, hi=hi: E.tensor_scalar(out=hi, in0=g, scalar1=0.25,
                                                           scalar2=None, op0=ALU.max))
        em("vector", lambda E, g=g, lo=lo: E.tensor_scalar(out=lo, in0=g, scalar1=-0.25,
                                                           scalar2=None, op0=ALU.min))
        em("vector", lambda E, g=g, sg=sg: E.tensor_scalar(out=sg, in0=g, scalar1=0.0,
                                                           scalar2=None, op0=ALU.is_ge))
        vedge()
        em("vector", lambda E, hi=hi, lo=lo: E.tensor_tensor(out=hi, in0=hi, in1=lo,
                                                             op=ALU.subtract))
        em("vector", lambda E, hi=hi, sg=sg: E.tensor_tensor(out=hi, in0=sg, in1=hi,
                                                             op=ALU.mult))
        em("vector", lambda E, g=g, hi=hi, lo=lo: E.tensor_tensor(out=g, in0=hi, in1=lo,
                                                                  op=ALU.add))
    modB_ready = inc("vector", sV)

    def stage_cast_q(dram, dest, q, dest_guard=None):
        if stage_cast.lastcast is None:
            stage_cast.lastcast = {0: wmod_cast_done[22], 1: wmod_cast_done[23]}
        src = dram.rearrange("(i p) o -> p i o", p=128)
        half = q % 2
        ceng, csem = CENG[half]
        wt("sync", csem, stage_cast.lastcast.get(half, 0))
        lv = dma("sync", sQ, stagh[half].rearrange("p (i o) -> p i o", o=256),
                 src[:, :, q * 256:(q + 1) * 256])
        wt(ceng, sQ, lv)
        if dest_guard is not None:
            wt(ceng, dest_guard[0], dest_guard[1])
        sh = stagh[half].rearrange("p (i o) -> p i o", o=256)
        dst = dest[:, :, 2 * q:2 * q + 2, :].rearrange("p i a b -> p i (a b)")
        if half == 0:
            em(ceng, lambda E, d=dst, s=sh: E.tensor_copy(out=d, in_=s))
        else:
            em(ceng, lambda E, d=dst, s=sh: E.activation(out=d, in_=s,
                                                         func=AF.Identity))
        v = inc(ceng, csem)
        stage_cast.lastcast[half] = v
        return v

    def stage_cast(dram, dest, dest_guard=None, nq=4):
        for q in range(nq):
            stage_cast_q(dram, dest, q, dest_guard=dest_guard if q in (0, 1) else None)
        return (stage_cast.lastcast[0], stage_cast.lastcast[1])
    stage_cast.lastcast = None

    def direct_weight(name, dest, dest_guard=None):
        return stage_cast(W_d[name + "T"], dest, dest_guard=dest_guard)

    def emit_ln(src_fn, cvec, svec, mod_wait=0, prep_fn=None, post_fn=None,
                src_qwait_fn=None):
        mmv_par = [0, 0]
        act_par = [0, 0]
        psum_rd = [0]
        pool_done = [0]
        pairs = [(app[:, 0, :], app[:, 1, :]),
                 (esc[:, 0:512], esc[:, 512:1024])]
        h2d = [0]

        def stats(tt):
            if prep_fn is not None:
                prep_fn(tt)
            for fc in range(8):
                par = fc % 2
                s = src_fn(fc, tt)
                wt("vector", sT, mmv_par[par])
                em("vector", lambda E, s=s, par=par: E.tensor_copy(out=bfx[:, par, :], in_=s))
                cv = inc("vector", sV)
                wt("scalar", sV, cv)
                wt("scalar", sT, mmv_par[par])
                em("scalar", lambda E, par=par:
                   E.activation(out=bfsq[:, par, :], in_=bfx[:, par, :], func=AF.Square))
                sq2 = inc("scalar", sS)
                wt("tensor", sV, cv)
                wt("tensor", sS, sq2)
                if fc == 0:
                    wt("tensor", sV, psum_rd[0])
                em("tensor", lambda E, par=par, fc=fc:
                   E.matmul(pS1, onesbf, bfx[:, par, :], start=(fc == 0), stop=(fc == 7)))
                em("tensor", lambda E, par=par, fc=fc:
                   E.matmul(pS2, onesbf, bfsq[:, par, :], start=(fc == 0), stop=(fc == 7)))
                mmv_par[par] = inc("tensor", sT)

        def chain_apply(tt, mmv_tail):
            wt("vector", sT, mmv_tail)
            wt("vector", sS, act_par[0])
            wt("vector", sS, act_par[1])
            wt("vector", sP, pool_done[0])
            em("vector", lambda E: E.tensor_scalar(out=stch[:, 0, :], in0=pS1,
                                                   scalar1=1.0 / 1024.0, scalar2=None, op0=ALU.mult))
            em("vector", lambda E: E.tensor_scalar(out=app[:, 1, :], in0=pS2,
                                                   scalar1=1.0 / 1024.0, scalar2=None, op0=ALU.mult))
            psum_rd[0] = inc("vector", sV)
            if tt < 3:
                stats(tt + 1)
            vedge()
            em("vector", lambda E: E.tensor_tensor(out=app[:, 0, :], in0=stch[:, 0, :],
                                                   in1=stch[:, 0, :], op=ALU.mult))
            em("vector", lambda E: E.tensor_tensor(out=app[:, 1, :], in0=app[:, 1, :],
                                                   in1=app[:, 0, :], op=ALU.subtract))
            chv = inc("vector", sV)
            wt("scalar", sV, chv)
            em("scalar", lambda E: E.activation(out=app[:, 0, :], in_=app[:, 1, :],
                                                func=AF.Sqrt, bias=1e-5, scale=1.0))
            sqv = inc("scalar", sS)
            wt("vector", sS, sqv)
            em("vector", lambda E: E.reciprocal(out=stch[:, 1, :], in_=app[:, 0, :]))
            vedge()
            em("vector", lambda E: E.tensor_tensor(out=stch[:, 2, :], in0=stch[:, 0, :],
                                                   in1=stch[:, 1, :], op=ALU.mult))
            chain_v = inc("vector", sV)
            for fc in range(8):
                par = fc % 2
                s = src_fn(fc, tt)
                t1, t2 = pairs[par]
                if mod_wait and tt == 0 and fc == 0:
                    wt("scalar", sV, mod_wait)
                aeng, asem = "vector", sV
                wt(aeng, sS, act_par[par])
                em(aeng, lambda E, s=s, t1=t1: E.tensor_tensor(out=t1, in0=s,
                                                               in1=stch[:, 1, :], op=ALU.mult))
                em(aeng, lambda E, t1=t1, t2=t2: E.tensor_tensor(out=t2, in0=t1,
                                                                 in1=stch[:, 2, :],
                                                                 op=ALU.subtract))
                av = inc(aeng, asem)
                wt("scalar", asem, av)
                for hf in range(2):
                    b = 2 * tt + hf
                    em("scalar", lambda E, fc=fc, tt=tt, hf=hf, b=b, t2=t2:
                       E.activation(out=H[:, fc, tt * 512 + hf * 256: tt * 512 + hf * 256 + 256],
                                    in_=t2[:, hf * 256:hf * 256 + 256], func=AF.Identity,
                                    bias=modF[:, svec, fc, b:b + 1],
                                    scale=modF[:, cvec, fc, b:b + 1]))
                act_par[par] = inc("scalar", sS)
                h2d[0] = act_par[par]
            if post_fn is not None:
                post_fn(tt)

        stats(0)
        tails = [0, 0, 0, 0]
        tails[0] = mmv_par[1]
        for tt in range(4):
            chain_apply(tt, tails[tt])
            tails[tt + 1 if tt < 3 else 0] = mmv_par[1]
        return h2d[0]

    wt("vector", sQ, xld)
    h1_done = emit_ln(lambda fc, tt: X[:, fc, tt * 512:(tt + 1) * 512], 1, 0,
                      mod_wait=modA_ready, src_qwait_fn=lambda tt: xld)

    wk_cast = stage_cast(W_d["WkTp"], Wslot, dest_guard=(sT, sT.n))
    wt_pair("tensor", wk_cast)
    wt("tensor", sS, h1_done)

    Hr = [H[:, ic, :].rearrange("p (b t) -> p b t", t=256) for ic in range(8)]
    rope_free = {}
    for i3 in range(16):
        hp, bg = i3 // 2, i3 % 2
        bank = pA if i3 % 2 == 0 else pB
        wt("tensor", sV, rope_free.get(i3 - 2, 0))
        for ic in range(8):
            em("tensor", lambda E, ic=ic, hp=hp, bg=bg, bank=bank:
               E.matmul(bank.rearrange("p (b c) -> p b c", c=128),
                        Wslot[:, ic, hp, :], Hr[ic][:, bg * 4:(bg + 1) * 4, 0:128],
                        start=(ic == 0), stop=(ic == 7)))
        mv = inc("tensor", sT)
        wt("vector", sT, mv)
        bank3 = bank.rearrange("p (b c) -> p b c", c=128)
        em("vector", lambda E, bank=bank: E.stream_shuffle(
            out=ropeB.rearrange("p b c -> p (b c)"), in_=bank, mask=SWAP16))
        em("vector", lambda E, bank3=bank3: E.tensor_tensor(
            out=ropeA, in0=bank3, in1=cosF.unsqueeze(1).to_broadcast([128, 4, 128]), op=ALU.mult))
        em("vector", lambda E: E.tensor_tensor(
            out=ropeB, in0=ropeB, in1=sinF.unsqueeze(1).to_broadcast([128, 4, 128]), op=ALU.mult))
        em("vector", lambda E, hp=hp, bg=bg: E.tensor_tensor(
            out=kS[:, hp, bg * 4:(bg + 1) * 4, :], in0=ropeA, in1=ropeB, op=ALU.add))
        rope_free[i3] = inc("vector", sV)

    wv_cast = stage_cast(W_d["WvT"], Wslot, dest_guard=(sT, sT.n))
    wt_pair("tensor", wv_cast)
    vt_free = {}
    for i3 in range(16):
        hp, bg = i3 // 2, i3 % 2
        bank = pA if i3 % 2 == 0 else pB
        tbank = pE if i3 % 2 == 0 else pF
        wtv("tensor", vt_free.get(i3 - 2, (sV, 0)))
        for ic in range(8):
            em("tensor", lambda E, ic=ic, hp=hp, bg=bg, bank=bank:
               E.matmul(bank.rearrange("p (b c) -> p b c", c=128),
                        Wslot[:, ic, hp, :], Hr[ic][:, bg * 4:(bg + 1) * 4, 0:128],
                        start=(ic == 0), stop=(ic == 7)))
        mv = inc("tensor", sT)
        wt("vector", sT, mv)
        em("vector", lambda E, bank=bank:
           E.tensor_copy(out=vtmp.rearrange("p b c -> p (b c)"), in_=bank))
        cv = inc("vector", sV)
        wt("tensor", sV, cv)
        for br in range(4):
            em("tensor", lambda E, br=br, tbank=tbank:
               E.transpose(tbank[:, br * 64:(br + 1) * 64].bitcast(BF16), vtmp[:, br, :], identbf))
        tv = inc("tensor", sT)
        wt("vector", sT, tv)
        em("vector", lambda E, hp=hp, bg=bg, tbank=tbank:
           E.tensor_copy(out=vS[:, bg * 4:(bg + 1) * 4, hp, :],
                         in_=tbank[:, 0:256].bitcast(BF16).rearrange("p (b c) -> p b c", c=128)))
        vt_free[i3] = (sV, inc("vector", sV))
    vproj_done_sT = sT.n

    wff_cast = {}
    wff_pending = [(n, q) for n in ("uf", "ui", "uc", "wf") for q in range(4)]

    def wff_feed(k=2):
        for _ in range(k):
            if not wff_pending:
                return
            n, q = wff_pending.pop(0)
            g = (sS, h1_done) if (n == "uf" and q in (0, 1)) else None
            stage_cast_q(W_d[n + "T"], Wff[n], q, dest_guard=g)
            if q == 3:
                wff_cast[n] = (stage_cast.lastcast[0], stage_cast.lastcast[1])

    def _finish_trunc():
        em("gpsimd", lambda E, v=sV.n: E.wait_ge(sV.h, v))
        em("gpsimd", lambda E, v=sT.n: E.wait_ge(sT.h, v))
        em("gpsimd", lambda E, v=sS.n: E.wait_ge(sS.h, v))
        with nc.Block() as block:
            @block.tensor
            def _(E):
                for fn in ops["tensor"]:
                    fn(E)
            @block.vector
            def _(E):
                for fn in ops["vector"]:
                    fn(E)
            @block.scalar
            def _(E):
                for fn in ops["scalar"]:
                    fn(E)
            @block.sync
            def _(E):
                for fn in ops["sync"]:
                    fn(E)
            @block.gpsimd
            def _(E):
                for fn in ops["gpsimd"]:
                    fn(E)
        st.close()
        return nc

    if trunc == "P3":
        return _finish_trunc()

    wt("tensor", sV, qbd_ready)
    probs_free = [0, 0]
    pEF_free_t = (sV, 0)
    ev_vec = [0]
    pT_free = 0
    pS1_free = 0
    probs_written = 0
    P4B = int(os.environ.get("K_P4B", "8"))
    P4S = int(os.environ.get("K_P4S", "99"))
    for b in range(min(8, P4B)):
        if b >= 1:
            wff_feed(2)
        pbuf = probsb[b % 2]
        for ch in range(2):
            wt("tensor", sV, pS1_free)
            for hp in range(8):
                em("tensor", lambda E, hp=hp, b=b, ch=ch:
                   E.matmul(pS1[:, hp * 64:(hp + 1) * 64], qbd[:, hp, :],
                            kS[:, hp, b, ch * 64:(ch + 1) * 64],
                            start=(hp == 0), stop=(hp == 7)))
            lcv = inc("tensor", sT)
            if P4S < 2:
                continue
            wt("tensor", sV, pT_free)
            for hp in range(8):
                em("tensor", lambda E, hp=hp, b=b, ch=ch:
                   E.matmul(pT[0:64, hp * 128:(hp + 1) * 128],
                            kS[:, hp, b, ch * 64:(ch + 1) * 64], qbd[:, hp, :],
                            start=(hp in (0, 4)), stop=False))
            tTv = inc("tensor", sT)
            if P4S < 3:
                continue
            wt("vector", sT, lcv)
            wt("vector", sS, probs_written)
            pS13 = pS1.rearrange("p (h c) -> p h c", c=64)
            esc3 = esc[:, 0:512].rearrange("p (h c) -> p h c", c=64)
            em("vector", lambda E, pS13=pS13: E.reduce_max(out=Rsc[:, 0:8], in_=pS13, axis=AXL.X))
            em("vector", lambda E, pS13=pS13, esc3=esc3: E.tensor_tensor(
                out=esc3, in0=pS13,
                in1=Rsc[:, 0:8].unsqueeze(2).to_broadcast([128, 8, 64]), op=ALU.subtract))
            em("vector", lambda E, esc3=esc3: E.tensor_tensor(
                out=esc3, in0=esc3,
                in1=masklc.unsqueeze(1).to_broadcast([128, 8, 64]), op=ALU.add))
            pS1_free = inc("vector", sV)
            wt("scalar", sV, pS1_free)
            em("scalar", lambda E: E.activation(out=esc[:, 512:1024], in_=esc[:, 0:512],
                                                func=AF.Exp, bias=0.0, scale=0.125))
            ev = inc("scalar", sS)
            wt("vector", sS, ev)
            em("vector", lambda E: E.reduce_sum(
                out=Rsc[:, 8:16], in_=esc[:, 512:1024].rearrange("p (h c) -> p h c", c=64),
                axis=AXL.X))
            vedge()
            em("vector", lambda E, ch=ch: E.reciprocal(out=rdt[:, ch, :], in_=Rsc[:, 8:16]))
            rvv = inc("vector", sV)
            if P4S < 4:
                continue
            wt("tensor", sV, rvv)
            em("tensor", lambda E: E.transpose(pS2[0:8, 0:128], Rsc[:, 0:8], ident))
            trv = inc("tensor", sT)
            wt("vector", sT, trv)
            em("vector", lambda E: E.tensor_copy(out=RT8, in_=pS2[0:8, 0:128]))
            rt8v = inc("vector", sV)
            if P4S < 5:
                continue
            wt("tensor", sV, rt8v)
            for hp in range(8):
                em("tensor", lambda E, hp=hp:
                   E.matmul(pT[0:64, hp * 128:(hp + 1) * 128], negeye[:, hp, :], RT8,
                            start=False, stop=(hp in (3, 7))))
            rsub = inc("tensor", sT)
            wt("vector", sT, rsub)
            pT3 = pT[0:64, :].rearrange("p (h c) -> p h c", c=128)
            esc4 = esc[0:64, :].rearrange("p (h c) -> p h c", c=128)
            em("vector", lambda E, pT3=pT3, esc4=esc4: E.tensor_tensor(
                out=esc4, in0=pT3, in1=maskT.unsqueeze(1).to_broadcast([64, 8, 128]), op=ALU.add))
            pT_free = inc("vector", sV)
            mkv = pT_free
            wt("scalar", sV, mkv)
            wt("scalar", sT, probs_free[b % 2])
            em("scalar", lambda E, pbuf=pbuf, ch=ch:
               E.activation(out=pbuf[64 * ch:64 * ch + 64, :, :].rearrange("p h c -> p (h c)"),
                            in_=esc[0:64, :], func=AF.Exp, bias=0.0, scale=0.125))
            probs_written = inc("scalar", sS)
        if P4S < 6:
            continue
        wt("tensor", sS, probs_written)
        wtv("tensor", pEF_free_t)
        for ch in range(2):
            cbank = pE if ch == 0 else pF
            for hp in range(8):
                for ph in range(2):
                    em("tensor", lambda E, ch=ch, hp=hp, ph=ph, b=b, cbank=cbank:
                       E.matmul(cbank[64 * ph:64 * ph + 64, hp * 64:(hp + 1) * 64],
                                probsb[b % 2][64 * ch:64 * ch + 64, hp, ph * 64:ph * 64 + 64],
                                vS[64 * ch:64 * ch + 64, b, hp, ph * 64:ph * 64 + 64],
                                start=True, stop=True, skip_group_check=True))
        probs_free[b % 2] = inc("tensor", sT)
        if P4S < 7:
            continue
        wt("vector", sT, probs_free[b % 2])
        em("vector", lambda E: E.tensor_tensor(
            out=app[:, 0, :].rearrange("p (h d) -> p h d", d=64),
            in0=pE.rearrange("p (h d) -> p h d", d=64),
            in1=rdt[:, 0, :].unsqueeze(2).to_broadcast([128, 8, 64]), op=ALU.mult))
        em("vector", lambda E: E.tensor_tensor(
            out=app[:, 1, :].rearrange("p (h d) -> p h d", d=64),
            in0=pF.rearrange("p (h d) -> p h d", d=64),
            in1=rdt[:, 1, :].unsqueeze(2).to_broadcast([128, 8, 64]), op=ALU.mult))
        em("vector", lambda E, b=b:
           E.tensor_tensor(out=attnB[:, b, :, :].rearrange("p h d -> p (h d)"),
                           in0=app[:, 0, :], in1=app[:, 1, :], op=ALU.add))
        evv = inc("vector", sV)
        pEF_free_t = (sV, evv)
        ev_vec[0] = evv
        if b == 3:
            wt("gpsimd", sV, ev_vec[0])
            arw1 = dma("gpsimd", sG, arin1[:, :],
                       attnB.rearrange("p b h d -> p (b h d)")[:, 0:2048])
            wt("gpsimd", sG, arw1)
            em("gpsimd", lambda E: E.collective_compute(
                "AllReduce", ALU.add, replica_groups=[list(range(NCORES))],
                ins=[arin1.ap().opt()], outs=[arout1.ap().opt()]).then_inc(sC.h, 1))
            cc1 = sC.bump(1)

    wff_feed(len(wff_pending))

    if trunc == "P4":
        return _finish_trunc()

    wt("gpsimd", sV, ev_vec[0])
    arw2 = dma("gpsimd", sG, arin2[:, :],
               attnB.rearrange("p b h d -> p (b h d)")[:, 2048:4096])
    wt("gpsimd", sG, arw2)
    em("gpsimd", lambda E: E.collective_compute(
        "AllReduce", ALU.add, replica_groups=[list(range(NCORES))],
        ins=[arin2.ap().opt()], outs=[arout2.ap().opt()]).then_inc(sC.h, 1))
    cc2 = sC.bump(1)
    wt("gpsimd", sC, cc1)
    ar_r1 = dma("gpsimd", sG, attnB.rearrange("p b h d -> p (b h d)")[:, 0:2048],
                arout1[:, :])
    wt("gpsimd", sC, cc2)
    ar_r = dma("gpsimd", sG, attnB.rearrange("p b h d -> p (b h d)")[:, 2048:4096],
               arout2[:, :])

    if trunc == "CC":
        wt("gpsimd", sG, ar_r)
        return _finish_trunc()

    wg_cast = stage_cast(W_d["WgT"], Wslot, dest_guard=(sT, vproj_done_sT))
    wt_pair("tensor", wg_cast)
    g_free = [0, 0]
    eg_free_t = (sV, 0)
    egb = esc[:, 0:256].bitcast(BF16)
    for pc in range(8):
        for tt in range(4):
            it_ = pc * 4 + tt
            bank = pA if it_ % 2 == 0 else pB
            wt("tensor", sS, g_free[it_ % 2])
            for ic in range(8):
                em("tensor", lambda E, ic=ic, pc=pc, tt=tt, bank=bank:
                   E.matmul(bank, Wslot[:, ic, pc, :], H[:, ic, tt * 512:(tt + 1) * 512],
                            start=(ic == 0), stop=(ic == 7)))
            gv = inc("tensor", sT)
            wt("scalar", sT, gv)
            wtv("scalar", eg_free_t)
            em("scalar", lambda E, bank=bank:
               E.activation(out=egb, in_=bank, func=AF.Exp, bias=0.0, scale=0.125))
            g_free[it_ % 2] = inc("scalar", sS)
            wt("tensor", sS, g_free[it_ % 2])
            em("tensor", lambda E: E.matmul(pS2[0:2, 0:512], ind2, egb, start=True, stop=True))
            dnv = inc("tensor", sT)
            wt("vector", sT, dnv)
            em("vector", lambda E: E.reciprocal(out=app[0:2, 0, :], in_=pS2[0:2, 0:512]))
            vedge()
            em("vector", lambda E: E.tensor_copy(out=rden, in_=app[0:2, 0, :]))
            rdv = inc("vector", sV)
            wt("tensor", sV, rdv)
            wtv("tensor", eg_free_t)
            em("tensor", lambda E: E.matmul(pS1, ind2T, rden, start=True, stop=True))
            exv = inc("tensor", sT)
            wt("vector", sT, exv)
            em("vector", lambda E, pc=pc, tt=tt:
               E.tensor_tensor(out=GW[:, pc, tt * 512:(tt + 1) * 512], in0=egb, in1=pS1,
                               op=ALU.mult))
            eg_free_t = (sV, inc("vector", sV))

    if trunc == "P5":
        return _finish_trunc()

    wp_cast = stage_cast(W_d["WpT"], Wslot, dest_guard=(sT, sT.n))



    wt("tensor", sG, ar_r1)
    wtv("tensor", eg_free_t)
    w_free = [(sV, 0), (sS, 0)]
    for b in range(8):
        if b == 4:
            wt("tensor", sG, ar_r)
        for hp in range(8):
            it_ = hp * 8 + b
            bank = pA if it_ % 2 == 0 else pB
            wtv("tensor", w_free[it_ % 2])
            for ph in range(2):
                em("tensor", lambda E, hp=hp, b=b, ph=ph, bank=bank:
                   E.matmul(bank[64 * ph:64 * ph + 64, 0:256],
                            attnB[64 * ph:64 * ph + 64, b, hp, :],
                            GW[64 * ph:64 * ph + 64, hp, b * 256:(b + 1) * 256],
                            start=True, stop=True, skip_group_check=True))
            wv = inc("tensor", sT)
            if it_ % 2 == 0:
                wt("vector", sT, wv)
                em("vector", lambda E, hp=hp, b=b, bank=bank:
                   E.tensor_copy(out=GW[:, hp, b * 256:(b + 1) * 256], in_=bank[:, 0:256]))
                w_free[0] = (sV, inc("vector", sV))
            else:
                wt("scalar", sT, wv)
                em("scalar", lambda E, hp=hp, b=b, bank=bank:
                   E.activation(out=GW[:, hp, b * 256:(b + 1) * 256], in_=bank[:, 0:256],
                                func=AF.Identity))
                w_free[1] = (sS, inc("scalar", sS))

    if trunc == "P6":
        return _finish_trunc()

    wt_pair("tensor", wp_cast)
    wtv("tensor", w_free[0])
    wtv("tensor", w_free[1])
    wt("vector", sV, modB_ready)
    x_r = x_d.rearrange("(f p) b t -> p f b t", p=128)
    x2_r = x2_d.rearrange("(f p) b t -> p f b t", p=128)
    p7_free = [0, 0]
    x2b_free = [0, 0]
    xrd_v = [0, 0]
    xrd_used = [0, 0]
    wt("gpsimd", sT, sT.n)
    for oc in range(8):
        wt("gpsimd", sV, xrd_used[oc % 2])
        xrd_v[oc % 2] = dma("gpsimd", sG, xrd[oc % 2].rearrange("p b t -> p (b t)"),
                            x_r[:, oc, :, :].rearrange("p b t -> p (b t)"))
        for tt in range(4):
            it_ = oc * 4 + tt
            bank = pA if it_ % 2 == 0 else pB
            wt("tensor", sV, p7_free[it_ % 2])
            for ic in range(8):
                em("tensor", lambda E, ic=ic, oc=oc, tt=tt, bank=bank:
                   E.matmul(bank, Wslot[:, ic, oc, :], GW[:, ic, tt * 512:(tt + 1) * 512],
                            start=(ic == 0), stop=(ic == 7)))
            pv = inc("tensor", sT)
            wt("vector", sT, pv)
            wt("vector", sG, xrd_v[oc % 2])
            wt("vector", sG, x2b_free[tt % 2])
            for hf in range(2):
                b = tt * 2 + hf
                em("vector", lambda E, oc=oc, tt=tt, hf=hf, b=b, bank=bank:
                   E.scalar_tensor_tensor(
                       out=x2b[tt % 2][:, hf, :],
                       in0=bank[:, hf * 256:hf * 256 + 256],
                       scalar=modF[:, 2, oc, b:b + 1],
                       in1=xrd[oc % 2][:, b, :],
                       op0=ALU.mult, op1=ALU.add))
            p7_free[it_ % 2] = inc("vector", sV)
            xrd_used[oc % 2] = p7_free[it_ % 2]
            wt("gpsimd", sV, p7_free[it_ % 2])
            x2b_free[tt % 2] = dma(
                "gpsimd", sG,
                x2_r[:, oc, 2 * tt:2 * tt + 2, :].rearrange("p b t -> p (b t)"),
                x2b[tt % 2].rearrange("p a t -> p (a t)"))
    x2_written = sG.n
    p7_done_sT = sT.n
    for n in ("wi", "wc"):
        wff_cast[n] = direct_weight(n, Wff[n], dest_guard=(sT, p7_done_sT) if n == "wi" else None)
    if trunc == "P7":
        return _finish_trunc()

    x2s_free = [(0, 0), (0, 0)]
    if trunc == "P7b":
        return _finish_trunc()

    x2bufs = [x2stg, x2stgB]
    p8_lv = [0, 0]

    def p8_load(tt):
        wt("sync", sG, x2_written)
        if tt % 2 == 0:
            wt("sync", sV, stage_cast.lastcast[0])
            wt("sync", sS, stage_cast.lastcast[1])
        wt("sync", sV, x2s_free[tt % 2][0])
        wt("sync", sP, x2s_free[tt % 2][1])
        return dma("sync", sQ, x2bufs[tt % 2], x2_r[:, :, 2 * tt:2 * tt + 2, :])

    def p8_prep(tt):
        p8_lv[tt % 2] = p8_load(tt)
        wt("vector", sQ, p8_lv[tt % 2])

    def p8_post(tt):
        x2s_free[tt % 2] = (sV.n, sP.n)

    h2_done = emit_ln(lambda fc, tt: x2bufs[tt % 2][:, fc, :, :].rearrange("p a t -> p (a t)"),
                      4, 3, prep_fn=p8_prep, post_fn=p8_post,
                      src_qwait_fn=lambda tt: p8_lv[tt % 2])

    wt("sync", sS, h2_done)
    c0v = dma("sync", sQ, cS.rearrange("p b t -> p (b t)"), inc0_d.rearrange("p b t -> p (b t)"))
    wt("vector", sQ, c0v)
    em("vector", lambda E: E.tensor_copy(out=cbf[0].rearrange("p b t -> p (b t)"),
                                         in_=cS.rearrange("p b t -> p (b t)")))
    c0_ready = inc("vector", sV)

    wt("tensor", sV, c0_ready)
    wt("tensor", sS, h2_done)

    GATES = (("f", AF.Sigmoid, 0), ("i", AF.Sigmoid, 1), ("c", AF.Tanh, 2))
    gate_waited = set()
    bank_free = {}
    step_done = 0
    for t in range(8):
        cbf_r = cbf[t % 2]
        cbf_w = cbf[(t + 1) % 2]
        wt("tensor", sV, step_done)
        for oc in range(8):
            par = oc % 2
            banks = {"f": (pA if par == 0 else pB),
                     "i": (pE if par == 0 else pF),
                     "c": (pT[:, 0:256] if par == 0 else pT[:, 512:768])}
            gv = {}
            for gname, gfunc, gidx in GATES:
                if gname not in gate_waited:
                    gate_waited.add(gname)
                    wt_pair("tensor", wff_cast["w" + gname])
                    wt_pair("tensor", wff_cast["u" + gname])
                wt("tensor", sV, bank_free.get((gname, par), 0))
                wname, uname = "w" + gname, "u" + gname
                for ic in range(8):
                    em("tensor", lambda E, ic=ic, oc=oc, t=t, gn=gname, bank=banks[gname]:
                       E.matmul(bank[:, 0:256] if bank.shape[1] > 256 else bank,
                                Wff["w" + gn][:, ic, oc, :], H[:, ic, t * 256:(t + 1) * 256],
                                start=(ic == 0), stop=False))
                for ic in range(8):
                    em("tensor", lambda E, ic=ic, oc=oc, gn=gname, bank=banks[gname], cb=cbf_r:
                       E.matmul(bank[:, 0:256] if bank.shape[1] > 256 else bank,
                                Wff["u" + gn][:, ic, oc, :], cb[:, ic, :],
                                start=False, stop=(ic == 7)))
                gv[gname] = inc("tensor", sT)
            dst = {"f": gscF, "i": gscI, "c": gscG}
            for gname, gfunc, gidx in GATES:
                wt("scalar", sT, gv[gname])
                em("scalar", lambda E, gn=gname, gf=gfunc, gi=gidx, oc=oc, par=par,
                   bank=banks[gname]:
                   E.activation(out=dst[gn][:, par, :],
                                in_=bank[:, 0:256] if bank.shape[1] > 256 else bank,
                                func=gf, bias=fbias[:, gi, oc:oc + 1], scale=1.0))
            actv = inc("scalar", sS)
            wt("vector", sS, actv)
            em("vector", lambda E, oc=oc, par=par: E.tensor_tensor(
                out=cS[:, oc, :], in0=gscF[:, par, :], in1=cS[:, oc, :], op=ALU.mult))
            em("vector", lambda E, par=par: E.tensor_tensor(
                out=gscI[:, par, :], in0=gscI[:, par, :], in1=gscG[:, par, :], op=ALU.mult))
            em("vector", lambda E, oc=oc, par=par: E.tensor_tensor(
                out=cS[:, oc, :], in0=cS[:, oc, :], in1=gscI[:, par, :], op=ALU.add))
            em("vector", lambda E, oc=oc, cw=cbf_w: E.tensor_copy(
                out=cw[:, oc, :], in_=cS[:, oc, :]))
            bv = inc("vector", sV)
            for gname, _, _ in GATES:
                bank_free[(gname, par)] = bv
        step_done = sV.n

    wt("sync", sS, h2_done)
    wov = stage_cast(W_d["woT"], Wff["wo"], dest_guard=(sT, p7_done_sT))
    uov = stage_cast(W_d["uoT"], Wff["uo"], dest_guard=(sT, sT.n))

    wt_pair("tensor", uov)
    wt("tensor", sS, sS.n)
    cbf_fin = cbf[0]
    for oc in range(8):
        bank = pE if oc % 2 == 0 else pF
        for ic in range(8):
            em("tensor", lambda E, ic=ic, oc=oc, bank=bank:
               E.matmul(bank[:, 0:256], Wff["uo"][:, ic, oc, :], cbf_fin[:, ic, :],
                        start=(ic == 0), stop=(ic == 7)))
        uv = inc("tensor", sT)
        wt("scalar", sT, uv)
        em("scalar", lambda E, oc=oc, bank=bank:
           E.activation(out=uodv[:, oc, :], in_=bank[:, 0:256], func=AF.Identity,
                        bias=fbias[:, 3, oc:oc + 1], scale=1.0))
    uod_done = inc("scalar", sS)

    wt_pair("tensor", wov)
    wt("tensor", sS, uod_done)
    xt_v = [0, 0]
    xt_used = [0, 0]
    tail_free = [0, 0]
    ybuf_free = [0, 0]
    yv_last = 0
    for oc in range(8):
        wt("sync", sV, max(xt_used[oc % 2], uov[0]))
        wt("sync", sS, uov[1])
        xt_v[oc % 2] = dma("sync", sQ, xt[oc % 2].rearrange("p b t -> p (b t)"),
                           x2_r[:, oc, :, :].rearrange("p b t -> p (b t)"))
        for t in range(8):
            par = (oc * 8 + t) % 2
            bank = pA if par == 0 else pB
            wt("tensor", sV, tail_free[par])
            for ic in range(8):
                em("tensor", lambda E, ic=ic, oc=oc, t=t, bank=bank:
                   E.matmul(bank[:, 0:256], Wff["wo"][:, ic, oc, :],
                            H[:, ic, t * 256:(t + 1) * 256],
                            start=(ic == 0), stop=(ic == 7)))
            xv = inc("tensor", sT)
            wt("vector", sT, xv)
            em("vector", lambda E, oc=oc, bank=bank, par=par: E.tensor_tensor(
                out=app[:, par, 0:256], in0=bank[:, 0:256], in1=uodv[:, oc, :], op=ALU.add))
            s1v = inc("vector", sV)
            wt("scalar", sV, s1v)
            em("scalar", lambda E, par=par: E.activation(
                out=app[:, par, 256:512], in_=app[:, par, 0:256], func=AF.Tanh))
            s2v = inc("scalar", sS)
            ybuf = ybufA if t < 4 else ybufB
            yb = 0 if t < 4 else 1
            wt("vector", sS, s2v)
            wt("vector", sQ, xt_v[oc % 2])
            if t in (0, 4):
                wt("vector", sO, ybuf_free[yb])
            em("vector", lambda E, oc=oc, t=t, par=par, ybuf=ybuf: E.scalar_tensor_tensor(
                out=ybuf[:, t % 4, :], in0=app[:, par, 256:512],
                scalar=modF[:, 5, oc, t:t + 1], in1=xt[oc % 2][:, t, :],
                op0=ALU.mult, op1=ALU.add))
            tail_free[par] = inc("vector", sV)
            xt_used[oc % 2] = tail_free[par]
            if t in (3, 7):
                wt("gpsimd", sV, tail_free[par])
                y_r = y_d.rearrange("(f p) b t -> p f b t", p=128)
                tb = t // 4
                ybuf_free[yb] = dma("gpsimd", sO,
                                    y_r[:, oc, 4 * tb:4 * tb + 4, :].rearrange("p b t -> p (b t)"),
                                    ybuf.rearrange("p a t -> p (a t)"))
                yv_last = ybuf_free[yb]

    if DBG:
        wt("gpsimd", sV, sV.n)
        wt("gpsimd", sS, sS.n)
        wt("gpsimd", sT, sT.n)
        dma("gpsimd", sO, dbg_d["h2"].bitcast(BF16), H.rearrange("p a b -> p (a b)"))
        dma("gpsimd", sO, dbg_d["x2"], x2_d)
        dma("gpsimd", sO, dbg_d["cfin"], cS.rearrange("p a b -> p (a b)"))
        dma("gpsimd", sO, dbg_d["uod"], uodv.rearrange("p a b -> p (a b)"))
    em("gpsimd", lambda E, v=sO.n: E.wait_ge(sO.h, v))

    with nc.Block() as block:
        @block.tensor
        def _(E):
            for fn in ops["tensor"]:
                fn(E)

        @block.vector
        def _(E):
            for fn in ops["vector"]:
                fn(E)

        @block.scalar
        def _(E):
            for fn in ops["scalar"]:
                fn(E)

        @block.sync
        def _(E):
            for fn in ops["sync"]:
                fn(E)

        @block.gpsimd
        def _(E):
            for fn in ops["gpsimd"]:
                fn(E)

    st.close()
    return nc


DBG_SRC = {}

_prog_cache = {}


def kernel(**inputs):
    shared = _shared_layouts(inputs)
    nc = build_program()
    in_maps = []
    for core in range(NCORES):
        m = dict(shared)
        m.update(_host_constants(core))
        m["x_fm"] = _core_x(inputs, core)
        dm = {"x_fm": m["x_fm"], "wmod": m["wmod"], "cT": m["cT"], "qbd": m["qbd"],
              "ident": m["ident"], "ident_bf": m["ident_bf"], "ones_bf": m["ones_bf"],
              "negeye": m["negeye"], "ind2": m["ind2"], "ind2T": m["ind2T"],
              "mask_lc": m["mask_lc"], "mask_T": m["mask_T"],
              "cosF": m["cosF"], "sinF": m["sinF"], "in_c0": m["in_c0"]}
        for n in ("WkTp", "WvT", "WgT", "WpT", "wfT", "ufT", "wiT", "uiT", "wcT", "ucT",
                  "woT", "uoT"):
            dm[n] = m[n]
        for n in ("bf", "bi", "bc", "bo"):
            dm[n + "r"] = m[n + "r"]
        in_maps.append(dm)
    res = run_bass_kernel_spmd(nc, in_maps, core_ids=list(range(NCORES)))
    return _assemble_output([res.results[i]["y_fm"] for i in range(NCORES)])


# version 18
# speedup vs baseline: 1.0504x; 1.0120x over previous
import os
import numpy as np
import ml_dtypes
from contextlib import ExitStack

import concourse.bass as bass
import concourse.mybir as mybir
from concourse.bass_utils import run_bass_kernel_spmd

F32 = mybir.dt.float32
BF16 = mybir.dt.bfloat16
U16 = mybir.dt.uint16
AF = mybir.ActivationFunctionType
ALU = mybir.AluOpType
AXL = mybir.AxisListType

NCORES = 8
B = 8
TL = 256
NEG = -1.0e30
SWAP16 = list(range(16, 32)) + list(range(16))

_PERM = np.concatenate([np.arange(0, 64), np.arange(128, 192),
                        np.arange(64, 128), np.arange(192, 256)])
_IPERM = np.argsort(_PERM)


def _dperm():
    p = np.arange(64)
    q16 = p // 16
    j = (q16 // 2) * 16 + p % 16
    return 2 * j + q16 % 2


_DPERM = _dperm()


def _kperm():
    out = np.zeros(1024, np.int64)
    for hp in range(8):
        for p in range(128):
            out[hp * 128 + p] = (2 * hp + p // 64) * 64 + _DPERM[p % 64]
    return out


_KPERM = _kperm()


def _bf(a):
    return np.ascontiguousarray(a.astype(ml_dtypes.bfloat16).view(np.uint16))


def _host_constants(core_id):
    c = {}
    c["ident"] = np.eye(128, dtype=np.float32)
    c["ident_bf"] = _bf(np.eye(128, dtype=np.float32))
    c["ones_bf"] = _bf(np.ones((128, 128), np.float32))
    ne = np.zeros((8, 8, 64), np.float32)
    for k in range(8):
        ne[k, k, :] = -1.0
    c["negeye"] = _bf(ne)
    ind2 = np.zeros((128, 2), np.float32)
    ind2[:64, 0] = 1.0
    ind2[64:, 1] = 1.0
    c["ind2"] = _bf(ind2)
    c["ind2T"] = _bf(ind2.T.copy())
    l_of_p = np.arange(128) % 64
    cc = np.arange(64)
    c["mask_lc"] = np.where(cc[None, :] <= l_of_p[:, None], 0.0, NEG).astype(np.float32)
    c["mask_T"] = np.where(cc[:, None] <= l_of_p[None, :], 0.0, NEG).astype(np.float32)
    p = np.arange(128) % 64
    q16 = p // 16
    j = (q16 // 2) * 16 + p % 16
    is_b = q16 % 2
    inv = 1.0 / 10000.0 ** (np.arange(0, 64, 2, dtype=np.float32) / 64.0)
    c2 = np.arange(128)
    pos = core_id * 256 + (c2 // 64) * 128 + c2 % 64
    ang = pos[None, :].astype(np.float32) * inv[j][:, None]
    c["cosF"] = np.cos(ang).astype(np.float32)
    sinF = np.sin(ang).astype(np.float32)
    c["sinF"] = np.where(is_b[:, None] == 0, -sinF, sinF).astype(np.float32)
    return c


def _shared_layouts(inputs):
    g = lambda n: np.asarray(inputs[n], dtype=np.float32)
    s = {}
    s["WkTp"] = np.ascontiguousarray(g("L0_attn_W_k").T[:, _KPERM])
    s["WvT"] = np.ascontiguousarray(g("L0_attn_W_v").T)
    s["WgT"] = np.ascontiguousarray(g("L0_attn_W_g").T)
    s["WpT"] = np.ascontiguousarray(g("L0_attn_W_p").T)
    for n in ("wf", "uf", "wi", "ui", "wc", "uc", "wo", "uo"):
        s[n + "T"] = np.ascontiguousarray(g("L0_ffn_" + n).T)
    s["wmod"] = np.ascontiguousarray(g("L0_wmod"))
    s["cT"] = np.ascontiguousarray(g("L0_c").T)
    lq = g("L0_attn_latent_queries")[0]
    qbd = np.zeros((8, 128, 128), np.float32)
    for hp in range(8):
        for ph in range(2):
            blk = lq[:, 2 * hp + ph, :][:, _DPERM]
            qbd[hp, 64 * ph:64 * ph + 64, 64 * ph:64 * ph + 64] = blk.T
    s["qbd"] = np.ascontiguousarray(qbd)
    for n in ("bf", "bi", "bc", "bo"):
        s[n + "r"] = np.ascontiguousarray(g("L0_ffn_" + n).reshape(8, 128).T)
    ic = g("L0_ffn_in_c").reshape(8, 128).T
    s["in_c0"] = np.ascontiguousarray(np.broadcast_to(ic[:, :, None], (128, 8, 256)))
    return s


def _core_x(inputs, core_id):
    x = np.asarray(inputs["L0_attn_x"], dtype=np.float32)
    xs = x[:, core_id * 256:(core_id + 1) * 256, :][:, _PERM, :]
    return np.ascontiguousarray(xs.transpose(2, 0, 1))


def _assemble_output(shards):
    y = np.empty((8, 2048, 1024), np.float32)
    for i, sh in enumerate(shards):
        ys = np.asarray(sh).reshape(1024, 8, 256).transpose(1, 2, 0)
        y[:, i * 256:(i + 1) * 256, :] = ys[:, _IPERM, :]
    return y


class Sem:
    def __init__(self, h):
        self.h = h
        self.n = 0

    def bump(self, k=1):
        self.n += k
        return self.n


def _view(arena, off, nslots, dtype, shape):
    ap = arena[:, off:off + nslots]
    if dtype == BF16:
        ap = ap.bitcast(BF16)
    total = 1
    for s in shape:
        total *= s
    assert total == ap.shape[1], (shape, ap.shape)
    if len(shape) > 1:
        pat = " ".join(f"d{i}" for i in range(len(shape)))
        ap = ap.rearrange(f"p ({pat}) -> p {pat}",
                          **{f"d{i}": shape[i] for i in range(len(shape))})
    return ap


FFN_W = ("uf", "ui", "uc", "wf", "wi", "wc", "wo", "uo")


def build_program(debug=(), trunc=None):
    nc = bass.Bass("TRN2", target_bir_lowering=False, detect_race_conditions=False)
    DI = lambda name, shape, dt=F32: nc.dram_tensor(name, shape, dt, kind="ExternalInput").ap()

    x_d = DI("x_fm", [1024, B, TL])
    W_d = {n: DI(n, [1024, 1024]) for n in
           ("WkTp", "WvT", "WgT", "WpT", "wfT", "ufT", "wiT", "uiT", "wcT", "ucT", "woT", "uoT")}
    wmod_d = DI("wmod", [1024, 6144])
    cT_d = DI("cT", [1024, 8])
    qbd_d = DI("qbd", [8, 128, 128])
    ident_d = DI("ident", [128, 128])
    identbf_d = DI("ident_bf", [128, 128], U16)
    onesbf_d = DI("ones_bf", [128, 128], U16)
    negeye_d = DI("negeye", [8, 8, 64], U16)
    ind2_d = DI("ind2", [128, 2], U16)
    ind2T_d = DI("ind2T", [2, 128], U16)
    masklc_d = DI("mask_lc", [128, 64])
    maskT_d = DI("mask_T", [64, 128])
    cosF_d = DI("cosF", [128, 128])
    sinF_d = DI("sinF", [128, 128])
    bias_d = {n: DI(n + "r", [128, 8]) for n in ("bf", "bi", "bc", "bo")}
    inc0_d = DI("in_c0", [128, 8, 256])

    y_d = nc.dram_tensor("y_fm", [1024, B, TL], F32, kind="ExternalOutput").ap()
    x2_d = nc.dram_tensor("x2_scr", [1024, B, TL], BF16).ap()
    arin1 = nc.dram_tensor("ar_in1", [128, 2048], BF16)
    arout1 = nc.dram_tensor("ar_out1", [128, 2048], BF16)
    arin2 = nc.dram_tensor("ar_in2", [128, 2048], BF16)
    arout2 = nc.dram_tensor("ar_out2", [128, 2048], BF16)
    arwi = nc.dram_tensor("ar_wi", [128, 64], BF16)
    arwo = nc.dram_tensor("ar_wo", [128, 64], BF16)

    DBG = os.environ.get("K_DBG", "") == "1"
    dbg_d = {}
    if DBG:
        dbg_d["gates"] = nc.dram_tensor("dbg_gates", [128, 16384], U16, kind="ExternalOutput").ap()
        dbg_d["attnB"] = nc.dram_tensor("dbg_attnB", [128, 4096], U16, kind="ExternalOutput").ap()
        dbg_d["attnP"] = nc.dram_tensor("dbg_attnP", [128, 4096], U16, kind="ExternalOutput").ap()
        dbg_d["h2"] = nc.dram_tensor("dbg_h2", [128, 16384], U16, kind="ExternalOutput").ap()
        dbg_d["x2"] = nc.dram_tensor("dbg_x2", [1024, B, TL], F32, kind="ExternalOutput").ap()
        dbg_d["cfin"] = nc.dram_tensor("dbg_cfin", [128, 2048], F32, kind="ExternalOutput").ap()
        dbg_d["uod"] = nc.dram_tensor("dbg_uod", [128, 2048], F32, kind="ExternalOutput").ap()

    st = ExitStack()
    sbuf = lambda name, shape, dt=F32: st.enter_context(nc.sbuf_tensor(name, shape, dt))[:]
    psum = lambda name, shape: st.enter_context(nc.psum_tensor(name, shape, F32))[:]

    arena = st.enter_context(nc.sbuf_tensor("arena", [128, 46080], F32))[:]

    ident = sbuf("identt", [128, 128])
    identbf = sbuf("identbft", [128, 128], BF16)
    onesbf = sbuf("onesbft", [128, 128], BF16)
    qbd = sbuf("qbdt", [128, 8, 128], BF16)
    negeye = sbuf("negeyet", [8, 8, 64], BF16)
    ind2 = sbuf("ind2t", [128, 2], BF16)
    ind2T = sbuf("ind2Tt", [2, 128], BF16)
    masklc = sbuf("masklct", [128, 64])
    maskT = sbuf("maskTt", [64, 128])
    cosF = sbuf("cosFt", [128, 128])
    sinF = sbuf("sinFt", [128, 128])
    modF = sbuf("modFt", [128, 6, 8, 8])
    cTt = sbuf("cTt", [128, 8, 8])
    silubf = sbuf("silubft", [128, 8, 8], BF16)
    Rsc = sbuf("Rsct", [128, 32])
    rdt = sbuf("rdtt", [128, 2, 8])
    RT8 = sbuf("RT8t", [8, 128], BF16)
    fbias = sbuf("fbiast", [128, 4, 8])
    rden = sbuf("rdent", [2, 512], BF16)
    consts = sbuf("constst", [128, 4])
    stch = sbuf("stcht", [128, 3, 512])
    app = sbuf("appt", [128, 2, 512])
    esc = sbuf("esct", [128, 1024])
    bfx = sbuf("bfxt", [128, 2, 512], BF16)
    bfsq = sbuf("bfsqt", [128, 2, 512], BF16)

    nc.const_aps.register(F32, 0.125, consts[:, 0:1])
    nc.const_aps.register(F32, 8.0, consts[:, 1:2])
    nc.const_aps.register(F32, 1e-5, consts[:, 2:3])
    nc.const_aps.register(F32, 1.0 / 1024.0, consts[:, 3:4])

    pA = psum("pA", [128, 512])
    pB = psum("pB", [128, 512])
    pE = psum("pE", [128, 512])
    pF = psum("pF", [128, 512])
    pT = psum("pT", [128, 1024])
    pS1 = psum("pS1", [128, 512])
    pS2 = psum("pS2", [128, 512])

    V = lambda off, n, dt, shape: _view(arena, off, n, dt, shape)
    H = V(0, 8192, BF16, (8, 2048))
    X = V(8192, 16384, F32, (8, 2048))
    Wff = {"uf": V(8192, 4096, BF16, (8, 8, 128)),
           "ui": V(12288, 4096, BF16, (8, 8, 128)),
           "uc": V(16384, 4096, BF16, (8, 8, 128)),
           "wf": V(20480, 4096, BF16, (8, 8, 128)),
           "wi": V(24576, 4096, BF16, (8, 8, 128)),
           "wc": V(28672, 4096, BF16, (8, 8, 128)),
           "wo": V(37888, 4096, BF16, (8, 8, 128)),
           "uo": V(12288, 4096, BF16, (8, 8, 128))}
    uodv = V(8192, 2048, F32, (8, 256))
    modtm = V(24576, 6144, F32, (6144,))
    kS = V(24576, 4096, BF16, (8, 8, 128))
    vS = V(28672, 4096, BF16, (8, 8, 128))
    GW = V(24576, 8192, BF16, (8, 2048))
    qbdf = V(32768, 1024, F32, (8, 128))
    ropeA = V(32768, 512, F32, (4, 128))
    ropeB = V(33280, 512, F32, (4, 128))
    vtmp = V(33792, 256, BF16, (4, 128))
    attnB = V(32768, 2048, BF16, (8, 8, 64))
    xrd = [V(32768, 2048, F32, (8, 256)), V(34816, 2048, F32, (8, 256))]
    cS = V(32768, 2048, F32, (8, 256))
    cbf = [V(34816, 1024, BF16, (8, 256)), V(35840, 1024, BF16, (8, 256))]
    probsb = [V(36864, 512, BF16, (8, 128)), V(37376, 512, BF16, (8, 128))]
    wbounce = [V(36864, 1024, BF16, (2048,)), V(36864, 1024, BF16, (2048,))]

    ptmp7 = V(36864, 1024, F32, (4, 256))
    gscG = V(36864, 512, F32, (2, 256))
    ybufB = V(36864, 1024, F32, (4, 256))
    Wslot = V(37888, 4096, BF16, (8, 8, 128))
    wmodbf = V(37888, 2048, BF16, (2, 8, 256))
    stagh = [V(41984, 2048, F32, (2048,)), V(44032, 2048, F32, (2048,))]
    x2stg = V(41984, 2048, BF16, (8, 2, 256))
    x2stgB = V(32768, 2048, BF16, (8, 2, 256))
    xt = [V(41984, 1024, BF16, (8, 256)), V(44032, 1024, BF16, (8, 256))]
    gscF = esc[:, 0:512].rearrange("p (a t) -> p a t", t=256)
    gscI = esc[:, 512:1024].rearrange("p (a t) -> p a t", t=256)
    x2b = [esc[:, 0:256].bitcast(BF16).rearrange("p (a t) -> p a t", t=256),
           esc[:, 256:512].bitcast(BF16).rearrange("p (a t) -> p a t", t=256)]
    ybufA = esc.rearrange("p (a t) -> p a t", t=256)

    ops = {e: [] for e in ("tensor", "vector", "scalar", "sync", "gpsimd")}
    em = lambda e, fn: ops[e].append(fn)

    sems = {}
    for name in ("sQ", "sQ2", "sG", "sT", "sV", "sS", "sC", "sO", "sP"):
        sems[name] = Sem(st.enter_context(nc.semaphore(name)))
    sQ, sQ2, sG, sT, sV, sS, sC, sO, sP = (sems[k] for k in
                                           ("sQ", "sQ2", "sG", "sT", "sV", "sS", "sC",
                                            "sO", "sP"))

    def dma(eng, sem, out, in_):
        v = sem.bump(16)
        em(eng, lambda E, o=out, i=in_, s=sem.h: E.dma_start(out=o, in_=i).then_inc(s, 16))
        return v

    def wt(eng, sem, v):
        if v:
            em(eng, lambda E, s=sem.h, vv=v: E.wait_ge(s, vv))

    def vedge():
        em("vector", lambda E: E.engine_nop())
        v = inc("vector", sV)
        wt("vector", sV, v)

    def wt_pair(eng, pair):
        wt(eng, sV, pair[0])
        wt(eng, sS, pair[1])

    def wtv(eng, sv):
        if sv[1]:
            wt(eng, sv[0], sv[1])

    def inc(eng, sem, k=1):
        v = sem.bump(k)
        last = ops[eng][-1]

        def wrapped(E, f=last, s=sem.h, kk=k):
            ins = f(E)
            assert ins is not None, "inc() must follow an instruction-emitting op"
            return ins.then_inc(s, kk)
        ops[eng][-1] = wrapped
        return v

    xld = dma("sync", sQ, X.rearrange("p f (b t) -> p f b t", t=256),
              x_d.rearrange("(f p) b t -> p f b t", p=128))
    for out, d in (
            (ident, ident_d), (identbf, identbf_d.bitcast(BF16)), (onesbf, onesbf_d.bitcast(BF16)),
            (qbdf, qbd_d.rearrange("h a b -> a h b")),
            (negeye.rearrange("k h c -> k (h c)"),
             negeye_d.rearrange("k h c -> k (h c)").bitcast(BF16)),
            (ind2, ind2_d.bitcast(BF16)), (ind2T, ind2T_d.bitcast(BF16)),
            (masklc, masklc_d), (maskT, maskT_d), (cosF, cosF_d), (sinF, sinF_d),
            (cTt, cT_d.rearrange("(i p) b -> p i b", p=128)),
            (fbias[:, 0, :], bias_d["bf"]), (fbias[:, 1, :], bias_d["bi"]),
            (fbias[:, 2, :], bias_d["bc"]), (fbias[:, 3, :], bias_d["bo"])):
        dma("gpsimd", sG, out, d)
    cver = sG.n

    wt("gpsimd", sG, cver)
    wdma = dma("gpsimd", sG, arwi[:, :], identbf[:, 0:64])
    wt("gpsimd", sG, wdma)
    em("gpsimd", lambda E: E.collective_compute(
        "AllReduce", ALU.add, replica_groups=[list(range(NCORES))],
        ins=[arwi.ap().opt()], outs=[arwo.ap().opt()]).then_inc(sC.h, 1))
    sC.bump(1)

    def _initc(E):
        E.memset(consts[:, 0:1], 0.125)
        E.memset(consts[:, 1:2], 8.0)
        E.memset(consts[:, 2:3], 1e-5)
        E.memset(consts[:, 3:4], 1.0 / 1024.0)
    em("vector", _initc)
    wt("vector", sG, cver)
    em("vector", lambda E: E.tensor_copy(out=qbd.rearrange("p h c -> p (h c)"),
                                         in_=qbdf.rearrange("p h c -> p (h c)")))
    qbd_ready = inc("vector", sV)

    wt("scalar", sG, cver)
    em("scalar", lambda E: E.activation(out=silubf.rearrange("p i b -> p (i b)"),
                                        in_=cTt.rearrange("p i b -> p (i b)"), func=AF.Silu))
    silu_ready = inc("scalar", sS)
    wt("tensor", sS, silu_ready)

    wmod_r = wmod_d.rearrange("(i p) j -> p i j", p=128)
    wmod_mm_done = {}
    wmod_cast_done = {}
    CENG = {0: ("vector", sV), 1: ("scalar", sS)}
    for jt in range(24):
        half = jt % 2
        ceng, csem = CENG[half]
        qeng, qsem = (("sync", sQ) if half == 0 else ("scalar", sQ2))
        if jt >= 2:
            wt(qeng, csem, wmod_cast_done[jt - 2])
        lv = dma(qeng, qsem, stagh[half].rearrange("p (i j) -> p i j", j=256),
                 wmod_r[:, :, jt * 256:(jt + 1) * 256])
        wt(ceng, qsem, lv)
        if jt >= 2:
            wt(ceng, sT, wmod_mm_done[jt - 2])
        dst = wmodbf[:, half, :, :].rearrange("p i j -> p (i j)")
        if half == 0:
            em(ceng, lambda E, d=dst, h=half: E.tensor_copy(out=d, in_=stagh[h]))
        else:
            em(ceng, lambda E, d=dst, h=half: E.activation(out=d, in_=stagh[h],
                                                           func=AF.Identity))
        wmod_cast_done[jt] = inc(ceng, csem)
        wt("tensor", csem, wmod_cast_done[jt])
        bank = pS1 if half == 0 else pS2
        for ic in range(8):
            em("tensor", lambda E, ic=ic, b=bank, h=half:
               E.matmul(b[0:8, 0:256], silubf[:, ic, :], wmodbf[:, h, ic, :],
                        start=(ic == 0), stop=(ic == 7)))
        wmod_mm_done[jt] = inc("tensor", sT)
        wt("vector", sT, wmod_mm_done[jt])
        em("vector", lambda E, jt=jt, b=bank:
           E.tensor_copy(out=modtm[0:8, jt * 256:(jt + 1) * 256], in_=b[0:8, 0:256]))
        inc("vector", sV)

    def emit_mod_tr(t48s):
        for t48 in t48s:
            vidx, fc = t48 // 8, t48 % 8
            em("tensor", lambda E, t48=t48:
               E.transpose(pA[:, 0:8], modtm[0:8, t48 * 128:(t48 + 1) * 128], ident[0:8, 0:8]))
            tv = inc("tensor", sT)
            wt("vector", sT, tv)
            em("vector", lambda E, vidx=vidx, fc=fc:
               E.tensor_copy(out=modF[:, vidx, fc, :], in_=pA[:, 0:8]))
            vv = inc("vector", sV)
            wt("tensor", sV, vv)
        return sV.n

    modA_ready = emit_mod_tr(range(16))

    vedge()
    em("vector", lambda E: E.tensor_scalar(out=modF[:, 1, :, :], in0=modF[:, 1, :, :],
                                           scalar1=1.0, scalar2=None, op0=ALU.add))
    modA_ready = inc("vector", sV)

    modB_ready = emit_mod_tr(range(16, 48))

    vedge()
    em("vector", lambda E: E.tensor_scalar(out=modF[:, 4, :, :], in0=modF[:, 4, :, :],
                                           scalar1=1.0, scalar2=None, op0=ALU.add))
    for vidx in (2, 5):
        g = modF[:, vidx, :, :]
        hi = app[:, 0, 0:64].rearrange("p (a b) -> p a b", b=8)
        lo = app[:, 0, 64:128].rearrange("p (a b) -> p a b", b=8)
        sg = app[:, 0, 128:192].rearrange("p (a b) -> p a b", b=8)
        vedge()
        em("vector", lambda E, g=g, hi=hi: E.tensor_scalar(out=hi, in0=g, scalar1=0.25,
                                                           scalar2=None, op0=ALU.max))
        em("vector", lambda E, g=g, lo=lo: E.tensor_scalar(out=lo, in0=g, scalar1=-0.25,
                                                           scalar2=None, op0=ALU.min))
        em("vector", lambda E, g=g, sg=sg: E.tensor_scalar(out=sg, in0=g, scalar1=0.0,
                                                           scalar2=None, op0=ALU.is_ge))
        vedge()
        em("vector", lambda E, hi=hi, lo=lo: E.tensor_tensor(out=hi, in0=hi, in1=lo,
                                                             op=ALU.subtract))
        em("vector", lambda E, hi=hi, sg=sg: E.tensor_tensor(out=hi, in0=sg, in1=hi,
                                                             op=ALU.mult))
        em("vector", lambda E, g=g, hi=hi, lo=lo: E.tensor_tensor(out=g, in0=hi, in1=lo,
                                                                  op=ALU.add))
    modB_ready = inc("vector", sV)

    def stage_cast_q(dram, dest, q, dest_guard=None):
        if stage_cast.lastcast is None:
            stage_cast.lastcast = {0: wmod_cast_done[22], 1: wmod_cast_done[23]}
        src = dram.rearrange("(i p) o -> p i o", p=128)
        half = q % 2
        ceng, csem = CENG[half]
        wt("sync", csem, stage_cast.lastcast.get(half, 0))
        lv = dma("sync", sQ, stagh[half].rearrange("p (i o) -> p i o", o=256),
                 src[:, :, q * 256:(q + 1) * 256])
        wt(ceng, sQ, lv)
        if dest_guard is not None:
            wt(ceng, dest_guard[0], dest_guard[1])
        sh = stagh[half].rearrange("p (i o) -> p i o", o=256)
        dst = dest[:, :, 2 * q:2 * q + 2, :].rearrange("p i a b -> p i (a b)")
        if half == 0:
            em(ceng, lambda E, d=dst, s=sh: E.tensor_copy(out=d, in_=s))
        else:
            em(ceng, lambda E, d=dst, s=sh: E.activation(out=d, in_=s,
                                                         func=AF.Identity))
        v = inc(ceng, csem)
        stage_cast.lastcast[half] = v
        return v

    def stage_cast(dram, dest, dest_guard=None, nq=4):
        for q in range(nq):
            stage_cast_q(dram, dest, q, dest_guard=dest_guard if q in (0, 1) else None)
        return (stage_cast.lastcast[0], stage_cast.lastcast[1])
    stage_cast.lastcast = None

    def direct_weight(name, dest, dest_guard=None):
        return stage_cast(W_d[name + "T"], dest, dest_guard=dest_guard)

    def emit_ln(src_fn, cvec, svec, mod_wait=0, prep_fn=None, post_fn=None,
                src_qwait_fn=None):
        mmv_par = [0, 0]
        act_par = [0, 0]
        psum_rd = [0]
        pool_done = [0]
        pairs = [(app[:, 0, :], app[:, 1, :]),
                 (esc[:, 0:512], esc[:, 512:1024])]
        h2d = [0]

        def stats(tt):
            if prep_fn is not None:
                prep_fn(tt)
            for fc in range(8):
                par = fc % 2
                s = src_fn(fc, tt)
                wt("vector", sT, mmv_par[par])
                em("vector", lambda E, s=s, par=par: E.tensor_copy(out=bfx[:, par, :], in_=s))
                cv = inc("vector", sV)
                wt("scalar", sV, cv)
                wt("scalar", sT, mmv_par[par])
                em("scalar", lambda E, par=par:
                   E.activation(out=bfsq[:, par, :], in_=bfx[:, par, :], func=AF.Square))
                sq2 = inc("scalar", sS)
                wt("tensor", sV, cv)
                wt("tensor", sS, sq2)
                if fc == 0:
                    wt("tensor", sV, psum_rd[0])
                em("tensor", lambda E, par=par, fc=fc:
                   E.matmul(pS1, onesbf, bfx[:, par, :], start=(fc == 0), stop=(fc == 7)))
                em("tensor", lambda E, par=par, fc=fc:
                   E.matmul(pS2, onesbf, bfsq[:, par, :], start=(fc == 0), stop=(fc == 7)))
                mmv_par[par] = inc("tensor", sT)

        def chain_apply(tt, mmv_tail):
            wt("vector", sT, mmv_tail)
            wt("vector", sS, act_par[0])
            wt("vector", sS, act_par[1])
            wt("vector", sP, pool_done[0])
            em("vector", lambda E: E.tensor_scalar(out=stch[:, 0, :], in0=pS1,
                                                   scalar1=1.0 / 1024.0, scalar2=None, op0=ALU.mult))
            em("vector", lambda E: E.tensor_scalar(out=app[:, 1, :], in0=pS2,
                                                   scalar1=1.0 / 1024.0, scalar2=None, op0=ALU.mult))
            psum_rd[0] = inc("vector", sV)
            if tt < 3:
                stats(tt + 1)
            vedge()
            em("vector", lambda E: E.tensor_tensor(out=app[:, 0, :], in0=stch[:, 0, :],
                                                   in1=stch[:, 0, :], op=ALU.mult))
            em("vector", lambda E: E.tensor_tensor(out=app[:, 1, :], in0=app[:, 1, :],
                                                   in1=app[:, 0, :], op=ALU.subtract))
            chv = inc("vector", sV)
            wt("scalar", sV, chv)
            em("scalar", lambda E: E.activation(out=app[:, 0, :], in_=app[:, 1, :],
                                                func=AF.Sqrt, bias=1e-5, scale=1.0))
            sqv = inc("scalar", sS)
            wt("vector", sS, sqv)
            em("vector", lambda E: E.reciprocal(out=stch[:, 1, :], in_=app[:, 0, :]))
            vedge()
            em("vector", lambda E: E.tensor_tensor(out=stch[:, 2, :], in0=stch[:, 0, :],
                                                   in1=stch[:, 1, :], op=ALU.mult))
            chain_v = inc("vector", sV)
            for fc in range(8):
                par = fc % 2
                s = src_fn(fc, tt)
                t1, t2 = pairs[par]
                if mod_wait and tt == 0 and fc == 0:
                    wt("scalar", sV, mod_wait)
                aeng, asem = "vector", sV
                wt(aeng, sS, act_par[par])
                em(aeng, lambda E, s=s, t1=t1: E.tensor_tensor(out=t1, in0=s,
                                                               in1=stch[:, 1, :], op=ALU.mult))
                em(aeng, lambda E, t1=t1, t2=t2: E.tensor_tensor(out=t2, in0=t1,
                                                                 in1=stch[:, 2, :],
                                                                 op=ALU.subtract))
                av = inc(aeng, asem)
                wt("scalar", asem, av)
                for hf in range(2):
                    b = 2 * tt + hf
                    em("scalar", lambda E, fc=fc, tt=tt, hf=hf, b=b, t2=t2:
                       E.activation(out=H[:, fc, tt * 512 + hf * 256: tt * 512 + hf * 256 + 256],
                                    in_=t2[:, hf * 256:hf * 256 + 256], func=AF.Identity,
                                    bias=modF[:, svec, fc, b:b + 1],
                                    scale=modF[:, cvec, fc, b:b + 1]))
                act_par[par] = inc("scalar", sS)
                h2d[0] = act_par[par]
            if post_fn is not None:
                post_fn(tt)

        stats(0)
        tails = [0, 0, 0, 0]
        tails[0] = mmv_par[1]
        for tt in range(4):
            chain_apply(tt, tails[tt])
            tails[tt + 1 if tt < 3 else 0] = mmv_par[1]
        return h2d[0]

    wt("vector", sQ, xld)
    h1_done = emit_ln(lambda fc, tt: X[:, fc, tt * 512:(tt + 1) * 512], 1, 0,
                      mod_wait=modA_ready, src_qwait_fn=lambda tt: xld)

    wk_cast = stage_cast(W_d["WkTp"], Wslot, dest_guard=(sT, sT.n))
    wt_pair("tensor", wk_cast)
    wt("tensor", sS, h1_done)

    Hr = [H[:, ic, :].rearrange("p (b t) -> p b t", t=256) for ic in range(8)]
    rope_free = {}
    for i3 in range(16):
        hp, bg = i3 // 2, i3 % 2
        bank = pA if i3 % 2 == 0 else pB
        wt("tensor", sV, rope_free.get(i3 - 2, 0))
        for ic in range(8):
            em("tensor", lambda E, ic=ic, hp=hp, bg=bg, bank=bank:
               E.matmul(bank.rearrange("p (b c) -> p b c", c=128),
                        Wslot[:, ic, hp, :], Hr[ic][:, bg * 4:(bg + 1) * 4, 0:128],
                        start=(ic == 0), stop=(ic == 7)))
        mv = inc("tensor", sT)
        wt("vector", sT, mv)
        bank3 = bank.rearrange("p (b c) -> p b c", c=128)
        em("vector", lambda E, bank=bank: E.stream_shuffle(
            out=ropeB.rearrange("p b c -> p (b c)"), in_=bank, mask=SWAP16))
        em("vector", lambda E, bank3=bank3: E.tensor_tensor(
            out=ropeA, in0=bank3, in1=cosF.unsqueeze(1).to_broadcast([128, 4, 128]), op=ALU.mult))
        em("vector", lambda E: E.tensor_tensor(
            out=ropeB, in0=ropeB, in1=sinF.unsqueeze(1).to_broadcast([128, 4, 128]), op=ALU.mult))
        em("vector", lambda E, hp=hp, bg=bg: E.tensor_tensor(
            out=kS[:, hp, bg * 4:(bg + 1) * 4, :], in0=ropeA, in1=ropeB, op=ALU.add))
        rope_free[i3] = inc("vector", sV)

    wv_cast = stage_cast(W_d["WvT"], Wslot, dest_guard=(sT, sT.n))
    wt_pair("tensor", wv_cast)
    vt_free = {}
    for i3 in range(16):
        hp, bg = i3 // 2, i3 % 2
        bank = pA if i3 % 2 == 0 else pB
        tbank = pE if i3 % 2 == 0 else pF
        wtv("tensor", vt_free.get(i3 - 2, (sV, 0)))
        for ic in range(8):
            em("tensor", lambda E, ic=ic, hp=hp, bg=bg, bank=bank:
               E.matmul(bank.rearrange("p (b c) -> p b c", c=128),
                        Wslot[:, ic, hp, :], Hr[ic][:, bg * 4:(bg + 1) * 4, 0:128],
                        start=(ic == 0), stop=(ic == 7)))
        mv = inc("tensor", sT)
        wt("vector", sT, mv)
        em("vector", lambda E, bank=bank:
           E.tensor_copy(out=vtmp.rearrange("p b c -> p (b c)"), in_=bank))
        cv = inc("vector", sV)
        wt("tensor", sV, cv)
        for br in range(4):
            em("tensor", lambda E, br=br, tbank=tbank:
               E.transpose(tbank[:, br * 64:(br + 1) * 64].bitcast(BF16), vtmp[:, br, :], identbf))
        tv = inc("tensor", sT)
        wt("vector", sT, tv)
        em("vector", lambda E, hp=hp, bg=bg, tbank=tbank:
           E.tensor_copy(out=vS[:, bg * 4:(bg + 1) * 4, hp, :],
                         in_=tbank[:, 0:256].bitcast(BF16).rearrange("p (b c) -> p b c", c=128)))
        vt_free[i3] = (sV, inc("vector", sV))
    vproj_done_sT = sT.n

    wff_cast = {}
    wff_pending = [(n, q) for n in ("uf", "ui", "uc", "wf") for q in range(4)]

    def wff_feed(k=2):
        for _ in range(k):
            if not wff_pending:
                return
            n, q = wff_pending.pop(0)
            g = (sS, h1_done) if (n == "uf" and q in (0, 1)) else None
            stage_cast_q(W_d[n + "T"], Wff[n], q, dest_guard=g)
            if q == 3:
                wff_cast[n] = (stage_cast.lastcast[0], stage_cast.lastcast[1])

    def _finish_trunc():
        em("gpsimd", lambda E, v=sV.n: E.wait_ge(sV.h, v))
        em("gpsimd", lambda E, v=sT.n: E.wait_ge(sT.h, v))
        em("gpsimd", lambda E, v=sS.n: E.wait_ge(sS.h, v))
        with nc.Block() as block:
            @block.tensor
            def _(E):
                for fn in ops["tensor"]:
                    fn(E)
            @block.vector
            def _(E):
                for fn in ops["vector"]:
                    fn(E)
            @block.scalar
            def _(E):
                for fn in ops["scalar"]:
                    fn(E)
            @block.sync
            def _(E):
                for fn in ops["sync"]:
                    fn(E)
            @block.gpsimd
            def _(E):
                for fn in ops["gpsimd"]:
                    fn(E)
        st.close()
        return nc

    if trunc == "P3":
        return _finish_trunc()

    wt("tensor", sV, qbd_ready)
    probs_free = [0, 0]
    pEF_free_t = (sV, 0)
    ev_vec = [0]
    pT_free = 0
    pS1_free = 0
    probs_written = 0
    P4B = int(os.environ.get("K_P4B", "8"))
    P4S = int(os.environ.get("K_P4S", "99"))
    for b in range(min(8, P4B)):
        if b >= 1:
            wff_feed(2)
        pbuf = probsb[b % 2]
        for ch in range(2):
            wt("tensor", sV, pS1_free)
            for hp in range(8):
                em("tensor", lambda E, hp=hp, b=b, ch=ch:
                   E.matmul(pS1[:, hp * 64:(hp + 1) * 64], qbd[:, hp, :],
                            kS[:, hp, b, ch * 64:(ch + 1) * 64],
                            start=(hp == 0), stop=(hp == 7)))
            lcv = inc("tensor", sT)
            if P4S < 2:
                continue
            wt("tensor", sV, pT_free)
            for hp in range(8):
                em("tensor", lambda E, hp=hp, b=b, ch=ch:
                   E.matmul(pT[0:64, hp * 128:(hp + 1) * 128],
                            kS[:, hp, b, ch * 64:(ch + 1) * 64], qbd[:, hp, :],
                            start=(hp in (0, 4)), stop=False))
            tTv = inc("tensor", sT)
            if P4S < 3:
                continue
            wt("vector", sT, lcv)
            wt("vector", sS, probs_written)
            pS13 = pS1.rearrange("p (h c) -> p h c", c=64)
            esc3 = esc[:, 0:512].rearrange("p (h c) -> p h c", c=64)
            em("vector", lambda E, pS13=pS13: E.reduce_max(out=Rsc[:, 0:8], in_=pS13, axis=AXL.X))
            em("vector", lambda E, pS13=pS13, esc3=esc3: E.tensor_tensor(
                out=esc3, in0=pS13,
                in1=Rsc[:, 0:8].unsqueeze(2).to_broadcast([128, 8, 64]), op=ALU.subtract))
            em("vector", lambda E, esc3=esc3: E.tensor_tensor(
                out=esc3, in0=esc3,
                in1=masklc.unsqueeze(1).to_broadcast([128, 8, 64]), op=ALU.add))
            pS1_free = inc("vector", sV)
            wt("scalar", sV, pS1_free)
            em("scalar", lambda E: E.activation(out=esc[:, 512:1024], in_=esc[:, 0:512],
                                                func=AF.Exp, bias=0.0, scale=0.125))
            ev = inc("scalar", sS)
            wt("vector", sS, ev)
            em("vector", lambda E: E.reduce_sum(
                out=Rsc[:, 8:16], in_=esc[:, 512:1024].rearrange("p (h c) -> p h c", c=64),
                axis=AXL.X))
            vedge()
            em("vector", lambda E, ch=ch: E.reciprocal(out=rdt[:, ch, :], in_=Rsc[:, 8:16]))
            rvv = inc("vector", sV)
            if P4S < 4:
                continue
            wt("tensor", sV, rvv)
            em("tensor", lambda E: E.transpose(pS2[0:8, 0:128], Rsc[:, 0:8], ident))
            trv = inc("tensor", sT)
            wt("vector", sT, trv)
            em("vector", lambda E: E.tensor_copy(out=RT8, in_=pS2[0:8, 0:128]))
            rt8v = inc("vector", sV)
            if P4S < 5:
                continue
            wt("tensor", sV, rt8v)
            for hp in range(8):
                em("tensor", lambda E, hp=hp:
                   E.matmul(pT[0:64, hp * 128:(hp + 1) * 128], negeye[:, hp, :], RT8,
                            start=False, stop=(hp in (3, 7))))
            rsub = inc("tensor", sT)
            wt("vector", sT, rsub)
            pT3 = pT[0:64, :].rearrange("p (h c) -> p h c", c=128)
            esc4 = esc[0:64, :].rearrange("p (h c) -> p h c", c=128)
            em("vector", lambda E, pT3=pT3, esc4=esc4: E.tensor_tensor(
                out=esc4, in0=pT3, in1=maskT.unsqueeze(1).to_broadcast([64, 8, 128]), op=ALU.add))
            pT_free = inc("vector", sV)
            mkv = pT_free
            wt("scalar", sV, mkv)
            wt("scalar", sT, probs_free[b % 2])
            em("scalar", lambda E, pbuf=pbuf, ch=ch:
               E.activation(out=pbuf[64 * ch:64 * ch + 64, :, :].rearrange("p h c -> p (h c)"),
                            in_=esc[0:64, :], func=AF.Exp, bias=0.0, scale=0.125))
            probs_written = inc("scalar", sS)
        if P4S < 6:
            continue
        wt("tensor", sS, probs_written)
        wtv("tensor", pEF_free_t)
        for ch in range(2):
            cbank = pE if ch == 0 else pF
            for hp in range(8):
                for ph in range(2):
                    em("tensor", lambda E, ch=ch, hp=hp, ph=ph, b=b, cbank=cbank:
                       E.matmul(cbank[64 * ph:64 * ph + 64, hp * 64:(hp + 1) * 64],
                                probsb[b % 2][64 * ch:64 * ch + 64, hp, ph * 64:ph * 64 + 64],
                                vS[64 * ch:64 * ch + 64, b, hp, ph * 64:ph * 64 + 64],
                                start=True, stop=True, skip_group_check=True))
        probs_free[b % 2] = inc("tensor", sT)
        if P4S < 7:
            continue
        wt("vector", sT, probs_free[b % 2])
        em("vector", lambda E: E.tensor_tensor(
            out=app[:, 0, :].rearrange("p (h d) -> p h d", d=64),
            in0=pE.rearrange("p (h d) -> p h d", d=64),
            in1=rdt[:, 0, :].unsqueeze(2).to_broadcast([128, 8, 64]), op=ALU.mult))
        em("vector", lambda E: E.tensor_tensor(
            out=app[:, 1, :].rearrange("p (h d) -> p h d", d=64),
            in0=pF.rearrange("p (h d) -> p h d", d=64),
            in1=rdt[:, 1, :].unsqueeze(2).to_broadcast([128, 8, 64]), op=ALU.mult))
        em("vector", lambda E, b=b:
           E.tensor_tensor(out=attnB[:, b, :, :].rearrange("p h d -> p (h d)"),
                           in0=app[:, 0, :], in1=app[:, 1, :], op=ALU.add))
        evv = inc("vector", sV)
        pEF_free_t = (sV, evv)
        ev_vec[0] = evv
        if b == 3:
            wt("gpsimd", sV, ev_vec[0])
            arw1 = dma("gpsimd", sG, arin1[:, :],
                       attnB.rearrange("p b h d -> p (b h d)")[:, 0:2048])
            wt("gpsimd", sG, arw1)
            em("gpsimd", lambda E: E.collective_compute(
                "AllReduce", ALU.add, replica_groups=[list(range(NCORES))],
                ins=[arin1.ap().opt()], outs=[arout1.ap().opt()]).then_inc(sC.h, 1))
            cc1 = sC.bump(1)

    wff_feed(len(wff_pending))

    if trunc == "P4":
        return _finish_trunc()

    wt("gpsimd", sV, ev_vec[0])
    arw2 = dma("gpsimd", sG, arin2[:, :],
               attnB.rearrange("p b h d -> p (b h d)")[:, 2048:4096])
    wt("gpsimd", sG, arw2)
    em("gpsimd", lambda E: E.collective_compute(
        "AllReduce", ALU.add, replica_groups=[list(range(NCORES))],
        ins=[arin2.ap().opt()], outs=[arout2.ap().opt()]).then_inc(sC.h, 1))
    cc2 = sC.bump(1)
    wt("gpsimd", sC, cc1)
    ar_r1 = dma("gpsimd", sG, attnB.rearrange("p b h d -> p (b h d)")[:, 0:2048],
                arout1[:, :])
    wt("gpsimd", sC, cc2)
    ar_r = dma("gpsimd", sG, attnB.rearrange("p b h d -> p (b h d)")[:, 2048:4096],
               arout2[:, :])

    if trunc == "CC":
        wt("gpsimd", sG, ar_r)
        return _finish_trunc()

    wg_cast = stage_cast(W_d["WgT"], Wslot, dest_guard=(sT, vproj_done_sT))
    wt_pair("tensor", wg_cast)
    g_free = [0, 0]
    eg_free_t = (sV, 0)
    egb = esc[:, 0:256].bitcast(BF16)
    for pc in range(8):
        for tt in range(4):
            it_ = pc * 4 + tt
            bank = pA if it_ % 2 == 0 else pB
            wt("tensor", sS, g_free[it_ % 2])
            for ic in range(8):
                em("tensor", lambda E, ic=ic, pc=pc, tt=tt, bank=bank:
                   E.matmul(bank, Wslot[:, ic, pc, :], H[:, ic, tt * 512:(tt + 1) * 512],
                            start=(ic == 0), stop=(ic == 7)))
            gv = inc("tensor", sT)
            wt("scalar", sT, gv)
            wtv("scalar", eg_free_t)
            em("scalar", lambda E, bank=bank:
               E.activation(out=egb, in_=bank, func=AF.Exp, bias=0.0, scale=0.125))
            g_free[it_ % 2] = inc("scalar", sS)
            wt("tensor", sS, g_free[it_ % 2])
            em("tensor", lambda E: E.matmul(pS2[0:2, 0:512], ind2, egb, start=True, stop=True))
            dnv = inc("tensor", sT)
            wt("vector", sT, dnv)
            em("vector", lambda E: E.reciprocal(out=app[0:2, 0, :], in_=pS2[0:2, 0:512]))
            vedge()
            em("vector", lambda E: E.tensor_copy(out=rden, in_=app[0:2, 0, :]))
            rdv = inc("vector", sV)
            wt("tensor", sV, rdv)
            wtv("tensor", eg_free_t)
            em("tensor", lambda E: E.matmul(pS1, ind2T, rden, start=True, stop=True))
            exv = inc("tensor", sT)
            wt("vector", sT, exv)
            em("vector", lambda E, pc=pc, tt=tt:
               E.tensor_tensor(out=GW[:, pc, tt * 512:(tt + 1) * 512], in0=egb, in1=pS1,
                               op=ALU.mult))
            eg_free_t = (sV, inc("vector", sV))

    if trunc == "P5":
        return _finish_trunc()

    wp_cast = stage_cast(W_d["WpT"], Wslot, dest_guard=(sT, sT.n))



    wt("tensor", sG, ar_r1)
    wtv("tensor", eg_free_t)
    w_free = [(sV, 0), (sS, 0)]
    for b in range(8):
        if b == 4:
            wt("tensor", sG, ar_r)
        for hp in range(8):
            it_ = hp * 8 + b
            bank = pA if it_ % 2 == 0 else pB
            wtv("tensor", w_free[it_ % 2])
            for ph in range(2):
                em("tensor", lambda E, hp=hp, b=b, ph=ph, bank=bank:
                   E.matmul(bank[64 * ph:64 * ph + 64, 0:256],
                            attnB[64 * ph:64 * ph + 64, b, hp, :],
                            GW[64 * ph:64 * ph + 64, hp, b * 256:(b + 1) * 256],
                            start=True, stop=True, skip_group_check=True))
            wv = inc("tensor", sT)
            if it_ % 2 == 0:
                wt("vector", sT, wv)
                em("vector", lambda E, hp=hp, b=b, bank=bank:
                   E.tensor_copy(out=GW[:, hp, b * 256:(b + 1) * 256], in_=bank[:, 0:256]))
                w_free[0] = (sV, inc("vector", sV))
            else:
                wt("scalar", sT, wv)
                em("scalar", lambda E, hp=hp, b=b, bank=bank:
                   E.activation(out=GW[:, hp, b * 256:(b + 1) * 256], in_=bank[:, 0:256],
                                func=AF.Identity))
                w_free[1] = (sS, inc("scalar", sS))

    if trunc == "P6":
        return _finish_trunc()

    wt_pair("tensor", wp_cast)
    wtv("tensor", w_free[0])
    wtv("tensor", w_free[1])
    wt("vector", sV, modB_ready)
    x_r = x_d.rearrange("(f p) b t -> p f b t", p=128)
    x2_r = x2_d.rearrange("(f p) b t -> p f b t", p=128)
    p7_free = [0, 0]
    x2b_free = [0, 0]
    xrd_v = [0, 0]
    xrd_used = [0, 0]
    wt("gpsimd", sT, sT.n)
    for oc in range(8):
        wt("gpsimd", sV, xrd_used[oc % 2])
        xrd_v[oc % 2] = dma("gpsimd", sG, xrd[oc % 2].rearrange("p b t -> p (b t)"),
                            x_r[:, oc, :, :].rearrange("p b t -> p (b t)"))
        for tt in range(4):
            it_ = oc * 4 + tt
            bank = pA if it_ % 2 == 0 else pB
            wt("tensor", sV, p7_free[it_ % 2])
            for ic in range(8):
                em("tensor", lambda E, ic=ic, oc=oc, tt=tt, bank=bank:
                   E.matmul(bank, Wslot[:, ic, oc, :], GW[:, ic, tt * 512:(tt + 1) * 512],
                            start=(ic == 0), stop=(ic == 7)))
            pv = inc("tensor", sT)
            wt("vector", sT, pv)
            wt("vector", sG, xrd_v[oc % 2])
            wt("vector", sG, x2b_free[tt % 2])
            for hf in range(2):
                b = tt * 2 + hf
                em("vector", lambda E, oc=oc, tt=tt, hf=hf, b=b, bank=bank:
                   E.scalar_tensor_tensor(
                       out=x2b[tt % 2][:, hf, :],
                       in0=bank[:, hf * 256:hf * 256 + 256],
                       scalar=modF[:, 2, oc, b:b + 1],
                       in1=xrd[oc % 2][:, b, :],
                       op0=ALU.mult, op1=ALU.add))
            p7_free[it_ % 2] = inc("vector", sV)
            xrd_used[oc % 2] = p7_free[it_ % 2]
            wt("gpsimd", sV, p7_free[it_ % 2])
            x2b_free[tt % 2] = dma(
                "gpsimd", sG,
                x2_r[:, oc, 2 * tt:2 * tt + 2, :].rearrange("p b t -> p (b t)"),
                x2b[tt % 2].rearrange("p a t -> p (a t)"))
    x2_written = sG.n
    p7_done_sT = sT.n
    for n in ("wi", "wc"):
        wff_cast[n] = direct_weight(n, Wff[n], dest_guard=(sT, p7_done_sT) if n == "wi" else None)
    if trunc == "P7":
        return _finish_trunc()

    x2s_free = [(0, 0), (0, 0)]
    if trunc == "P7b":
        return _finish_trunc()

    x2bufs = [x2stg, x2stgB]
    p8_lv = [0, 0]

    def p8_load(tt):
        wt("sync", sG, x2_written)
        if tt % 2 == 0:
            wt("sync", sV, stage_cast.lastcast[0])
            wt("sync", sS, stage_cast.lastcast[1])
        wt("sync", sV, x2s_free[tt % 2][0])
        wt("sync", sP, x2s_free[tt % 2][1])
        return dma("sync", sQ, x2bufs[tt % 2], x2_r[:, :, 2 * tt:2 * tt + 2, :])

    def p8_prep(tt):
        p8_lv[tt % 2] = p8_load(tt)
        wt("vector", sQ, p8_lv[tt % 2])

    def p8_post(tt):
        x2s_free[tt % 2] = (sV.n, sP.n)

    h2_done = emit_ln(lambda fc, tt: x2bufs[tt % 2][:, fc, :, :].rearrange("p a t -> p (a t)"),
                      4, 3, prep_fn=p8_prep, post_fn=p8_post,
                      src_qwait_fn=lambda tt: p8_lv[tt % 2])

    wt("sync", sS, h2_done)
    c0v = dma("sync", sQ, cS.rearrange("p b t -> p (b t)"), inc0_d.rearrange("p b t -> p (b t)"))
    wt("vector", sQ, c0v)
    em("vector", lambda E: E.tensor_copy(out=cbf[0].rearrange("p b t -> p (b t)"),
                                         in_=cS.rearrange("p b t -> p (b t)")))
    c0_ready = inc("vector", sV)

    wt("tensor", sV, c0_ready)
    wt("tensor", sS, h2_done)

    GATES = (("f", AF.Sigmoid, 0), ("i", AF.Sigmoid, 1), ("c", AF.Tanh, 2))
    gate_waited = set()
    bank_free = {}
    step_done = 0
    for t in range(8):
        cbf_r = cbf[t % 2]
        cbf_w = cbf[(t + 1) % 2]
        wt("tensor", sV, step_done)
        for oc in range(8):
            par = oc % 2
            banks = {"f": (pA if par == 0 else pB),
                     "i": (pE if par == 0 else pF),
                     "c": (pT[:, 0:256] if par == 0 else pT[:, 512:768])}
            gv = {}
            for gname, gfunc, gidx in GATES:
                if gname not in gate_waited:
                    gate_waited.add(gname)
                    wt_pair("tensor", wff_cast["w" + gname])
                    wt_pair("tensor", wff_cast["u" + gname])
                wt("tensor", sV, bank_free.get((gname, par), 0))
                wname, uname = "w" + gname, "u" + gname
                for ic in range(8):
                    em("tensor", lambda E, ic=ic, oc=oc, t=t, gn=gname, bank=banks[gname]:
                       E.matmul(bank[:, 0:256] if bank.shape[1] > 256 else bank,
                                Wff["w" + gn][:, ic, oc, :], H[:, ic, t * 256:(t + 1) * 256],
                                start=(ic == 0), stop=False))
                for ic in range(8):
                    em("tensor", lambda E, ic=ic, oc=oc, gn=gname, bank=banks[gname], cb=cbf_r:
                       E.matmul(bank[:, 0:256] if bank.shape[1] > 256 else bank,
                                Wff["u" + gn][:, ic, oc, :], cb[:, ic, :],
                                start=False, stop=(ic == 7)))
                gv[gname] = inc("tensor", sT)
            dst = {"f": gscF, "i": gscI, "c": gscG}
            for gname, gfunc, gidx in GATES:
                wt("scalar", sT, gv[gname])
                em("scalar", lambda E, gn=gname, gf=gfunc, gi=gidx, oc=oc, par=par,
                   bank=banks[gname]:
                   E.activation(out=dst[gn][:, par, :],
                                in_=bank[:, 0:256] if bank.shape[1] > 256 else bank,
                                func=gf, bias=fbias[:, gi, oc:oc + 1], scale=1.0))
            actv = inc("scalar", sS)
            wt("vector", sS, actv)
            em("vector", lambda E, oc=oc, par=par: E.tensor_tensor(
                out=cS[:, oc, :], in0=gscF[:, par, :], in1=cS[:, oc, :], op=ALU.mult))
            em("vector", lambda E, par=par: E.tensor_tensor(
                out=gscI[:, par, :], in0=gscI[:, par, :], in1=gscG[:, par, :], op=ALU.mult))
            em("vector", lambda E, oc=oc, par=par: E.tensor_tensor(
                out=cS[:, oc, :], in0=cS[:, oc, :], in1=gscI[:, par, :], op=ALU.add))
            em("vector", lambda E, oc=oc, cw=cbf_w: E.tensor_copy(
                out=cw[:, oc, :], in_=cS[:, oc, :]))
            bv = inc("vector", sV)
            for gname, _, _ in GATES:
                bank_free[(gname, par)] = bv
        step_done = sV.n

    wt("sync", sS, h2_done)
    wov = stage_cast(W_d["woT"], Wff["wo"], dest_guard=(sT, p7_done_sT))
    uov = stage_cast(W_d["uoT"], Wff["uo"], dest_guard=(sT, sT.n))

    wt_pair("tensor", uov)
    wt("tensor", sS, sS.n)
    cbf_fin = cbf[0]
    for oc in range(8):
        bank = pE if oc % 2 == 0 else pF
        for ic in range(8):
            em("tensor", lambda E, ic=ic, oc=oc, bank=bank:
               E.matmul(bank[:, 0:256], Wff["uo"][:, ic, oc, :], cbf_fin[:, ic, :],
                        start=(ic == 0), stop=(ic == 7)))
        uv = inc("tensor", sT)
        wt("scalar", sT, uv)
        em("scalar", lambda E, oc=oc, bank=bank:
           E.activation(out=uodv[:, oc, :], in_=bank[:, 0:256], func=AF.Identity,
                        bias=fbias[:, 3, oc:oc + 1], scale=1.0))
    uod_done = inc("scalar", sS)

    wt_pair("tensor", wov)
    wt("tensor", sS, uod_done)
    xt_v = [0, 0]
    xt_used = [0, 0]
    tail_free = [0, 0]
    ybuf_free = [0, 0]
    yv_last = 0
    for oc in range(8):
        wt("sync", sV, max(xt_used[oc % 2], uov[0]))
        wt("sync", sS, uov[1])
        xt_v[oc % 2] = dma("sync", sQ, xt[oc % 2].rearrange("p b t -> p (b t)"),
                           x2_r[:, oc, :, :].rearrange("p b t -> p (b t)"))
        for t in range(8):
            par = (oc * 8 + t) % 2
            bank = pA if par == 0 else pB
            wt("tensor", sV, tail_free[par])
            for ic in range(8):
                em("tensor", lambda E, ic=ic, oc=oc, t=t, bank=bank:
                   E.matmul(bank[:, 0:256], Wff["wo"][:, ic, oc, :],
                            H[:, ic, t * 256:(t + 1) * 256],
                            start=(ic == 0), stop=(ic == 7)))
            xv = inc("tensor", sT)
            wt("vector", sT, xv)
            em("vector", lambda E, oc=oc, bank=bank, par=par: E.tensor_tensor(
                out=app[:, par, 0:256], in0=bank[:, 0:256], in1=uodv[:, oc, :], op=ALU.add))
            s1v = inc("vector", sV)
            wt("scalar", sV, s1v)
            em("scalar", lambda E, par=par: E.activation(
                out=app[:, par, 256:512], in_=app[:, par, 0:256], func=AF.Tanh))
            s2v = inc("scalar", sS)
            ybuf = ybufA if t < 4 else ybufB
            yb = 0 if t < 4 else 1
            wt("vector", sS, s2v)
            wt("vector", sQ, xt_v[oc % 2])
            if t in (0, 4):
                wt("vector", sO, ybuf_free[yb])
            em("vector", lambda E, oc=oc, t=t, par=par, ybuf=ybuf: E.scalar_tensor_tensor(
                out=ybuf[:, t % 4, :], in0=app[:, par, 256:512],
                scalar=modF[:, 5, oc, t:t + 1], in1=xt[oc % 2][:, t, :],
                op0=ALU.mult, op1=ALU.add))
            tail_free[par] = inc("vector", sV)
            xt_used[oc % 2] = tail_free[par]
            if t in (3, 7):
                wt("gpsimd", sV, tail_free[par])
                y_r = y_d.rearrange("(f p) b t -> p f b t", p=128)
                tb = t // 4
                ybuf_free[yb] = dma("gpsimd", sO,
                                    y_r[:, oc, 4 * tb:4 * tb + 4, :].rearrange("p b t -> p (b t)"),
                                    ybuf.rearrange("p a t -> p (a t)"))
                yv_last = ybuf_free[yb]

    if DBG:
        wt("gpsimd", sV, sV.n)
        wt("gpsimd", sS, sS.n)
        wt("gpsimd", sT, sT.n)
        dma("gpsimd", sO, dbg_d["h2"].bitcast(BF16), H.rearrange("p a b -> p (a b)"))
        dma("gpsimd", sO, dbg_d["x2"], x2_d)
        dma("gpsimd", sO, dbg_d["cfin"], cS.rearrange("p a b -> p (a b)"))
        dma("gpsimd", sO, dbg_d["uod"], uodv.rearrange("p a b -> p (a b)"))
    em("gpsimd", lambda E, v=sO.n: E.wait_ge(sO.h, v))

    with nc.Block() as block:
        @block.tensor
        def _(E):
            for fn in ops["tensor"]:
                fn(E)

        @block.vector
        def _(E):
            for fn in ops["vector"]:
                fn(E)

        @block.scalar
        def _(E):
            for fn in ops["scalar"]:
                fn(E)

        @block.sync
        def _(E):
            for fn in ops["sync"]:
                fn(E)

        @block.gpsimd
        def _(E):
            for fn in ops["gpsimd"]:
                fn(E)

    st.close()
    return nc


DBG_SRC = {}

_prog_cache = {}


def kernel(**inputs):
    shared = _shared_layouts(inputs)
    nc = build_program()
    in_maps = []
    for core in range(NCORES):
        m = dict(shared)
        m.update(_host_constants(core))
        m["x_fm"] = _core_x(inputs, core)
        dm = {"x_fm": m["x_fm"], "wmod": m["wmod"], "cT": m["cT"], "qbd": m["qbd"],
              "ident": m["ident"], "ident_bf": m["ident_bf"], "ones_bf": m["ones_bf"],
              "negeye": m["negeye"], "ind2": m["ind2"], "ind2T": m["ind2T"],
              "mask_lc": m["mask_lc"], "mask_T": m["mask_T"],
              "cosF": m["cosF"], "sinF": m["sinF"], "in_c0": m["in_c0"]}
        for n in ("WkTp", "WvT", "WgT", "WpT", "wfT", "ufT", "wiT", "uiT", "wcT", "ucT",
                  "woT", "uoT"):
            dm[n] = m[n]
        for n in ("bf", "bi", "bc", "bo"):
            dm[n + "r"] = m[n + "r"]
        in_maps.append(dm)
    res = run_bass_kernel_spmd(nc, in_maps, core_ids=list(range(NCORES)))
    return _assemble_output([res.results[i]["y_fm"] for i in range(NCORES)])


# version 19
# speedup vs baseline: 1.0586x; 1.0078x over previous
import os
import numpy as np
import ml_dtypes
from contextlib import ExitStack

import concourse.bass as bass
import concourse.mybir as mybir
from concourse.bass_utils import run_bass_kernel_spmd

F32 = mybir.dt.float32
BF16 = mybir.dt.bfloat16
U16 = mybir.dt.uint16
AF = mybir.ActivationFunctionType
ALU = mybir.AluOpType
AXL = mybir.AxisListType

NCORES = 8
B = 8
TL = 256
NEG = -1.0e30
SWAP16 = list(range(16, 32)) + list(range(16))

_PERM = np.concatenate([np.arange(0, 64), np.arange(128, 192),
                        np.arange(64, 128), np.arange(192, 256)])
_IPERM = np.argsort(_PERM)


def _dperm():
    p = np.arange(64)
    q16 = p // 16
    j = (q16 // 2) * 16 + p % 16
    return 2 * j + q16 % 2


_DPERM = _dperm()


def _kperm():
    out = np.zeros(1024, np.int64)
    for hp in range(8):
        for p in range(128):
            out[hp * 128 + p] = (2 * hp + p // 64) * 64 + _DPERM[p % 64]
    return out


_KPERM = _kperm()


def _bf(a):
    return np.ascontiguousarray(a.astype(ml_dtypes.bfloat16).view(np.uint16))


def _host_constants(core_id):
    c = {}
    c["ident"] = np.eye(128, dtype=np.float32)
    c["ident_bf"] = _bf(np.eye(128, dtype=np.float32))
    c["ones_bf"] = _bf(np.ones((128, 128), np.float32))
    ne = np.zeros((8, 8, 64), np.float32)
    for k in range(8):
        ne[k, k, :] = -1.0
    c["negeye"] = _bf(ne)
    ind2 = np.zeros((128, 2), np.float32)
    ind2[:64, 0] = 1.0
    ind2[64:, 1] = 1.0
    c["ind2"] = _bf(ind2)
    c["ind2T"] = _bf(ind2.T.copy())
    l_of_p = np.arange(128) % 64
    cc = np.arange(64)
    c["mask_lc"] = np.where(cc[None, :] <= l_of_p[:, None], 0.0, NEG).astype(np.float32)
    c["mask_T"] = np.where(cc[:, None] <= l_of_p[None, :], 0.0, NEG).astype(np.float32)
    p = np.arange(128) % 64
    q16 = p // 16
    j = (q16 // 2) * 16 + p % 16
    is_b = q16 % 2
    inv = 1.0 / 10000.0 ** (np.arange(0, 64, 2, dtype=np.float32) / 64.0)
    c2 = np.arange(128)
    pos = core_id * 256 + (c2 // 64) * 128 + c2 % 64
    ang = pos[None, :].astype(np.float32) * inv[j][:, None]
    c["cosF"] = np.cos(ang).astype(np.float32)
    sinF = np.sin(ang).astype(np.float32)
    c["sinF"] = np.where(is_b[:, None] == 0, -sinF, sinF).astype(np.float32)
    return c


def _shared_layouts(inputs):
    g = lambda n: np.asarray(inputs[n], dtype=np.float32)
    s = {}
    s["WkTp"] = np.ascontiguousarray(g("L0_attn_W_k").T[:, _KPERM])
    s["WvT"] = np.ascontiguousarray(g("L0_attn_W_v").T)
    s["WgT"] = np.ascontiguousarray(g("L0_attn_W_g").T)
    s["WpT"] = np.ascontiguousarray(g("L0_attn_W_p").T)
    for n in ("wf", "uf", "wi", "ui", "wc", "uc", "wo", "uo"):
        s[n + "T"] = np.ascontiguousarray(g("L0_ffn_" + n).T)
    s["wmod"] = np.ascontiguousarray(g("L0_wmod"))
    s["cT"] = np.ascontiguousarray(g("L0_c").T)
    lq = g("L0_attn_latent_queries")[0]
    qbd = np.zeros((8, 128, 128), np.float32)
    for hp in range(8):
        for ph in range(2):
            blk = lq[:, 2 * hp + ph, :][:, _DPERM]
            qbd[hp, 64 * ph:64 * ph + 64, 64 * ph:64 * ph + 64] = blk.T
    s["qbd"] = np.ascontiguousarray(qbd)
    for n in ("bf", "bi", "bc", "bo"):
        s[n + "r"] = np.ascontiguousarray(g("L0_ffn_" + n).reshape(8, 128).T)
    ic = g("L0_ffn_in_c").reshape(8, 128).T
    s["in_c0"] = np.ascontiguousarray(np.broadcast_to(ic[:, :, None], (128, 8, 256)))
    return s


def _core_x(inputs, core_id):
    x = np.asarray(inputs["L0_attn_x"], dtype=np.float32)
    xs = x[:, core_id * 256:(core_id + 1) * 256, :][:, _PERM, :]
    return np.ascontiguousarray(xs.transpose(2, 0, 1))


def _assemble_output(shards):
    y = np.empty((8, 2048, 1024), np.float32)
    for i, sh in enumerate(shards):
        ys = np.asarray(sh).reshape(1024, 8, 256).transpose(1, 2, 0)
        y[:, i * 256:(i + 1) * 256, :] = ys[:, _IPERM, :]
    return y


class Sem:
    def __init__(self, h):
        self.h = h
        self.n = 0

    def bump(self, k=1):
        self.n += k
        return self.n


def _view(arena, off, nslots, dtype, shape):
    ap = arena[:, off:off + nslots]
    if dtype == BF16:
        ap = ap.bitcast(BF16)
    total = 1
    for s in shape:
        total *= s
    assert total == ap.shape[1], (shape, ap.shape)
    if len(shape) > 1:
        pat = " ".join(f"d{i}" for i in range(len(shape)))
        ap = ap.rearrange(f"p ({pat}) -> p {pat}",
                          **{f"d{i}": shape[i] for i in range(len(shape))})
    return ap


FFN_W = ("uf", "ui", "uc", "wf", "wi", "wc", "wo", "uo")


def build_program(debug=(), trunc=None):
    nc = bass.Bass("TRN2", target_bir_lowering=False, detect_race_conditions=False)
    DI = lambda name, shape, dt=F32: nc.dram_tensor(name, shape, dt, kind="ExternalInput").ap()

    x_d = DI("x_fm", [1024, B, TL])
    W_d = {n: DI(n, [1024, 1024]) for n in
           ("WkTp", "WvT", "WgT", "WpT", "wfT", "ufT", "wiT", "uiT", "wcT", "ucT", "woT", "uoT")}
    wmod_d = DI("wmod", [1024, 6144])
    cT_d = DI("cT", [1024, 8])
    qbd_d = DI("qbd", [8, 128, 128])
    ident_d = DI("ident", [128, 128])
    identbf_d = DI("ident_bf", [128, 128], U16)
    onesbf_d = DI("ones_bf", [128, 128], U16)
    negeye_d = DI("negeye", [8, 8, 64], U16)
    ind2_d = DI("ind2", [128, 2], U16)
    ind2T_d = DI("ind2T", [2, 128], U16)
    masklc_d = DI("mask_lc", [128, 64])
    maskT_d = DI("mask_T", [64, 128])
    cosF_d = DI("cosF", [128, 128])
    sinF_d = DI("sinF", [128, 128])
    bias_d = {n: DI(n + "r", [128, 8]) for n in ("bf", "bi", "bc", "bo")}
    inc0_d = DI("in_c0", [128, 8, 256])

    y_d = nc.dram_tensor("y_fm", [1024, B, TL], F32, kind="ExternalOutput").ap()
    x2_d = nc.dram_tensor("x2_scr", [1024, B, TL], BF16).ap()
    arin1 = nc.dram_tensor("ar_in1", [128, 2048], BF16)
    arout1 = nc.dram_tensor("ar_out1", [128, 2048], BF16)
    arin2 = nc.dram_tensor("ar_in2", [128, 2048], BF16)
    arout2 = nc.dram_tensor("ar_out2", [128, 2048], BF16)
    arwi = nc.dram_tensor("ar_wi", [128, 64], BF16)
    arwo = nc.dram_tensor("ar_wo", [128, 64], BF16)

    DBG = os.environ.get("K_DBG", "") == "1"
    dbg_d = {}
    if DBG:
        dbg_d["gates"] = nc.dram_tensor("dbg_gates", [128, 16384], U16, kind="ExternalOutput").ap()
        dbg_d["attnB"] = nc.dram_tensor("dbg_attnB", [128, 4096], U16, kind="ExternalOutput").ap()
        dbg_d["attnP"] = nc.dram_tensor("dbg_attnP", [128, 4096], U16, kind="ExternalOutput").ap()
        dbg_d["h2"] = nc.dram_tensor("dbg_h2", [128, 16384], U16, kind="ExternalOutput").ap()
        dbg_d["x2"] = nc.dram_tensor("dbg_x2", [1024, B, TL], F32, kind="ExternalOutput").ap()
        dbg_d["cfin"] = nc.dram_tensor("dbg_cfin", [128, 2048], F32, kind="ExternalOutput").ap()
        dbg_d["uod"] = nc.dram_tensor("dbg_uod", [128, 2048], F32, kind="ExternalOutput").ap()

    st = ExitStack()
    sbuf = lambda name, shape, dt=F32: st.enter_context(nc.sbuf_tensor(name, shape, dt))[:]
    psum = lambda name, shape: st.enter_context(nc.psum_tensor(name, shape, F32))[:]

    arena = st.enter_context(nc.sbuf_tensor("arena", [128, 46080], F32))[:]

    ident = sbuf("identt", [128, 128])
    identbf = sbuf("identbft", [128, 128], BF16)
    onesbf = sbuf("onesbft", [128, 128], BF16)
    qbd = sbuf("qbdt", [128, 8, 128], BF16)
    negeye = sbuf("negeyet", [8, 8, 64], BF16)
    ind2 = sbuf("ind2t", [128, 2], BF16)
    ind2T = sbuf("ind2Tt", [2, 128], BF16)
    masklc = sbuf("masklct", [128, 64])
    maskT = sbuf("maskTt", [64, 128])
    cosF = sbuf("cosFt", [128, 128])
    sinF = sbuf("sinFt", [128, 128])
    modF = sbuf("modFt", [128, 6, 8, 8])
    cTt = sbuf("cTt", [128, 8, 8])
    silubf = sbuf("silubft", [128, 8, 8], BF16)
    Rsc = sbuf("Rsct", [128, 32])
    rdt = sbuf("rdtt", [128, 2, 8])
    RT8 = sbuf("RT8t", [8, 128], BF16)
    fbias = sbuf("fbiast", [128, 4, 8])
    rden = sbuf("rdent", [2, 512], BF16)
    consts = sbuf("constst", [128, 4])
    stch = sbuf("stcht", [128, 3, 512])
    app = sbuf("appt", [128, 2, 512])
    esc = sbuf("esct", [128, 1024])
    bfx = sbuf("bfxt", [128, 2, 512], BF16)
    bfsq = sbuf("bfsqt", [128, 2, 512], BF16)

    nc.const_aps.register(F32, 0.125, consts[:, 0:1])
    nc.const_aps.register(F32, 8.0, consts[:, 1:2])
    nc.const_aps.register(F32, 1e-5, consts[:, 2:3])
    nc.const_aps.register(F32, 1.0 / 1024.0, consts[:, 3:4])

    pA = psum("pA", [128, 512])
    pB = psum("pB", [128, 512])
    pE = psum("pE", [128, 512])
    pF = psum("pF", [128, 512])
    pT = psum("pT", [128, 1024])
    pS1 = psum("pS1", [128, 512])
    pS2 = psum("pS2", [128, 512])

    V = lambda off, n, dt, shape: _view(arena, off, n, dt, shape)
    H = V(0, 8192, BF16, (8, 2048))
    X = V(8192, 16384, F32, (8, 2048))
    Wff = {"uf": V(8192, 4096, BF16, (8, 8, 128)),
           "ui": V(12288, 4096, BF16, (8, 8, 128)),
           "uc": V(16384, 4096, BF16, (8, 8, 128)),
           "wf": V(20480, 4096, BF16, (8, 8, 128)),
           "wi": V(24576, 4096, BF16, (8, 8, 128)),
           "wc": V(28672, 4096, BF16, (8, 8, 128)),
           "wo": V(37888, 4096, BF16, (8, 8, 128)),
           "uo": V(12288, 4096, BF16, (8, 8, 128))}
    uodv = V(8192, 2048, F32, (8, 256))
    modtm = V(24576, 6144, F32, (6144,))
    kS = V(24576, 4096, BF16, (8, 8, 128))
    vS = V(28672, 4096, BF16, (8, 8, 128))
    GW = V(24576, 8192, BF16, (8, 2048))
    qbdf = V(32768, 1024, F32, (8, 128))
    ropeA = V(32768, 512, F32, (4, 128))
    ropeB = V(33280, 512, F32, (4, 128))
    vtmp = V(33792, 256, BF16, (4, 128))
    attnB = V(32768, 2048, BF16, (8, 8, 64))
    xrd = [V(32768, 2048, F32, (8, 256)), V(34816, 2048, F32, (8, 256))]
    cS = V(32768, 2048, F32, (8, 256))
    cbf = [V(34816, 1024, BF16, (8, 256)), V(35840, 1024, BF16, (8, 256))]
    probsb = [V(36864, 512, BF16, (8, 128)), V(37376, 512, BF16, (8, 128))]
    wbounce = [V(36864, 1024, BF16, (2048,)), V(36864, 1024, BF16, (2048,))]

    ptmp7 = V(36864, 1024, F32, (4, 256))
    gscG = V(36864, 512, F32, (2, 256))
    ybufB = V(36864, 1024, F32, (4, 256))
    Wslot = V(37888, 4096, BF16, (8, 8, 128))
    wmodbf = V(37888, 2048, BF16, (2, 8, 256))
    stagh = [V(41984, 2048, F32, (2048,)), V(44032, 2048, F32, (2048,))]
    x2stg = V(41984, 2048, BF16, (8, 2, 256))
    x2stgB = V(32768, 2048, BF16, (8, 2, 256))
    xt = [V(41984, 1024, BF16, (8, 256)), V(44032, 1024, BF16, (8, 256))]
    gscF = esc[:, 0:512].rearrange("p (a t) -> p a t", t=256)
    gscI = esc[:, 512:1024].rearrange("p (a t) -> p a t", t=256)
    x2b = [esc[:, 0:256].bitcast(BF16).rearrange("p (a t) -> p a t", t=256),
           esc[:, 256:512].bitcast(BF16).rearrange("p (a t) -> p a t", t=256)]
    ybufA = esc.rearrange("p (a t) -> p a t", t=256)

    ops = {e: [] for e in ("tensor", "vector", "scalar", "sync", "gpsimd")}
    em = lambda e, fn: ops[e].append(fn)

    sems = {}
    for name in ("sQ", "sQ2", "sG", "sT", "sV", "sS", "sC", "sO", "sP"):
        sems[name] = Sem(st.enter_context(nc.semaphore(name)))
    sQ, sQ2, sG, sT, sV, sS, sC, sO, sP = (sems[k] for k in
                                           ("sQ", "sQ2", "sG", "sT", "sV", "sS", "sC",
                                            "sO", "sP"))

    def dma(eng, sem, out, in_):
        v = sem.bump(16)
        em(eng, lambda E, o=out, i=in_, s=sem.h: E.dma_start(out=o, in_=i).then_inc(s, 16))
        return v

    def wt(eng, sem, v):
        if v:
            em(eng, lambda E, s=sem.h, vv=v: E.wait_ge(s, vv))

    def vedge():
        em("vector", lambda E: E.engine_nop())
        v = inc("vector", sV)
        wt("vector", sV, v)

    def wt_pair(eng, pair):
        wt(eng, sV, pair[0])
        wt(eng, sS, pair[1])

    def wtv(eng, sv):
        if sv[1]:
            wt(eng, sv[0], sv[1])

    def inc(eng, sem, k=1):
        v = sem.bump(k)
        last = ops[eng][-1]

        def wrapped(E, f=last, s=sem.h, kk=k):
            ins = f(E)
            assert ins is not None, "inc() must follow an instruction-emitting op"
            return ins.then_inc(s, kk)
        ops[eng][-1] = wrapped
        return v

    xld = dma("sync", sQ, X.rearrange("p f (b t) -> p f b t", t=256),
              x_d.rearrange("(f p) b t -> p f b t", p=128))
    for out, d in (
            (ident, ident_d), (identbf, identbf_d.bitcast(BF16)), (onesbf, onesbf_d.bitcast(BF16)),
            (qbdf, qbd_d.rearrange("h a b -> a h b")),
            (negeye.rearrange("k h c -> k (h c)"),
             negeye_d.rearrange("k h c -> k (h c)").bitcast(BF16)),
            (ind2, ind2_d.bitcast(BF16)), (ind2T, ind2T_d.bitcast(BF16)),
            (masklc, masklc_d), (maskT, maskT_d), (cosF, cosF_d), (sinF, sinF_d),
            (cTt, cT_d.rearrange("(i p) b -> p i b", p=128)),
            (fbias[:, 0, :], bias_d["bf"]), (fbias[:, 1, :], bias_d["bi"]),
            (fbias[:, 2, :], bias_d["bc"]), (fbias[:, 3, :], bias_d["bo"])):
        dma("gpsimd", sG, out, d)
    cver = sG.n

    wt("gpsimd", sG, cver)
    wdma = dma("gpsimd", sG, arwi[:, :], identbf[:, 0:64])
    wt("gpsimd", sG, wdma)
    em("gpsimd", lambda E: E.collective_compute(
        "AllReduce", ALU.add, replica_groups=[list(range(NCORES))],
        ins=[arwi.ap().opt()], outs=[arwo.ap().opt()]).then_inc(sC.h, 1))
    sC.bump(1)

    def _initc(E):
        E.memset(consts[:, 0:1], 0.125)
        E.memset(consts[:, 1:2], 8.0)
        E.memset(consts[:, 2:3], 1e-5)
        E.memset(consts[:, 3:4], 1.0 / 1024.0)
    em("vector", _initc)
    wt("vector", sG, cver)
    em("vector", lambda E: E.tensor_copy(out=qbd.rearrange("p h c -> p (h c)"),
                                         in_=qbdf.rearrange("p h c -> p (h c)")))
    qbd_ready = inc("vector", sV)

    wt("scalar", sG, cver)
    em("scalar", lambda E: E.activation(out=silubf.rearrange("p i b -> p (i b)"),
                                        in_=cTt.rearrange("p i b -> p (i b)"), func=AF.Silu))
    silu_ready = inc("scalar", sS)
    wt("tensor", sS, silu_ready)

    wmod_r = wmod_d.rearrange("(i p) j -> p i j", p=128)
    wmod_mm_done = {}
    wmod_cast_done = {}
    mod_tr_last = [0]
    CENG = {0: ("vector", sV), 1: ("scalar", sS)}
    for jt in range(24):
        half = jt % 2
        ceng, csem = CENG[half]
        qeng, qsem = (("sync", sQ) if half == 0 else ("scalar", sQ2))
        if jt >= 2:
            wt(qeng, csem, wmod_cast_done[jt - 2])
        lv = dma(qeng, qsem, stagh[half].rearrange("p (i j) -> p i j", j=256),
                 wmod_r[:, :, jt * 256:(jt + 1) * 256])
        wt(ceng, qsem, lv)
        if jt >= 2:
            wt(ceng, sT, wmod_mm_done[jt - 2])
        dst = wmodbf[:, half, :, :].rearrange("p i j -> p (i j)")
        if half == 0:
            em(ceng, lambda E, d=dst, h=half: E.tensor_copy(out=d, in_=stagh[h]))
        else:
            em(ceng, lambda E, d=dst, h=half: E.activation(out=d, in_=stagh[h],
                                                           func=AF.Identity))
        wmod_cast_done[jt] = inc(ceng, csem)
        wt("tensor", csem, wmod_cast_done[jt])
        bank = pS1 if half == 0 else pS2
        for ic in range(8):
            em("tensor", lambda E, ic=ic, b=bank, h=half:
               E.matmul(b[0:8, 0:256], silubf[:, ic, :], wmodbf[:, h, ic, :],
                        start=(ic == 0), stop=(ic == 7)))
        wmod_mm_done[jt] = inc("tensor", sT)
        wt("vector", sT, wmod_mm_done[jt])
        em("vector", lambda E, jt=jt, b=bank:
           E.tensor_copy(out=modtm[0:8, jt * 256:(jt + 1) * 256], in_=b[0:8, 0:256]))
        evv = inc("vector", sV)
        wt("tensor", sV, evv)
        for t48 in (2 * jt, 2 * jt + 1):
            vidx, fc = t48 // 8, t48 % 8
            wt("tensor", sV, mod_tr_last[0])
            em("tensor", lambda E, t48=t48:
               E.transpose(pA[:, 0:8], modtm[0:8, t48 * 128:(t48 + 1) * 128], ident[0:8, 0:8]))
            tv = inc("tensor", sT)
            wt("vector", sT, tv)
            em("vector", lambda E, vidx=vidx, fc=fc:
               E.tensor_copy(out=modF[:, vidx, fc, :], in_=pA[:, 0:8]))
            mod_tr_last[0] = inc("vector", sV)

    vedge()
    em("vector", lambda E: E.tensor_scalar(out=modF[:, 1, :, :], in0=modF[:, 1, :, :],
                                           scalar1=1.0, scalar2=None, op0=ALU.add))
    modA_ready = inc("vector", sV)

    vedge()
    em("vector", lambda E: E.tensor_scalar(out=modF[:, 4, :, :], in0=modF[:, 4, :, :],
                                           scalar1=1.0, scalar2=None, op0=ALU.add))
    for vidx in (2, 5):
        g = modF[:, vidx, :, :]
        hi = app[:, 0, 0:64].rearrange("p (a b) -> p a b", b=8)
        lo = app[:, 0, 64:128].rearrange("p (a b) -> p a b", b=8)
        sg = app[:, 0, 128:192].rearrange("p (a b) -> p a b", b=8)
        vedge()
        em("vector", lambda E, g=g, hi=hi: E.tensor_scalar(out=hi, in0=g, scalar1=0.25,
                                                           scalar2=None, op0=ALU.max))
        em("vector", lambda E, g=g, lo=lo: E.tensor_scalar(out=lo, in0=g, scalar1=-0.25,
                                                           scalar2=None, op0=ALU.min))
        em("vector", lambda E, g=g, sg=sg: E.tensor_scalar(out=sg, in0=g, scalar1=0.0,
                                                           scalar2=None, op0=ALU.is_ge))
        vedge()
        em("vector", lambda E, hi=hi, lo=lo: E.tensor_tensor(out=hi, in0=hi, in1=lo,
                                                             op=ALU.subtract))
        em("vector", lambda E, hi=hi, sg=sg: E.tensor_tensor(out=hi, in0=sg, in1=hi,
                                                             op=ALU.mult))
        em("vector", lambda E, g=g, hi=hi, lo=lo: E.tensor_tensor(out=g, in0=hi, in1=lo,
                                                                  op=ALU.add))
    modB_ready = inc("vector", sV)

    def stage_cast_q(dram, dest, q, dest_guard=None):
        if stage_cast.lastcast is None:
            stage_cast.lastcast = {0: wmod_cast_done[22], 1: wmod_cast_done[23]}
        src = dram.rearrange("(i p) o -> p i o", p=128)
        half = q % 2
        ceng, csem = CENG[half]
        wt("sync", csem, stage_cast.lastcast.get(half, 0))
        lv = dma("sync", sQ, stagh[half].rearrange("p (i o) -> p i o", o=256),
                 src[:, :, q * 256:(q + 1) * 256])
        wt(ceng, sQ, lv)
        if dest_guard is not None:
            wt(ceng, dest_guard[0], dest_guard[1])
        sh = stagh[half].rearrange("p (i o) -> p i o", o=256)
        dst = dest[:, :, 2 * q:2 * q + 2, :].rearrange("p i a b -> p i (a b)")
        if half == 0:
            em(ceng, lambda E, d=dst, s=sh: E.tensor_copy(out=d, in_=s))
        else:
            em(ceng, lambda E, d=dst, s=sh: E.activation(out=d, in_=s,
                                                         func=AF.Identity))
        v = inc(ceng, csem)
        stage_cast.lastcast[half] = v
        return v

    def stage_cast(dram, dest, dest_guard=None, nq=4):
        for q in range(nq):
            stage_cast_q(dram, dest, q, dest_guard=dest_guard if q in (0, 1) else None)
        return (stage_cast.lastcast[0], stage_cast.lastcast[1])
    stage_cast.lastcast = None

    def direct_weight(name, dest, dest_guard=None):
        return stage_cast(W_d[name + "T"], dest, dest_guard=dest_guard)

    def emit_ln(src_fn, cvec, svec, mod_wait=0, prep_fn=None, post_fn=None,
                src_qwait_fn=None):
        mmv_par = [0, 0]
        act_par = [0, 0]
        psum_rd = [0]
        pool_done = [0]
        pairs = [(app[:, 0, :], app[:, 1, :]),
                 (esc[:, 0:512], esc[:, 512:1024])]
        h2d = [0]

        def stats(tt):
            if prep_fn is not None:
                prep_fn(tt)
            for fc in range(8):
                par = fc % 2
                s = src_fn(fc, tt)
                wt("vector", sT, mmv_par[par])
                em("vector", lambda E, s=s, par=par: E.tensor_copy(out=bfx[:, par, :], in_=s))
                cv = inc("vector", sV)
                wt("scalar", sV, cv)
                wt("scalar", sT, mmv_par[par])
                em("scalar", lambda E, par=par:
                   E.activation(out=bfsq[:, par, :], in_=bfx[:, par, :], func=AF.Square))
                sq2 = inc("scalar", sS)
                wt("tensor", sV, cv)
                wt("tensor", sS, sq2)
                if fc == 0:
                    wt("tensor", sV, psum_rd[0])
                em("tensor", lambda E, par=par, fc=fc:
                   E.matmul(pS1, onesbf, bfx[:, par, :], start=(fc == 0), stop=(fc == 7)))
                em("tensor", lambda E, par=par, fc=fc:
                   E.matmul(pS2, onesbf, bfsq[:, par, :], start=(fc == 0), stop=(fc == 7)))
                mmv_par[par] = inc("tensor", sT)

        def chain_apply(tt, mmv_tail):
            wt("vector", sT, mmv_tail)
            wt("vector", sS, act_par[0])
            wt("vector", sS, act_par[1])
            wt("vector", sP, pool_done[0])
            em("vector", lambda E: E.tensor_scalar(out=stch[:, 0, :], in0=pS1,
                                                   scalar1=1.0 / 1024.0, scalar2=None, op0=ALU.mult))
            em("vector", lambda E: E.tensor_scalar(out=app[:, 1, :], in0=pS2,
                                                   scalar1=1.0 / 1024.0, scalar2=None, op0=ALU.mult))
            psum_rd[0] = inc("vector", sV)
            if tt < 3:
                stats(tt + 1)
            vedge()
            em("vector", lambda E: E.tensor_tensor(out=app[:, 0, :], in0=stch[:, 0, :],
                                                   in1=stch[:, 0, :], op=ALU.mult))
            em("vector", lambda E: E.tensor_tensor(out=app[:, 1, :], in0=app[:, 1, :],
                                                   in1=app[:, 0, :], op=ALU.subtract))
            chv = inc("vector", sV)
            wt("scalar", sV, chv)
            em("scalar", lambda E: E.activation(out=app[:, 0, :], in_=app[:, 1, :],
                                                func=AF.Sqrt, bias=1e-5, scale=1.0))
            sqv = inc("scalar", sS)
            wt("vector", sS, sqv)
            em("vector", lambda E: E.reciprocal(out=stch[:, 1, :], in_=app[:, 0, :]))
            vedge()
            em("vector", lambda E: E.tensor_tensor(out=stch[:, 2, :], in0=stch[:, 0, :],
                                                   in1=stch[:, 1, :], op=ALU.mult))
            chain_v = inc("vector", sV)
            for fc in range(8):
                par = fc % 2
                s = src_fn(fc, tt)
                t1, t2 = pairs[par]
                if mod_wait and tt == 0 and fc == 0:
                    wt("scalar", sV, mod_wait)
                aeng, asem = "vector", sV
                wt(aeng, sS, act_par[par])
                em(aeng, lambda E, s=s, t1=t1: E.tensor_tensor(out=t1, in0=s,
                                                               in1=stch[:, 1, :], op=ALU.mult))
                em(aeng, lambda E, t1=t1, t2=t2: E.tensor_tensor(out=t2, in0=t1,
                                                                 in1=stch[:, 2, :],
                                                                 op=ALU.subtract))
                av = inc(aeng, asem)
                wt("scalar", asem, av)
                for hf in range(2):
                    b = 2 * tt + hf
                    em("scalar", lambda E, fc=fc, tt=tt, hf=hf, b=b, t2=t2:
                       E.activation(out=H[:, fc, tt * 512 + hf * 256: tt * 512 + hf * 256 + 256],
                                    in_=t2[:, hf * 256:hf * 256 + 256], func=AF.Identity,
                                    bias=modF[:, svec, fc, b:b + 1],
                                    scale=modF[:, cvec, fc, b:b + 1]))
                act_par[par] = inc("scalar", sS)
                h2d[0] = act_par[par]
            if post_fn is not None:
                post_fn(tt)

        stats(0)
        tails = [0, 0, 0, 0]
        tails[0] = mmv_par[1]
        for tt in range(4):
            chain_apply(tt, tails[tt])
            tails[tt + 1 if tt < 3 else 0] = mmv_par[1]
        return h2d[0]

    wt("vector", sQ, xld)
    h1_done = emit_ln(lambda fc, tt: X[:, fc, tt * 512:(tt + 1) * 512], 1, 0,
                      mod_wait=modA_ready, src_qwait_fn=lambda tt: xld)

    wk_cast = stage_cast(W_d["WkTp"], Wslot, dest_guard=(sT, sT.n))
    wt_pair("tensor", wk_cast)
    wt("tensor", sS, h1_done)

    Hr = [H[:, ic, :].rearrange("p (b t) -> p b t", t=256) for ic in range(8)]
    rope_free = {}
    for i3 in range(16):
        hp, bg = i3 // 2, i3 % 2
        bank = pA if i3 % 2 == 0 else pB
        wt("tensor", sV, rope_free.get(i3 - 2, 0))
        for ic in range(8):
            em("tensor", lambda E, ic=ic, hp=hp, bg=bg, bank=bank:
               E.matmul(bank.rearrange("p (b c) -> p b c", c=128),
                        Wslot[:, ic, hp, :], Hr[ic][:, bg * 4:(bg + 1) * 4, 0:128],
                        start=(ic == 0), stop=(ic == 7)))
        mv = inc("tensor", sT)
        wt("vector", sT, mv)
        bank3 = bank.rearrange("p (b c) -> p b c", c=128)
        em("vector", lambda E, bank=bank: E.stream_shuffle(
            out=ropeB.rearrange("p b c -> p (b c)"), in_=bank, mask=SWAP16))
        em("vector", lambda E, bank3=bank3: E.tensor_tensor(
            out=ropeA, in0=bank3, in1=cosF.unsqueeze(1).to_broadcast([128, 4, 128]), op=ALU.mult))
        em("vector", lambda E: E.tensor_tensor(
            out=ropeB, in0=ropeB, in1=sinF.unsqueeze(1).to_broadcast([128, 4, 128]), op=ALU.mult))
        em("vector", lambda E, hp=hp, bg=bg: E.tensor_tensor(
            out=kS[:, hp, bg * 4:(bg + 1) * 4, :], in0=ropeA, in1=ropeB, op=ALU.add))
        rope_free[i3] = inc("vector", sV)

    wv_cast = stage_cast(W_d["WvT"], Wslot, dest_guard=(sT, sT.n))
    wt_pair("tensor", wv_cast)
    vt_free = {}
    for i3 in range(16):
        hp, bg = i3 // 2, i3 % 2
        bank = pA if i3 % 2 == 0 else pB
        tbank = pE if i3 % 2 == 0 else pF
        wtv("tensor", vt_free.get(i3 - 2, (sV, 0)))
        for ic in range(8):
            em("tensor", lambda E, ic=ic, hp=hp, bg=bg, bank=bank:
               E.matmul(bank.rearrange("p (b c) -> p b c", c=128),
                        Wslot[:, ic, hp, :], Hr[ic][:, bg * 4:(bg + 1) * 4, 0:128],
                        start=(ic == 0), stop=(ic == 7)))
        mv = inc("tensor", sT)
        wt("vector", sT, mv)
        em("vector", lambda E, bank=bank:
           E.tensor_copy(out=vtmp.rearrange("p b c -> p (b c)"), in_=bank))
        cv = inc("vector", sV)
        wt("tensor", sV, cv)
        for br in range(4):
            em("tensor", lambda E, br=br, tbank=tbank:
               E.transpose(tbank[:, br * 64:(br + 1) * 64].bitcast(BF16), vtmp[:, br, :], identbf))
        tv = inc("tensor", sT)
        wt("vector", sT, tv)
        em("vector", lambda E, hp=hp, bg=bg, tbank=tbank:
           E.tensor_copy(out=vS[:, bg * 4:(bg + 1) * 4, hp, :],
                         in_=tbank[:, 0:256].bitcast(BF16).rearrange("p (b c) -> p b c", c=128)))
        vt_free[i3] = (sV, inc("vector", sV))
    vproj_done_sT = sT.n

    wff_cast = {}
    wff_pending = [(n, q) for n in ("uf", "ui", "uc", "wf") for q in range(4)]

    def wff_feed(k=2):
        for _ in range(k):
            if not wff_pending:
                return
            n, q = wff_pending.pop(0)
            g = (sS, h1_done) if (n == "uf" and q in (0, 1)) else None
            stage_cast_q(W_d[n + "T"], Wff[n], q, dest_guard=g)
            if q == 3:
                wff_cast[n] = (stage_cast.lastcast[0], stage_cast.lastcast[1])

    def _finish_trunc():
        em("gpsimd", lambda E, v=sV.n: E.wait_ge(sV.h, v))
        em("gpsimd", lambda E, v=sT.n: E.wait_ge(sT.h, v))
        em("gpsimd", lambda E, v=sS.n: E.wait_ge(sS.h, v))
        with nc.Block() as block:
            @block.tensor
            def _(E):
                for fn in ops["tensor"]:
                    fn(E)
            @block.vector
            def _(E):
                for fn in ops["vector"]:
                    fn(E)
            @block.scalar
            def _(E):
                for fn in ops["scalar"]:
                    fn(E)
            @block.sync
            def _(E):
                for fn in ops["sync"]:
                    fn(E)
            @block.gpsimd
            def _(E):
                for fn in ops["gpsimd"]:
                    fn(E)
        st.close()
        return nc

    if trunc == "P3":
        return _finish_trunc()

    wt("tensor", sV, qbd_ready)
    probs_free = [0, 0]
    pEF_free_t = (sV, 0)
    ev_vec = [0]
    pT_free = 0
    pS1_free = 0
    probs_written = 0
    P4B = int(os.environ.get("K_P4B", "8"))
    P4S = int(os.environ.get("K_P4S", "99"))
    for b in range(min(8, P4B)):
        if b >= 1:
            wff_feed(2)
        pbuf = probsb[b % 2]
        for ch in range(2):
            wt("tensor", sV, pS1_free)
            for hp in range(8):
                em("tensor", lambda E, hp=hp, b=b, ch=ch:
                   E.matmul(pS1[:, hp * 64:(hp + 1) * 64], qbd[:, hp, :],
                            kS[:, hp, b, ch * 64:(ch + 1) * 64],
                            start=(hp == 0), stop=(hp == 7)))
            lcv = inc("tensor", sT)
            if P4S < 2:
                continue
            wt("tensor", sV, pT_free)
            for hp in range(8):
                em("tensor", lambda E, hp=hp, b=b, ch=ch:
                   E.matmul(pT[0:64, hp * 128:(hp + 1) * 128],
                            kS[:, hp, b, ch * 64:(ch + 1) * 64], qbd[:, hp, :],
                            start=(hp in (0, 4)), stop=False))
            tTv = inc("tensor", sT)
            if P4S < 3:
                continue
            wt("vector", sT, lcv)
            wt("vector", sS, probs_written)
            pS13 = pS1.rearrange("p (h c) -> p h c", c=64)
            esc3 = esc[:, 0:512].rearrange("p (h c) -> p h c", c=64)
            em("vector", lambda E, pS13=pS13: E.reduce_max(out=Rsc[:, 0:8], in_=pS13, axis=AXL.X))
            em("vector", lambda E, pS13=pS13, esc3=esc3: E.tensor_tensor(
                out=esc3, in0=pS13,
                in1=Rsc[:, 0:8].unsqueeze(2).to_broadcast([128, 8, 64]), op=ALU.subtract))
            em("vector", lambda E, esc3=esc3: E.tensor_tensor(
                out=esc3, in0=esc3,
                in1=masklc.unsqueeze(1).to_broadcast([128, 8, 64]), op=ALU.add))
            pS1_free = inc("vector", sV)
            wt("scalar", sV, pS1_free)
            em("scalar", lambda E: E.activation(out=esc[:, 512:1024], in_=esc[:, 0:512],
                                                func=AF.Exp, bias=0.0, scale=0.125))
            ev = inc("scalar", sS)
            wt("vector", sS, ev)
            em("vector", lambda E: E.reduce_sum(
                out=Rsc[:, 8:16], in_=esc[:, 512:1024].rearrange("p (h c) -> p h c", c=64),
                axis=AXL.X))
            vedge()
            em("vector", lambda E, ch=ch: E.reciprocal(out=rdt[:, ch, :], in_=Rsc[:, 8:16]))
            rvv = inc("vector", sV)
            if P4S < 4:
                continue
            wt("tensor", sV, rvv)
            em("tensor", lambda E: E.transpose(pS2[0:8, 0:128], Rsc[:, 0:8], ident))
            trv = inc("tensor", sT)
            wt("vector", sT, trv)
            em("vector", lambda E: E.tensor_copy(out=RT8, in_=pS2[0:8, 0:128]))
            rt8v = inc("vector", sV)
            if P4S < 5:
                continue
            wt("tensor", sV, rt8v)
            for hp in range(8):
                em("tensor", lambda E, hp=hp:
                   E.matmul(pT[0:64, hp * 128:(hp + 1) * 128], negeye[:, hp, :], RT8,
                            start=False, stop=(hp in (3, 7))))
            rsub = inc("tensor", sT)
            wt("vector", sT, rsub)
            pT3 = pT[0:64, :].rearrange("p (h c) -> p h c", c=128)
            esc4 = esc[0:64, :].rearrange("p (h c) -> p h c", c=128)
            em("vector", lambda E, pT3=pT3, esc4=esc4: E.tensor_tensor(
                out=esc4, in0=pT3, in1=maskT.unsqueeze(1).to_broadcast([64, 8, 128]), op=ALU.add))
            pT_free = inc("vector", sV)
            mkv = pT_free
            wt("scalar", sV, mkv)
            wt("scalar", sT, probs_free[b % 2])
            em("scalar", lambda E, pbuf=pbuf, ch=ch:
               E.activation(out=pbuf[64 * ch:64 * ch + 64, :, :].rearrange("p h c -> p (h c)"),
                            in_=esc[0:64, :], func=AF.Exp, bias=0.0, scale=0.125))
            probs_written = inc("scalar", sS)
        if P4S < 6:
            continue
        wt("tensor", sS, probs_written)
        wtv("tensor", pEF_free_t)
        for ch in range(2):
            cbank = pE if ch == 0 else pF
            for hp in range(8):
                for ph in range(2):
                    em("tensor", lambda E, ch=ch, hp=hp, ph=ph, b=b, cbank=cbank:
                       E.matmul(cbank[64 * ph:64 * ph + 64, hp * 64:(hp + 1) * 64],
                                probsb[b % 2][64 * ch:64 * ch + 64, hp, ph * 64:ph * 64 + 64],
                                vS[64 * ch:64 * ch + 64, b, hp, ph * 64:ph * 64 + 64],
                                start=True, stop=True, skip_group_check=True))
        probs_free[b % 2] = inc("tensor", sT)
        if P4S < 7:
            continue
        wt("vector", sT, probs_free[b % 2])
        em("vector", lambda E: E.tensor_tensor(
            out=app[:, 0, :].rearrange("p (h d) -> p h d", d=64),
            in0=pE.rearrange("p (h d) -> p h d", d=64),
            in1=rdt[:, 0, :].unsqueeze(2).to_broadcast([128, 8, 64]), op=ALU.mult))
        em("vector", lambda E: E.tensor_tensor(
            out=app[:, 1, :].rearrange("p (h d) -> p h d", d=64),
            in0=pF.rearrange("p (h d) -> p h d", d=64),
            in1=rdt[:, 1, :].unsqueeze(2).to_broadcast([128, 8, 64]), op=ALU.mult))
        em("vector", lambda E, b=b:
           E.tensor_tensor(out=attnB[:, b, :, :].rearrange("p h d -> p (h d)"),
                           in0=app[:, 0, :], in1=app[:, 1, :], op=ALU.add))
        evv = inc("vector", sV)
        pEF_free_t = (sV, evv)
        ev_vec[0] = evv
        if b == 3:
            wt("gpsimd", sV, ev_vec[0])
            arw1 = dma("gpsimd", sG, arin1[:, :],
                       attnB.rearrange("p b h d -> p (b h d)")[:, 0:2048])
            wt("gpsimd", sG, arw1)
            em("gpsimd", lambda E: E.collective_compute(
                "AllReduce", ALU.add, replica_groups=[list(range(NCORES))],
                ins=[arin1.ap().opt()], outs=[arout1.ap().opt()]).then_inc(sC.h, 1))
            cc1 = sC.bump(1)

    wff_feed(len(wff_pending))

    if trunc == "P4":
        return _finish_trunc()

    wt("gpsimd", sV, ev_vec[0])
    arw2 = dma("gpsimd", sG, arin2[:, :],
               attnB.rearrange("p b h d -> p (b h d)")[:, 2048:4096])
    wt("gpsimd", sG, arw2)
    em("gpsimd", lambda E: E.collective_compute(
        "AllReduce", ALU.add, replica_groups=[list(range(NCORES))],
        ins=[arin2.ap().opt()], outs=[arout2.ap().opt()]).then_inc(sC.h, 1))
    cc2 = sC.bump(1)
    wt("gpsimd", sC, cc1)
    ar_r1 = dma("gpsimd", sG, attnB.rearrange("p b h d -> p (b h d)")[:, 0:2048],
                arout1[:, :])
    wt("gpsimd", sC, cc2)
    ar_r = dma("gpsimd", sG, attnB.rearrange("p b h d -> p (b h d)")[:, 2048:4096],
               arout2[:, :])

    if trunc == "CC":
        wt("gpsimd", sG, ar_r)
        return _finish_trunc()

    wg_cast = stage_cast(W_d["WgT"], Wslot, dest_guard=(sT, vproj_done_sT))
    wt_pair("tensor", wg_cast)
    g_free = [0, 0]
    eg_free_t = (sV, 0)
    egb = esc[:, 0:256].bitcast(BF16)
    for pc in range(8):
        for tt in range(4):
            it_ = pc * 4 + tt
            bank = pA if it_ % 2 == 0 else pB
            wt("tensor", sS, g_free[it_ % 2])
            for ic in range(8):
                em("tensor", lambda E, ic=ic, pc=pc, tt=tt, bank=bank:
                   E.matmul(bank, Wslot[:, ic, pc, :], H[:, ic, tt * 512:(tt + 1) * 512],
                            start=(ic == 0), stop=(ic == 7)))
            gv = inc("tensor", sT)
            wt("scalar", sT, gv)
            wtv("scalar", eg_free_t)
            em("scalar", lambda E, bank=bank:
               E.activation(out=egb, in_=bank, func=AF.Exp, bias=0.0, scale=0.125))
            g_free[it_ % 2] = inc("scalar", sS)
            wt("tensor", sS, g_free[it_ % 2])
            em("tensor", lambda E: E.matmul(pS2[0:2, 0:512], ind2, egb, start=True, stop=True))
            dnv = inc("tensor", sT)
            wt("vector", sT, dnv)
            em("vector", lambda E: E.reciprocal(out=app[0:2, 0, :], in_=pS2[0:2, 0:512]))
            vedge()
            em("vector", lambda E: E.tensor_copy(out=rden, in_=app[0:2, 0, :]))
            rdv = inc("vector", sV)
            wt("tensor", sV, rdv)
            wtv("tensor", eg_free_t)
            em("tensor", lambda E: E.matmul(pS1, ind2T, rden, start=True, stop=True))
            exv = inc("tensor", sT)
            wt("vector", sT, exv)
            em("vector", lambda E, pc=pc, tt=tt:
               E.tensor_tensor(out=GW[:, pc, tt * 512:(tt + 1) * 512], in0=egb, in1=pS1,
                               op=ALU.mult))
            eg_free_t = (sV, inc("vector", sV))

    if trunc == "P5":
        return _finish_trunc()

    wp_cast = stage_cast(W_d["WpT"], Wslot, dest_guard=(sT, sT.n))



    wt("tensor", sG, ar_r1)
    wtv("tensor", eg_free_t)
    w_free = [(sV, 0), (sS, 0)]
    for b in range(8):
        if b == 4:
            wt("tensor", sG, ar_r)
        for hp in range(8):
            it_ = hp * 8 + b
            bank = pA if it_ % 2 == 0 else pB
            wtv("tensor", w_free[it_ % 2])
            for ph in range(2):
                em("tensor", lambda E, hp=hp, b=b, ph=ph, bank=bank:
                   E.matmul(bank[64 * ph:64 * ph + 64, 0:256],
                            attnB[64 * ph:64 * ph + 64, b, hp, :],
                            GW[64 * ph:64 * ph + 64, hp, b * 256:(b + 1) * 256],
                            start=True, stop=True, skip_group_check=True))
            wv = inc("tensor", sT)
            if it_ % 2 == 0:
                wt("vector", sT, wv)
                em("vector", lambda E, hp=hp, b=b, bank=bank:
                   E.tensor_copy(out=GW[:, hp, b * 256:(b + 1) * 256], in_=bank[:, 0:256]))
                w_free[0] = (sV, inc("vector", sV))
            else:
                wt("scalar", sT, wv)
                em("scalar", lambda E, hp=hp, b=b, bank=bank:
                   E.activation(out=GW[:, hp, b * 256:(b + 1) * 256], in_=bank[:, 0:256],
                                func=AF.Identity))
                w_free[1] = (sS, inc("scalar", sS))

    if trunc == "P6":
        return _finish_trunc()

    wt_pair("tensor", wp_cast)
    wtv("tensor", w_free[0])
    wtv("tensor", w_free[1])
    wt("vector", sV, modB_ready)
    x_r = x_d.rearrange("(f p) b t -> p f b t", p=128)
    x2_r = x2_d.rearrange("(f p) b t -> p f b t", p=128)
    p7_free = [0, 0]
    x2b_free = [0, 0]
    xrd_v = [0, 0]
    xrd_used = [0, 0]
    wt("gpsimd", sT, sT.n)
    for oc in range(8):
        wt("gpsimd", sV, xrd_used[oc % 2])
        xrd_v[oc % 2] = dma("gpsimd", sG, xrd[oc % 2].rearrange("p b t -> p (b t)"),
                            x_r[:, oc, :, :].rearrange("p b t -> p (b t)"))
        for tt in range(4):
            it_ = oc * 4 + tt
            bank = pA if it_ % 2 == 0 else pB
            wt("tensor", sV, p7_free[it_ % 2])
            for ic in range(8):
                em("tensor", lambda E, ic=ic, oc=oc, tt=tt, bank=bank:
                   E.matmul(bank, Wslot[:, ic, oc, :], GW[:, ic, tt * 512:(tt + 1) * 512],
                            start=(ic == 0), stop=(ic == 7)))
            pv = inc("tensor", sT)
            wt("vector", sT, pv)
            wt("vector", sG, xrd_v[oc % 2])
            wt("vector", sG, x2b_free[tt % 2])
            for hf in range(2):
                b = tt * 2 + hf
                em("vector", lambda E, oc=oc, tt=tt, hf=hf, b=b, bank=bank:
                   E.scalar_tensor_tensor(
                       out=x2b[tt % 2][:, hf, :],
                       in0=bank[:, hf * 256:hf * 256 + 256],
                       scalar=modF[:, 2, oc, b:b + 1],
                       in1=xrd[oc % 2][:, b, :],
                       op0=ALU.mult, op1=ALU.add))
            p7_free[it_ % 2] = inc("vector", sV)
            xrd_used[oc % 2] = p7_free[it_ % 2]
            wt("gpsimd", sV, p7_free[it_ % 2])
            x2b_free[tt % 2] = dma(
                "gpsimd", sG,
                x2_r[:, oc, 2 * tt:2 * tt + 2, :].rearrange("p b t -> p (b t)"),
                x2b[tt % 2].rearrange("p a t -> p (a t)"))
    x2_written = sG.n
    p7_done_sT = sT.n
    for n in ("wi", "wc"):
        wff_cast[n] = direct_weight(n, Wff[n], dest_guard=(sT, p7_done_sT) if n == "wi" else None)
    if trunc == "P7":
        return _finish_trunc()

    x2s_free = [(0, 0), (0, 0)]
    if trunc == "P7b":
        return _finish_trunc()

    x2bufs = [x2stg, x2stgB]
    p8_lv = [0, 0]

    def p8_load(tt):
        wt("sync", sG, x2_written)
        if tt % 2 == 0:
            wt("sync", sV, stage_cast.lastcast[0])
            wt("sync", sS, stage_cast.lastcast[1])
        wt("sync", sV, x2s_free[tt % 2][0])
        wt("sync", sP, x2s_free[tt % 2][1])
        return dma("sync", sQ, x2bufs[tt % 2], x2_r[:, :, 2 * tt:2 * tt + 2, :])

    def p8_prep(tt):
        p8_lv[tt % 2] = p8_load(tt)
        wt("vector", sQ, p8_lv[tt % 2])

    def p8_post(tt):
        x2s_free[tt % 2] = (sV.n, sP.n)

    h2_done = emit_ln(lambda fc, tt: x2bufs[tt % 2][:, fc, :, :].rearrange("p a t -> p (a t)"),
                      4, 3, prep_fn=p8_prep, post_fn=p8_post,
                      src_qwait_fn=lambda tt: p8_lv[tt % 2])

    wt("sync", sS, h2_done)
    c0v = dma("sync", sQ, cS.rearrange("p b t -> p (b t)"), inc0_d.rearrange("p b t -> p (b t)"))
    wt("vector", sQ, c0v)
    em("vector", lambda E: E.tensor_copy(out=cbf[0].rearrange("p b t -> p (b t)"),
                                         in_=cS.rearrange("p b t -> p (b t)")))
    c0_ready = inc("vector", sV)

    wt("sync", sS, h2_done)
    wov = stage_cast(W_d["woT"], Wff["wo"], dest_guard=(sT, p7_done_sT))

    wt("tensor", sV, c0_ready)
    wt("tensor", sS, h2_done)

    GATES = (("f", AF.Sigmoid, 0), ("i", AF.Sigmoid, 1), ("c", AF.Tanh, 2))
    gate_waited = set()
    bank_free = {}
    step_done = 0
    for t in range(8):
        cbf_r = cbf[t % 2]
        cbf_w = cbf[(t + 1) % 2]
        wt("tensor", sV, step_done)
        for oc in range(8):
            par = oc % 2
            banks = {"f": (pA if par == 0 else pB),
                     "i": (pE if par == 0 else pF),
                     "c": (pT[:, 0:256] if par == 0 else pT[:, 512:768])}
            gv = {}
            for gname, gfunc, gidx in GATES:
                if gname not in gate_waited:
                    gate_waited.add(gname)
                    wt_pair("tensor", wff_cast["w" + gname])
                    wt_pair("tensor", wff_cast["u" + gname])
                wt("tensor", sV, bank_free.get((gname, par), 0))
                wname, uname = "w" + gname, "u" + gname
                for ic in range(8):
                    em("tensor", lambda E, ic=ic, oc=oc, t=t, gn=gname, bank=banks[gname]:
                       E.matmul(bank[:, 0:256] if bank.shape[1] > 256 else bank,
                                Wff["w" + gn][:, ic, oc, :], H[:, ic, t * 256:(t + 1) * 256],
                                start=(ic == 0), stop=False))
                for ic in range(8):
                    em("tensor", lambda E, ic=ic, oc=oc, gn=gname, bank=banks[gname], cb=cbf_r:
                       E.matmul(bank[:, 0:256] if bank.shape[1] > 256 else bank,
                                Wff["u" + gn][:, ic, oc, :], cb[:, ic, :],
                                start=False, stop=(ic == 7)))
                gv[gname] = inc("tensor", sT)
            dst = {"f": gscF, "i": gscI, "c": gscG}
            for gname, gfunc, gidx in GATES:
                wt("scalar", sT, gv[gname])
                em("scalar", lambda E, gn=gname, gf=gfunc, gi=gidx, oc=oc, par=par,
                   bank=banks[gname]:
                   E.activation(out=dst[gn][:, par, :],
                                in_=bank[:, 0:256] if bank.shape[1] > 256 else bank,
                                func=gf, bias=fbias[:, gi, oc:oc + 1], scale=1.0))
            actv = inc("scalar", sS)
            wt("vector", sS, actv)
            em("vector", lambda E, oc=oc, par=par: E.tensor_tensor(
                out=cS[:, oc, :], in0=gscF[:, par, :], in1=cS[:, oc, :], op=ALU.mult))
            em("vector", lambda E, par=par: E.tensor_tensor(
                out=gscI[:, par, :], in0=gscI[:, par, :], in1=gscG[:, par, :], op=ALU.mult))
            em("vector", lambda E, oc=oc, par=par: E.tensor_tensor(
                out=cS[:, oc, :], in0=cS[:, oc, :], in1=gscI[:, par, :], op=ALU.add))
            em("vector", lambda E, oc=oc, cw=cbf_w: E.tensor_copy(
                out=cw[:, oc, :], in_=cS[:, oc, :]))
            bv = inc("vector", sV)
            for gname, _, _ in GATES:
                bank_free[(gname, par)] = bv
        step_done = sV.n

    uov = stage_cast(W_d["uoT"], Wff["uo"], dest_guard=(sT, sT.n))

    wt_pair("tensor", uov)
    wt("tensor", sS, sS.n)
    cbf_fin = cbf[0]
    for oc in range(8):
        bank = pE if oc % 2 == 0 else pF
        for ic in range(8):
            em("tensor", lambda E, ic=ic, oc=oc, bank=bank:
               E.matmul(bank[:, 0:256], Wff["uo"][:, ic, oc, :], cbf_fin[:, ic, :],
                        start=(ic == 0), stop=(ic == 7)))
        uv = inc("tensor", sT)
        wt("scalar", sT, uv)
        em("scalar", lambda E, oc=oc, bank=bank:
           E.activation(out=uodv[:, oc, :], in_=bank[:, 0:256], func=AF.Identity,
                        bias=fbias[:, 3, oc:oc + 1], scale=1.0))
    uod_done = inc("scalar", sS)

    wt_pair("tensor", wov)
    wt("tensor", sS, uod_done)
    xt_v = [0, 0]
    xt_used = [0, 0]
    tail_free = [0, 0]
    ybuf_free = [0, 0]
    yv_last = 0
    for oc in range(8):
        wt("sync", sV, max(xt_used[oc % 2], uov[0]))
        wt("sync", sS, uov[1])
        xt_v[oc % 2] = dma("sync", sQ, xt[oc % 2].rearrange("p b t -> p (b t)"),
                           x2_r[:, oc, :, :].rearrange("p b t -> p (b t)"))
        for t in range(8):
            par = (oc * 8 + t) % 2
            bank = pA if par == 0 else pB
            wt("tensor", sV, tail_free[par])
            for ic in range(8):
                em("tensor", lambda E, ic=ic, oc=oc, t=t, bank=bank:
                   E.matmul(bank[:, 0:256], Wff["wo"][:, ic, oc, :],
                            H[:, ic, t * 256:(t + 1) * 256],
                            start=(ic == 0), stop=(ic == 7)))
            xv = inc("tensor", sT)
            wt("vector", sT, xv)
            em("vector", lambda E, oc=oc, bank=bank, par=par: E.tensor_tensor(
                out=app[:, par, 0:256], in0=bank[:, 0:256], in1=uodv[:, oc, :], op=ALU.add))
            s1v = inc("vector", sV)
            wt("scalar", sV, s1v)
            em("scalar", lambda E, par=par: E.activation(
                out=app[:, par, 256:512], in_=app[:, par, 0:256], func=AF.Tanh))
            s2v = inc("scalar", sS)
            ybuf = ybufA if t < 4 else ybufB
            yb = 0 if t < 4 else 1
            wt("vector", sS, s2v)
            wt("vector", sQ, xt_v[oc % 2])
            if t in (0, 4):
                wt("vector", sO, ybuf_free[yb])
            em("vector", lambda E, oc=oc, t=t, par=par, ybuf=ybuf: E.scalar_tensor_tensor(
                out=ybuf[:, t % 4, :], in0=app[:, par, 256:512],
                scalar=modF[:, 5, oc, t:t + 1], in1=xt[oc % 2][:, t, :],
                op0=ALU.mult, op1=ALU.add))
            tail_free[par] = inc("vector", sV)
            xt_used[oc % 2] = tail_free[par]
            if t in (3, 7):
                wt("gpsimd", sV, tail_free[par])
                y_r = y_d.rearrange("(f p) b t -> p f b t", p=128)
                tb = t // 4
                ybuf_free[yb] = dma("gpsimd", sO,
                                    y_r[:, oc, 4 * tb:4 * tb + 4, :].rearrange("p b t -> p (b t)"),
                                    ybuf.rearrange("p a t -> p (a t)"))
                yv_last = ybuf_free[yb]

    if DBG:
        wt("gpsimd", sV, sV.n)
        wt("gpsimd", sS, sS.n)
        wt("gpsimd", sT, sT.n)
        dma("gpsimd", sO, dbg_d["h2"].bitcast(BF16), H.rearrange("p a b -> p (a b)"))
        dma("gpsimd", sO, dbg_d["x2"], x2_d)
        dma("gpsimd", sO, dbg_d["cfin"], cS.rearrange("p a b -> p (a b)"))
        dma("gpsimd", sO, dbg_d["uod"], uodv.rearrange("p a b -> p (a b)"))
    em("gpsimd", lambda E, v=sO.n: E.wait_ge(sO.h, v))

    with nc.Block() as block:
        @block.tensor
        def _(E):
            for fn in ops["tensor"]:
                fn(E)

        @block.vector
        def _(E):
            for fn in ops["vector"]:
                fn(E)

        @block.scalar
        def _(E):
            for fn in ops["scalar"]:
                fn(E)

        @block.sync
        def _(E):
            for fn in ops["sync"]:
                fn(E)

        @block.gpsimd
        def _(E):
            for fn in ops["gpsimd"]:
                fn(E)

    st.close()
    return nc


DBG_SRC = {}

_prog_cache = {}


def kernel(**inputs):
    shared = _shared_layouts(inputs)
    nc = build_program()
    in_maps = []
    for core in range(NCORES):
        m = dict(shared)
        m.update(_host_constants(core))
        m["x_fm"] = _core_x(inputs, core)
        dm = {"x_fm": m["x_fm"], "wmod": m["wmod"], "cT": m["cT"], "qbd": m["qbd"],
              "ident": m["ident"], "ident_bf": m["ident_bf"], "ones_bf": m["ones_bf"],
              "negeye": m["negeye"], "ind2": m["ind2"], "ind2T": m["ind2T"],
              "mask_lc": m["mask_lc"], "mask_T": m["mask_T"],
              "cosF": m["cosF"], "sinF": m["sinF"], "in_c0": m["in_c0"]}
        for n in ("WkTp", "WvT", "WgT", "WpT", "wfT", "ufT", "wiT", "uiT", "wcT", "ucT",
                  "woT", "uoT"):
            dm[n] = m[n]
        for n in ("bf", "bi", "bc", "bo"):
            dm[n + "r"] = m[n + "r"]
        in_maps.append(dm)
    res = run_bass_kernel_spmd(nc, in_maps, core_ids=list(range(NCORES)))
    return _assemble_output([res.results[i]["y_fm"] for i in range(NCORES)])


# version 20
# speedup vs baseline: 1.0814x; 1.0215x over previous
import os
import numpy as np
import ml_dtypes
from contextlib import ExitStack

import concourse.bass as bass
import concourse.mybir as mybir
from concourse.bass_utils import run_bass_kernel_spmd

F32 = mybir.dt.float32
BF16 = mybir.dt.bfloat16
U16 = mybir.dt.uint16
AF = mybir.ActivationFunctionType
ALU = mybir.AluOpType
AXL = mybir.AxisListType

NCORES = 8
B = 8
TL = 256
NEG = -1.0e30
SWAP16 = list(range(16, 32)) + list(range(16))

_PERM = np.concatenate([np.arange(0, 64), np.arange(128, 192),
                        np.arange(64, 128), np.arange(192, 256)])
_IPERM = np.argsort(_PERM)


def _dperm():
    p = np.arange(64)
    q16 = p // 16
    j = (q16 // 2) * 16 + p % 16
    return 2 * j + q16 % 2


_DPERM = _dperm()


def _kperm():
    out = np.zeros(1024, np.int64)
    for hp in range(8):
        for p in range(128):
            out[hp * 128 + p] = (2 * hp + p // 64) * 64 + _DPERM[p % 64]
    return out


_KPERM = _kperm()


def _bf(a):
    return np.ascontiguousarray(a.astype(ml_dtypes.bfloat16).view(np.uint16))


def _host_constants(core_id):
    c = {}
    c["ident"] = np.eye(128, dtype=np.float32)
    c["ident_bf"] = _bf(np.eye(128, dtype=np.float32))
    c["ones_bf"] = _bf(np.ones((128, 128), np.float32))
    ne = np.zeros((8, 8, 64), np.float32)
    for k in range(8):
        ne[k, k, :] = -1.0
    c["negeye"] = _bf(ne)
    ind2 = np.zeros((128, 2), np.float32)
    ind2[:64, 0] = 1.0
    ind2[64:, 1] = 1.0
    c["ind2"] = _bf(ind2)
    c["ind2T"] = _bf(ind2.T.copy())
    l_of_p = np.arange(128) % 64
    cc = np.arange(64)
    c["mask_lc"] = np.where(cc[None, :] <= l_of_p[:, None], 0.0, NEG).astype(np.float32)
    c["mask_T"] = np.where(cc[:, None] <= l_of_p[None, :], 0.0, NEG).astype(np.float32)
    p = np.arange(128) % 64
    q16 = p // 16
    j = (q16 // 2) * 16 + p % 16
    is_b = q16 % 2
    inv = 1.0 / 10000.0 ** (np.arange(0, 64, 2, dtype=np.float32) / 64.0)
    c2 = np.arange(128)
    pos = core_id * 256 + (c2 // 64) * 128 + c2 % 64
    ang = pos[None, :].astype(np.float32) * inv[j][:, None]
    c["cosF"] = np.cos(ang).astype(np.float32)
    sinF = np.sin(ang).astype(np.float32)
    c["sinF"] = np.where(is_b[:, None] == 0, -sinF, sinF).astype(np.float32)
    return c


def _shared_layouts(inputs):
    g = lambda n: np.asarray(inputs[n], dtype=np.float32)
    s = {}
    s["WkTp"] = np.ascontiguousarray(g("L0_attn_W_k").T[:, _KPERM])
    s["WvT"] = np.ascontiguousarray(g("L0_attn_W_v").T)
    s["WgT"] = np.ascontiguousarray(g("L0_attn_W_g").T)
    s["WpT"] = np.ascontiguousarray(g("L0_attn_W_p").T)
    for n in ("wf", "uf", "wi", "ui", "wc", "uc", "wo", "uo"):
        s[n + "T"] = np.ascontiguousarray(g("L0_ffn_" + n).T)
    s["wmod"] = np.ascontiguousarray(g("L0_wmod"))
    s["cT"] = np.ascontiguousarray(g("L0_c").T)
    lq = g("L0_attn_latent_queries")[0]
    qbd = np.zeros((8, 128, 128), np.float32)
    for hp in range(8):
        for ph in range(2):
            blk = lq[:, 2 * hp + ph, :][:, _DPERM]
            qbd[hp, 64 * ph:64 * ph + 64, 64 * ph:64 * ph + 64] = blk.T
    s["qbd"] = np.ascontiguousarray(qbd)
    for n in ("bf", "bi", "bc", "bo"):
        s[n + "r"] = np.ascontiguousarray(g("L0_ffn_" + n).reshape(8, 128).T)
    ic = g("L0_ffn_in_c").reshape(8, 128).T
    s["in_c0"] = np.ascontiguousarray(np.broadcast_to(ic[:, :, None], (128, 8, 256)))
    return s


def _core_x(inputs, core_id):
    x = np.asarray(inputs["L0_attn_x"], dtype=np.float32)
    xs = x[:, core_id * 256:(core_id + 1) * 256, :][:, _PERM, :]
    return np.ascontiguousarray(xs.transpose(2, 0, 1))


def _assemble_output(shards):
    y = np.empty((8, 2048, 1024), np.float32)
    for i, sh in enumerate(shards):
        ys = np.asarray(sh).reshape(1024, 8, 256).transpose(1, 2, 0)
        y[:, i * 256:(i + 1) * 256, :] = ys[:, _IPERM, :]
    return y


class Sem:
    def __init__(self, h):
        self.h = h
        self.n = 0

    def bump(self, k=1):
        self.n += k
        return self.n


def _view(arena, off, nslots, dtype, shape):
    ap = arena[:, off:off + nslots]
    if dtype == BF16:
        ap = ap.bitcast(BF16)
    total = 1
    for s in shape:
        total *= s
    assert total == ap.shape[1], (shape, ap.shape)
    if len(shape) > 1:
        pat = " ".join(f"d{i}" for i in range(len(shape)))
        ap = ap.rearrange(f"p ({pat}) -> p {pat}",
                          **{f"d{i}": shape[i] for i in range(len(shape))})
    return ap


FFN_W = ("uf", "ui", "uc", "wf", "wi", "wc", "wo", "uo")


def build_program(debug=(), trunc=None):
    nc = bass.Bass("TRN2", target_bir_lowering=False, detect_race_conditions=False)
    DI = lambda name, shape, dt=F32: nc.dram_tensor(name, shape, dt, kind="ExternalInput").ap()

    x_d = DI("x_fm", [1024, B, TL])
    W_d = {n: DI(n, [1024, 1024]) for n in
           ("WkTp", "WvT", "WgT", "WpT", "wfT", "ufT", "wiT", "uiT", "wcT", "ucT", "woT", "uoT")}
    wmod_d = DI("wmod", [1024, 6144])
    cT_d = DI("cT", [1024, 8])
    qbd_d = DI("qbd", [8, 128, 128])
    ident_d = DI("ident", [128, 128])
    identbf_d = DI("ident_bf", [128, 128], U16)
    onesbf_d = DI("ones_bf", [128, 128], U16)
    negeye_d = DI("negeye", [8, 8, 64], U16)
    ind2_d = DI("ind2", [128, 2], U16)
    ind2T_d = DI("ind2T", [2, 128], U16)
    masklc_d = DI("mask_lc", [128, 64])
    maskT_d = DI("mask_T", [64, 128])
    cosF_d = DI("cosF", [128, 128])
    sinF_d = DI("sinF", [128, 128])
    bias_d = {n: DI(n + "r", [128, 8]) for n in ("bf", "bi", "bc", "bo")}
    inc0_d = DI("in_c0", [128, 8, 256])

    y_d = nc.dram_tensor("y_fm", [1024, B, TL], F32, kind="ExternalOutput").ap()
    x2_d = nc.dram_tensor("x2_scr", [1024, B, TL], BF16).ap()
    arin1 = nc.dram_tensor("ar_in1", [128, 2048], BF16)
    arout1 = nc.dram_tensor("ar_out1", [128, 2048], BF16)
    arin2 = nc.dram_tensor("ar_in2", [128, 2048], BF16)
    arout2 = nc.dram_tensor("ar_out2", [128, 2048], BF16)
    arwi = nc.dram_tensor("ar_wi", [128, 64], BF16)
    arwo = nc.dram_tensor("ar_wo", [128, 64], BF16)

    DBG = os.environ.get("K_DBG", "") == "1"
    dbg_d = {}
    if DBG:
        dbg_d["gates"] = nc.dram_tensor("dbg_gates", [128, 16384], U16, kind="ExternalOutput").ap()
        dbg_d["attnB"] = nc.dram_tensor("dbg_attnB", [128, 4096], U16, kind="ExternalOutput").ap()
        dbg_d["attnP"] = nc.dram_tensor("dbg_attnP", [128, 4096], U16, kind="ExternalOutput").ap()
        dbg_d["h2"] = nc.dram_tensor("dbg_h2", [128, 16384], U16, kind="ExternalOutput").ap()
        dbg_d["x2"] = nc.dram_tensor("dbg_x2", [1024, B, TL], F32, kind="ExternalOutput").ap()
        dbg_d["cfin"] = nc.dram_tensor("dbg_cfin", [128, 2048], F32, kind="ExternalOutput").ap()
        dbg_d["uod"] = nc.dram_tensor("dbg_uod", [128, 2048], F32, kind="ExternalOutput").ap()

    st = ExitStack()
    sbuf = lambda name, shape, dt=F32: st.enter_context(nc.sbuf_tensor(name, shape, dt))[:]
    psum = lambda name, shape: st.enter_context(nc.psum_tensor(name, shape, F32))[:]

    arena = st.enter_context(nc.sbuf_tensor("arena", [128, 46080], F32))[:]

    ident = sbuf("identt", [128, 128])
    identbf = sbuf("identbft", [128, 128], BF16)
    onesbf = sbuf("onesbft", [128, 128], BF16)
    qbd = sbuf("qbdt", [128, 8, 128], BF16)
    negeye = sbuf("negeyet", [8, 8, 64], BF16)
    ind2 = sbuf("ind2t", [128, 2], BF16)
    ind2T = sbuf("ind2Tt", [2, 128], BF16)
    masklc = sbuf("masklct", [128, 64])
    maskT = sbuf("maskTt", [64, 128])
    cosF = sbuf("cosFt", [128, 128])
    sinF = sbuf("sinFt", [128, 128])
    modF = sbuf("modFt", [128, 6, 8, 8])
    cTt = sbuf("cTt", [128, 8, 8])
    silubf = sbuf("silubft", [128, 8, 8], BF16)
    Rsc = sbuf("Rsct", [128, 32])
    rdt = sbuf("rdtt", [128, 2, 8])
    RT8 = sbuf("RT8t", [8, 128], BF16)
    fbias = sbuf("fbiast", [128, 4, 8])
    rden = sbuf("rdent", [2, 512], BF16)
    consts = sbuf("constst", [128, 4])
    stch = sbuf("stcht", [128, 3, 512])
    app = sbuf("appt", [128, 2, 512])
    esc = sbuf("esct", [128, 1024])
    bfx = sbuf("bfxt", [128, 2, 512], BF16)
    bfsq = sbuf("bfsqt", [128, 2, 512], BF16)

    nc.const_aps.register(F32, 0.125, consts[:, 0:1])
    nc.const_aps.register(F32, 8.0, consts[:, 1:2])
    nc.const_aps.register(F32, 1e-5, consts[:, 2:3])
    nc.const_aps.register(F32, 1.0 / 1024.0, consts[:, 3:4])

    pA = psum("pA", [128, 512])
    pB = psum("pB", [128, 512])
    pE = psum("pE", [128, 512])
    pF = psum("pF", [128, 512])
    pT = psum("pT", [128, 1024])
    pS1 = psum("pS1", [128, 512])
    pS2 = psum("pS2", [128, 512])

    V = lambda off, n, dt, shape: _view(arena, off, n, dt, shape)
    H = V(0, 8192, BF16, (8, 2048))
    X = V(8192, 16384, F32, (8, 2048))
    Wff = {"uf": V(8192, 4096, BF16, (8, 8, 128)),
           "ui": V(12288, 4096, BF16, (8, 8, 128)),
           "uc": V(16384, 4096, BF16, (8, 8, 128)),
           "wf": V(20480, 4096, BF16, (8, 8, 128)),
           "wi": V(24576, 4096, BF16, (8, 8, 128)),
           "wc": V(28672, 4096, BF16, (8, 8, 128)),
           "wo": V(37888, 4096, BF16, (8, 8, 128)),
           "uo": V(12288, 4096, BF16, (8, 8, 128))}
    uodv = V(8192, 2048, F32, (8, 256))
    modtm = V(24576, 6144, F32, (6144,))
    kS = V(24576, 4096, BF16, (8, 8, 128))
    vS = V(28672, 4096, BF16, (8, 8, 128))
    GW = V(24576, 8192, BF16, (8, 2048))
    qbdf = V(32768, 1024, F32, (8, 128))
    ropeA = V(32768, 512, F32, (4, 128))
    ropeB = V(33280, 512, F32, (4, 128))
    vtmp = V(33792, 256, BF16, (4, 128))
    attnB = V(32768, 2048, BF16, (8, 8, 64))
    xrd = [V(32768, 2048, F32, (8, 256)), V(34816, 2048, F32, (8, 256))]
    cS = V(32768, 2048, F32, (8, 256))
    cbf = [V(34816, 1024, BF16, (8, 256)), V(35840, 1024, BF16, (8, 256))]
    probsb = [V(36864, 512, BF16, (8, 128)), V(37376, 512, BF16, (8, 128))]
    wbounce = [V(36864, 1024, BF16, (2048,)), V(36864, 1024, BF16, (2048,))]

    ptmp7 = V(36864, 1024, F32, (4, 256))
    gscG = V(36864, 512, F32, (2, 256))
    ybufB = V(36864, 1024, F32, (4, 256))
    Wslot = V(37888, 4096, BF16, (8, 8, 128))
    wmodbf = V(37888, 2048, BF16, (2, 8, 256))
    stagh = [V(41984, 2048, F32, (2048,)), V(44032, 2048, F32, (2048,))]
    x2stg = V(41984, 2048, BF16, (8, 2, 256))
    x2stgB = V(32768, 2048, BF16, (8, 2, 256))
    xt = [V(41984, 1024, BF16, (8, 256)), V(44032, 1024, BF16, (8, 256))]
    gscF = esc[:, 0:512].rearrange("p (a t) -> p a t", t=256)
    gscI = esc[:, 512:1024].rearrange("p (a t) -> p a t", t=256)
    x2b = [esc[:, 0:256].bitcast(BF16).rearrange("p (a t) -> p a t", t=256),
           esc[:, 256:512].bitcast(BF16).rearrange("p (a t) -> p a t", t=256)]
    ybufA = esc.rearrange("p (a t) -> p a t", t=256)

    ops = {e: [] for e in ("tensor", "vector", "scalar", "sync", "gpsimd")}
    em = lambda e, fn: ops[e].append(fn)

    sems = {}
    for name in ("sQ", "sQ2", "sG", "sT", "sV", "sS", "sC", "sO", "sP"):
        sems[name] = Sem(st.enter_context(nc.semaphore(name)))
    sQ, sQ2, sG, sT, sV, sS, sC, sO, sP = (sems[k] for k in
                                           ("sQ", "sQ2", "sG", "sT", "sV", "sS", "sC",
                                            "sO", "sP"))

    def dma(eng, sem, out, in_):
        v = sem.bump(16)
        em(eng, lambda E, o=out, i=in_, s=sem.h: E.dma_start(out=o, in_=i).then_inc(s, 16))
        return v

    def wt(eng, sem, v):
        if v:
            em(eng, lambda E, s=sem.h, vv=v: E.wait_ge(s, vv))

    def vedge():
        em("vector", lambda E: E.engine_nop())
        v = inc("vector", sV)
        wt("vector", sV, v)

    def wt_pair(eng, pair):
        wt(eng, sV, pair[0])
        wt(eng, sS, pair[1])

    def wtv(eng, sv):
        if sv[1]:
            wt(eng, sv[0], sv[1])

    def inc(eng, sem, k=1):
        v = sem.bump(k)
        last = ops[eng][-1]

        def wrapped(E, f=last, s=sem.h, kk=k):
            ins = f(E)
            assert ins is not None, "inc() must follow an instruction-emitting op"
            return ins.then_inc(s, kk)
        ops[eng][-1] = wrapped
        return v

    xld = dma("sync", sQ, X.rearrange("p f (b t) -> p f b t", t=256),
              x_d.rearrange("(f p) b t -> p f b t", p=128))
    for out, d in (
            (ident, ident_d), (identbf, identbf_d.bitcast(BF16)), (onesbf, onesbf_d.bitcast(BF16)),
            (qbdf, qbd_d.rearrange("h a b -> a h b")),
            (negeye.rearrange("k h c -> k (h c)"),
             negeye_d.rearrange("k h c -> k (h c)").bitcast(BF16)),
            (ind2, ind2_d.bitcast(BF16)), (ind2T, ind2T_d.bitcast(BF16)),
            (masklc, masklc_d), (maskT, maskT_d), (cosF, cosF_d), (sinF, sinF_d),
            (cTt, cT_d.rearrange("(i p) b -> p i b", p=128)),
            (fbias[:, 0, :], bias_d["bf"]), (fbias[:, 1, :], bias_d["bi"]),
            (fbias[:, 2, :], bias_d["bc"]), (fbias[:, 3, :], bias_d["bo"])):
        dma("gpsimd", sG, out, d)
    cver = sG.n

    wt("gpsimd", sG, cver)
    wdma = dma("gpsimd", sG, arwi[:, :], identbf[:, 0:64])
    wt("gpsimd", sG, wdma)
    em("gpsimd", lambda E: E.collective_compute(
        "AllReduce", ALU.add, replica_groups=[list(range(NCORES))],
        ins=[arwi.ap().opt()], outs=[arwo.ap().opt()]).then_inc(sC.h, 1))
    sC.bump(1)

    def _initc(E):
        E.memset(consts[:, 0:1], 0.125)
        E.memset(consts[:, 1:2], 8.0)
        E.memset(consts[:, 2:3], 1e-5)
        E.memset(consts[:, 3:4], 1.0 / 1024.0)
    em("vector", _initc)
    wt("vector", sG, cver)
    em("vector", lambda E: E.tensor_copy(out=qbd.rearrange("p h c -> p (h c)"),
                                         in_=qbdf.rearrange("p h c -> p (h c)")))
    qbd_ready = inc("vector", sV)

    wt("scalar", sG, cver)
    em("scalar", lambda E: E.activation(out=silubf.rearrange("p i b -> p (i b)"),
                                        in_=cTt.rearrange("p i b -> p (i b)"), func=AF.Silu))
    silu_ready = inc("scalar", sS)
    wt("tensor", sS, silu_ready)

    wmod_r = wmod_d.rearrange("(i p) j -> p i j", p=128)
    wmod_mm_done = {}
    wmod_cast_done = {}
    mod_tr_last = [0]
    CENG = {0: ("vector", sV), 1: ("scalar", sS)}
    for jt in range(24):
        half = jt % 2
        ceng, csem = CENG[half]
        qeng, qsem = (("sync", sQ) if half == 0 else ("scalar", sQ2))
        if jt >= 2:
            wt(qeng, csem, wmod_cast_done[jt - 2])
        lv = dma(qeng, qsem, stagh[half].rearrange("p (i j) -> p i j", j=256),
                 wmod_r[:, :, jt * 256:(jt + 1) * 256])
        wt(ceng, qsem, lv)
        if jt >= 2:
            wt(ceng, sT, wmod_mm_done[jt - 2])
        dst = wmodbf[:, half, :, :].rearrange("p i j -> p (i j)")
        if half == 0:
            em(ceng, lambda E, d=dst, h=half: E.tensor_copy(out=d, in_=stagh[h]))
        else:
            em(ceng, lambda E, d=dst, h=half: E.activation(out=d, in_=stagh[h],
                                                           func=AF.Identity))
        wmod_cast_done[jt] = inc(ceng, csem)
        wt("tensor", csem, wmod_cast_done[jt])
        bank = pS1 if half == 0 else pS2
        for ic in range(8):
            em("tensor", lambda E, ic=ic, b=bank, h=half:
               E.matmul(b[0:8, 0:256], silubf[:, ic, :], wmodbf[:, h, ic, :],
                        start=(ic == 0), stop=(ic == 7)))
        wmod_mm_done[jt] = inc("tensor", sT)
        wt("vector", sT, wmod_mm_done[jt])
        em("vector", lambda E, jt=jt, b=bank:
           E.tensor_copy(out=modtm[0:8, jt * 256:(jt + 1) * 256], in_=b[0:8, 0:256]))
        evv = inc("vector", sV)
        wt("tensor", sV, evv)
        for t48 in (2 * jt, 2 * jt + 1):
            vidx, fc = t48 // 8, t48 % 8
            wt("tensor", sV, mod_tr_last[0])
            em("tensor", lambda E, t48=t48:
               E.transpose(pA[:, 0:8], modtm[0:8, t48 * 128:(t48 + 1) * 128], ident[0:8, 0:8]))
            tv = inc("tensor", sT)
            wt("vector", sT, tv)
            em("vector", lambda E, vidx=vidx, fc=fc:
               E.tensor_copy(out=modF[:, vidx, fc, :], in_=pA[:, 0:8]))
            mod_tr_last[0] = inc("vector", sV)

    vedge()
    em("vector", lambda E: E.tensor_scalar(out=modF[:, 1, :, :], in0=modF[:, 1, :, :],
                                           scalar1=1.0, scalar2=None, op0=ALU.add))
    modA_ready = inc("vector", sV)

    vedge()
    em("vector", lambda E: E.tensor_scalar(out=modF[:, 4, :, :], in0=modF[:, 4, :, :],
                                           scalar1=1.0, scalar2=None, op0=ALU.add))
    for vidx in (2, 5):
        g = modF[:, vidx, :, :]
        hi = app[:, 0, 0:64].rearrange("p (a b) -> p a b", b=8)
        lo = app[:, 0, 64:128].rearrange("p (a b) -> p a b", b=8)
        sg = app[:, 0, 128:192].rearrange("p (a b) -> p a b", b=8)
        vedge()
        em("vector", lambda E, g=g, hi=hi: E.tensor_scalar(out=hi, in0=g, scalar1=0.25,
                                                           scalar2=None, op0=ALU.max))
        em("vector", lambda E, g=g, lo=lo: E.tensor_scalar(out=lo, in0=g, scalar1=-0.25,
                                                           scalar2=None, op0=ALU.min))
        em("vector", lambda E, g=g, sg=sg: E.tensor_scalar(out=sg, in0=g, scalar1=0.0,
                                                           scalar2=None, op0=ALU.is_ge))
        vedge()
        em("vector", lambda E, hi=hi, lo=lo: E.tensor_tensor(out=hi, in0=hi, in1=lo,
                                                             op=ALU.subtract))
        em("vector", lambda E, hi=hi, sg=sg: E.tensor_tensor(out=hi, in0=sg, in1=hi,
                                                             op=ALU.mult))
        em("vector", lambda E, g=g, hi=hi, lo=lo: E.tensor_tensor(out=g, in0=hi, in1=lo,
                                                                  op=ALU.add))
    modB_ready = inc("vector", sV)

    def stage_cast_q(dram, dest, q, dest_guard=None):
        if stage_cast.lastcast is None:
            stage_cast.lastcast = {0: wmod_cast_done[22], 1: wmod_cast_done[23]}
        src = dram.rearrange("(i p) o -> p i o", p=128)
        half = q % 2
        ceng, csem = CENG[half]
        wt("sync", csem, stage_cast.lastcast.get(half, 0))
        lv = dma("sync", sQ, stagh[half].rearrange("p (i o) -> p i o", o=256),
                 src[:, :, q * 256:(q + 1) * 256])
        wt(ceng, sQ, lv)
        if dest_guard is not None:
            wt(ceng, dest_guard[0], dest_guard[1])
        sh = stagh[half].rearrange("p (i o) -> p i o", o=256)
        dst = dest[:, :, 2 * q:2 * q + 2, :].rearrange("p i a b -> p i (a b)")
        if half == 0:
            em(ceng, lambda E, d=dst, s=sh: E.tensor_copy(out=d, in_=s))
        else:
            em(ceng, lambda E, d=dst, s=sh: E.activation(out=d, in_=s,
                                                         func=AF.Identity))
        v = inc(ceng, csem)
        stage_cast.lastcast[half] = v
        return v

    def stage_cast(dram, dest, dest_guard=None, nq=4):
        for q in range(nq):
            stage_cast_q(dram, dest, q, dest_guard=dest_guard if q in (0, 1) else None)
        return (stage_cast.lastcast[0], stage_cast.lastcast[1])
    stage_cast.lastcast = None

    def direct_weight(name, dest, dest_guard=None):
        return stage_cast(W_d[name + "T"], dest, dest_guard=dest_guard)

    def emit_ln(src_fn, cvec, svec, mod_wait=0, prep_fn=None, post_fn=None,
                src_qwait_fn=None):
        mmv_par = [0, 0]
        act_par = [0, 0]
        psum_rd = [0]
        pool_done = [0]
        pairs = [(app[:, 0, :], app[:, 1, :]),
                 (esc[:, 0:512], esc[:, 512:1024])]
        h2d = [0]

        def stats(tt):
            if prep_fn is not None:
                prep_fn(tt)
            for fc in range(8):
                par = fc % 2
                s = src_fn(fc, tt)
                wt("vector", sT, mmv_par[par])
                em("vector", lambda E, s=s, par=par: E.tensor_copy(out=bfx[:, par, :], in_=s))
                cv = inc("vector", sV)
                wt("scalar", sV, cv)
                wt("scalar", sT, mmv_par[par])
                em("scalar", lambda E, par=par:
                   E.activation(out=bfsq[:, par, :], in_=bfx[:, par, :], func=AF.Square))
                sq2 = inc("scalar", sS)
                wt("tensor", sV, cv)
                wt("tensor", sS, sq2)
                if fc == 0:
                    wt("tensor", sV, psum_rd[0])
                em("tensor", lambda E, par=par, fc=fc:
                   E.matmul(pS1, onesbf, bfx[:, par, :], start=(fc == 0), stop=(fc == 7)))
                em("tensor", lambda E, par=par, fc=fc:
                   E.matmul(pS2, onesbf, bfsq[:, par, :], start=(fc == 0), stop=(fc == 7)))
                mmv_par[par] = inc("tensor", sT)

        def chain_apply(tt, mmv_tail):
            wt("vector", sT, mmv_tail)
            wt("vector", sS, act_par[0])
            wt("vector", sS, act_par[1])
            wt("vector", sP, pool_done[0])
            em("vector", lambda E: E.tensor_scalar(out=stch[:, 0, :], in0=pS1,
                                                   scalar1=1.0 / 1024.0, scalar2=None, op0=ALU.mult))
            em("vector", lambda E: E.tensor_scalar(out=app[:, 1, :], in0=pS2,
                                                   scalar1=1.0 / 1024.0, scalar2=None, op0=ALU.mult))
            psum_rd[0] = inc("vector", sV)
            if tt < 3:
                stats(tt + 1)
            vedge()
            em("vector", lambda E: E.tensor_tensor(out=app[:, 0, :], in0=stch[:, 0, :],
                                                   in1=stch[:, 0, :], op=ALU.mult))
            em("vector", lambda E: E.tensor_tensor(out=app[:, 1, :], in0=app[:, 1, :],
                                                   in1=app[:, 0, :], op=ALU.subtract))
            chv = inc("vector", sV)
            wt("scalar", sV, chv)
            em("scalar", lambda E: E.activation(out=app[:, 0, :], in_=app[:, 1, :],
                                                func=AF.Sqrt, bias=1e-5, scale=1.0))
            sqv = inc("scalar", sS)
            wt("vector", sS, sqv)
            em("vector", lambda E: E.reciprocal(out=stch[:, 1, :], in_=app[:, 0, :]))
            vedge()
            em("vector", lambda E: E.tensor_tensor(out=stch[:, 2, :], in0=stch[:, 0, :],
                                                   in1=stch[:, 1, :], op=ALU.mult))
            chain_v = inc("vector", sV)
            for fc in range(8):
                par = fc % 2
                s = src_fn(fc, tt)
                t1, t2 = pairs[par]
                if mod_wait and tt == 0 and fc == 0:
                    wt("scalar", sV, mod_wait)
                aeng, asem = "vector", sV
                wt(aeng, sS, act_par[par])
                em(aeng, lambda E, s=s, t1=t1: E.tensor_tensor(out=t1, in0=s,
                                                               in1=stch[:, 1, :], op=ALU.mult))
                em(aeng, lambda E, t1=t1, t2=t2: E.tensor_tensor(out=t2, in0=t1,
                                                                 in1=stch[:, 2, :],
                                                                 op=ALU.subtract))
                av = inc(aeng, asem)
                wt("scalar", asem, av)
                for hf in range(2):
                    b = 2 * tt + hf
                    em("scalar", lambda E, fc=fc, tt=tt, hf=hf, b=b, t2=t2:
                       E.activation(out=H[:, fc, tt * 512 + hf * 256: tt * 512 + hf * 256 + 256],
                                    in_=t2[:, hf * 256:hf * 256 + 256], func=AF.Identity,
                                    bias=modF[:, svec, fc, b:b + 1],
                                    scale=modF[:, cvec, fc, b:b + 1]))
                act_par[par] = inc("scalar", sS)
                h2d[0] = act_par[par]
            if post_fn is not None:
                post_fn(tt)

        stats(0)
        tails = [0, 0, 0, 0]
        tails[0] = mmv_par[1]
        for tt in range(4):
            chain_apply(tt, tails[tt])
            tails[tt + 1 if tt < 3 else 0] = mmv_par[1]
        return h2d[0]

    wt("vector", sQ, xld)
    h1_done = emit_ln(lambda fc, tt: X[:, fc, tt * 512:(tt + 1) * 512], 1, 0,
                      mod_wait=modA_ready, src_qwait_fn=lambda tt: xld)

    wk_cast = stage_cast(W_d["WkTp"], Wslot, dest_guard=(sT, sT.n))
    wt_pair("tensor", wk_cast)
    wt("tensor", sS, h1_done)

    Hr = [H[:, ic, :].rearrange("p (b t) -> p b t", t=256) for ic in range(8)]
    rope_free = {}
    for i3 in range(16):
        hp, bg = i3 // 2, i3 % 2
        bank = pA if i3 % 2 == 0 else pB
        wt("tensor", sV, rope_free.get(i3 - 2, 0))
        for ic in range(8):
            em("tensor", lambda E, ic=ic, hp=hp, bg=bg, bank=bank:
               E.matmul(bank.rearrange("p (b c) -> p b c", c=128),
                        Wslot[:, ic, hp, :], Hr[ic][:, bg * 4:(bg + 1) * 4, 0:128],
                        start=(ic == 0), stop=(ic == 7)))
        mv = inc("tensor", sT)
        wt("vector", sT, mv)
        bank3 = bank.rearrange("p (b c) -> p b c", c=128)
        em("vector", lambda E, bank=bank: E.stream_shuffle(
            out=ropeB.rearrange("p b c -> p (b c)"), in_=bank, mask=SWAP16))
        em("vector", lambda E, bank3=bank3: E.tensor_tensor(
            out=ropeA, in0=bank3, in1=cosF.unsqueeze(1).to_broadcast([128, 4, 128]), op=ALU.mult))
        em("vector", lambda E: E.tensor_tensor(
            out=ropeB, in0=ropeB, in1=sinF.unsqueeze(1).to_broadcast([128, 4, 128]), op=ALU.mult))
        em("vector", lambda E, hp=hp, bg=bg: E.tensor_tensor(
            out=kS[:, hp, bg * 4:(bg + 1) * 4, :], in0=ropeA, in1=ropeB, op=ALU.add))
        rope_free[i3] = inc("vector", sV)

    wv_cast = stage_cast(W_d["WvT"], Wslot, dest_guard=(sT, sT.n))
    wt_pair("tensor", wv_cast)
    vt_free = {}
    for i3 in range(16):
        hp, bg = i3 // 2, i3 % 2
        bank = pA if i3 % 2 == 0 else pB
        tbank = pE if i3 % 2 == 0 else pF
        wtv("tensor", vt_free.get(i3 - 2, (sV, 0)))
        for ic in range(8):
            em("tensor", lambda E, ic=ic, hp=hp, bg=bg, bank=bank:
               E.matmul(bank.rearrange("p (b c) -> p b c", c=128),
                        Wslot[:, ic, hp, :], Hr[ic][:, bg * 4:(bg + 1) * 4, 0:128],
                        start=(ic == 0), stop=(ic == 7)))
        mv = inc("tensor", sT)
        wt("vector", sT, mv)
        em("vector", lambda E, bank=bank:
           E.tensor_copy(out=vtmp.rearrange("p b c -> p (b c)"), in_=bank))
        cv = inc("vector", sV)
        wt("tensor", sV, cv)
        for br in range(4):
            em("tensor", lambda E, br=br, tbank=tbank:
               E.transpose(tbank[:, br * 64:(br + 1) * 64].bitcast(BF16), vtmp[:, br, :], identbf))
        tv = inc("tensor", sT)
        wt("vector", sT, tv)
        em("vector", lambda E, hp=hp, bg=bg, tbank=tbank:
           E.tensor_copy(out=vS[:, bg * 4:(bg + 1) * 4, hp, :],
                         in_=tbank[:, 0:256].bitcast(BF16).rearrange("p (b c) -> p b c", c=128)))
        vt_free[i3] = (sV, inc("vector", sV))
    vproj_done_sT = sT.n

    wff_cast = {}
    wff_pending = [(n, q) for n in ("uf", "ui", "uc", "wf") for q in range(4)]

    def wff_feed(k=2):
        for _ in range(k):
            if not wff_pending:
                return
            n, q = wff_pending.pop(0)
            g = (sS, h1_done) if (n == "uf" and q in (0, 1)) else None
            stage_cast_q(W_d[n + "T"], Wff[n], q, dest_guard=g)
            if q == 3:
                wff_cast[n] = (stage_cast.lastcast[0], stage_cast.lastcast[1])

    def _finish_trunc():
        em("gpsimd", lambda E, v=sV.n: E.wait_ge(sV.h, v))
        em("gpsimd", lambda E, v=sT.n: E.wait_ge(sT.h, v))
        em("gpsimd", lambda E, v=sS.n: E.wait_ge(sS.h, v))
        with nc.Block() as block:
            @block.tensor
            def _(E):
                for fn in ops["tensor"]:
                    fn(E)
            @block.vector
            def _(E):
                for fn in ops["vector"]:
                    fn(E)
            @block.scalar
            def _(E):
                for fn in ops["scalar"]:
                    fn(E)
            @block.sync
            def _(E):
                for fn in ops["sync"]:
                    fn(E)
            @block.gpsimd
            def _(E):
                for fn in ops["gpsimd"]:
                    fn(E)
        st.close()
        return nc

    if trunc == "P3":
        return _finish_trunc()

    wt("tensor", sV, qbd_ready)
    probs_free = [0, 0]
    pEF_free_t = (sV, 0)
    ev_vec = [0]
    pT_free = 0
    pS1_free = 0
    probs_written = 0
    P4B = int(os.environ.get("K_P4B", "8"))
    P4S = int(os.environ.get("K_P4S", "99"))
    for b in range(min(8, P4B)):
        if b >= 1:
            wff_feed(2)
        pbuf = probsb[b % 2]
        for ch in range(2):
            wt("tensor", sV, pS1_free)
            for hp in range(8):
                em("tensor", lambda E, hp=hp, b=b, ch=ch:
                   E.matmul(pS1[:, hp * 64:(hp + 1) * 64], qbd[:, hp, :],
                            kS[:, hp, b, ch * 64:(ch + 1) * 64],
                            start=(hp == 0), stop=(hp == 7)))
            lcv = inc("tensor", sT)
            if P4S < 2:
                continue
            wt("tensor", sV, pT_free)
            for hp in range(8):
                em("tensor", lambda E, hp=hp, b=b, ch=ch:
                   E.matmul(pT[0:64, hp * 128:(hp + 1) * 128],
                            kS[:, hp, b, ch * 64:(ch + 1) * 64], qbd[:, hp, :],
                            start=(hp in (0, 4)), stop=False))
            tTv = inc("tensor", sT)
            if P4S < 3:
                continue
            wt("vector", sT, lcv)
            wt("vector", sS, probs_written)
            pS13 = pS1.rearrange("p (h c) -> p h c", c=64)
            esc3 = esc[:, 0:512].rearrange("p (h c) -> p h c", c=64)
            em("vector", lambda E, pS13=pS13: E.reduce_max(out=Rsc[:, 0:8], in_=pS13, axis=AXL.X))
            em("vector", lambda E, pS13=pS13, esc3=esc3: E.tensor_tensor(
                out=esc3, in0=pS13,
                in1=Rsc[:, 0:8].unsqueeze(2).to_broadcast([128, 8, 64]), op=ALU.subtract))
            em("vector", lambda E, esc3=esc3: E.tensor_tensor(
                out=esc3, in0=esc3,
                in1=masklc.unsqueeze(1).to_broadcast([128, 8, 64]), op=ALU.add))
            pS1_free = inc("vector", sV)
            wt("scalar", sV, pS1_free)
            em("scalar", lambda E: E.activation(out=esc[:, 512:1024], in_=esc[:, 0:512],
                                                func=AF.Exp, bias=0.0, scale=0.125))
            ev = inc("scalar", sS)
            wt("vector", sS, ev)
            em("vector", lambda E, ch=ch: E.reduce_sum(
                out=Rsc[:, 8 + 8 * ch:16 + 8 * ch],
                in_=esc[:, 512:1024].rearrange("p (h c) -> p h c", c=64),
                axis=AXL.X))
            rvv = inc("vector", sV)
            if ch == 1:
                vedge()
                em("vector", lambda E: E.reciprocal(
                    out=rdt.rearrange("p a b -> p (a b)"), in_=Rsc[:, 8:24]))
                inc("vector", sV)
            if P4S < 4:
                continue
            wt("tensor", sV, rvv)
            em("tensor", lambda E: E.transpose(pS2[0:8, 0:128], Rsc[:, 0:8], ident))
            trv = inc("tensor", sT)
            wt("vector", sT, trv)
            em("vector", lambda E: E.tensor_copy(out=RT8, in_=pS2[0:8, 0:128]))
            rt8v = inc("vector", sV)
            if P4S < 5:
                continue
            wt("tensor", sV, rt8v)
            for hp in range(8):
                em("tensor", lambda E, hp=hp:
                   E.matmul(pT[0:64, hp * 128:(hp + 1) * 128], negeye[:, hp, :], RT8,
                            start=False, stop=(hp in (3, 7))))
            rsub = inc("tensor", sT)
            wt("vector", sT, rsub)
            pT3 = pT[0:64, :].rearrange("p (h c) -> p h c", c=128)
            esc4 = esc[0:64, :].rearrange("p (h c) -> p h c", c=128)
            em("vector", lambda E, pT3=pT3, esc4=esc4: E.tensor_tensor(
                out=esc4, in0=pT3, in1=maskT.unsqueeze(1).to_broadcast([64, 8, 128]), op=ALU.add))
            pT_free = inc("vector", sV)
            mkv = pT_free
            wt("scalar", sV, mkv)
            wt("scalar", sT, probs_free[b % 2])
            em("scalar", lambda E, pbuf=pbuf, ch=ch:
               E.activation(out=pbuf[64 * ch:64 * ch + 64, :, :].rearrange("p h c -> p (h c)"),
                            in_=esc[0:64, :], func=AF.Exp, bias=0.0, scale=0.125))
            probs_written = inc("scalar", sS)
        if P4S < 6:
            continue
        wt("tensor", sS, probs_written)
        wtv("tensor", pEF_free_t)
        for ch in range(2):
            cbank = pE if ch == 0 else pF
            for hp in range(8):
                for ph in range(2):
                    em("tensor", lambda E, ch=ch, hp=hp, ph=ph, b=b, cbank=cbank:
                       E.matmul(cbank[64 * ph:64 * ph + 64, hp * 64:(hp + 1) * 64],
                                probsb[b % 2][64 * ch:64 * ch + 64, hp, ph * 64:ph * 64 + 64],
                                vS[64 * ch:64 * ch + 64, b, hp, ph * 64:ph * 64 + 64],
                                start=True, stop=True, skip_group_check=True))
        probs_free[b % 2] = inc("tensor", sT)
        if P4S < 7:
            continue
        wt("vector", sT, probs_free[b % 2])
        em("vector", lambda E: E.tensor_tensor(
            out=app[:, 0, :].rearrange("p (h d) -> p h d", d=64),
            in0=pE.rearrange("p (h d) -> p h d", d=64),
            in1=rdt[:, 0, :].unsqueeze(2).to_broadcast([128, 8, 64]), op=ALU.mult))
        em("vector", lambda E: E.tensor_tensor(
            out=app[:, 1, :].rearrange("p (h d) -> p h d", d=64),
            in0=pF.rearrange("p (h d) -> p h d", d=64),
            in1=rdt[:, 1, :].unsqueeze(2).to_broadcast([128, 8, 64]), op=ALU.mult))
        em("vector", lambda E, b=b:
           E.tensor_tensor(out=attnB[:, b, :, :].rearrange("p h d -> p (h d)"),
                           in0=app[:, 0, :], in1=app[:, 1, :], op=ALU.add))
        evv = inc("vector", sV)
        pEF_free_t = (sV, evv)
        ev_vec[0] = evv
        if b == 3:
            wt("gpsimd", sV, ev_vec[0])
            arw1 = dma("gpsimd", sG, arin1[:, :],
                       attnB.rearrange("p b h d -> p (b h d)")[:, 0:2048])
            wt("gpsimd", sG, arw1)
            em("gpsimd", lambda E: E.collective_compute(
                "AllReduce", ALU.add, replica_groups=[list(range(NCORES))],
                ins=[arin1.ap().opt()], outs=[arout1.ap().opt()]).then_inc(sC.h, 1))
            cc1 = sC.bump(1)

    wff_feed(len(wff_pending))

    if trunc == "P4":
        return _finish_trunc()

    wt("gpsimd", sV, ev_vec[0])
    arw2 = dma("gpsimd", sG, arin2[:, :],
               attnB.rearrange("p b h d -> p (b h d)")[:, 2048:4096])
    wt("gpsimd", sG, arw2)
    em("gpsimd", lambda E: E.collective_compute(
        "AllReduce", ALU.add, replica_groups=[list(range(NCORES))],
        ins=[arin2.ap().opt()], outs=[arout2.ap().opt()]).then_inc(sC.h, 1))
    cc2 = sC.bump(1)
    wt("gpsimd", sC, cc1)
    ar_r1 = dma("gpsimd", sG, attnB.rearrange("p b h d -> p (b h d)")[:, 0:2048],
                arout1[:, :])
    wt("gpsimd", sC, cc2)
    ar_r = dma("gpsimd", sG, attnB.rearrange("p b h d -> p (b h d)")[:, 2048:4096],
               arout2[:, :])

    if trunc == "CC":
        wt("gpsimd", sG, ar_r)
        return _finish_trunc()

    wg_cast = stage_cast(W_d["WgT"], Wslot, dest_guard=(sT, vproj_done_sT))
    wt_pair("tensor", wg_cast)
    g_free = [0, 0]
    eg_free_t = (sV, 0)
    egb = esc[:, 0:256].bitcast(BF16)
    for pc in range(8):
        for tt in range(4):
            it_ = pc * 4 + tt
            bank = pA if it_ % 2 == 0 else pB
            wt("tensor", sS, g_free[it_ % 2])
            for ic in range(8):
                em("tensor", lambda E, ic=ic, pc=pc, tt=tt, bank=bank:
                   E.matmul(bank, Wslot[:, ic, pc, :], H[:, ic, tt * 512:(tt + 1) * 512],
                            start=(ic == 0), stop=(ic == 7)))
            gv = inc("tensor", sT)
            wt("scalar", sT, gv)
            wtv("scalar", eg_free_t)
            em("scalar", lambda E, bank=bank:
               E.activation(out=egb, in_=bank, func=AF.Exp, bias=0.0, scale=0.125))
            g_free[it_ % 2] = inc("scalar", sS)
            wt("tensor", sS, g_free[it_ % 2])
            em("tensor", lambda E: E.matmul(pS2[0:2, 0:512], ind2, egb, start=True, stop=True))
            dnv = inc("tensor", sT)
            wt("vector", sT, dnv)
            em("vector", lambda E: E.reciprocal(out=app[0:2, 0, :], in_=pS2[0:2, 0:512]))
            vedge()
            em("vector", lambda E: E.tensor_copy(out=rden, in_=app[0:2, 0, :]))
            rdv = inc("vector", sV)
            wt("tensor", sV, rdv)
            wtv("tensor", eg_free_t)
            em("tensor", lambda E: E.matmul(pS1, ind2T, rden, start=True, stop=True))
            exv = inc("tensor", sT)
            wt("vector", sT, exv)
            em("vector", lambda E, pc=pc, tt=tt:
               E.tensor_tensor(out=GW[:, pc, tt * 512:(tt + 1) * 512], in0=egb, in1=pS1,
                               op=ALU.mult))
            eg_free_t = (sV, inc("vector", sV))

    if trunc == "P5":
        return _finish_trunc()

    wp_cast = stage_cast(W_d["WpT"], Wslot, dest_guard=(sT, sT.n))



    wt("tensor", sG, ar_r1)
    wtv("tensor", eg_free_t)
    w_free = [(sV, 0), (sS, 0)]
    for b in range(8):
        if b == 4:
            wt("tensor", sG, ar_r)
        for hp in range(8):
            it_ = hp * 8 + b
            bank = pA if it_ % 2 == 0 else pB
            wtv("tensor", w_free[it_ % 2])
            for ph in range(2):
                em("tensor", lambda E, hp=hp, b=b, ph=ph, bank=bank:
                   E.matmul(bank[64 * ph:64 * ph + 64, 0:256],
                            attnB[64 * ph:64 * ph + 64, b, hp, :],
                            GW[64 * ph:64 * ph + 64, hp, b * 256:(b + 1) * 256],
                            start=True, stop=True, skip_group_check=True))
            wv = inc("tensor", sT)
            if it_ % 2 == 0:
                wt("vector", sT, wv)
                em("vector", lambda E, hp=hp, b=b, bank=bank:
                   E.tensor_copy(out=GW[:, hp, b * 256:(b + 1) * 256], in_=bank[:, 0:256]))
                w_free[0] = (sV, inc("vector", sV))
            else:
                wt("scalar", sT, wv)
                em("scalar", lambda E, hp=hp, b=b, bank=bank:
                   E.activation(out=GW[:, hp, b * 256:(b + 1) * 256], in_=bank[:, 0:256],
                                func=AF.Identity))
                w_free[1] = (sS, inc("scalar", sS))

    if trunc == "P6":
        return _finish_trunc()

    wt_pair("tensor", wp_cast)
    wtv("tensor", w_free[0])
    wtv("tensor", w_free[1])
    wt("vector", sV, modB_ready)
    x_r = x_d.rearrange("(f p) b t -> p f b t", p=128)
    x2_r = x2_d.rearrange("(f p) b t -> p f b t", p=128)
    p7_free = [0, 0]
    x2b_free = [0, 0]
    xrd_v = [0, 0]
    xrd_used = [0, 0]
    wt("gpsimd", sT, sT.n)
    for oc in range(8):
        wt("gpsimd", sV, xrd_used[oc % 2])
        xrd_v[oc % 2] = dma("gpsimd", sG, xrd[oc % 2].rearrange("p b t -> p (b t)"),
                            x_r[:, oc, :, :].rearrange("p b t -> p (b t)"))
        for tt in range(4):
            it_ = oc * 4 + tt
            bank = pA if it_ % 2 == 0 else pB
            wt("tensor", sV, p7_free[it_ % 2])
            for ic in range(8):
                em("tensor", lambda E, ic=ic, oc=oc, tt=tt, bank=bank:
                   E.matmul(bank, Wslot[:, ic, oc, :], GW[:, ic, tt * 512:(tt + 1) * 512],
                            start=(ic == 0), stop=(ic == 7)))
            pv = inc("tensor", sT)
            wt("vector", sT, pv)
            wt("vector", sG, xrd_v[oc % 2])
            wt("vector", sG, x2b_free[tt % 2])
            for hf in range(2):
                b = tt * 2 + hf
                em("vector", lambda E, oc=oc, tt=tt, hf=hf, b=b, bank=bank:
                   E.scalar_tensor_tensor(
                       out=x2b[tt % 2][:, hf, :],
                       in0=bank[:, hf * 256:hf * 256 + 256],
                       scalar=modF[:, 2, oc, b:b + 1],
                       in1=xrd[oc % 2][:, b, :],
                       op0=ALU.mult, op1=ALU.add))
            p7_free[it_ % 2] = inc("vector", sV)
            xrd_used[oc % 2] = p7_free[it_ % 2]
            wt("gpsimd", sV, p7_free[it_ % 2])
            x2b_free[tt % 2] = dma(
                "gpsimd", sG,
                x2_r[:, oc, 2 * tt:2 * tt + 2, :].rearrange("p b t -> p (b t)"),
                x2b[tt % 2].rearrange("p a t -> p (a t)"))
    x2_written = sG.n
    p7_done_sT = sT.n
    for n in ("wi", "wc"):
        wff_cast[n] = direct_weight(n, Wff[n], dest_guard=(sT, p7_done_sT) if n == "wi" else None)
    if trunc == "P7":
        return _finish_trunc()

    x2s_free = [(0, 0), (0, 0)]
    if trunc == "P7b":
        return _finish_trunc()

    x2bufs = [x2stg, x2stgB]
    p8_lv = [0, 0]

    def p8_load(tt):
        wt("sync", sG, x2_written)
        if tt % 2 == 0:
            wt("sync", sV, stage_cast.lastcast[0])
            wt("sync", sS, stage_cast.lastcast[1])
        wt("sync", sV, x2s_free[tt % 2][0])
        wt("sync", sP, x2s_free[tt % 2][1])
        return dma("sync", sQ, x2bufs[tt % 2], x2_r[:, :, 2 * tt:2 * tt + 2, :])

    def p8_prep(tt):
        p8_lv[tt % 2] = p8_load(tt)
        wt("vector", sQ, p8_lv[tt % 2])

    def p8_post(tt):
        x2s_free[tt % 2] = (sV.n, sP.n)

    h2_done = emit_ln(lambda fc, tt: x2bufs[tt % 2][:, fc, :, :].rearrange("p a t -> p (a t)"),
                      4, 3, prep_fn=p8_prep, post_fn=p8_post,
                      src_qwait_fn=lambda tt: p8_lv[tt % 2])

    wt("sync", sS, h2_done)
    c0v = dma("sync", sQ, cS.rearrange("p b t -> p (b t)"), inc0_d.rearrange("p b t -> p (b t)"))
    wt("vector", sQ, c0v)
    em("vector", lambda E: E.tensor_copy(out=cbf[0].rearrange("p b t -> p (b t)"),
                                         in_=cS.rearrange("p b t -> p (b t)")))
    c0_ready = inc("vector", sV)

    wt("sync", sS, h2_done)
    wov = stage_cast(W_d["woT"], Wff["wo"], dest_guard=(sT, p7_done_sT))

    wt("tensor", sV, c0_ready)
    wt("tensor", sS, h2_done)

    GATES = (("f", AF.Sigmoid, 0), ("i", AF.Sigmoid, 1), ("c", AF.Tanh, 2))
    gate_waited = set()
    bank_free = {}
    step_done = 0
    for t in range(8):
        cbf_r = cbf[t % 2]
        cbf_w = cbf[(t + 1) % 2]
        wt("tensor", sV, step_done)
        for oc in range(8):
            par = oc % 2
            banks = {"f": (pA if par == 0 else pB),
                     "i": (pE if par == 0 else pF),
                     "c": (pT[:, 0:256] if par == 0 else pT[:, 512:768])}
            gv = {}
            for gname, gfunc, gidx in GATES:
                if gname not in gate_waited:
                    gate_waited.add(gname)
                    wt_pair("tensor", wff_cast["w" + gname])
                    wt_pair("tensor", wff_cast["u" + gname])
                wt("tensor", sV, bank_free.get((gname, par), 0))
                wname, uname = "w" + gname, "u" + gname
                for ic in range(8):
                    em("tensor", lambda E, ic=ic, oc=oc, t=t, gn=gname, bank=banks[gname]:
                       E.matmul(bank[:, 0:256] if bank.shape[1] > 256 else bank,
                                Wff["w" + gn][:, ic, oc, :], H[:, ic, t * 256:(t + 1) * 256],
                                start=(ic == 0), stop=False))
                for ic in range(8):
                    em("tensor", lambda E, ic=ic, oc=oc, gn=gname, bank=banks[gname], cb=cbf_r:
                       E.matmul(bank[:, 0:256] if bank.shape[1] > 256 else bank,
                                Wff["u" + gn][:, ic, oc, :], cb[:, ic, :],
                                start=False, stop=(ic == 7)))
                gv[gname] = inc("tensor", sT)
            dst = {"f": gscF, "i": gscI, "c": gscG}
            for gname, gfunc, gidx in GATES:
                wt("scalar", sT, gv[gname])
                em("scalar", lambda E, gn=gname, gf=gfunc, gi=gidx, oc=oc, par=par,
                   bank=banks[gname]:
                   E.activation(out=dst[gn][:, par, :],
                                in_=bank[:, 0:256] if bank.shape[1] > 256 else bank,
                                func=gf, bias=fbias[:, gi, oc:oc + 1], scale=1.0))
            actv = inc("scalar", sS)
            wt("vector", sS, actv)
            em("vector", lambda E, oc=oc, par=par: E.tensor_tensor(
                out=cS[:, oc, :], in0=gscF[:, par, :], in1=cS[:, oc, :], op=ALU.mult))
            em("vector", lambda E, par=par: E.tensor_tensor(
                out=gscI[:, par, :], in0=gscI[:, par, :], in1=gscG[:, par, :], op=ALU.mult))
            em("vector", lambda E, oc=oc, par=par: E.tensor_tensor(
                out=cS[:, oc, :], in0=cS[:, oc, :], in1=gscI[:, par, :], op=ALU.add))
            em("vector", lambda E, oc=oc, cw=cbf_w: E.tensor_copy(
                out=cw[:, oc, :], in_=cS[:, oc, :]))
            bv = inc("vector", sV)
            for gname, _, _ in GATES:
                bank_free[(gname, par)] = bv
        step_done = sV.n

    uov = stage_cast(W_d["uoT"], Wff["uo"], dest_guard=(sT, sT.n))

    wt_pair("tensor", uov)
    wt("tensor", sS, sS.n)
    cbf_fin = cbf[0]
    for oc in range(8):
        bank = pE if oc % 2 == 0 else pF
        for ic in range(8):
            em("tensor", lambda E, ic=ic, oc=oc, bank=bank:
               E.matmul(bank[:, 0:256], Wff["uo"][:, ic, oc, :], cbf_fin[:, ic, :],
                        start=(ic == 0), stop=(ic == 7)))
        uv = inc("tensor", sT)
        wt("scalar", sT, uv)
        em("scalar", lambda E, oc=oc, bank=bank:
           E.activation(out=uodv[:, oc, :], in_=bank[:, 0:256], func=AF.Identity,
                        bias=fbias[:, 3, oc:oc + 1], scale=1.0))
    uod_done = inc("scalar", sS)

    wt_pair("tensor", wov)
    wt("tensor", sS, uod_done)
    xt_v = [0, 0]
    xt_used = [0, 0]
    tail_free = [0, 0]
    ybuf_free = [0, 0]
    yv_last = 0
    for oc in range(8):
        wt("sync", sV, max(xt_used[oc % 2], uov[0]))
        wt("sync", sS, uov[1])
        xt_v[oc % 2] = dma("sync", sQ, xt[oc % 2].rearrange("p b t -> p (b t)"),
                           x2_r[:, oc, :, :].rearrange("p b t -> p (b t)"))
        for t in range(8):
            par = (oc * 8 + t) % 2
            bank = pA if par == 0 else pB
            wt("tensor", sV, tail_free[par])
            for ic in range(8):
                em("tensor", lambda E, ic=ic, oc=oc, t=t, bank=bank:
                   E.matmul(bank[:, 0:256], Wff["wo"][:, ic, oc, :],
                            H[:, ic, t * 256:(t + 1) * 256],
                            start=(ic == 0), stop=(ic == 7)))
            xv = inc("tensor", sT)
            wt("vector", sT, xv)
            em("vector", lambda E, oc=oc, bank=bank, par=par: E.tensor_tensor(
                out=app[:, par, 0:256], in0=bank[:, 0:256], in1=uodv[:, oc, :], op=ALU.add))
            s1v = inc("vector", sV)
            wt("scalar", sV, s1v)
            em("scalar", lambda E, par=par: E.activation(
                out=app[:, par, 256:512], in_=app[:, par, 0:256], func=AF.Tanh))
            s2v = inc("scalar", sS)
            ybuf = ybufA if t < 4 else ybufB
            yb = 0 if t < 4 else 1
            wt("vector", sS, s2v)
            wt("vector", sQ, xt_v[oc % 2])
            if t in (0, 4):
                wt("vector", sO, ybuf_free[yb])
            em("vector", lambda E, oc=oc, t=t, par=par, ybuf=ybuf: E.scalar_tensor_tensor(
                out=ybuf[:, t % 4, :], in0=app[:, par, 256:512],
                scalar=modF[:, 5, oc, t:t + 1], in1=xt[oc % 2][:, t, :],
                op0=ALU.mult, op1=ALU.add))
            tail_free[par] = inc("vector", sV)
            xt_used[oc % 2] = tail_free[par]
            if t in (3, 7):
                wt("gpsimd", sV, tail_free[par])
                y_r = y_d.rearrange("(f p) b t -> p f b t", p=128)
                tb = t // 4
                ybuf_free[yb] = dma("gpsimd", sO,
                                    y_r[:, oc, 4 * tb:4 * tb + 4, :].rearrange("p b t -> p (b t)"),
                                    ybuf.rearrange("p a t -> p (a t)"))
                yv_last = ybuf_free[yb]

    if DBG:
        wt("gpsimd", sV, sV.n)
        wt("gpsimd", sS, sS.n)
        wt("gpsimd", sT, sT.n)
        dma("gpsimd", sO, dbg_d["h2"].bitcast(BF16), H.rearrange("p a b -> p (a b)"))
        dma("gpsimd", sO, dbg_d["x2"], x2_d)
        dma("gpsimd", sO, dbg_d["cfin"], cS.rearrange("p a b -> p (a b)"))
        dma("gpsimd", sO, dbg_d["uod"], uodv.rearrange("p a b -> p (a b)"))
    em("gpsimd", lambda E, v=sO.n: E.wait_ge(sO.h, v))

    with nc.Block() as block:
        @block.tensor
        def _(E):
            for fn in ops["tensor"]:
                fn(E)

        @block.vector
        def _(E):
            for fn in ops["vector"]:
                fn(E)

        @block.scalar
        def _(E):
            for fn in ops["scalar"]:
                fn(E)

        @block.sync
        def _(E):
            for fn in ops["sync"]:
                fn(E)

        @block.gpsimd
        def _(E):
            for fn in ops["gpsimd"]:
                fn(E)

    st.close()
    return nc


DBG_SRC = {}

_prog_cache = {}


def kernel(**inputs):
    shared = _shared_layouts(inputs)
    nc = build_program()
    in_maps = []
    for core in range(NCORES):
        m = dict(shared)
        m.update(_host_constants(core))
        m["x_fm"] = _core_x(inputs, core)
        dm = {"x_fm": m["x_fm"], "wmod": m["wmod"], "cT": m["cT"], "qbd": m["qbd"],
              "ident": m["ident"], "ident_bf": m["ident_bf"], "ones_bf": m["ones_bf"],
              "negeye": m["negeye"], "ind2": m["ind2"], "ind2T": m["ind2T"],
              "mask_lc": m["mask_lc"], "mask_T": m["mask_T"],
              "cosF": m["cosF"], "sinF": m["sinF"], "in_c0": m["in_c0"]}
        for n in ("WkTp", "WvT", "WgT", "WpT", "wfT", "ufT", "wiT", "uiT", "wcT", "ucT",
                  "woT", "uoT"):
            dm[n] = m[n]
        for n in ("bf", "bi", "bc", "bo"):
            dm[n + "r"] = m[n + "r"]
        in_maps.append(dm)
    res = run_bass_kernel_spmd(nc, in_maps, core_ids=list(range(NCORES)))
    return _assemble_output([res.results[i]["y_fm"] for i in range(NCORES)])


# version 21 (final)
# speedup vs baseline: 1.1359x; 1.0504x over previous
import os
import numpy as np
import ml_dtypes
from contextlib import ExitStack

import concourse.bass as bass
import concourse.mybir as mybir
from concourse.bass_utils import run_bass_kernel_spmd

F32 = mybir.dt.float32
BF16 = mybir.dt.bfloat16
U16 = mybir.dt.uint16
AF = mybir.ActivationFunctionType
ALU = mybir.AluOpType
AXL = mybir.AxisListType

NCORES = 8
B = 8
TL = 256
NEG = -1.0e30
SWAP16 = list(range(16, 32)) + list(range(16))

_PERM = np.concatenate([np.arange(0, 64), np.arange(128, 192),
                        np.arange(64, 128), np.arange(192, 256)])
_IPERM = np.argsort(_PERM)


def _dperm():
    p = np.arange(64)
    q16 = p // 16
    j = (q16 // 2) * 16 + p % 16
    return 2 * j + q16 % 2


_DPERM = _dperm()


def _kperm():
    out = np.zeros(1024, np.int64)
    for hp in range(8):
        for p in range(128):
            out[hp * 128 + p] = (2 * hp + p // 64) * 64 + _DPERM[p % 64]
    return out


_KPERM = _kperm()


def _bf(a):
    return np.ascontiguousarray(a.astype(ml_dtypes.bfloat16).view(np.uint16))


def _host_constants(core_id):
    c = {}
    c["ident"] = np.eye(128, dtype=np.float32)
    c["ident_bf"] = _bf(np.eye(128, dtype=np.float32))
    c["ones_bf"] = _bf(np.ones((128, 128), np.float32))
    ne = np.zeros((8, 8, 64), np.float32)
    for k in range(8):
        ne[k, k, :] = -1.0
    c["negeye"] = _bf(ne)
    ind2 = np.zeros((128, 2), np.float32)
    ind2[:64, 0] = 1.0
    ind2[64:, 1] = 1.0
    c["ind2"] = _bf(ind2)
    c["ind2T"] = _bf(ind2.T.copy())
    ind4 = np.zeros((2, 128, 4), np.float32)
    for j in range(2):
        ind4[j, :64, 2 * j] = 1.0
        ind4[j, 64:, 2 * j + 1] = 1.0
    c["ind4a"] = _bf(ind4[0])
    c["ind4b"] = _bf(ind4[1])
    c["ind4T0"] = _bf(ind4[0].T.copy())
    c["ind4T1"] = _bf(ind4[1].T.copy())
    l_of_p = np.arange(128) % 64
    cc = np.arange(64)
    c["mask_lc"] = np.where(cc[None, :] <= l_of_p[:, None], 0.0, NEG).astype(np.float32)
    c["mask_T"] = np.where(cc[:, None] <= l_of_p[None, :], 0.0, NEG).astype(np.float32)
    p = np.arange(128) % 64
    q16 = p // 16
    j = (q16 // 2) * 16 + p % 16
    is_b = q16 % 2
    inv = 1.0 / 10000.0 ** (np.arange(0, 64, 2, dtype=np.float32) / 64.0)
    c2 = np.arange(128)
    pos = core_id * 256 + (c2 // 64) * 128 + c2 % 64
    ang = pos[None, :].astype(np.float32) * inv[j][:, None]
    c["cosF"] = np.cos(ang).astype(np.float32)
    sinF = np.sin(ang).astype(np.float32)
    c["sinF"] = np.where(is_b[:, None] == 0, -sinF, sinF).astype(np.float32)
    return c


def _shared_layouts(inputs):
    g = lambda n: np.asarray(inputs[n], dtype=np.float32)
    s = {}
    s["WkTp"] = np.ascontiguousarray(g("L0_attn_W_k").T[:, _KPERM])
    s["WvT"] = np.ascontiguousarray(g("L0_attn_W_v").T)
    s["WgT"] = np.ascontiguousarray(g("L0_attn_W_g").T)
    s["WpT"] = np.ascontiguousarray(g("L0_attn_W_p").T)
    for n in ("wf", "uf", "wi", "ui", "wc", "uc", "wo", "uo"):
        s[n + "T"] = np.ascontiguousarray(g("L0_ffn_" + n).T)
    s["wmod"] = np.ascontiguousarray(g("L0_wmod"))
    s["cT"] = np.ascontiguousarray(g("L0_c").T)
    lq = g("L0_attn_latent_queries")[0]
    qbd = np.zeros((8, 128, 128), np.float32)
    for hp in range(8):
        for ph in range(2):
            blk = lq[:, 2 * hp + ph, :][:, _DPERM]
            qbd[hp, 64 * ph:64 * ph + 64, 64 * ph:64 * ph + 64] = blk.T
    s["qbd"] = np.ascontiguousarray(qbd)
    for n in ("bf", "bi", "bc", "bo"):
        s[n + "r"] = np.ascontiguousarray(g("L0_ffn_" + n).reshape(8, 128).T)
    ic = g("L0_ffn_in_c").reshape(8, 128).T
    s["in_c0"] = np.ascontiguousarray(np.broadcast_to(ic[:, :, None], (128, 8, 256)))
    return s


def _core_x(inputs, core_id):
    x = np.asarray(inputs["L0_attn_x"], dtype=np.float32)
    xs = x[:, core_id * 256:(core_id + 1) * 256, :][:, _PERM, :]
    return np.ascontiguousarray(xs.transpose(2, 0, 1))


def _assemble_output(shards):
    y = np.empty((8, 2048, 1024), np.float32)
    for i, sh in enumerate(shards):
        ys = np.asarray(sh).reshape(1024, 8, 256).transpose(1, 2, 0)
        y[:, i * 256:(i + 1) * 256, :] = ys[:, _IPERM, :]
    return y


class Sem:
    def __init__(self, h):
        self.h = h
        self.n = 0

    def bump(self, k=1):
        self.n += k
        return self.n


def _view(arena, off, nslots, dtype, shape):
    ap = arena[:, off:off + nslots]
    if dtype == BF16:
        ap = ap.bitcast(BF16)
    total = 1
    for s in shape:
        total *= s
    assert total == ap.shape[1], (shape, ap.shape)
    if len(shape) > 1:
        pat = " ".join(f"d{i}" for i in range(len(shape)))
        ap = ap.rearrange(f"p ({pat}) -> p {pat}",
                          **{f"d{i}": shape[i] for i in range(len(shape))})
    return ap


FFN_W = ("uf", "ui", "uc", "wf", "wi", "wc", "wo", "uo")


def build_program(debug=(), trunc=None):
    nc = bass.Bass("TRN2", target_bir_lowering=False, detect_race_conditions=False)
    DI = lambda name, shape, dt=F32: nc.dram_tensor(name, shape, dt, kind="ExternalInput").ap()

    x_d = DI("x_fm", [1024, B, TL])
    W_d = {n: DI(n, [1024, 1024]) for n in
           ("WkTp", "WvT", "WgT", "WpT", "wfT", "ufT", "wiT", "uiT", "wcT", "ucT", "woT", "uoT")}
    wmod_d = DI("wmod", [1024, 6144])
    cT_d = DI("cT", [1024, 8])
    qbd_d = DI("qbd", [8, 128, 128])
    ident_d = DI("ident", [128, 128])
    identbf_d = DI("ident_bf", [128, 128], U16)
    onesbf_d = DI("ones_bf", [128, 128], U16)
    negeye_d = DI("negeye", [8, 8, 64], U16)
    ind2_d = DI("ind2", [128, 2], U16)
    ind2T_d = DI("ind2T", [2, 128], U16)
    ind4a_d = DI("ind4a", [128, 4], U16)
    ind4b_d = DI("ind4b", [128, 4], U16)
    ind4T0_d = DI("ind4T0", [4, 128], U16)
    ind4T1_d = DI("ind4T1", [4, 128], U16)
    masklc_d = DI("mask_lc", [128, 64])
    maskT_d = DI("mask_T", [64, 128])
    cosF_d = DI("cosF", [128, 128])
    sinF_d = DI("sinF", [128, 128])
    bias_d = {n: DI(n + "r", [128, 8]) for n in ("bf", "bi", "bc", "bo")}
    inc0_d = DI("in_c0", [128, 8, 256])

    y_d = nc.dram_tensor("y_fm", [1024, B, TL], F32, kind="ExternalOutput").ap()
    x2_d = nc.dram_tensor("x2_scr", [1024, B, TL], BF16).ap()
    arin1 = nc.dram_tensor("ar_in1", [128, 2048], BF16)
    arout1 = nc.dram_tensor("ar_out1", [128, 2048], BF16)
    arin2 = nc.dram_tensor("ar_in2", [128, 2048], BF16)
    arout2 = nc.dram_tensor("ar_out2", [128, 2048], BF16)
    arwi = nc.dram_tensor("ar_wi", [128, 64], BF16)
    arwo = nc.dram_tensor("ar_wo", [128, 64], BF16)

    DBG = os.environ.get("K_DBG", "") == "1"
    dbg_d = {}
    if DBG:
        dbg_d["gates"] = nc.dram_tensor("dbg_gates", [128, 16384], U16, kind="ExternalOutput").ap()
        dbg_d["attnB"] = nc.dram_tensor("dbg_attnB", [128, 4096], U16, kind="ExternalOutput").ap()
        dbg_d["attnP"] = nc.dram_tensor("dbg_attnP", [128, 4096], U16, kind="ExternalOutput").ap()
        dbg_d["h2"] = nc.dram_tensor("dbg_h2", [128, 16384], U16, kind="ExternalOutput").ap()
        dbg_d["x2"] = nc.dram_tensor("dbg_x2", [1024, B, TL], F32, kind="ExternalOutput").ap()
        dbg_d["cfin"] = nc.dram_tensor("dbg_cfin", [128, 2048], F32, kind="ExternalOutput").ap()
        dbg_d["uod"] = nc.dram_tensor("dbg_uod", [128, 2048], F32, kind="ExternalOutput").ap()

    st = ExitStack()
    sbuf = lambda name, shape, dt=F32: st.enter_context(nc.sbuf_tensor(name, shape, dt))[:]
    psum = lambda name, shape: st.enter_context(nc.psum_tensor(name, shape, F32))[:]

    arena = st.enter_context(nc.sbuf_tensor("arena", [128, 46080], F32))[:]

    ident = sbuf("identt", [128, 128])
    identbf = sbuf("identbft", [128, 128], BF16)
    onesbf = sbuf("onesbft", [128, 128], BF16)
    qbd = sbuf("qbdt", [128, 8, 128], BF16)
    negeye = sbuf("negeyet", [8, 8, 64], BF16)
    ind4s = [sbuf("ind4at", [128, 4], BF16), sbuf("ind4bt", [128, 4], BF16)]
    ind4Ts = [sbuf("ind4T0t", [4, 128], BF16), sbuf("ind4T1t", [4, 128], BF16)]
    masklc = sbuf("masklct", [128, 64])
    maskT = sbuf("maskTt", [64, 128])
    cosF = sbuf("cosFt", [128, 128])
    sinF = sbuf("sinFt", [128, 128])
    modF = sbuf("modFt", [128, 6, 8, 8])
    cTt = sbuf("cTt", [128, 8, 8])
    silubf = sbuf("silubft", [128, 8, 8], BF16)
    Rsc = sbuf("Rsct", [128, 32])
    rdt = sbuf("rdtt", [128, 2, 8])
    RT8 = sbuf("RT8t", [8, 128], BF16)
    fbias = sbuf("fbiast", [128, 4, 8])
    rden = sbuf("rdent", [4, 512], BF16)
    consts = sbuf("constst", [128, 4])
    stch = sbuf("stcht", [128, 3, 512])
    app = sbuf("appt", [128, 2, 512])
    esc = sbuf("esct", [128, 1024])
    bfx = sbuf("bfxt", [128, 2, 512], BF16)
    bfsq = sbuf("bfsqt", [128, 2, 512], BF16)

    nc.const_aps.register(F32, 0.125, consts[:, 0:1])
    nc.const_aps.register(F32, 8.0, consts[:, 1:2])
    nc.const_aps.register(F32, 1e-5, consts[:, 2:3])
    nc.const_aps.register(F32, 1.0 / 1024.0, consts[:, 3:4])

    pA = psum("pA", [128, 512])
    pB = psum("pB", [128, 512])
    pE = psum("pE", [128, 512])
    pF = psum("pF", [128, 512])
    pT = psum("pT", [128, 1024])
    pS1 = psum("pS1", [128, 512])
    pS2 = psum("pS2", [128, 512])

    V = lambda off, n, dt, shape: _view(arena, off, n, dt, shape)
    H = V(0, 8192, BF16, (8, 2048))
    X = V(8192, 16384, F32, (8, 2048))
    Wff = {"uf": V(8192, 4096, BF16, (8, 8, 128)),
           "ui": V(12288, 4096, BF16, (8, 8, 128)),
           "uc": V(16384, 4096, BF16, (8, 8, 128)),
           "wf": V(20480, 4096, BF16, (8, 8, 128)),
           "wi": V(24576, 4096, BF16, (8, 8, 128)),
           "wc": V(28672, 4096, BF16, (8, 8, 128)),
           "wo": V(37888, 4096, BF16, (8, 8, 128)),
           "uo": V(12288, 4096, BF16, (8, 8, 128))}
    uodv = V(8192, 2048, F32, (8, 256))
    modtm = V(24576, 6144, F32, (6144,))
    kS = V(24576, 4096, BF16, (8, 8, 128))
    vS = V(28672, 4096, BF16, (8, 8, 128))
    GW = V(24576, 8192, BF16, (8, 2048))
    qbdf = V(32768, 1024, F32, (8, 128))
    ropeA = V(32768, 512, F32, (4, 128))
    ropeB = V(33280, 512, F32, (4, 128))
    vtmp = V(33792, 256, BF16, (4, 128))
    attnB = V(32768, 2048, BF16, (8, 8, 64))
    xrd = [V(32768, 2048, F32, (8, 256)), V(34816, 2048, F32, (8, 256))]
    cS = V(32768, 2048, F32, (8, 256))
    cbf = [V(34816, 1024, BF16, (8, 256)), V(35840, 1024, BF16, (8, 256))]
    probsb = [V(36864, 512, BF16, (8, 128)), V(37376, 512, BF16, (8, 128))]
    wbounce = [V(36864, 1024, BF16, (2048,)), V(36864, 1024, BF16, (2048,))]

    ptmp7 = V(36864, 1024, F32, (4, 256))
    gscG = V(36864, 512, F32, (2, 256))
    ybufB = V(36864, 1024, F32, (4, 256))
    Wslot = V(37888, 4096, BF16, (8, 8, 128))
    wmodbf = V(37888, 2048, BF16, (2, 8, 256))
    stagh = [V(41984, 2048, F32, (2048,)), V(44032, 2048, F32, (2048,))]
    x2stg = V(41984, 2048, BF16, (8, 2, 256))
    x2stgB = V(32768, 2048, BF16, (8, 2, 256))
    xt = [V(41984, 1024, BF16, (8, 256)), V(44032, 1024, BF16, (8, 256))]
    gscF = esc[:, 0:512].rearrange("p (a t) -> p a t", t=256)
    gscI = esc[:, 512:1024].rearrange("p (a t) -> p a t", t=256)
    x2b = [esc[:, 0:256].bitcast(BF16).rearrange("p (a t) -> p a t", t=256),
           esc[:, 256:512].bitcast(BF16).rearrange("p (a t) -> p a t", t=256)]
    ybufA = esc.rearrange("p (a t) -> p a t", t=256)

    ops = {e: [] for e in ("tensor", "vector", "scalar", "sync", "gpsimd")}
    em = lambda e, fn: ops[e].append(fn)

    sems = {}
    for name in ("sQ", "sQ2", "sG", "sT", "sV", "sS", "sC", "sO", "sP"):
        sems[name] = Sem(st.enter_context(nc.semaphore(name)))
    sQ, sQ2, sG, sT, sV, sS, sC, sO, sP = (sems[k] for k in
                                           ("sQ", "sQ2", "sG", "sT", "sV", "sS", "sC",
                                            "sO", "sP"))

    def dma(eng, sem, out, in_):
        v = sem.bump(16)
        em(eng, lambda E, o=out, i=in_, s=sem.h: E.dma_start(out=o, in_=i).then_inc(s, 16))
        return v

    def wt(eng, sem, v):
        if v:
            em(eng, lambda E, s=sem.h, vv=v: E.wait_ge(s, vv))

    def vedge():
        em("vector", lambda E: E.engine_nop())
        v = inc("vector", sV)
        wt("vector", sV, v)

    def wt_pair(eng, pair):
        wt(eng, sV, pair[0])
        wt(eng, sS, pair[1])

    def wtv(eng, sv):
        if sv[1]:
            wt(eng, sv[0], sv[1])

    def inc(eng, sem, k=1):
        v = sem.bump(k)
        last = ops[eng][-1]

        def wrapped(E, f=last, s=sem.h, kk=k):
            ins = f(E)
            assert ins is not None, "inc() must follow an instruction-emitting op"
            return ins.then_inc(s, kk)
        ops[eng][-1] = wrapped
        return v

    xld = dma("sync", sQ, X.rearrange("p f (b t) -> p f b t", t=256),
              x_d.rearrange("(f p) b t -> p f b t", p=128))
    for out, d in (
            (ident, ident_d), (identbf, identbf_d.bitcast(BF16)), (onesbf, onesbf_d.bitcast(BF16)),
            (qbdf, qbd_d.rearrange("h a b -> a h b")),
            (negeye.rearrange("k h c -> k (h c)"),
             negeye_d.rearrange("k h c -> k (h c)").bitcast(BF16)),
            (ind4s[0], ind4a_d.bitcast(BF16)), (ind4s[1], ind4b_d.bitcast(BF16)),
            (ind4Ts[0], ind4T0_d.bitcast(BF16)), (ind4Ts[1], ind4T1_d.bitcast(BF16)),
            (masklc, masklc_d), (maskT, maskT_d), (cosF, cosF_d), (sinF, sinF_d),
            (cTt, cT_d.rearrange("(i p) b -> p i b", p=128)),
            (fbias[:, 0, :], bias_d["bf"]), (fbias[:, 1, :], bias_d["bi"]),
            (fbias[:, 2, :], bias_d["bc"]), (fbias[:, 3, :], bias_d["bo"])):
        dma("gpsimd", sG, out, d)
    cver = sG.n

    wt("gpsimd", sG, cver)
    wdma = dma("gpsimd", sG, arwi[:, :], identbf[:, 0:64])
    wt("gpsimd", sG, wdma)
    em("gpsimd", lambda E: E.collective_compute(
        "AllReduce", ALU.add, replica_groups=[list(range(NCORES))],
        ins=[arwi.ap().opt()], outs=[arwo.ap().opt()]).then_inc(sC.h, 1))
    sC.bump(1)

    def _initc(E):
        E.memset(consts[:, 0:1], 0.125)
        E.memset(consts[:, 1:2], 8.0)
        E.memset(consts[:, 2:3], 1e-5)
        E.memset(consts[:, 3:4], 1.0 / 1024.0)
    em("vector", _initc)
    wt("vector", sG, cver)
    em("vector", lambda E: E.tensor_copy(out=qbd.rearrange("p h c -> p (h c)"),
                                         in_=qbdf.rearrange("p h c -> p (h c)")))
    qbd_ready = inc("vector", sV)

    wt("scalar", sG, cver)
    em("scalar", lambda E: E.activation(out=silubf.rearrange("p i b -> p (i b)"),
                                        in_=cTt.rearrange("p i b -> p (i b)"), func=AF.Silu))
    silu_ready = inc("scalar", sS)
    wt("tensor", sS, silu_ready)

    wmod_r = wmod_d.rearrange("(i p) j -> p i j", p=128)
    wmod_mm_done = {}
    wmod_cast_done = {}
    mod_tr_last = [0]
    CENG = {0: ("vector", sV), 1: ("scalar", sS)}
    for jt in range(24):
        half = jt % 2
        ceng, csem = CENG[half]
        qeng, qsem = (("sync", sQ) if half == 0 else ("scalar", sQ2))
        if jt >= 2:
            wt(qeng, csem, wmod_cast_done[jt - 2])
        lv = dma(qeng, qsem, stagh[half].rearrange("p (i j) -> p i j", j=256),
                 wmod_r[:, :, jt * 256:(jt + 1) * 256])
        wt(ceng, qsem, lv)
        if jt >= 2:
            wt(ceng, sT, wmod_mm_done[jt - 2])
        dst = wmodbf[:, half, :, :].rearrange("p i j -> p (i j)")
        if half == 0:
            em(ceng, lambda E, d=dst, h=half: E.tensor_copy(out=d, in_=stagh[h]))
        else:
            em(ceng, lambda E, d=dst, h=half: E.activation(out=d, in_=stagh[h],
                                                           func=AF.Identity))
        wmod_cast_done[jt] = inc(ceng, csem)
        wt("tensor", csem, wmod_cast_done[jt])
        bank = pS1 if half == 0 else pS2
        for ic in range(8):
            em("tensor", lambda E, ic=ic, b=bank, h=half:
               E.matmul(b[0:8, 0:256], silubf[:, ic, :], wmodbf[:, h, ic, :],
                        start=(ic == 0), stop=(ic == 7)))
        wmod_mm_done[jt] = inc("tensor", sT)
        wt("vector", sT, wmod_mm_done[jt])
        em("vector", lambda E, jt=jt, b=bank:
           E.tensor_copy(out=modtm[0:8, jt * 256:(jt + 1) * 256], in_=b[0:8, 0:256]))
        evv = inc("vector", sV)
        wt("tensor", sV, evv)
        for t48 in (2 * jt, 2 * jt + 1):
            vidx, fc = t48 // 8, t48 % 8
            wt("tensor", sV, mod_tr_last[0])
            em("tensor", lambda E, t48=t48:
               E.transpose(pA[:, 0:8], modtm[0:8, t48 * 128:(t48 + 1) * 128], ident[0:8, 0:8]))
            tv = inc("tensor", sT)
            wt("vector", sT, tv)
            em("vector", lambda E, vidx=vidx, fc=fc:
               E.tensor_copy(out=modF[:, vidx, fc, :], in_=pA[:, 0:8]))
            mod_tr_last[0] = inc("vector", sV)

    vedge()
    em("vector", lambda E: E.tensor_scalar(out=modF[:, 1, :, :], in0=modF[:, 1, :, :],
                                           scalar1=1.0, scalar2=None, op0=ALU.add))
    modA_ready = inc("vector", sV)

    vedge()
    em("vector", lambda E: E.tensor_scalar(out=modF[:, 4, :, :], in0=modF[:, 4, :, :],
                                           scalar1=1.0, scalar2=None, op0=ALU.add))
    for vidx in (2, 5):
        g = modF[:, vidx, :, :]
        hi = app[:, 0, 0:64].rearrange("p (a b) -> p a b", b=8)
        lo = app[:, 0, 64:128].rearrange("p (a b) -> p a b", b=8)
        sg = app[:, 0, 128:192].rearrange("p (a b) -> p a b", b=8)
        vedge()
        em("vector", lambda E, g=g, hi=hi: E.tensor_scalar(out=hi, in0=g, scalar1=0.25,
                                                           scalar2=None, op0=ALU.max))
        em("vector", lambda E, g=g, lo=lo: E.tensor_scalar(out=lo, in0=g, scalar1=-0.25,
                                                           scalar2=None, op0=ALU.min))
        em("vector", lambda E, g=g, sg=sg: E.tensor_scalar(out=sg, in0=g, scalar1=0.0,
                                                           scalar2=None, op0=ALU.is_ge))
        vedge()
        em("vector", lambda E, hi=hi, lo=lo: E.tensor_tensor(out=hi, in0=hi, in1=lo,
                                                             op=ALU.subtract))
        em("vector", lambda E, hi=hi, sg=sg: E.tensor_tensor(out=hi, in0=sg, in1=hi,
                                                             op=ALU.mult))
        em("vector", lambda E, g=g, hi=hi, lo=lo: E.tensor_tensor(out=g, in0=hi, in1=lo,
                                                                  op=ALU.add))
    modB_ready = inc("vector", sV)

    def stage_cast_q(dram, dest, q, dest_guard=None):
        if stage_cast.lastcast is None:
            stage_cast.lastcast = {0: wmod_cast_done[22], 1: wmod_cast_done[23]}
        src = dram.rearrange("(i p) o -> p i o", p=128)
        half = q % 2
        ceng, csem = CENG[half]
        wt("sync", csem, stage_cast.lastcast.get(half, 0))
        lv = dma("sync", sQ, stagh[half].rearrange("p (i o) -> p i o", o=256),
                 src[:, :, q * 256:(q + 1) * 256])
        wt(ceng, sQ, lv)
        if dest_guard is not None:
            wt(ceng, dest_guard[0], dest_guard[1])
        sh = stagh[half].rearrange("p (i o) -> p i o", o=256)
        dst = dest[:, :, 2 * q:2 * q + 2, :].rearrange("p i a b -> p i (a b)")
        if half == 0:
            em(ceng, lambda E, d=dst, s=sh: E.tensor_copy(out=d, in_=s))
        else:
            em(ceng, lambda E, d=dst, s=sh: E.activation(out=d, in_=s,
                                                         func=AF.Identity))
        v = inc(ceng, csem)
        stage_cast.lastcast[half] = v
        return v

    def stage_cast(dram, dest, dest_guard=None, nq=4):
        for q in range(nq):
            stage_cast_q(dram, dest, q, dest_guard=dest_guard if q in (0, 1) else None)
        return (stage_cast.lastcast[0], stage_cast.lastcast[1])
    stage_cast.lastcast = None

    def direct_weight(name, dest, dest_guard=None):
        return stage_cast(W_d[name + "T"], dest, dest_guard=dest_guard)

    def emit_ln(src_fn, cvec, svec, mod_wait=0, prep_fn=None, post_fn=None,
                src_qwait_fn=None):
        mmv_par = [0, 0]
        act_par = [0, 0]
        psum_rd = [0]
        pool_done = [0]
        pairs = [(app[:, 0, :], app[:, 1, :]),
                 (esc[:, 0:512], esc[:, 512:1024])]
        h2d = [0]

        def stats(tt):
            if prep_fn is not None:
                prep_fn(tt)
            for fc in range(8):
                par = fc % 2
                s = src_fn(fc, tt)
                wt("vector", sT, mmv_par[par])
                em("vector", lambda E, s=s, par=par: E.tensor_copy(out=bfx[:, par, :], in_=s))
                cv = inc("vector", sV)
                wt("scalar", sV, cv)
                wt("scalar", sT, mmv_par[par])
                em("scalar", lambda E, par=par:
                   E.activation(out=bfsq[:, par, :], in_=bfx[:, par, :], func=AF.Square))
                sq2 = inc("scalar", sS)
                wt("tensor", sV, cv)
                wt("tensor", sS, sq2)
                if fc == 0:
                    wt("tensor", sV, psum_rd[0])
                em("tensor", lambda E, par=par, fc=fc:
                   E.matmul(pS1, onesbf, bfx[:, par, :], start=(fc == 0), stop=(fc == 7)))
                em("tensor", lambda E, par=par, fc=fc:
                   E.matmul(pS2, onesbf, bfsq[:, par, :], start=(fc == 0), stop=(fc == 7)))
                mmv_par[par] = inc("tensor", sT)

        def chain_apply(tt, mmv_tail):
            wt("vector", sT, mmv_tail)
            wt("vector", sS, act_par[0])
            wt("vector", sS, act_par[1])
            wt("vector", sP, pool_done[0])
            em("vector", lambda E: E.tensor_scalar(out=stch[:, 0, :], in0=pS1,
                                                   scalar1=1.0 / 1024.0, scalar2=None, op0=ALU.mult))
            em("vector", lambda E: E.tensor_scalar(out=app[:, 1, :], in0=pS2,
                                                   scalar1=1.0 / 1024.0, scalar2=None, op0=ALU.mult))
            psum_rd[0] = inc("vector", sV)
            if tt < 3:
                stats(tt + 1)
            vedge()
            em("vector", lambda E: E.tensor_tensor(out=app[:, 0, :], in0=stch[:, 0, :],
                                                   in1=stch[:, 0, :], op=ALU.mult))
            em("vector", lambda E: E.tensor_tensor(out=app[:, 1, :], in0=app[:, 1, :],
                                                   in1=app[:, 0, :], op=ALU.subtract))
            chv = inc("vector", sV)
            wt("scalar", sV, chv)
            em("scalar", lambda E: E.activation(out=app[:, 0, :], in_=app[:, 1, :],
                                                func=AF.Sqrt, bias=1e-5, scale=1.0))
            sqv = inc("scalar", sS)
            wt("vector", sS, sqv)
            em("vector", lambda E: E.reciprocal(out=stch[:, 1, :], in_=app[:, 0, :]))
            vedge()
            em("vector", lambda E: E.tensor_tensor(out=stch[:, 2, :], in0=stch[:, 0, :],
                                                   in1=stch[:, 1, :], op=ALU.mult))
            chain_v = inc("vector", sV)
            for fc in range(8):
                par = fc % 2
                s = src_fn(fc, tt)
                t1, t2 = pairs[par]
                if mod_wait and tt == 0 and fc == 0:
                    wt("scalar", sV, mod_wait)
                aeng, asem = "vector", sV
                wt(aeng, sS, act_par[par])
                em(aeng, lambda E, s=s, t1=t1: E.tensor_tensor(out=t1, in0=s,
                                                               in1=stch[:, 1, :], op=ALU.mult))
                em(aeng, lambda E, t1=t1, t2=t2: E.tensor_tensor(out=t2, in0=t1,
                                                                 in1=stch[:, 2, :],
                                                                 op=ALU.subtract))
                av = inc(aeng, asem)
                wt("scalar", asem, av)
                for hf in range(2):
                    b = 2 * tt + hf
                    em("scalar", lambda E, fc=fc, tt=tt, hf=hf, b=b, t2=t2:
                       E.activation(out=H[:, fc, tt * 512 + hf * 256: tt * 512 + hf * 256 + 256],
                                    in_=t2[:, hf * 256:hf * 256 + 256], func=AF.Identity,
                                    bias=modF[:, svec, fc, b:b + 1],
                                    scale=modF[:, cvec, fc, b:b + 1]))
                act_par[par] = inc("scalar", sS)
                h2d[0] = act_par[par]
            if post_fn is not None:
                post_fn(tt)

        stats(0)
        tails = [0, 0, 0, 0]
        tails[0] = mmv_par[1]
        for tt in range(4):
            chain_apply(tt, tails[tt])
            tails[tt + 1 if tt < 3 else 0] = mmv_par[1]
        return h2d[0]

    wt("vector", sQ, xld)
    h1_done = emit_ln(lambda fc, tt: X[:, fc, tt * 512:(tt + 1) * 512], 1, 0,
                      mod_wait=modA_ready, src_qwait_fn=lambda tt: xld)

    wk_cast = stage_cast(W_d["WkTp"], Wslot, dest_guard=(sT, sT.n))
    wt_pair("tensor", wk_cast)
    wt("tensor", sS, h1_done)

    Hr = [H[:, ic, :].rearrange("p (b t) -> p b t", t=256) for ic in range(8)]
    rope_free = {}
    for i3 in range(16):
        hp, bg = i3 // 2, i3 % 2
        bank = pA if i3 % 2 == 0 else pB
        wt("tensor", sV, rope_free.get(i3 - 2, 0))
        for ic in range(8):
            em("tensor", lambda E, ic=ic, hp=hp, bg=bg, bank=bank:
               E.matmul(bank.rearrange("p (b c) -> p b c", c=128),
                        Wslot[:, ic, hp, :], Hr[ic][:, bg * 4:(bg + 1) * 4, 0:128],
                        start=(ic == 0), stop=(ic == 7)))
        mv = inc("tensor", sT)
        wt("vector", sT, mv)
        bank3 = bank.rearrange("p (b c) -> p b c", c=128)
        em("vector", lambda E, bank=bank: E.stream_shuffle(
            out=ropeB.rearrange("p b c -> p (b c)"), in_=bank, mask=SWAP16))
        em("vector", lambda E, bank3=bank3: E.tensor_tensor(
            out=ropeA, in0=bank3, in1=cosF.unsqueeze(1).to_broadcast([128, 4, 128]), op=ALU.mult))
        em("vector", lambda E: E.tensor_tensor(
            out=ropeB, in0=ropeB, in1=sinF.unsqueeze(1).to_broadcast([128, 4, 128]), op=ALU.mult))
        em("vector", lambda E, hp=hp, bg=bg: E.tensor_tensor(
            out=kS[:, hp, bg * 4:(bg + 1) * 4, :], in0=ropeA, in1=ropeB, op=ALU.add))
        rope_free[i3] = inc("vector", sV)

    wv_cast = stage_cast(W_d["WvT"], Wslot, dest_guard=(sT, sT.n))
    wt_pair("tensor", wv_cast)
    vt_free = {}
    for i3 in range(16):
        hp, bg = i3 // 2, i3 % 2
        bank = pA if i3 % 2 == 0 else pB
        tbank = pE if i3 % 2 == 0 else pF
        wtv("tensor", vt_free.get(i3 - 2, (sV, 0)))
        for ic in range(8):
            em("tensor", lambda E, ic=ic, hp=hp, bg=bg, bank=bank:
               E.matmul(bank.rearrange("p (b c) -> p b c", c=128),
                        Wslot[:, ic, hp, :], Hr[ic][:, bg * 4:(bg + 1) * 4, 0:128],
                        start=(ic == 0), stop=(ic == 7)))
        mv = inc("tensor", sT)
        wt("vector", sT, mv)
        em("vector", lambda E, bank=bank:
           E.tensor_copy(out=vtmp.rearrange("p b c -> p (b c)"), in_=bank))
        cv = inc("vector", sV)
        wt("tensor", sV, cv)
        for br in range(4):
            em("tensor", lambda E, br=br, tbank=tbank:
               E.transpose(tbank[:, br * 64:(br + 1) * 64].bitcast(BF16), vtmp[:, br, :], identbf))
        tv = inc("tensor", sT)
        wt("vector", sT, tv)
        em("vector", lambda E, hp=hp, bg=bg, tbank=tbank:
           E.tensor_copy(out=vS[:, bg * 4:(bg + 1) * 4, hp, :],
                         in_=tbank[:, 0:256].bitcast(BF16).rearrange("p (b c) -> p b c", c=128)))
        vt_free[i3] = (sV, inc("vector", sV))
    vproj_done_sT = sT.n

    wff_cast = {}
    wff_pending = [(n, q) for n in ("uf", "ui", "uc", "wf") for q in range(4)]

    def wff_feed(k=2):
        for _ in range(k):
            if not wff_pending:
                return
            n, q = wff_pending.pop(0)
            g = (sS, h1_done) if (n == "uf" and q in (0, 1)) else None
            stage_cast_q(W_d[n + "T"], Wff[n], q, dest_guard=g)
            if q == 3:
                wff_cast[n] = (stage_cast.lastcast[0], stage_cast.lastcast[1])

    def _finish_trunc():
        em("gpsimd", lambda E, v=sV.n: E.wait_ge(sV.h, v))
        em("gpsimd", lambda E, v=sT.n: E.wait_ge(sT.h, v))
        em("gpsimd", lambda E, v=sS.n: E.wait_ge(sS.h, v))
        with nc.Block() as block:
            @block.tensor
            def _(E):
                for fn in ops["tensor"]:
                    fn(E)
            @block.vector
            def _(E):
                for fn in ops["vector"]:
                    fn(E)
            @block.scalar
            def _(E):
                for fn in ops["scalar"]:
                    fn(E)
            @block.sync
            def _(E):
                for fn in ops["sync"]:
                    fn(E)
            @block.gpsimd
            def _(E):
                for fn in ops["gpsimd"]:
                    fn(E)
        st.close()
        return nc

    if trunc == "P3":
        return _finish_trunc()

    wt("tensor", sV, qbd_ready)
    probs_free = [0, 0]
    pEF_free_t = (sV, 0)
    ev_vec = [0]
    pT_free = 0
    pS1_free = 0
    probs_written = 0
    P4B = int(os.environ.get("K_P4B", "8"))
    P4S = int(os.environ.get("K_P4S", "99"))
    for b in range(min(8, P4B)):
        if b >= 1:
            wff_feed(2)
        pbuf = probsb[b % 2]
        for ch in range(2):
            wt("tensor", sV, pS1_free)
            for hp in range(8):
                em("tensor", lambda E, hp=hp, b=b, ch=ch:
                   E.matmul(pS1[:, hp * 64:(hp + 1) * 64], qbd[:, hp, :],
                            kS[:, hp, b, ch * 64:(ch + 1) * 64],
                            start=(hp == 0), stop=(hp == 7)))
            lcv = inc("tensor", sT)
            if P4S < 2:
                continue
            wt("tensor", sV, pT_free)
            for hp in range(8):
                em("tensor", lambda E, hp=hp, b=b, ch=ch:
                   E.matmul(pT[0:64, hp * 128:(hp + 1) * 128],
                            kS[:, hp, b, ch * 64:(ch + 1) * 64], qbd[:, hp, :],
                            start=(hp in (0, 4)), stop=False))
            tTv = inc("tensor", sT)
            if P4S < 3:
                continue
            wt("vector", sT, lcv)
            wt("vector", sS, probs_written)
            pS13 = pS1.rearrange("p (h c) -> p h c", c=64)
            esc3 = esc[:, 0:512].rearrange("p (h c) -> p h c", c=64)
            em("vector", lambda E, pS13=pS13: E.reduce_max(out=Rsc[:, 0:8], in_=pS13, axis=AXL.X))
            em("vector", lambda E, pS13=pS13, esc3=esc3: E.tensor_tensor(
                out=esc3, in0=pS13,
                in1=Rsc[:, 0:8].unsqueeze(2).to_broadcast([128, 8, 64]), op=ALU.subtract))
            em("vector", lambda E, esc3=esc3: E.tensor_tensor(
                out=esc3, in0=esc3,
                in1=masklc.unsqueeze(1).to_broadcast([128, 8, 64]), op=ALU.add))
            pS1_free = inc("vector", sV)
            wt("scalar", sV, pS1_free)
            em("scalar", lambda E: E.activation(out=esc[:, 512:1024], in_=esc[:, 0:512],
                                                func=AF.Exp, bias=0.0, scale=0.125))
            ev = inc("scalar", sS)
            wt("vector", sS, ev)
            em("vector", lambda E, ch=ch: E.reduce_sum(
                out=Rsc[:, 8 + 8 * ch:16 + 8 * ch],
                in_=esc[:, 512:1024].rearrange("p (h c) -> p h c", c=64),
                axis=AXL.X))
            rvv = inc("vector", sV)
            if ch == 1:
                vedge()
                em("vector", lambda E: E.reciprocal(
                    out=rdt.rearrange("p a b -> p (a b)"), in_=Rsc[:, 8:24]))
                inc("vector", sV)
            if P4S < 4:
                continue
            wt("tensor", sV, rvv)
            em("tensor", lambda E: E.transpose(pS2[0:8, 0:128], Rsc[:, 0:8], ident))
            trv = inc("tensor", sT)
            wt("vector", sT, trv)
            em("vector", lambda E: E.tensor_copy(out=RT8, in_=pS2[0:8, 0:128]))
            rt8v = inc("vector", sV)
            if P4S < 5:
                continue
            wt("tensor", sV, rt8v)
            for hp in range(8):
                em("tensor", lambda E, hp=hp:
                   E.matmul(pT[0:64, hp * 128:(hp + 1) * 128], negeye[:, hp, :], RT8,
                            start=False, stop=(hp in (3, 7))))
            rsub = inc("tensor", sT)
            wt("vector", sT, rsub)
            pT3 = pT[0:64, :].rearrange("p (h c) -> p h c", c=128)
            esc4 = esc[0:64, :].rearrange("p (h c) -> p h c", c=128)
            em("vector", lambda E, pT3=pT3, esc4=esc4: E.tensor_tensor(
                out=esc4, in0=pT3, in1=maskT.unsqueeze(1).to_broadcast([64, 8, 128]), op=ALU.add))
            pT_free = inc("vector", sV)
            mkv = pT_free
            wt("scalar", sV, mkv)
            wt("scalar", sT, probs_free[b % 2])
            em("scalar", lambda E, pbuf=pbuf, ch=ch:
               E.activation(out=pbuf[64 * ch:64 * ch + 64, :, :].rearrange("p h c -> p (h c)"),
                            in_=esc[0:64, :], func=AF.Exp, bias=0.0, scale=0.125))
            probs_written = inc("scalar", sS)
        if P4S < 6:
            continue
        wt("tensor", sS, probs_written)
        wtv("tensor", pEF_free_t)
        for ch in range(2):
            cbank = pE if ch == 0 else pF
            for hp in range(8):
                for ph in range(2):
                    em("tensor", lambda E, ch=ch, hp=hp, ph=ph, b=b, cbank=cbank:
                       E.matmul(cbank[64 * ph:64 * ph + 64, hp * 64:(hp + 1) * 64],
                                probsb[b % 2][64 * ch:64 * ch + 64, hp, ph * 64:ph * 64 + 64],
                                vS[64 * ch:64 * ch + 64, b, hp, ph * 64:ph * 64 + 64],
                                start=True, stop=True, skip_group_check=True))
        probs_free[b % 2] = inc("tensor", sT)
        if P4S < 7:
            continue
        wt("vector", sT, probs_free[b % 2])
        em("vector", lambda E: E.tensor_tensor(
            out=app[:, 0, :].rearrange("p (h d) -> p h d", d=64),
            in0=pE.rearrange("p (h d) -> p h d", d=64),
            in1=rdt[:, 0, :].unsqueeze(2).to_broadcast([128, 8, 64]), op=ALU.mult))
        em("vector", lambda E: E.tensor_tensor(
            out=app[:, 1, :].rearrange("p (h d) -> p h d", d=64),
            in0=pF.rearrange("p (h d) -> p h d", d=64),
            in1=rdt[:, 1, :].unsqueeze(2).to_broadcast([128, 8, 64]), op=ALU.mult))
        em("vector", lambda E, b=b:
           E.tensor_tensor(out=attnB[:, b, :, :].rearrange("p h d -> p (h d)"),
                           in0=app[:, 0, :], in1=app[:, 1, :], op=ALU.add))
        evv = inc("vector", sV)
        pEF_free_t = (sV, evv)
        ev_vec[0] = evv
        if b == 3:
            wt("gpsimd", sV, ev_vec[0])
            arw1 = dma("gpsimd", sG, arin1[:, :],
                       attnB.rearrange("p b h d -> p (b h d)")[:, 0:2048])
            wt("gpsimd", sG, arw1)
            em("gpsimd", lambda E: E.collective_compute(
                "AllReduce", ALU.add, replica_groups=[list(range(NCORES))],
                ins=[arin1.ap().opt()], outs=[arout1.ap().opt()]).then_inc(sC.h, 1))
            cc1 = sC.bump(1)

    wff_feed(len(wff_pending))

    if trunc == "P4":
        return _finish_trunc()

    wt("gpsimd", sV, ev_vec[0])
    arw2 = dma("gpsimd", sG, arin2[:, :],
               attnB.rearrange("p b h d -> p (b h d)")[:, 2048:4096])
    wt("gpsimd", sG, arw2)
    em("gpsimd", lambda E: E.collective_compute(
        "AllReduce", ALU.add, replica_groups=[list(range(NCORES))],
        ins=[arin2.ap().opt()], outs=[arout2.ap().opt()]).then_inc(sC.h, 1))
    cc2 = sC.bump(1)
    wt("gpsimd", sC, cc1)
    ar_r1 = dma("gpsimd", sG, attnB.rearrange("p b h d -> p (b h d)")[:, 0:2048],
                arout1[:, :])
    wt("gpsimd", sC, cc2)
    ar_r = dma("gpsimd", sG, attnB.rearrange("p b h d -> p (b h d)")[:, 2048:4096],
               arout2[:, :])

    if trunc == "CC":
        wt("gpsimd", sG, ar_r)
        return _finish_trunc()

    wg_cast = stage_cast(W_d["WgT"], Wslot, dest_guard=(sT, vproj_done_sT))
    wt_pair("tensor", wg_cast)
    g_free = [0, 0]
    eg_free_pair = [0, 0]
    egb2 = [esc[:, 0:256].bitcast(BF16), esc[:, 256:512].bitcast(BF16)]
    xbank = [pS1, pT[:, 0:512]]
    for pc in range(8):
        for pair in range(2):
            dnv = 0
            for j in range(2):
                tt = 2 * pair + j
                bank = pA if j == 0 else pB
                wt("tensor", sS, g_free[j])
                for ic in range(8):
                    em("tensor", lambda E, ic=ic, pc=pc, tt=tt, bank=bank:
                       E.matmul(bank, Wslot[:, ic, pc, :], H[:, ic, tt * 512:(tt + 1) * 512],
                                start=(ic == 0), stop=(ic == 7)))
                gv = inc("tensor", sT)
                wt("scalar", sT, gv)
                wt("scalar", sV, eg_free_pair[j])
                em("scalar", lambda E, bank=bank, j=j:
                   E.activation(out=egb2[j], in_=bank, func=AF.Exp, bias=0.0, scale=0.125))
                g_free[j] = inc("scalar", sS)
                wt("tensor", sS, g_free[j])
                em("tensor", lambda E, j=j:
                   E.matmul(pS2[0:4, 0:512], ind4s[j], egb2[j],
                            start=(j == 0), stop=(j == 1), skip_group_check=True))
                dnv = inc("tensor", sT)
            wt("vector", sT, dnv)
            em("vector", lambda E: E.reciprocal(out=app[0:4, 0, :], in_=pS2[0:4, 0:512]))
            vedge()
            em("vector", lambda E: E.tensor_copy(out=rden, in_=app[0:4, 0, :]))
            rdv = inc("vector", sV)
            wt("tensor", sV, rdv)
            for j in range(2):
                tt = 2 * pair + j
                wt("tensor", sV, eg_free_pair[j])
                em("tensor", lambda E, j=j:
                   E.matmul(xbank[j], ind4Ts[j], rden[0:4, :],
                            start=True, stop=True))
                exv = inc("tensor", sT)
                wt("vector", sT, exv)
                em("vector", lambda E, pc=pc, tt=tt, j=j:
                   E.tensor_tensor(out=GW[:, pc, tt * 512:(tt + 1) * 512], in0=egb2[j],
                                   in1=xbank[j], op=ALU.mult))
                eg_free_pair[j] = inc("vector", sV)

    if trunc == "P5":
        return _finish_trunc()

    wp_cast = stage_cast(W_d["WpT"], Wslot, dest_guard=(sT, sT.n))



    wt("tensor", sG, ar_r1)
    wt("tensor", sV, max(eg_free_pair))
    w_free = [(sV, 0), (sS, 0)]
    for b in range(8):
        if b == 4:
            wt("tensor", sG, ar_r)
        for hp in range(8):
            it_ = hp * 8 + b
            bank = pA if it_ % 2 == 0 else pB
            wtv("tensor", w_free[it_ % 2])
            for ph in range(2):
                em("tensor", lambda E, hp=hp, b=b, ph=ph, bank=bank:
                   E.matmul(bank[64 * ph:64 * ph + 64, 0:256],
                            attnB[64 * ph:64 * ph + 64, b, hp, :],
                            GW[64 * ph:64 * ph + 64, hp, b * 256:(b + 1) * 256],
                            start=True, stop=True, skip_group_check=True))
            wv = inc("tensor", sT)
            if it_ % 2 == 0:
                wt("vector", sT, wv)
                em("vector", lambda E, hp=hp, b=b, bank=bank:
                   E.tensor_copy(out=GW[:, hp, b * 256:(b + 1) * 256], in_=bank[:, 0:256]))
                w_free[0] = (sV, inc("vector", sV))
            else:
                wt("scalar", sT, wv)
                em("scalar", lambda E, hp=hp, b=b, bank=bank:
                   E.activation(out=GW[:, hp, b * 256:(b + 1) * 256], in_=bank[:, 0:256],
                                func=AF.Identity))
                w_free[1] = (sS, inc("scalar", sS))

    if trunc == "P6":
        return _finish_trunc()

    wt_pair("tensor", wp_cast)
    wtv("tensor", w_free[0])
    wtv("tensor", w_free[1])
    wt("vector", sV, modB_ready)
    x_r = x_d.rearrange("(f p) b t -> p f b t", p=128)
    x2_r = x2_d.rearrange("(f p) b t -> p f b t", p=128)
    p7_free = [0, 0]
    x2b_free = [0, 0]
    xrd_v = [0, 0]
    xrd_used = [0, 0]
    wt("gpsimd", sT, sT.n)
    for oc in range(8):
        wt("gpsimd", sV, xrd_used[oc % 2])
        xrd_v[oc % 2] = dma("gpsimd", sG, xrd[oc % 2].rearrange("p b t -> p (b t)"),
                            x_r[:, oc, :, :].rearrange("p b t -> p (b t)"))
        for tt in range(4):
            it_ = oc * 4 + tt
            bank = pA if it_ % 2 == 0 else pB
            wt("tensor", sV, p7_free[it_ % 2])
            for ic in range(8):
                em("tensor", lambda E, ic=ic, oc=oc, tt=tt, bank=bank:
                   E.matmul(bank, Wslot[:, ic, oc, :], GW[:, ic, tt * 512:(tt + 1) * 512],
                            start=(ic == 0), stop=(ic == 7)))
            pv = inc("tensor", sT)
            wt("vector", sT, pv)
            wt("vector", sG, xrd_v[oc % 2])
            wt("vector", sG, x2b_free[tt % 2])
            for hf in range(2):
                b = tt * 2 + hf
                em("vector", lambda E, oc=oc, tt=tt, hf=hf, b=b, bank=bank:
                   E.scalar_tensor_tensor(
                       out=x2b[tt % 2][:, hf, :],
                       in0=bank[:, hf * 256:hf * 256 + 256],
                       scalar=modF[:, 2, oc, b:b + 1],
                       in1=xrd[oc % 2][:, b, :],
                       op0=ALU.mult, op1=ALU.add))
            p7_free[it_ % 2] = inc("vector", sV)
            xrd_used[oc % 2] = p7_free[it_ % 2]
            wt("gpsimd", sV, p7_free[it_ % 2])
            x2b_free[tt % 2] = dma(
                "gpsimd", sG,
                x2_r[:, oc, 2 * tt:2 * tt + 2, :].rearrange("p b t -> p (b t)"),
                x2b[tt % 2].rearrange("p a t -> p (a t)"))
    x2_written = sG.n
    p7_done_sT = sT.n
    for n in ("wi", "wc"):
        wff_cast[n] = direct_weight(n, Wff[n], dest_guard=(sT, p7_done_sT) if n == "wi" else None)
    if trunc == "P7":
        return _finish_trunc()

    x2s_free = [(0, 0), (0, 0)]
    if trunc == "P7b":
        return _finish_trunc()

    x2bufs = [x2stg, x2stgB]
    p8_lv = [0, 0]

    def p8_load(tt):
        wt("sync", sG, x2_written)
        if tt % 2 == 0:
            wt("sync", sV, stage_cast.lastcast[0])
            wt("sync", sS, stage_cast.lastcast[1])
        wt("sync", sV, x2s_free[tt % 2][0])
        wt("sync", sP, x2s_free[tt % 2][1])
        return dma("sync", sQ, x2bufs[tt % 2], x2_r[:, :, 2 * tt:2 * tt + 2, :])

    def p8_prep(tt):
        p8_lv[tt % 2] = p8_load(tt)
        wt("vector", sQ, p8_lv[tt % 2])

    def p8_post(tt):
        x2s_free[tt % 2] = (sV.n, sP.n)

    h2_done = emit_ln(lambda fc, tt: x2bufs[tt % 2][:, fc, :, :].rearrange("p a t -> p (a t)"),
                      4, 3, prep_fn=p8_prep, post_fn=p8_post,
                      src_qwait_fn=lambda tt: p8_lv[tt % 2])

    wt("sync", sS, h2_done)
    c0v = dma("sync", sQ, cS.rearrange("p b t -> p (b t)"), inc0_d.rearrange("p b t -> p (b t)"))
    wt("vector", sQ, c0v)
    em("vector", lambda E: E.tensor_copy(out=cbf[0].rearrange("p b t -> p (b t)"),
                                         in_=cS.rearrange("p b t -> p (b t)")))
    c0_ready = inc("vector", sV)

    wt("sync", sS, h2_done)
    wov = stage_cast(W_d["woT"], Wff["wo"], dest_guard=(sT, p7_done_sT))

    wt("tensor", sV, c0_ready)
    wt("tensor", sS, h2_done)

    GATES = (("f", AF.Sigmoid, 0), ("i", AF.Sigmoid, 1), ("c", AF.Tanh, 2))
    gate_waited = set()
    bank_free = {}
    step_done = 0
    for t in range(8):
        cbf_r = cbf[t % 2]
        cbf_w = cbf[(t + 1) % 2]
        wt("tensor", sV, step_done)
        for oc in range(8):
            par = oc % 2
            banks = {"f": (pA if par == 0 else pB),
                     "i": (pE if par == 0 else pF),
                     "c": (pT[:, 0:256] if par == 0 else pT[:, 512:768])}
            gv = {}
            for gname, gfunc, gidx in GATES:
                if gname not in gate_waited:
                    gate_waited.add(gname)
                    wt_pair("tensor", wff_cast["w" + gname])
                    wt_pair("tensor", wff_cast["u" + gname])
                wt("tensor", sV, bank_free.get((gname, par), 0))
                wname, uname = "w" + gname, "u" + gname
                for ic in range(8):
                    em("tensor", lambda E, ic=ic, oc=oc, t=t, gn=gname, bank=banks[gname]:
                       E.matmul(bank[:, 0:256] if bank.shape[1] > 256 else bank,
                                Wff["w" + gn][:, ic, oc, :], H[:, ic, t * 256:(t + 1) * 256],
                                start=(ic == 0), stop=False))
                for ic in range(8):
                    em("tensor", lambda E, ic=ic, oc=oc, gn=gname, bank=banks[gname], cb=cbf_r:
                       E.matmul(bank[:, 0:256] if bank.shape[1] > 256 else bank,
                                Wff["u" + gn][:, ic, oc, :], cb[:, ic, :],
                                start=False, stop=(ic == 7)))
                gv[gname] = inc("tensor", sT)
            dst = {"f": gscF, "i": gscI, "c": gscG}
            for gname, gfunc, gidx in GATES:
                wt("scalar", sT, gv[gname])
                em("scalar", lambda E, gn=gname, gf=gfunc, gi=gidx, oc=oc, par=par,
                   bank=banks[gname]:
                   E.activation(out=dst[gn][:, par, :],
                                in_=bank[:, 0:256] if bank.shape[1] > 256 else bank,
                                func=gf, bias=fbias[:, gi, oc:oc + 1], scale=1.0))
            actv = inc("scalar", sS)
            wt("vector", sS, actv)
            em("vector", lambda E, oc=oc, par=par: E.tensor_tensor(
                out=cS[:, oc, :], in0=gscF[:, par, :], in1=cS[:, oc, :], op=ALU.mult))
            em("vector", lambda E, par=par: E.tensor_tensor(
                out=gscI[:, par, :], in0=gscI[:, par, :], in1=gscG[:, par, :], op=ALU.mult))
            em("vector", lambda E, oc=oc, par=par: E.tensor_tensor(
                out=cS[:, oc, :], in0=cS[:, oc, :], in1=gscI[:, par, :], op=ALU.add))
            em("vector", lambda E, oc=oc, cw=cbf_w: E.tensor_copy(
                out=cw[:, oc, :], in_=cS[:, oc, :]))
            bv = inc("vector", sV)
            for gname, _, _ in GATES:
                bank_free[(gname, par)] = bv
        step_done = sV.n

    uov = stage_cast(W_d["uoT"], Wff["uo"], dest_guard=(sT, sT.n))

    wt_pair("tensor", uov)
    wt("tensor", sS, sS.n)
    cbf_fin = cbf[0]
    for oc in range(8):
        bank = pE if oc % 2 == 0 else pF
        for ic in range(8):
            em("tensor", lambda E, ic=ic, oc=oc, bank=bank:
               E.matmul(bank[:, 0:256], Wff["uo"][:, ic, oc, :], cbf_fin[:, ic, :],
                        start=(ic == 0), stop=(ic == 7)))
        uv = inc("tensor", sT)
        wt("scalar", sT, uv)
        em("scalar", lambda E, oc=oc, bank=bank:
           E.activation(out=uodv[:, oc, :], in_=bank[:, 0:256], func=AF.Identity,
                        bias=fbias[:, 3, oc:oc + 1], scale=1.0))
    uod_done = inc("scalar", sS)

    wt_pair("tensor", wov)
    wt("tensor", sS, uod_done)
    xt_v = [0, 0]
    xt_used = [0, 0]
    tail_free = [0, 0]
    ybuf_free = [0, 0]
    yv_last = 0
    for oc in range(8):
        wt("sync", sV, max(xt_used[oc % 2], uov[0]))
        wt("sync", sS, uov[1])
        xt_v[oc % 2] = dma("sync", sQ, xt[oc % 2].rearrange("p b t -> p (b t)"),
                           x2_r[:, oc, :, :].rearrange("p b t -> p (b t)"))
        for t in range(8):
            par = (oc * 8 + t) % 2
            bank = pA if par == 0 else pB
            wt("tensor", sV, tail_free[par])
            for ic in range(8):
                em("tensor", lambda E, ic=ic, oc=oc, t=t, bank=bank:
                   E.matmul(bank[:, 0:256], Wff["wo"][:, ic, oc, :],
                            H[:, ic, t * 256:(t + 1) * 256],
                            start=(ic == 0), stop=(ic == 7)))
            xv = inc("tensor", sT)
            wt("vector", sT, xv)
            em("vector", lambda E, oc=oc, bank=bank, par=par: E.tensor_tensor(
                out=app[:, par, 0:256], in0=bank[:, 0:256], in1=uodv[:, oc, :], op=ALU.add))
            s1v = inc("vector", sV)
            wt("scalar", sV, s1v)
            em("scalar", lambda E, par=par: E.activation(
                out=app[:, par, 256:512], in_=app[:, par, 0:256], func=AF.Tanh))
            s2v = inc("scalar", sS)
            ybuf = ybufA if t < 4 else ybufB
            yb = 0 if t < 4 else 1
            wt("vector", sS, s2v)
            wt("vector", sQ, xt_v[oc % 2])
            if t in (0, 4):
                wt("vector", sO, ybuf_free[yb])
            em("vector", lambda E, oc=oc, t=t, par=par, ybuf=ybuf: E.scalar_tensor_tensor(
                out=ybuf[:, t % 4, :], in0=app[:, par, 256:512],
                scalar=modF[:, 5, oc, t:t + 1], in1=xt[oc % 2][:, t, :],
                op0=ALU.mult, op1=ALU.add))
            tail_free[par] = inc("vector", sV)
            xt_used[oc % 2] = tail_free[par]
            if t in (3, 7):
                wt("gpsimd", sV, tail_free[par])
                y_r = y_d.rearrange("(f p) b t -> p f b t", p=128)
                tb = t // 4
                ybuf_free[yb] = dma("gpsimd", sO,
                                    y_r[:, oc, 4 * tb:4 * tb + 4, :].rearrange("p b t -> p (b t)"),
                                    ybuf.rearrange("p a t -> p (a t)"))
                yv_last = ybuf_free[yb]

    if DBG:
        wt("gpsimd", sV, sV.n)
        wt("gpsimd", sS, sS.n)
        wt("gpsimd", sT, sT.n)
        dma("gpsimd", sO, dbg_d["h2"].bitcast(BF16), H.rearrange("p a b -> p (a b)"))
        dma("gpsimd", sO, dbg_d["x2"], x2_d)
        dma("gpsimd", sO, dbg_d["cfin"], cS.rearrange("p a b -> p (a b)"))
        dma("gpsimd", sO, dbg_d["uod"], uodv.rearrange("p a b -> p (a b)"))
    em("gpsimd", lambda E, v=sO.n: E.wait_ge(sO.h, v))

    with nc.Block() as block:
        @block.tensor
        def _(E):
            for fn in ops["tensor"]:
                fn(E)

        @block.vector
        def _(E):
            for fn in ops["vector"]:
                fn(E)

        @block.scalar
        def _(E):
            for fn in ops["scalar"]:
                fn(E)

        @block.sync
        def _(E):
            for fn in ops["sync"]:
                fn(E)

        @block.gpsimd
        def _(E):
            for fn in ops["gpsimd"]:
                fn(E)

    st.close()
    return nc


DBG_SRC = {}

_prog_cache = {}


def kernel(**inputs):
    shared = _shared_layouts(inputs)
    nc = build_program()
    in_maps = []
    for core in range(NCORES):
        m = dict(shared)
        m.update(_host_constants(core))
        m["x_fm"] = _core_x(inputs, core)
        dm = {"x_fm": m["x_fm"], "wmod": m["wmod"], "cT": m["cT"], "qbd": m["qbd"],
              "ident": m["ident"], "ident_bf": m["ident_bf"], "ones_bf": m["ones_bf"],
              "negeye": m["negeye"], "ind2": m["ind2"], "ind2T": m["ind2T"],
              "ind4a": m["ind4a"], "ind4b": m["ind4b"],
              "ind4T0": m["ind4T0"], "ind4T1": m["ind4T1"],
              "mask_lc": m["mask_lc"], "mask_T": m["mask_T"],
              "cosF": m["cosF"], "sinF": m["sinF"], "in_c0": m["in_c0"]}
        for n in ("WkTp", "WvT", "WgT", "WpT", "wfT", "ufT", "wiT", "uiT", "wcT", "ucT",
                  "woT", "uoT"):
            dm[n] = m[n]
        for n in ("bf", "bi", "bc", "bo"):
            dm[n + "r"] = m[n + "r"]
        in_maps.append(dm)
    res = run_bass_kernel_spmd(nc, in_maps, core_ids=list(range(NCORES)))
    return _assemble_output([res.results[i]["y_fm"] for i in range(NCORES)])
